# version 03
# baseline (speedup 1.0000x reference)
import numpy as np
from contextlib import ExitStack

import concourse.bass as bass
import concourse.mybir as mybir
from concourse.bass_utils import run_bass_kernel_spmd

F32 = mybir.dt.float32
BF16 = mybir.dt.bfloat16
I32 = mybir.dt.int32
AF = mybir.ActivationFunctionType
ALU = mybir.AluOpType
AX = mybir.AxisListType

BATCH, T, D = 2, 8192, 1024
NCORE = 8
TOK = (BATCH * T) // NCORE
NCH = D // 128
NTT = TOK // 512
HEADS = 16
NHP = 8
DH = 64
L0_CS = 1024
FF_I = 4096
L0_E = 8
L1_E = 6
L1_I = 1024
L1_C = 2
EPS_LN = 1e-5
EPS_RMS = 1e-6

_STAGE = None
_PARTS = None
_ROUTER_RAW = False
_L1_ITERS = 2
_L1_PARTS = None
_ATTN_PARTS = None
_WO_MM_ONLY = False

ENGINES = ("T", "S", "V", "G", "Y")


class Sem:
    __slots__ = ("h", "n", "name")

    def __init__(self, h, name):
        self.h, self.n, self.name = h, 0, name


class Arena:
    def __init__(self, t, cols, name):
        self.t, self.cols, self.name = t, cols, name
        self.off = 0

    def alloc(self, n):
        assert self.off + n <= self.cols, f"{self.name} overflow {self.off}+{n}>{self.cols}"
        r = self.off
        self.off += n
        return r

    def mark(self):
        return self.off

    def release(self, m):
        self.off = m


class SmallMap:
    def __init__(self):
        self._o = 0
        t = self.take
        self.bout = t(8); self.sh_b1 = t(32); self.sh_b2 = t(8)
        self.eb1 = t(256); self.eb2 = t(64)
        self.bq = t(8); self.bk = t(8); self.bv = t(8); self.bo = t(8)
        self.exp_bup = t(48); self.exp_bg = t(48); self.exp_bdown = t(48)
        self.share_bup = t(16); self.share_bg = t(8); self.share_bdown = t(8)
        self.norm_w = t(16); self.gate_w = t(64); self.router_W = t(96)
        self.logpos = t(16)
        self.wl1 = t(1); self.bl1 = t(1); self.wl2t = t(16); self.bl2 = t(1)
        self.gate_b = t(1); self.router_b = t(2); self.gate_bias = t(2)
        self.q0col = t(1)
        self.hsel = t(2)
        self.rb_row = t(12)
        self.gb_row = t(12)
        self.persist = self._o
        self.logpos_row = t(TOK)
        self.wl1row = t(64)
        self.total = self._o

    def take(self, n):
        r = self._o
        self._o += n
        return r


SM = SmallMap()


def _pack_small(inp, core):
    cols = []

    def vec(v):
        v = np.asarray(v, np.float32).reshape(-1)
        assert v.size % 128 == 0
        cols.append(v.reshape(-1, 128).T)

    def pad_col(v):
        c = np.zeros((128, 1), np.float32)
        v = np.asarray(v, np.float32).reshape(-1)
        c[: v.size, 0] = v
        cols.append(c)

    q0 = core * TOK
    vec(inp["L0_attn_bout"]); vec(inp["L0_ffn_sh_b1"]); vec(inp["L0_ffn_sh_b2"])
    vec(np.asarray(inp["L0_ffn_eb1"]).reshape(-1))
    vec(np.asarray(inp["L0_ffn_eb2"]).reshape(-1))
    vec(inp["L1_attn_bq"]); vec(inp["L1_attn_bk"]); vec(inp["L1_attn_bv"]); vec(inp["L1_attn_bo"])
    vec(np.asarray(inp["L1_ffn_exp_bup"]).reshape(-1))
    vec(np.asarray(inp["L1_ffn_exp_bg"]).reshape(-1))
    vec(np.asarray(inp["L1_ffn_exp_bdown"]).reshape(-1))
    vec(inp["L1_ffn_share_bup"]); vec(inp["L1_ffn_share_bg"]); vec(inp["L1_ffn_share_bdown"])
    vec(np.asarray(inp["L1_ffn_norm_w"]).reshape(-1))
    gw = np.asarray(inp["L0_ffn_gate_w"], np.float32)
    cols.append(gw.reshape(8, 128, 8).transpose(1, 0, 2).reshape(128, 64))
    rw = np.asarray(inp["L1_ffn_router_W"], np.float32)
    cols.append(rw.reshape(2, 8, 128, 6).transpose(2, 0, 1, 3).reshape(128, 96))
    lp = np.log(np.arange(q0, q0 + TOK, dtype=np.float64) + 1.0).astype(np.float32)
    cols.append(lp.reshape(16, 128).T)
    pad_col(np.asarray(inp["L0_attn_Wl1"], np.float32)[:, 0])
    pad_col(inp["L0_attn_bl1"])
    w2t = np.zeros((128, 16), np.float32)
    w2t[:64, :] = np.asarray(inp["L0_attn_Wl2"], np.float32).T
    cols.append(w2t)
    pad_col(inp["L0_attn_bl2"])
    pad_col(inp["L0_ffn_gate_b"])
    rb = np.zeros((128, 2), np.float32)
    rb[:6, :] = np.asarray(inp["L1_ffn_router_b"], np.float32).T
    cols.append(rb)
    gb = np.zeros((128, 2), np.float32)
    gb[:6, :] = np.asarray(inp["L1_ffn_gate_bias"], np.float32).T
    cols.append(gb)
    cols.append(np.full((128, 1), float(q0), np.float32))
    hs = np.zeros((128, 2), np.float32)
    hs[:64, 0] = 1.0
    hs[64:, 1] = 1.0
    cols.append(hs)
    rr = np.zeros((128, 12), np.float32)
    rr[0, :] = np.asarray(inp["L1_ffn_router_b"], np.float32).reshape(-1)
    cols.append(rr)
    gg = np.zeros((128, 12), np.float32)
    gg[0, :] = np.asarray(inp["L1_ffn_gate_bias"], np.float32).reshape(-1)
    cols.append(gg)
    lr = np.zeros((128, TOK), np.float32)
    lr[0, :] = lp
    cols.append(lr)
    w1r = np.zeros((128, 64), np.float32)
    w1r[0, :] = np.asarray(inp["L0_attn_Wl1"], np.float32)[:, 0]
    cols.append(w1r)
    out = np.concatenate(cols, axis=1).astype(np.float32)
    assert out.shape[1] == SM.total
    return np.ascontiguousarray(out)


def _pack_consts():
    import ml_dtypes
    ident = np.eye(128, dtype=np.float32)
    ones = np.ones((128, 128), dtype=np.float32)
    return np.ascontiguousarray(
        np.concatenate([ident, ones], axis=1).astype(ml_dtypes.bfloat16))


class Builder:
    def __init__(self, stage=None):
        self.stage = stage
        self.nc = bass.Bass("TRN2", target_bir_lowering=False, debug=False)
        self.q = {e: [] for e in ENGINES}
        self.floor = {e: {} for e in ENGINES}
        self.st = ExitStack()

    def wait(self, e, sem, val):
        if val is None or val <= 0:
            return
        if self.floor[e].get(sem.name, 0) >= val:
            return
        self.floor[e][sem.name] = val
        self.q[e].append(lambda eng, s=sem.h, v=val: eng.wait_ge(s, v))

    def do(self, e, build, sem=None, k=1):
        if sem is None:
            self.q[e].append(build)
            return None
        sem.n += k
        val = sem.n
        self.q[e].append(lambda eng, b=build, s=sem.h, kk=k: b(eng).then_inc(s, kk))
        return val

    def dma(self, qeng, build):
        pool = self.dmapool[qeng]
        idx = self.dmaidx[qeng]
        self.dmaidx[qeng] = (idx + 1) % len(pool)
        s = pool[idx]
        self.wait(qeng, s, s.n)
        val = self.do(qeng, build, s, 16)
        return (s, val)

    def dwait(self, e, tok):
        self.wait(e, tok[0], tok[1])

    def barrier(self):
        for qe, pool in self.dmapool.items():
            for s in pool:
                self.wait(qe, s, s.n)
        for e, s in (("T", self.TS), ("S", self.SS), ("V", self.VS), ("G", self.GS)):
            self.wait(e, s, s.n)
        toks = {}
        for e in ENGINES:
            s = self.BAR[e]
            s.n += 1
            toks[e] = s.n
            self.q[e].append(lambda eng, sh=s.h: eng.nop(nofuse=True).then_inc(sh, 1))
        for e in ENGINES:
            for o in ENGINES:
                if o != e:
                    self.wait(e, self.BAR[o], toks[o])

    def sem(self, name):
        return Sem(self.st.enter_context(self.nc.semaphore(name)), name)

    def build(self):
        nc = self.nc
        stage = self.stage

        def din(name, shape, dt=F32):
            return nc.dram_tensor(name, shape, dt, kind="ExternalInput").ap()

        self.x_fm = din("x_fm", [D, TOK])
        self.smallp = din("smallp", [128, SM.total])
        self.constb = din("constb", [128, 256], BF16)
        self.w_qkvT = din("w_qkvT", [D, 3 * D])
        self.w_outT = din("w_outT", [D, D])
        self.sh_w1 = din("sh_w1", [D, FF_I])
        self.sh_w2 = din("sh_w2", [FF_I, D])
        self.ew1 = din("ew1", [L0_E, D, FF_I])
        self.ew2 = din("ew2", [L0_E, FF_I, D])
        self.w_qT = din("w_qT", [D, D])
        self.w_kT = din("w_kT", [D, D])
        self.w_vT = din("w_vT", [D, D])
        self.w_oT = din("w_oT", [D, D])
        self.exp_Wup = din("exp_Wup", [L1_E, D, L1_I])
        self.exp_Wg = din("exp_Wg", [L1_E, L1_I, L1_I])
        self.exp_Wdown = din("exp_Wdown", [L1_E, L1_I, D])
        self.share_Wup = din("share_Wup", [D, 2 * L1_I])
        self.share_Wg = din("share_Wg", [L1_I, L1_I])
        self.share_Wdown = din("share_Wdown", [L1_I, D])

        self.out_fm = nc.dram_tensor("out_fm", [D, TOK], F32, kind="ExternalOutput").ap()
        self.xres = nc.dram_tensor("xres", [128, NCH * TOK], F32).ap()
        self.s_sp = nc.dram_tensor("s_sp", [128, NCH * TOK], F32).ap()
        self.f_sp = nc.dram_tensor("f_sp", [128, NCH * TOK], F32).ap()
        self.KV_K0 = 0
        self.KV_V0 = NCH * TOK
        self.KV_ST = self.KV_V0 + (TOK // 128) * HEADS * 65
        self.KV_C = self.KV_ST + 1
        self.mr_sp = nc.dram_tensor("mr_sp", [1, 2 * TOK], F32).ap()
        self.kv_loc = nc.dram_tensor("kv_loc", [128, self.KV_C], BF16).ap()
        self.kv_in_h = nc.dram_tensor("kv_in", [128, self.KV_C], BF16)
        self.kv_out_h = nc.dram_tensor("kv_out", [4 * 128, self.KV_C], BF16)
        self.kv_in = self.kv_in_h.ap()
        self.kv_out = self.kv_out_h.ap()
        self.dbg = None
        if stage is not None:
            self.dbg = nc.dram_tensor("dbg", [128, stage[1]], F32, kind="ExternalOutput").ap()

        st = self.st
        with st:
            self.BAR = {e: self.sem("bar" + e) for e in ENGINES}
            self.CC = self.sem("cc")
            self.TS = self.sem("t")
            self.SS = self.sem("s")
            self.VS = self.sem("v")
            self.GS = self.sem("g")
            self.dmapool = {
                "Y": [self.sem(f"dy{i}") for i in range(10)],
                "G": [self.sem(f"dg{i}") for i in range(10)],
            }
            self.dmaidx = {"Y": 0, "G": 0}

            sb = lambda n, s, d=F32: st.enter_context(nc.sbuf_tensor(n, s, d))
            self.small = sb("small", [128, SM.persist])
            self.cidb = sb("cidb", [128, 128], BF16)
            self.cones = sb("cones", [128, 128], BF16)
            self.onesf = sb("onesf", [128, 128], F32)
            self.identf = sb("identf", [128, 128], F32)
            self.iox = st.enter_context(nc.sbuf_tensor("iox", [128, 512], I32))
            self.A_t = sb("A", [128, 49152], BF16)
            self.FA_t = sb("FA", [128, 16384], F32)
            self.W_t = sb("W", [128, 8192], BF16)
            self.SC_t = sb("SC", [128, 5120], F32)
            self.PS_t = st.enter_context(nc.psum_tensor("PS", [128, 3584], F32))
            self.PSB_t = st.enter_context(nc.psum_tensor("PSB", [128, 1024], BF16))
            self.A = Arena(self.A_t, 49152, "A")
            self.FA = Arena(self.FA_t, 16384, "FA")
            self.W = Arena(self.W_t, 8192, "W")
            self.SC = Arena(self.SC_t, 5120, "SC")

            self.emit_all()
            with nc.Block() as blk:
                for key, dec in (("T", blk.tensor), ("S", blk.scalar),
                                 ("V", blk.vector), ("G", blk.gpsimd),
                                 ("Y", blk.sync)):
                    def make(key):
                        def body(eng):
                            for fn in self.q[key]:
                                fn(eng)
                        return body
                    dec(make(key))
        return nc

    def bank(self, i, n=512, p0=0, p1=128):
        assert 0 <= i < 7
        return self.PS_t[p0:p1, i * 512: i * 512 + n]

    def smc(self, off, n=1, p0=0, p1=128):
        return self.small[p0:p1, off: off + n]

    def emit_all(self):
        em = self
        t1 = em.dma("Y", lambda e: e.dma_start(out=em.small[:], in_=em.smallp[:, 0:SM.persist]))
        t2 = em.dma("Y", lambda e: e.dma_start(out=em.cidb[:], in_=em.constb[:, 0:128]))
        t3 = em.dma("Y", lambda e: e.dma_start(out=em.cones[:], in_=em.constb[:, 128:256]))
        vt = em.do("V", lambda e: e.memset(em.onesf[:], 1.0), em.VS, 1)
        em.dwait("V", t2)
        vt2 = em.do("V", lambda e: e.tensor_copy(em.identf[:], em.cidb[:]), em.VS, 1)
        for e in ENGINES:
            for t in (t1, t2, t3):
                em.dwait(e, t)
            em.wait(e, em.VS, vt2)

        self.MBC, self.RBC, self.SCT = 0, 2048, 4096

        self.XN = self.A.alloc(NCH * TOK)
        self.QA = self.A.alloc(NCH * TOK)
        self.LAM = self.SCT
        self.B0C = self.SCT + 256
        self.B1C = self.SCT + 257
        self.QM2 = self.SCT + 258
        self.KM2 = self.SCT + 259
        self.TMP1 = self.SCT + 260
        self.KALL = self.SCT + 264

        self.xn_tok = self.emit_ln(lambda c: self.x_fm[c * 128:(c + 1) * 128, :], EPS_LN)
        if self.stage and self.stage[0] == "ln0":
            self.dump_stage([(self.A_t, self.XN, NCH * TOK, True),
                             (self.SC_t, 0, 2 * TOK, False)])
            return
        self.barrier()
        self.emit_qkv(0)
        self.barrier()
        if self.stage and self.stage[0] == "qkv0":
            self.dump_stage([(self.A_t, self.QA, NCH * TOK, True)])
            return
        self.emit_attn(0)
        self.barrier()
        if self.stage and self.stage[0] == "s0":
            self.dump_from_dram(self.s_sp)
            return
        self.emit_ffn(0)
        if getattr(self, "_early", False):
            return
        self.barrier()
        if self.stage and self.stage[0] == "f0":
            self.dump_from_dram(self.f_sp)
            return
        self.emit_combine0()
        self.barrier()
        if self.stage and self.stage[0] == "x1":
            self.dump_from_dram(self.xres)
            return
        self.xn_tok = self.emit_ln(lambda c: self.xres[:, c * TOK:(c + 1) * TOK], EPS_LN)
        self.barrier()
        self.emit_qkv(1)
        self.barrier()
        self.emit_ffn(1)
        if getattr(self, "_early", False):
            return
        self.barrier()
        if self.stage and self.stage[0] == "f1":
            self.dump_from_dram(self.f_sp)
            return
        self.emit_attn(1)
        self.barrier()
        if self.stage and self.stage[0] == "s1":
            self.dump_from_dram(self.s_sp)
            return

    def dump_from_dram(self, src):
        em = self
        self.barrier()
        for c in range(NCH):
            for tt in range(NTT):
                t = em.dma("Y", lambda e, c=c, tt=tt: e.dma_start(
                    out=self.SC_t[:, 0:512],
                    in_=src[:, c * TOK + tt * 512: c * TOK + (tt + 1) * 512]))
                em.dwait("Y", t)
                t2 = em.dma("Y", lambda e, c=c, tt=tt: e.dma_start(
                    out=self.dbg[:, c * TOK + tt * 512: c * TOK + (tt + 1) * 512],
                    in_=self.SC_t[:, 0:512]))
                em.dwait("Y", t2)
        self.barrier()

    def emit_proj(self, MC, KC, wsrc_fn, rhs_fn, evac_fn, tts=None, name="", tdeps=()):
        em = self
        W_t = self.W_t
        if tts is None:
            tts = list(range(NTT))
        wm = self.W.mark()
        R = min(3, (self.W.cols - self.W.off) // (KC * 128))
        assert R >= 2, f"W ring too small for KC={KC}"
        slots = [self.W.alloc(KC * 128) for _ in range(R)]
        slot_rel = [self.TS.n] * R
        wtok = {}

        def prefetch(m):
            s = m % R
            if slot_rel[s] is not None:
                em.wait("G", em.TS, slot_rel[s])
            wtok[m] = em.dma("G", lambda e, m=m, s=s: e.dma_start(
                out=W_t[:, slots[s]: slots[s] + KC * 128].rearrange(
                    "p (kc m) -> p kc m", m=128),
                in_=wsrc_fn(m), max_dma_last_dim=4096))

        for m in range(min(MC, R - 1)):
            prefetch(m)
        for sem_, val_ in tdeps:
            self.wait("T", sem_, val_)
        for m in range(MC):
            s = m % R
            if m + R - 1 < MC:
                prefetch(m + R - 1)
            em.dwait("T", wtok[m])
            last_tok = None
            for ti, tt in enumerate(tts):
                b = self.pb_rot
                self.pb_rot = (self.pb_rot + 1) % 6
                if self.bank_rel.get(b) is not None:
                    em.wait("T", self.bank_rel[b][0], self.bank_rel[b][1])
                tok = None
                for k in range(KC):
                    tok = em.do("T", lambda e, tt=tt, k=k, s=s, b=b: e.matmul(
                        self.bank(b),
                        W_t[:, slots[s] + k * 128: slots[s] + (k + 1) * 128],
                        rhs_fn(k, tt), start=(k == 0), stop=(k == KC - 1)),
                        em.TS if k == KC - 1 else None, 1)
                self.bank_rel[b] = evac_fn(m, tt, b, tok)
                last_tok = tok
            slot_rel[s] = last_tok
        self.W.release(wm)

    def emit_qkv(self, layer):
        em = self
        A_t, FA_t, SC_t, W_t = self.A_t, self.FA_t, self.SC_t, self.W_t
        self.pb_rot = 0
        self.bank_rel = {}
        amark = self.A.mark()
        famark = self.FA.mark()
        ktile = [self.A.alloc(TOK) for _ in range(2)]
        vtile = [self.A.alloc(TOK) for _ in range(2)]
        vexp = [self.A.alloc(16 * 130) for _ in range(2)]
        stb = [self.A.alloc(1) for _ in range(2)]
        sqr = [self.FA.alloc(TOK) for _ in range(2)]
        kvdst = self.kv_loc if layer == 0 else self.kv_in
        mz0 = em.do("V", lambda e: e.memset(A_t[:, vexp[0]: vexp[0] + 16 * 130], 1.0), em.VS, 1)
        mz1 = em.do("V", lambda e: e.memset(A_t[:, vexp[1]: vexp[1] + 16 * 130], 1.0), em.VS, 1)
        zq = em.do("V", lambda e: e.memset(SC_t[:, self.QM2: self.QM2 + 2], 0.0), em.VS, 1)
        ring_rel = {}

        def rhs_xn(k, tt):
            return A_t[:, self.XN + k * TOK + tt * 512: self.XN + k * TOK + (tt + 1) * 512]

        def stats_emit(tile_a0, dstcol, m, evtoks):
            r = m % 2
            if ("sq", r) in ring_rel:
                em.wait("S", em.TS, ring_rel[("sq", r)])
            for ev in evtoks:
                em.wait("S", ev[0], ev[1])
            sq = em.do("S", lambda e, tile_a0=tile_a0, r=r: e.activation(
                out=FA_t[:, sqr[r]: sqr[r] + TOK],
                in_=A_t[:, tile_a0: tile_a0 + TOK], func=AF.Square), em.SS, 1)
            em.wait("T", em.SS, sq)
            for tt in range(NTT):
                tk = em.do("T", lambda e, tt=tt: e.matmul(
                    self.bank(6, 512, 0, 2),
                    self.smc(SM.hsel, 2),
                    FA_t[:, sqr[r] + tt * 512: sqr[r] + (tt + 1) * 512],
                    start=True, stop=True), em.TS, 1)
                em.wait("V", em.TS, tk)
                v1 = em.do("V", lambda e: e.tensor_reduce(
                    SC_t[0:2, self.TMP1: self.TMP1 + 1],
                    self.bank(6, 512, 0, 2), axis=AX.X, op=ALU.max), em.VS, 1)
                em.wait("V", em.VS, v1)
                v2 = em.do("V", lambda e, dstcol=dstcol: e.tensor_tensor(
                    out=SC_t[0:2, dstcol: dstcol + 1],
                    in0=SC_t[0:2, self.TMP1: self.TMP1 + 1],
                    in1=SC_t[0:2, dstcol: dstcol + 1], op=ALU.max), em.VS, 1)
                em.wait("T", em.VS, v2)
            ring_rel[("sq", r)] = em.TS.n
            return (em.SS, sq)

        if layer == 0:
            wsrc = lambda m: self.w_qkvT[:, m * 128:(m + 1) * 128].rearrange(
                "(kc p) m -> p kc m", p=128)
            MC = 24

            def evac(m, tt, b, tok):
                if m < 8:
                    em.wait("S", em.TS, tok)
                    v = em.do("S", lambda e, m=m, tt=tt, b=b: e.activation(
                        out=A_t[:, self.QA + m * TOK + tt * 512: self.QA + m * TOK + (tt + 1) * 512],
                        in_=self.bank(b), func=AF.Identity), em.SS, 1)
                    return (em.SS, v)
                kind = "k" if m < 16 else "v"
                mc = m - 8 if m < 16 else m - 16
                r = mc % 2
                ring = ktile if kind == "k" else vtile
                if tt == 0:
                    for rt in ring_rel.get((kind, r), []):
                        em.wait("S", rt[0], rt[1])
                em.wait("S", em.TS, tok)
                v = em.do("S", lambda e, ring=ring, r=r, tt=tt, b=b: e.activation(
                    out=A_t[:, ring[r] + tt * 512: ring[r] + (tt + 1) * 512],
                    in_=self.bank(b), func=AF.Identity), em.SS, 1)
                if tt == NTT - 1:
                    self.finish_tile(kind, mc, r, ring, vexp, kvdst, ring_rel, stats_emit)
                return (em.SS, v)
            em.wait("T", em.VS, self.xn_tok)
            self.emit_proj(MC, NCH, wsrc, rhs_xn, evac, name="qkv0")
            self.emit_lam()
        else:
            for which, wt, bia in (("q", self.w_qT, SM.bq), ("k", self.w_kT, SM.bk),
                                   ("v", self.w_vT, SM.bv)):
                wsrc = lambda m, wt=wt: wt[:, m * 128:(m + 1) * 128].rearrange(
                    "(kc p) m -> p kc m", p=128)

                def evac(m, tt, b, tok, which=which, bia=bia):
                    if which == "q":
                        em.wait("S", em.TS, tok)
                        v = em.do("S", lambda e, m=m, tt=tt, b=b: e.activation(
                            out=A_t[:, self.QA + m * TOK + tt * 512: self.QA + m * TOK + (tt + 1) * 512],
                            in_=self.bank(b), func=AF.Identity,
                            bias=self.smc(SM.bq + m)), em.SS, 1)
                        return (em.SS, v)
                    kind = which
                    r = m % 2
                    ring = ktile if kind == "k" else vtile
                    if tt == 0:
                        for rt in ring_rel.get((kind, r), []):
                            em.wait("S", rt[0], rt[1])
                    em.wait("S", em.TS, tok)
                    v = em.do("S", lambda e, ring=ring, r=r, tt=tt, b=b, m=m, bia=bia: e.activation(
                        out=A_t[:, ring[r] + tt * 512: ring[r] + (tt + 1) * 512],
                        in_=self.bank(b), func=AF.Identity,
                        bias=self.smc(bia + m)), em.SS, 1)
                    if tt == NTT - 1:
                        self.finish_tile(kind, m, r, ring, vexp, kvdst, ring_rel, stats_emit)
                    return (em.SS, v)
                em.wait("T", em.VS, self.xn_tok)
                self.emit_proj(8, NCH, wsrc, rhs_xn, evac, name="qkv1" + which)
        self.finalize_stats(layer, stb)
        if layer == 1:
            for qe, pool in self.dmapool.items():
                for s in pool:
                    em.wait(qe, s, s.n)
                    em.wait("G", s, s.n)
            em.wait("G", em.VS, em.VS.n)
            em.wait("G", em.SS, em.SS.n)
            em.wait("G", em.TS, em.TS.n)
            cc = em.do("G", lambda e: e.collective_compute(
                "AllGather", ALU.bypass, ins=[self.kv_in],
                outs=[self.kv_out],
                replica_groups=[[0, 1, 2, 3], [4, 5, 6, 7]]), em.CC, 1)
            self.cc_tok = cc
        self.A.release(amark)
        self.FA.release(famark)

    def finish_tile(self, kind, mc, r, ring, vexp, kvdst, ring_rel, stats_emit):
        em = self
        A_t = self.A_t
        evtok = (em.SS, em.SS.n)
        if kind == "k":
            sqtok = stats_emit(ring[r], self.KM2, mc, [evtok])
            em.wait("Y", evtok[0], evtok[1])
            t = em.dma("Y", lambda e, r=r, mc=mc, ring=ring: e.dma_start(
                out=kvdst[:, self.KV_K0 + mc * TOK: self.KV_K0 + (mc + 1) * TOK],
                in_=A_t[:, ring[r]: ring[r] + TOK]))
            ring_rel[(kind, r)] = [t, sqtok]
        else:
            if ("vx", r) in ring_rel:
                em.wait("V", *ring_rel[("vx", r)])
            vt = getattr(self, "psb_rel", None)
            tr = None
            for tc in range(16):
                em.wait("T", evtok[0], evtok[1])
                if vt is not None:
                    em.wait("T", em.VS, vt)
                tr = em.do("T", lambda e, r=r, tc=tc: e.transpose(
                    self.PSB_t[:, 0:128],
                    A_t[:, ring[r] + tc * 128: ring[r] + (tc + 1) * 128],
                    self.cidb[:]), em.TS, 1)
                em.wait("V", em.TS, tr)
                em.do("V", lambda e, r=r, tc=tc: e.tensor_copy(
                    A_t[:, vexp[r] + tc * 130: vexp[r] + tc * 130 + 64],
                    self.PSB_t[:, 0:64]), em.VS, 1)
                vt = em.do("V", lambda e, r=r, tc=tc: e.tensor_copy(
                    A_t[:, vexp[r] + tc * 130 + 65: vexp[r] + tc * 130 + 129],
                    self.PSB_t[:, 64:128]), em.VS, 1)
            self.psb_rel = vt
            em.wait("Y", em.VS, vt)
            t = em.dma("Y", lambda e, r=r, mc=mc, kvdst=kvdst: e.dma_start(
                out=kvdst[:, self.KV_V0: self.KV_V0 + 16 * 1040].rearrange(
                    "p (tc cc) -> p tc cc", cc=1040)[:, :, 2 * mc * 65: 2 * mc * 65 + 130],
                in_=A_t[:, vexp[r]: vexp[r] + 16 * 130].rearrange(
                    "p (tc cc) -> p tc cc", cc=130)))
            ring_rel[(kind, r)] = [(em.TS, tr)]
            ring_rel[("vx", r)] = t

    def finalize_stats(self, layer, stb):
        em = self
        SC_t = self.SC_t
        for col in (self.QM2, self.KM2):
            t = em.dma("Y", lambda e, col=col: e.dma_start(
                out=SC_t[0:1, self.TMP1 + 1: self.TMP1 + 2],
                in_=SC_t[1:2, col: col + 1],
                allow_slow_non_contiguous=True))
            em.dwait("V", t)
            em.wait("V", em.VS, em.VS.n)
            v = em.do("V", lambda e, col=col: e.tensor_tensor(
                out=SC_t[0:1, col: col + 1], in0=SC_t[0:1, self.TMP1 + 1: self.TMP1 + 2],
                in1=SC_t[0:1, col: col + 1], op=ALU.max), em.VS, 1)
            em.wait("Y", em.VS, v)
        vals = {}
        for nm, col in (("q", self.QM2), ("k", self.KM2)):
            em.wait("T", em.VS, em.VS.n)
            tk = em.do("T", lambda e, col=col: e.matmul(
                self.bank(6, 1), self.onesf[0:1, 0:128],
                SC_t[0:1, col: col + 1], start=True, stop=True), em.TS, 1)
            em.wait("V", em.TS, tk)
            dst = self.TMP1 + 2 if nm == "q" else self.TMP1 + 3
            v = em.do("V", lambda e, dst=dst: e.tensor_copy(
                SC_t[:, dst: dst + 1], self.bank(6, 1)), em.VS, 1)
            em.wait("T", em.VS, v)
            vals[nm] = dst
            vals[nm + "tok"] = v
        if layer == 1:
            em.wait("V", em.VS, vals["ktok"])
            vb = em.do("V", lambda e: e.tensor_copy(
                self.A_t[:, stb[0]: stb[0] + 1], SC_t[:, vals["k"]: vals["k"] + 1]), em.VS, 1)
            em.wait("Y", em.VS, vb)
            em.dma("Y", lambda e: e.dma_start(
                out=self.kv_in[:, self.KV_ST: self.KV_ST + 1],
                in_=self.A_t[:, stb[0]: stb[0] + 1],
                allow_slow_non_contiguous=True))
            self.qm_bc = vals["q"]
            return
        em.wait("V", em.VS, vals["ktok"])
        em.wait("V", em.VS, vals["qtok"])
        v1 = em.do("V", lambda e: e.tensor_tensor(
            out=SC_t[:, self.B0C: self.B0C + 1], in0=SC_t[:, vals["q"]: vals["q"] + 1],
            in1=SC_t[:, vals["k"]: vals["k"] + 1], op=ALU.mult), em.VS, 1)
        em.wait("S", em.VS, v1)
        s1 = em.do("S", lambda e: e.activation(
            out=SC_t[:, self.B0C: self.B0C + 1], in_=SC_t[:, self.B0C: self.B0C + 1],
            func=AF.Sqrt), em.SS, 1)
        em.wait("V", em.SS, s1)
        em.do("V", lambda e: e.tensor_scalar(
            out=SC_t[:, self.B0C: self.B0C + 1], in0=SC_t[:, self.B0C: self.B0C + 1],
            scalar1=-0.125, scalar2=None, op0=ALU.mult), em.VS, 1)

    def emit_b1(self):
        em = self
        SC_t = self.SC_t
        famark = self.FA.mark()
        stf = self.FA.alloc(4)
        for e in ("Y", "G"):
            em.wait(e, em.CC, self.cc_tok)
        t = em.dma("G", lambda e: e.dma_start(
            out=self.FA_t[:, stf: stf + 4].rearrange("p (r c) -> p r c", c=1),
            in_=self.kv_out.rearrange("(r p) c -> p r c", p=128)[:, :, self.KV_ST: self.KV_ST + 1],
            allow_slow_non_contiguous=True))
        em.dwait("V", t)
        v1 = em.do("V", lambda e: e.tensor_reduce(
            SC_t[:, self.KALL: self.KALL + 1], self.FA_t[:, stf: stf + 4],
            axis=AX.X, op=ALU.max), em.VS, 1)
        em.wait("V", em.VS, v1)
        v2 = em.do("V", lambda e: e.tensor_tensor(
            out=SC_t[:, self.B1C: self.B1C + 1], in0=SC_t[:, self.KALL: self.KALL + 1],
            in1=SC_t[:, self.qm_bc: self.qm_bc + 1], op=ALU.mult), em.VS, 1)
        em.wait("S", em.VS, v2)
        s1 = em.do("S", lambda e: e.activation(
            out=SC_t[:, self.B1C: self.B1C + 1], in_=SC_t[:, self.B1C: self.B1C + 1],
            func=AF.Sqrt), em.SS, 1)
        em.wait("V", em.SS, s1)
        em.do("V", lambda e: e.tensor_scalar(
            out=SC_t[:, self.B1C: self.B1C + 1], in0=SC_t[:, self.B1C: self.B1C + 1],
            scalar1=-0.125, scalar2=-1.0, op0=ALU.mult, op1=ALU.add), em.VS, 1)
        self.FA.release(famark)

    def emit_lam(self):
        em = self
        FA_t, SC_t = self.FA_t, self.SC_t
        famark = self.FA.mark()
        lrow = self.FA.alloc(TOK)
        wrow = self.FA.alloc(64)
        h1 = self.FA.alloc(TOK)
        lfm = self.FA.alloc(TOK)
        t1 = em.dma("Y", lambda e: e.dma_start(
            out=FA_t[0:1, lrow: lrow + TOK], in_=self.smallp[0:1, SM.logpos_row: SM.logpos_row + TOK]))
        t2 = em.dma("Y", lambda e: e.dma_start(
            out=FA_t[0:1, wrow: wrow + 64], in_=self.smallp[0:1, SM.wl1row: SM.wl1row + 64]))
        for tt in range(NTT):
            em.dwait("T", t1)
            em.dwait("T", t2)
            em.wait("T", em.SS, em.SS.n)
            mk = em.do("T", lambda e, tt=tt: e.matmul(
                self.bank(5, 512, 0, 64), FA_t[0:1, wrow: wrow + 64],
                FA_t[0:1, lrow + tt * 512: lrow + (tt + 1) * 512],
                start=True, stop=True), em.TS, 1)
            em.wait("S", em.TS, mk)
            s1 = em.do("S", lambda e, tt=tt: e.activation(
                out=FA_t[0:64, h1 + tt * 512: h1 + (tt + 1) * 512],
                in_=self.bank(5, 512, 0, 64), func=AF.Relu,
                bias=self.smc(SM.bl1, 1, 0, 64)), em.SS, 1)
            em.wait("T", em.SS, s1)
            mk2 = em.do("T", lambda e, tt=tt: e.matmul(
                self.bank(5, 512, 0, 16), self.smc(SM.wl2t, 16, 0, 64),
                FA_t[0:64, h1 + tt * 512: h1 + (tt + 1) * 512],
                start=True, stop=True), em.TS, 1)
            em.wait("S", em.TS, mk2)
            s2 = em.do("S", lambda e, tt=tt: e.activation(
                out=FA_t[0:16, lfm + tt * 512: lfm + (tt + 1) * 512],
                in_=self.bank(5, 512, 0, 16), func=AF.Sigmoid,
                bias=self.smc(SM.bl2, 1, 0, 16)), em.SS, 1)
            em.wait("T", em.SS, s2)
        vtok = None
        for tc in range(16):
            if vtok is not None:
                em.wait("T", em.VS, vtok)
            tr = em.do("T", lambda e, tc=tc: e.matmul(
                self.bank(5, 16), FA_t[0:16, lfm + tc * 128: lfm + (tc + 1) * 128],
                self.identf[0:16, 0:16], is_transpose=True,
                start=True, stop=True), em.TS, 1)
            em.wait("V", em.TS, tr)
            vtok = em.do("V", lambda e, tc=tc: e.tensor_scalar(
                out=SC_t[:, self.LAM + tc * 16: self.LAM + (tc + 1) * 16],
                in0=self.bank(5, 16), scalar1=0.125, scalar2=None,
                op0=ALU.mult), em.VS, 1)
        em.wait("T", em.VS, vtok)
        self.FA.release(famark)

    def emit_ln(self, xsrc_fn, eps):
        em = self
        FA_t, A_t, SC_t, PS_t = self.FA_t, self.A_t, self.SC_t, self.PS_t
        famark = self.FA.mark()
        X0 = self.FA.alloc(NCH * TOK)
        MBC, RBC, SCT = self.MBC, self.RBC, self.SCT
        XN = self.XN
        ld = []
        for c in range(NCH):
            ld.append(em.dma("Y", lambda e, c=c: e.dma_start(
                out=FA_t[:, X0 + c * TOK: X0 + (c + 1) * TOK], in_=xsrc_fn(c))))
        t_sum = None
        for c in range(NCH):
            em.dwait("T", ld[c])
            for tt in range(NTT):
                t_sum = em.do("T", lambda e, c=c, tt=tt: e.matmul(
                    PS_t[0:1, tt * 512:(tt + 1) * 512], em.onesf[:, 0:1],
                    FA_t[:, X0 + c * TOK + tt * 512: X0 + c * TOK + (tt + 1) * 512],
                    start=(c == 0), stop=(c == NCH - 1)),
                    em.TS if c == NCH - 1 else None, 1)
        em.wait("V", em.TS, t_sum)
        mrow = SC_t[0:1, MBC: MBC + TOK]
        v_m = em.do("V", lambda e: e.tensor_scalar(
            out=mrow, in0=PS_t[0:1, 0:TOK], scalar1=1.0 / D, scalar2=None,
            op0=ALU.mult), em.VS, 1)
        mb_tok = self.bcast_inplace(MBC, v_m)
        sq_tok = None
        d_cols = SC_t[:, SCT: SCT + 1024]
        for c in range(NCH):
            for half in range(2):
                sl = slice(X0 + c * TOK + half * 1024, X0 + c * TOK + (half + 1) * 1024)
                asl = slice(XN + c * TOK + half * 1024, XN + c * TOK + (half + 1) * 1024)
                em.wait("V", em.VS, mb_tok)
                if sq_tok is not None:
                    em.wait("V", em.SS, sq_tok)
                v_d = em.do("V", lambda e, sl=sl: e.tensor_tensor(
                    out=d_cols, in0=FA_t[:, sl],
                    in1=SC_t[:, MBC + (sl.start - X0) % TOK: MBC + (sl.start - X0) % TOK + 1024],
                    op=ALU.subtract), em.VS, 1)
                v_b = em.do("V", lambda e, sl=sl, asl=asl: e.tensor_tensor(
                    out=A_t[:, asl], in0=FA_t[:, sl],
                    in1=SC_t[:, MBC + (sl.start - X0) % TOK: MBC + (sl.start - X0) % TOK + 1024],
                    op=ALU.subtract), em.VS, 1)
                em.wait("S", em.VS, v_b)
                sq_tok = em.do("S", lambda e, sl=sl: e.activation(
                    out=FA_t[:, sl], in_=d_cols, func=AF.Square), em.SS, 1)
                em.wait("T", em.SS, sq_tok)
                for tt in range(2):
                    t_sq = em.do("T", lambda e, sl=sl, tt=tt, c=c, half=half: e.matmul(
                        PS_t[0:1, half * 1024 + tt * 512: half * 1024 + (tt + 1) * 512],
                        em.onesf[:, 0:1],
                        FA_t[:, sl.start + tt * 512: sl.start + (tt + 1) * 512],
                        start=(c == 0), stop=(c == NCH - 1)),
                        em.TS if c == NCH - 1 else None, 1)
        em.wait("V", em.TS, t_sq)
        rrow = SC_t[0:1, RBC: RBC + TOK]
        v_v = em.do("V", lambda e: e.tensor_scalar(
            out=rrow, in0=PS_t[0:1, 0:TOK], scalar1=1.0 / D,
            scalar2=eps, op0=ALU.mult, op1=ALU.add), em.VS, 1)
        em.wait("S", em.VS, v_v)
        s_r = em.do("S", lambda e: e.activation(out=rrow, in_=rrow, func=AF.Sqrt), em.SS, 1)
        em.wait("V", em.SS, s_r)
        v_r = em.do("V", lambda e: e.reciprocal(rrow, rrow), em.VS, 1)
        rb_tok = self.bcast_inplace(RBC, v_r)
        xn_tok = None
        for c in range(NCH):
            em.wait("V", em.VS, rb_tok)
            xn_tok = em.do("V", lambda e, c=c: e.tensor_tensor(
                out=A_t[:, self.XN + c * TOK: self.XN + (c + 1) * TOK],
                in0=A_t[:, self.XN + c * TOK: self.XN + (c + 1) * TOK],
                in1=SC_t[:, RBC: RBC + TOK], op=ALU.mult), em.VS, 1)
        self.FA.release(famark)
        return xn_tok

    def bcast_inplace(self, col0, row_tok):
        em = self
        SC_t, PS_t = self.SC_t, self.PS_t
        for b in (4, 5):
            br = getattr(self, "bank_rel", {})
            if br.get(b) is not None:
                em.wait("T", br[b][0], br[b][1])
        em.wait("T", em.VS, row_tok)
        toks = []
        for tt in range(NTT):
            t1 = em.do("T", lambda e, tt=tt: e.matmul(
                self.bank(4 + (tt % 2)), em.onesf[0:1, 0:128],
                SC_t[0:1, col0 + tt * 512: col0 + (tt + 1) * 512],
                start=True, stop=True), em.TS, 1)
            em.wait("V", em.TS, t1)
            v1 = em.do("V", lambda e, tt=tt: e.tensor_copy(
                SC_t[:, col0 + tt * 512: col0 + (tt + 1) * 512],
                self.bank(4 + (tt % 2))), em.VS, 1)
            em.wait("T", em.VS, v1)
            toks.append(v1)
        return toks[-1]

    def emit_attn(self, layer):
        em = self
        A_t, FA_t, SC_t, W_t = self.A_t, self.FA_t, self.SC_t, self.W_t
        self.pb_rot = 0
        self.bank_rel = {}
        kvsrc = self.kv_loc if layer == 0 else None
        if layer == 0:
            KCOLS, VCOLS, NKC = 1024, 8 * 130, 8
            amark = self.A.mark()
            KS = [self.A.alloc(KCOLS) for _ in range(3)]
            VSL = [self.A.alloc(VCOLS) for _ in range(3)]
            AT = [self.A.alloc(1024) for _ in range(3)]
            HB = self.A.alloc(8 * 512)
            famark = self.FA.mark()
            dnm_c = self.FA.alloc(512)
            sring = [self.FA.alloc(512) for _ in range(2)]
            bias_col = SC_t[:, self.B0C: self.B0C + 1]
        else:
            KCOLS, VCOLS, NKC = 4096, 32 * 130, 64
            KS = [0, 4096, 8192]
            HB = 12288
            VSL = [32768, 36928, 41088]
            AT = [45248, 46272, 47296]
            dnm_c = 0
            sring = [512, 1024]
            gcol = 1536
            x1r = [2048, 2560]
            f1r = [3072, 3584]
            bias_col = SC_t[:, self.B1C: self.B1C + 1]
            self.emit_b1()
            em.wait("S", em.VS, em.VS.n)
            em.wait("V", em.VS, em.VS.n)
        nhalf = NKC * 128 // (KCOLS)
        cph = NKC // nhalf
        wmark = self.W.mark()
        WR = [self.W.alloc(4096) for _ in range(2)]
        wo_src = self.w_outT if layer == 0 else self.w_oT
        wotok = {}
        wo_rel = {}

        def wo_prefetch(tq, mh):
            i = tq * 2 + mh
            s = i % 2
            if s in wo_rel:
                em.wait("G", em.TS, wo_rel[s])
            wotok[i] = em.dma("G", lambda e, mh=mh, s=s: e.dma_start(
                out=W_t[:, WR[s]: WR[s] + 4096].rearrange("p (kc m) -> p kc m", m=512),
                in_=wo_src[:, mh * 512:(mh + 1) * 512].rearrange(
                    "(kc p) m -> p kc m", p=128), max_dma_last_dim=4096))

        slab_tok = {}
        slab_rel = {}

        def load_slab(tq, idx):
            hp, half = divmod(idx, nhalf)
            sl = idx % 3
            if sl in slab_rel:
                em.wait("Y", em.TS, slab_rel[sl])
            cq = tq // 2
            if layer == 0:
                ktok = em.dma("Y", lambda e, hp=hp, sl=sl, cq=cq: e.dma_start(
                    out=A_t[:, KS[sl]: KS[sl] + KCOLS],
                    in_=kvsrc[:, hp * TOK + cq * 1024: hp * TOK + cq * 1024 + 1024]))
                vtok = em.dma("Y", lambda e, hp=hp, sl=sl, cq=cq: e.dma_start(
                    out=A_t[:, VSL[sl]: VSL[sl] + VCOLS].rearrange(
                        "p (tc cc) -> p tc cc", cc=130),
                    in_=kvsrc[:, self.KV_V0 + cq * 8 * 1040:
                              self.KV_V0 + (cq * 8 + 8) * 1040].rearrange(
                        "p (tc cc) -> p tc cc", cc=1040)[:, :, 2 * hp * 65: 2 * hp * 65 + 130]))
            else:
                kvo = self.kv_out.rearrange("(r p) c -> p r c", p=128)
                r0 = half * 2
                ktok = em.dma("Y", lambda e, hp=hp, sl=sl, r0=r0, kvo=kvo: e.dma_start(
                    out=A_t[:, KS[sl]: KS[sl] + KCOLS].rearrange(
                        "p (r t) -> p r t", t=TOK),
                    in_=kvo[:, r0: r0 + 2, hp * TOK: (hp + 1) * TOK]))
                vts = []
                for ri in range(2):
                    vts.append(em.dma("Y", lambda e, hp=hp, sl=sl, r0=r0, ri=ri, kvo=kvo: e.dma_start(
                        out=A_t[:, VSL[sl] + ri * 16 * 130: VSL[sl] + (ri + 1) * 16 * 130].rearrange(
                            "p (tc cc) -> p tc cc", cc=130),
                        in_=kvo[:, r0 + ri,
                                self.KV_V0: self.KV_V0 + 16 * 1040].rearrange(
                            "p (tc cc) -> p tc cc", cc=1040)[:, :, 2 * hp * 65: 2 * hp * 65 + 130])))
                vtok = vts
            slab_tok[idx] = (ktok, vtok)

        def delta_build(tq):
            rlist = [c for c in range(64) if c * 128 + 127 - tq * 512 > 4097]
            pos = {}
            gtok = None
            vtok = None
            for i, c in enumerate(rlist):
                pos[c] = i * 512
                base = tq * 512 - c * 128 + 8191 - 2047
                if gtok is not None:
                    em.wait("G", em.VS, vtok)
                gtok = em.do("G", lambda e, base=base: e.iota(
                    self.iox[:], base=base, pattern=[[1, 512]],
                    channel_multiplier=-1), em.GS, 1)
                em.wait("V", em.GS, gtok)
                i0 = i * 512
                v0 = em.do("V", lambda e, i0=i0: e.tensor_copy(
                    FA_t[:, i0: i0 + 512], self.iox[:]), em.VS, 1)
                em.wait("V", em.VS, v0)
                v1 = em.do("V", lambda e, i0=i0: e.tensor_scalar(
                    out=FA_t[:, i0: i0 + 512], in0=FA_t[:, i0: i0 + 512],
                    scalar1=self.smc(SM.q0col), scalar2=None, op0=ALU.add), em.VS, 1)
                em.wait("V", em.VS, v1)
                v1b = em.do("V", lambda e, i0=i0: e.tensor_scalar(
                    out=SC_t[:, 4608: 4608 + 512], in0=FA_t[:, i0: i0 + 512],
                    scalar1=-1.0, scalar2=None, op0=ALU.mult), em.VS, 1)
                em.wait("V", em.VS, v1b)
                v2 = em.do("V", lambda e, i0=i0: e.tensor_tensor(
                    out=FA_t[:, i0: i0 + 512], in0=FA_t[:, i0: i0 + 512],
                    in1=SC_t[:, 4608: 4608 + 512], op=ALU.max), em.VS, 1)
                em.wait("V", em.VS, v2)
                vtok = em.do("V", lambda e, i0=i0: e.tensor_scalar(
                    out=FA_t[:, i0: i0 + 512], in0=FA_t[:, i0: i0 + 512],
                    scalar1=-1.0 / 256.0, scalar2=2047.0 / 256.0,
                    op0=ALU.mult, op1=ALU.add), em.VS, 1)
                em.wait("V", em.VS, vtok)
                vtok = em.do("V", lambda e, i0=i0: e.tensor_scalar(
                    out=FA_t[:, i0: i0 + 512], in0=FA_t[:, i0: i0 + 512],
                    scalar1=0.0, scalar2=None, op0=ALU.max), em.VS, 1)
            return pos, vtok

        at_rel = {}
        exp_tok = {}
        av_bank_rel = None
        for tq in range(4):
            wo_prefetch(tq, 0)
            wo_prefetch(tq, 1)
            if layer == 1:
                dpos, dtok = delta_build(tq)
            else:
                dpos, dtok = {}, None
            load_slab(tq, 0)
            nidx = NHP * nhalf
            for idx in range(nidx):
                hp, half = divmod(idx, nhalf)
                if idx + 1 < nidx:
                    load_slab(tq, idx + 1)
                elif tq < 3:
                    pass
                sl = idx % 3
                ktok, vtok_d = slab_tok[idx]
                em.dwait("T", ktok)
                for vv in (vtok_d if isinstance(vtok_d, list) else [vtok_d]):
                    em.dwait("T", vv)
                for kcl in range(cph):
                    kc = half * cph + kcl
                    be = (kc % 2) * 2
                    bo = be + 1
                    for b in (be, bo):
                        if self.bank_rel.get(b) is not None:
                            em.wait("T", self.bank_rel[b][0], self.bank_rel[b][1])
                    mm_e = em.do("T", lambda e, sl=sl, kcl=kcl, hp=hp, tq=tq, be=be: e.matmul(
                        self.bank(be),
                        A_t[0:64, KS[sl] + kcl * 128: KS[sl] + (kcl + 1) * 128],
                        A_t[0:64, self.QA + hp * TOK + tq * 512: self.QA + hp * TOK + (tq + 1) * 512],
                        start=True, stop=True), em.TS, 1)
                    mm_o = em.do("T", lambda e, sl=sl, kcl=kcl, hp=hp, tq=tq, bo=bo: e.matmul(
                        self.bank(bo),
                        A_t[64:128, KS[sl] + kcl * 128: KS[sl] + (kcl + 1) * 128],
                        A_t[64:128, self.QA + hp * TOK + tq * 512: self.QA + hp * TOK + (tq + 1) * 512],
                        start=True, stop=True), em.TS, 1)
                    dtk = None
                    if layer == 1 and kc in dpos:
                        em.wait("V", em.TS, mm_o)
                        em.wait("V", em.VS, dtok)
                        em.do("V", lambda e, be=be, kc=kc, dpos=dpos: e.tensor_tensor(
                            out=self.bank(be), in0=self.bank(be),
                            in1=FA_t[:, dpos[kc]: dpos[kc] + 512], op=ALU.add), em.VS, 1)
                        dtk = em.do("V", lambda e, bo=bo, kc=kc, dpos=dpos: e.tensor_tensor(
                            out=self.bank(bo), in0=self.bank(bo),
                            in1=FA_t[:, dpos[kc]: dpos[kc] + 512], op=ALU.add), em.VS, 1)
                    ats = kc % 3
                    if at_rel.get(ats) is not None:
                        em.wait("S", at_rel[ats][0], at_rel[ats][1])
                    em.wait("S", em.TS, mm_o)
                    if dtk is not None:
                        em.wait("S", em.VS, dtk)
                    if layer == 1:
                        ex = em.do("S", lambda e, be=be, ats=ats: e.activation(
                            out=A_t[:, AT[ats]: AT[ats] + 1024],
                            in_=self.PS_t[:, be * 512: (be + 2) * 512],
                            func=AF.Exp, scale=0.125, bias=bias_col), em.SS, 1)
                    else:
                        cq = tq // 2
                        em.do("S", lambda e, be=be, ats=ats, cq=cq, kcl=kcl, hp=hp: e.activation(
                            out=A_t[:, AT[ats]: AT[ats] + 512],
                            in_=self.bank(be),
                            func=AF.Exp,
                            scale=SC_t[:, self.LAM + (cq * 8 + kcl) * 16 + 2 * hp:
                                       self.LAM + (cq * 8 + kcl) * 16 + 2 * hp + 1],
                            bias=bias_col), em.SS, 1)
                        ex = em.do("S", lambda e, bo=bo, ats=ats, cq=cq, kcl=kcl, hp=hp: e.activation(
                            out=A_t[:, AT[ats] + 512: AT[ats] + 1024],
                            in_=self.bank(bo),
                            func=AF.Exp,
                            scale=SC_t[:, self.LAM + (cq * 8 + kcl) * 16 + 2 * hp + 1:
                                       self.LAM + (cq * 8 + kcl) * 16 + 2 * hp + 2],
                            bias=bias_col), em.SS, 1)
                    self.bank_rel[be] = (em.SS, ex)
                    self.bank_rel[bo] = (em.SS, ex)
                    if _ATTN_PARTS is not None and "av" not in _ATTN_PARTS:
                        at_rel[ats] = (em.SS, ex)
                        self.last_av = None
                        continue
                    em.wait("T", em.SS, ex)
                    if kc == 0 and av_bank_rel is not None:
                        em.wait("T", av_bank_rel[0], av_bank_rel[1])
                    ave = em.do("T", lambda e, sl=sl, kcl=kcl, ats=ats, kc=kc: e.matmul(
                        self.bank(4, 512, 0, 65),
                        A_t[:, VSL[sl] + kcl * 130: VSL[sl] + kcl * 130 + 65],
                        A_t[:, AT[ats]: AT[ats] + 512],
                        start=(kc == 0), stop=(kc == NKC - 1)),
                        em.TS, 1)
                    avo = em.do("T", lambda e, sl=sl, kcl=kcl, ats=ats, kc=kc: e.matmul(
                        self.bank(5, 512, 0, 65),
                        A_t[:, VSL[sl] + kcl * 130 + 65: VSL[sl] + kcl * 130 + 130],
                        A_t[:, AT[ats] + 512: AT[ats] + 1024],
                        start=(kc == 0), stop=(kc == NKC - 1)),
                        em.TS, 1)
                    at_rel[ats] = (em.TS, avo)
                    self.last_av = (ave, avo)
                ap_ = _ATTN_PARTS if _ATTN_PARTS is not None else {"av", "evac", "wo"}
                slab_rel[sl] = em.TS.n
                ap_ = _ATTN_PARTS if _ATTN_PARTS is not None else {"av", "evac", "wo"}
                if half == nhalf - 1 and "evac" in ap_ and "av" in ap_:
                    av_bank_rel = self.emit_av_evac(layer, hp, HB, dnm_c)
            ap_ = _ATTN_PARTS if _ATTN_PARTS is not None else {"av", "evac", "wo"}
            if "wo" in ap_:
                self.emit_wo(layer, tq, HB, WR, wotok, wo_rel, sring,
                             None if layer == 0 else (gcol, x1r, f1r), dnm_c)
        self.W.release(wmark)
        if layer == 0:
            self.A.release(amark)
            self.FA.release(famark)

    def emit_av_evac(self, layer, hp, HB, dnm_c):
        em = self
        A_t, SC_t, FA_t = self.A_t, self.SC_t, self.FA_t
        dn = (FA_t if layer == 0 else SC_t)
        ave, avo = self.last_av
        rel = None
        for j, avtok in ((0, ave), (1, avo)):
            bk = 4 + j
            em.wait("V", em.TS, avtok)
            v1 = em.do("V", lambda e, bk=bk: e.tensor_copy(
                dn[64:65, dnm_c: dnm_c + 512], self.bank(bk, 512, 64, 65)), em.VS, 1)
            em.wait("V", em.VS, v1)
            v2 = em.do("V", lambda e: e.reciprocal(
                dn[64:65, dnm_c: dnm_c + 512], dn[64:65, dnm_c: dnm_c + 512]), em.VS, 1)
            em.wait("T", em.VS, v2)
            t1 = em.do("T", lambda e: e.matmul(
                self.bank(6, 512, 0, 64), self.onesf[64:65, 0:64],
                dn[64:65, dnm_c: dnm_c + 512], start=True, stop=True), em.TS, 1)
            em.wait("V", em.TS, t1)
            v3 = em.do("V", lambda e: e.tensor_copy(
                dn[0:64, dnm_c: dnm_c + 512], self.bank(6, 512, 0, 64)), em.VS, 1)
            em.wait("V", em.VS, v3)
            v4 = em.do("V", lambda e, bk=bk, j=j, hp=hp: e.tensor_tensor(
                out=A_t[64 * j: 64 * j + 64, HB + hp * 512: HB + (hp + 1) * 512],
                in0=self.bank(bk, 512, 0, 64), in1=dn[0:64, dnm_c: dnm_c + 512],
                op=ALU.mult), em.VS, 1)
            rel = (em.VS, v4)
        return rel

    def emit_wo(self, layer, tq, HB, WR, wotok, wo_rel, sring, comb, dnm_c):
        em = self
        A_t, SC_t, FA_t, W_t = self.A_t, self.SC_t, self.FA_t, self.W_t
        hb_tok = em.VS.n
        if layer == 1:
            gcol, x1r, f1r = comb
            ldx = {}
            ldf = {}
        sr_rel = {}
        for mh in range(2):
            i = tq * 2 + mh
            s = i % 2
            em.dwait("T", wotok[i])
            for ml in range(4):
                m = mh * 4 + ml
                b = (self.pb_rot % 3) * 2
                self.pb_rot += 1
                for bb in (b, b + 1):
                    if self.bank_rel.get(bb) is not None:
                        em.wait("T", self.bank_rel[bb][0], self.bank_rel[bb][1])
                em.wait("T", em.VS, hb_tok)
                tok = None
                for par in range(2):
                    for hh in range(8):
                        j = 2 * hh + par
                        tok = em.do("T", lambda e, s=s, j=j, ml=ml, b=b, par=par, hh=hh: e.matmul(
                            self.bank(b + par),
                            W_t[64 * par: 64 * par + 64,
                                WR[s] + (j // 2) * 512 + ml * 128: WR[s] + (j // 2) * 512 + (ml + 1) * 128],
                            A_t[64 * par: 64 * par + 64, HB + (j // 2) * 512: HB + (j // 2 + 1) * 512],
                            start=(hh == 0), stop=(hh == 7)),
                            em.TS if hh == 7 else None, 1)
                wo_rel[s] = tok
                r = m % 2
                if _WO_MM_ONLY:
                    self.bank_rel[b] = (em.TS, tok)
                    self.bank_rel[b + 1] = (em.TS, tok)
                    continue
                if layer == 0:
                    if ("s", r) in sr_rel:
                        for rt in sr_rel[("s", r)]:
                            em.wait("S", rt[0], rt[1]) if rt[0].name.startswith("d") else em.wait("S", rt[0], rt[1])
                    em.wait("S", em.TS, tok)
                    sv = em.do("S", lambda e, m=m, b=b, r=r: e.activation(
                        out=FA_t[:, sring[r]: sring[r] + 512], in_=self.bank(b),
                        func=AF.Identity, bias=self.smc(SM.bout + m)), em.SS, 1)
                    self.bank_rel[b] = (em.SS, sv)
                    em.wait("V", em.SS, sv)
                    sv2 = em.do("V", lambda e, b=b, r=r: e.tensor_tensor(
                        out=FA_t[:, sring[r]: sring[r] + 512],
                        in0=self.bank(b + 1), in1=FA_t[:, sring[r]: sring[r] + 512],
                        op=ALU.add), em.VS, 1)
                    self.bank_rel[b + 1] = (em.VS, sv2)
                    em.wait("Y", em.VS, sv2)
                    dt = em.dma("Y", lambda e, m=m, tq=tq, r=r: e.dma_start(
                        out=self.s_sp[:, m * TOK + tq * 512: m * TOK + (tq + 1) * 512],
                        in_=FA_t[:, sring[r]: sring[r] + 512]))
                    sr_rel[("s", r)] = [dt, (em.VS, sv2)]
                else:
                    if ("x", r) in sr_rel:
                        em.dwait("Y", sr_rel[("x", r)])
                    ldx[m] = em.dma("Y", lambda e, m=m, tq=tq, r=r: e.dma_start(
                        out=SC_t[:, x1r[r]: x1r[r] + 512],
                        in_=self.xres[:, m * TOK + tq * 512: m * TOK + (tq + 1) * 512]))
                    if ("f", r) in sr_rel:
                        em.wait("Y", sr_rel[("f", r)][0], sr_rel[("f", r)][1])
                    ldf[m] = em.dma("Y", lambda e, m=m, tq=tq, r=r: e.dma_start(
                        out=SC_t[:, f1r[r]: f1r[r] + 512],
                        in_=self.f_sp[:, m * TOK + tq * 512: m * TOK + (tq + 1) * 512]))
                    em.wait("S", em.TS, tok)
                    if ("s", r) in sr_rel:
                        em.wait("S", sr_rel[("s", r)][0], sr_rel[("s", r)][1])
                        em.wait("V", sr_rel[("s", r)][0], sr_rel[("s", r)][1])
                    sv0 = em.do("S", lambda e, m=m, b=b, r=r: e.activation(
                        out=SC_t[:, sring[r]: sring[r] + 512], in_=self.bank(b),
                        func=AF.Identity, bias=self.smc(SM.bo + m)), em.SS, 1)
                    self.bank_rel[b] = (em.SS, sv0)
                    em.wait("V", em.SS, sv0)
                    sv = em.do("V", lambda e, b=b, r=r: e.tensor_tensor(
                        out=SC_t[:, sring[r]: sring[r] + 512],
                        in0=self.bank(b + 1), in1=SC_t[:, sring[r]: sring[r] + 512],
                        op=ALU.add), em.VS, 1)
                    self.bank_rel[b + 1] = (em.VS, sv)
                    em.dwait("S", ldf[m])
                    if ("g", 0) in sr_rel:
                        em.wait("S", sr_rel[("g", 0)][0], sr_rel[("g", 0)][1])
                    gv = em.do("S", lambda e, r=r: e.activation(
                        out=SC_t[:, gcol: gcol + 512], in_=SC_t[:, f1r[r]: f1r[r] + 512],
                        func=AF.Silu), em.SS, 1)
                    sr_rel[("f", r)] = (em.SS, gv)
                    em.wait("V", em.SS, gv)
                    em.wait("V", em.VS, sv)
                    v1 = em.do("V", lambda e, r=r: e.tensor_tensor(
                        out=SC_t[:, sring[r]: sring[r] + 512],
                        in0=SC_t[:, sring[r]: sring[r] + 512],
                        in1=SC_t[:, gcol: gcol + 512], op=ALU.mult), em.VS, 1)
                    sr_rel[("s", r)] = (em.VS, v1)
                    sr_rel[("g", 0)] = (em.VS, v1)
                    em.dwait("V", ldx[m])
                    em.wait("V", em.VS, v1)
                    v2 = em.do("V", lambda e, r=r: e.tensor_tensor(
                        out=SC_t[:, x1r[r]: x1r[r] + 512],
                        in0=SC_t[:, sring[r]: sring[r] + 512],
                        in1=SC_t[:, x1r[r]: x1r[r] + 512], op=ALU.add), em.VS, 1)
                    em.wait("Y", em.VS, v2)
                    ot = em.dma("Y", lambda e, m=m, tq=tq, r=r: e.dma_start(
                        out=self.out_fm[m * 128:(m + 1) * 128, tq * 512:(tq + 1) * 512],
                        in_=SC_t[:, x1r[r]: x1r[r] + 512]))
                    sr_rel[("x", r)] = ot

    def emit_router(self, E, lhsT_fn, rhs_fn, wE_col, l1=None, ntok=TOK):
        em = self
        SC_t = self.SC_t
        NTP = ntok // 128
        for tp in range(NTP):
            em.wait("T", self.bank_rel.get(66, (em.VS, 0))[0],
                    self.bank_rel.get(66, (em.VS, 0))[1])
            tok = None
            for k in range(NCH):
                tok = em.do("T", lambda e, tp=tp, k=k, E=E: e.matmul(
                    self.bank(6, E), lhsT_fn(tp, k), rhs_fn(k),
                    start=(k == 0), stop=(k == NCH - 1)),
                    em.TS if k == NCH - 1 else None, 1)
            C0 = self.SCT + 768
            p_ = SC_t[:, wE_col + tp * E: wE_col + (tp + 1) * E]
            t1c = SC_t[:, C0: C0 + E]
            gate_src = t1c
            m1 = SC_t[:, C0 + 8: C0 + 9]
            m2 = SC_t[:, C0 + 9: C0 + 10]
            m3 = SC_t[:, C0 + 10: C0 + 11]
            em.wait("V", em.TS, tok)
            em.wait("V", em.SS, self.SS.n if False else em.SS.n)
            em.wait("V", em.VS, em.VS.n)
            if l1 is None:
                v = em.do("V", lambda e, p_=p_: e.tensor_tensor(
                    out=t1c, in0=self.bank(6, E), in1=SC_t[:, self.gb_bc_col: self.gb_bc_col + E],
                    op=ALU.add), em.VS, 1)
            else:
                rbc, gbc = l1
                v = em.do("V", lambda e, p_=p_, gate_src=gate_src: e.tensor_tensor(
                    out=t1c, in0=self.bank(6, E), in1=SC_t[:, rbc: rbc + E],
                    op=ALU.add), em.VS, 1)
            self.bank_rel[66] = (em.VS, v)
            em.wait("V", em.VS, v)
            if _ROUTER_RAW == "logits":
                em.do("V", lambda e, p_=p_: e.tensor_copy(p_, t1c), em.VS, 1)
                continue
            if l1 is None:
                v1 = em.do("V", lambda e, p_=p_, gate_src=gate_src: e.tensor_reduce(m1, gate_src, axis=AX.X, op=ALU.max), em.VS, 1)
                em.wait("V", em.VS, v1)
                if _ROUTER_RAW == "logits2":
                    em.do("V", lambda e, p_=p_: e.tensor_copy(p_, gate_src), em.VS, 1)
                    continue
                if _ROUTER_RAW == "max":
                    for jj in range(E):
                        em.do("V", lambda e, p_=p_, jj=jj: e.tensor_copy(
                            p_[:, jj: jj + 1], m1), em.VS, 1)
                    continue
                v2 = em.do("V", lambda e, p_=p_, gate_src=gate_src: e.tensor_scalar(
                    out=m2, in0=m1, scalar1=-1.0, scalar2=None, op0=ALU.mult), em.VS, 1)
                em.wait("S", em.VS, v2)
                s1 = em.do("S", lambda e, p_=p_, gate_src=gate_src: e.activation(
                    out=p_, in_=gate_src, func=AF.Exp, bias=m2, accum_out=m3), em.SS, 1)
                if _ROUTER_RAW == "exp":
                    em.wait("V", em.SS, s1)
                    continue
                if _ROUTER_RAW == "sum":
                    em.wait("V", em.SS, s1)
                    em.do("V", lambda e, p_=p_: e.tensor_scalar(
                        out=p_, in0=m3, scalar1=0.0, scalar2=None, op0=ALU.add), em.VS, 1)
                    continue
                em.wait("V", em.SS, s1)
                v3 = em.do("V", lambda e, p_=p_, gate_src=gate_src: e.reciprocal(m1, m3), em.VS, 1)
                em.wait("V", em.VS, v3)
                v4 = em.do("V", lambda e, p_=p_, gate_src=gate_src: e.tensor_scalar(
                    out=p_, in0=p_, scalar1=m1, scalar2=None, op0=ALU.mult), em.VS, 1)
                if _ROUTER_RAW == "softmax":
                    continue
                em.wait("V", em.VS, v4)
                v5 = em.do("V", lambda e, p_=p_, gate_src=gate_src: e.tensor_reduce(m1, p_, axis=AX.X, op=ALU.max), em.VS, 1)
                em.wait("V", em.VS, v5)
                v6 = em.do("V", lambda e, p_=p_, gate_src=gate_src: e.tensor_scalar(
                    out=t1c, in0=p_, scalar1=m1, scalar2=None, op0=ALU.is_ge), em.VS, 1)
                em.wait("V", em.VS, v6)
                v7 = em.do("V", lambda e, p_=p_, gate_src=gate_src: e.scalar_tensor_tensor(
                    out=t1c, in0=t1c, scalar=-2.0, in1=p_, op0=ALU.mult, op1=ALU.add), em.VS, 1)
                em.wait("V", em.VS, v7)
                v8 = em.do("V", lambda e, p_=p_, gate_src=gate_src: e.tensor_reduce(m2, t1c, axis=AX.X, op=ALU.max), em.VS, 1)
                em.wait("V", em.VS, v8)
                v9 = em.do("V", lambda e, p_=p_, gate_src=gate_src: e.tensor_tensor(out=m3, in0=m1, in1=m2, op=ALU.add), em.VS, 1)
                em.wait("V", em.VS, v9)
                v10 = em.do("V", lambda e, p_=p_, gate_src=gate_src: e.tensor_scalar(
                    out=m3, in0=m3, scalar1=1e-9, scalar2=None, op0=ALU.add), em.VS, 1)
                em.wait("V", em.VS, v10)
                v11 = em.do("V", lambda e, p_=p_, gate_src=gate_src: e.reciprocal(m3, m3), em.VS, 1)
                em.wait("V", em.VS, v11)
                v12 = em.do("V", lambda e, p_=p_, gate_src=gate_src: e.tensor_scalar(
                    out=t1c, in0=p_, scalar1=m2, scalar2=None, op0=ALU.is_ge), em.VS, 1)
                em.wait("V", em.VS, v12)
                v13 = em.do("V", lambda e, p_=p_, gate_src=gate_src: e.tensor_tensor(out=p_, in0=p_, in1=t1c, op=ALU.mult), em.VS, 1)
                em.wait("V", em.VS, v13)
                vfin = em.do("V", lambda e, p_=p_, gate_src=gate_src: e.tensor_scalar(
                    out=p_, in0=p_, scalar1=m3, scalar2=None, op0=ALU.mult), em.VS, 1)
                if _ROUTER_RAW == "snap":
                    em.wait("V", em.VS, vfin)
                    em.do("V", lambda e, p_=p_, tp=tp, E=E: e.tensor_copy(
                        SC_t[:, tp * E: (tp + 1) * E], p_), em.VS, 1)
            else:
                bsd = SC_t[:, C0 + 16: C0 + 16 + E]
                v1 = em.do("V", lambda e, gbc=gbc: e.tensor_tensor(
                    out=bsd, in0=gate_src, in1=SC_t[:, gbc: gbc + E], op=ALU.add), em.VS, 1)
                em.wait("V", em.VS, v1)
                v2 = em.do("V", lambda e, p_=p_, gate_src=gate_src: e.tensor_reduce(m1, bsd, axis=AX.X, op=ALU.max), em.VS, 1)
                em.wait("V", em.VS, v2)
                v3 = em.do("V", lambda e, p_=p_, gate_src=gate_src: e.tensor_scalar(
                    out=p_, in0=bsd, scalar1=m1, scalar2=None, op0=ALU.is_ge), em.VS, 1)
                em.wait("V", em.VS, v3)
                v4 = em.do("V", lambda e, p_=p_, gate_src=gate_src: e.scalar_tensor_tensor(
                    out=bsd, in0=p_, scalar=-1e30, in1=bsd, op0=ALU.mult, op1=ALU.add), em.VS, 1)
                em.wait("V", em.VS, v4)
                v5 = em.do("V", lambda e, p_=p_, gate_src=gate_src: e.tensor_reduce(m2, bsd, axis=AX.X, op=ALU.max), em.VS, 1)
                em.wait("V", em.VS, v5)
                v6 = em.do("V", lambda e, gbc=gbc: e.tensor_tensor(
                    out=bsd, in0=gate_src, in1=SC_t[:, gbc: gbc + E], op=ALU.add), em.VS, 1)
                em.wait("V", em.VS, v6)
                v7 = em.do("V", lambda e, p_=p_, gate_src=gate_src: e.tensor_scalar(
                    out=bsd, in0=bsd, scalar1=m2, scalar2=None, op0=ALU.is_ge), em.VS, 1)
                em.wait("V", em.VS, v7)
                v8 = em.do("V", lambda e, p_=p_, gate_src=gate_src: e.tensor_tensor(out=p_, in0=gate_src, in1=bsd, op=ALU.mult), em.VS, 1)
                em.wait("V", em.VS, v8)
                v9 = em.do("V", lambda e, p_=p_, gate_src=gate_src: e.tensor_scalar(
                    out=bsd, in0=bsd, scalar1=1e30, scalar2=-1e30, op0=ALU.mult, op1=ALU.add), em.VS, 1)
                em.wait("V", em.VS, v9)
                v10 = em.do("V", lambda e, p_=p_, gate_src=gate_src: e.tensor_tensor(out=p_, in0=p_, in1=bsd, op=ALU.add), em.VS, 1)
                em.wait("V", em.VS, v10)
                v11 = em.do("V", lambda e, p_=p_, gate_src=gate_src: e.tensor_reduce(m1, p_, axis=AX.X, op=ALU.max), em.VS, 1)
                em.wait("V", em.VS, v11)
                v12 = em.do("V", lambda e, p_=p_, gate_src=gate_src: e.tensor_scalar(
                    out=m2, in0=m1, scalar1=-1.0, scalar2=None, op0=ALU.mult), em.VS, 1)
                em.wait("S", em.VS, v12)
                s1 = em.do("S", lambda e, p_=p_, gate_src=gate_src: e.activation(
                    out=p_, in_=p_, func=AF.Exp, bias=m2, accum_out=m3), em.SS, 1)
                em.wait("V", em.SS, s1)
                v13 = em.do("V", lambda e, p_=p_, gate_src=gate_src: e.reciprocal(m1, m3), em.VS, 1)
                em.wait("V", em.VS, v13)
                em.do("V", lambda e, p_=p_, gate_src=gate_src: e.tensor_scalar(
                    out=p_, in0=p_, scalar1=m1, scalar2=None, op0=ALU.mult), em.VS, 1)

    def emit_wbc(self, E, wE_col, wT_col, wbc_col, e, ntok=TOK):
        em = self
        SC_t = self.SC_t
        NTP = ntok // 128
        if e == 0:
            for tp in range(NTP):
                em.wait("T", em.VS, em.VS.n)
                tr = em.do("T", lambda e_, tp=tp, E=E: e_.matmul(
                    self.bank(6, 128, 0, E),
                    SC_t[:, wE_col + tp * E: wE_col + (tp + 1) * E],
                    self.identf[:, 0:128], is_transpose=True,
                    start=True, stop=True), em.TS, 1)
                em.wait("V", em.TS, tr)
                vv = em.do("V", lambda e_, tp=tp, E=E: e_.tensor_copy(
                    SC_t[0:E, wT_col + tp * 128: wT_col + (tp + 1) * 128],
                    self.bank(6, 128, 0, E)), em.VS, 1)
                em.wait("T", em.VS, vv)
        for tt in range(ntok // 512):
            em.wait("T", em.VS, em.VS.n)
            t1 = em.do("T", lambda e_, tt=tt, E=E, e=e: e_.matmul(
                self.bank(6, 512, 0, 1), self.identf[0:E, e: e + 1],
                SC_t[0:E, wT_col + tt * 512: wT_col + (tt + 1) * 512],
                start=True, stop=True), em.TS, 1)
            em.wait("V", em.TS, t1)
            v1 = em.do("V", lambda e_, tt=tt: e_.tensor_copy(
                SC_t[0:1, wbc_col + tt * 512: wbc_col + (tt + 1) * 512],
                self.bank(6, 512, 0, 1)), em.VS, 1)
            em.wait("T", em.VS, v1)
            t2 = em.do("T", lambda e_, tt=tt: e_.matmul(
                self.bank(6, 512), self.onesf[0:1, 0:128],
                SC_t[0:1, wbc_col + tt * 512: wbc_col + (tt + 1) * 512],
                start=True, stop=True), em.TS, 1)
            em.wait("V", em.TS, t2)
            v2 = em.do("V", lambda e_, tt=tt: e_.tensor_copy(
                SC_t[:, wbc_col + tt * 512: wbc_col + (tt + 1) * 512],
                self.bank(6, 512)), em.VS, 1)
            em.wait("T", em.VS, v2)
        return em.VS.n

    def emit_ffn(self, layer):
        if layer == 0:
            self.emit_ffn0()
        else:
            self.emit_ffn1()

    def emit_ffn0(self):
        em = self
        A_t, FA_t, SC_t = self.A_t, self.FA_t, self.SC_t
        self.pb_rot = 0
        self.bank_rel = {}
        amark = self.A.mark()
        famark = self.FA.mark()
        H0 = self.QA
        assert self.A.mark() == 32768
        ACC = self.FA.alloc(NCH * TOK)
        WE, WTC, WBC = self.SCT, 0, 2048
        WE = self.SCT + 640
        ring = [self.SCT, None]
        lastv = None
        prev_rd = None
        for c in range(NCH):
            for tt in range(NTT):
                if prev_rd is not None:
                    em.wait("Y", em.VS, prev_rd)
                t = em.dma("Y", lambda e, c=c, tt=tt: e.dma_start(
                    out=SC_t[:, self.SCT: self.SCT + 512],
                    in_=self.x_fm[c * 128:(c + 1) * 128, tt * 512:(tt + 1) * 512]))
                em.dwait("V", t)
                if lastv is not None:
                    em.wait("V", em.VS, lastv)
                dst = slice(ACC + c * TOK + tt * 512, ACC + c * TOK + (tt + 1) * 512)
                v1 = em.do("V", lambda e, dst=dst, tt=tt: e.tensor_tensor(
                    out=FA_t[:, dst], in0=SC_t[:, self.SCT: self.SCT + 512],
                    in1=SC_t[:, self.MBC + tt * 512: self.MBC + (tt + 1) * 512],
                    op=ALU.subtract), em.VS, 1)
                prev_rd = v1
                em.wait("V", em.VS, v1)
                v2 = em.do("V", lambda e, dst=dst, tt=tt: e.tensor_tensor(
                    out=FA_t[:, dst], in0=FA_t[:, dst],
                    in1=SC_t[:, self.RBC + tt * 512: self.RBC + (tt + 1) * 512],
                    op=ALU.mult), em.VS, 1)
                em.wait("V", em.VS, v2)
                lastv = em.do("V", lambda e, dst=dst, c=c: e.tensor_scalar(
                    out=FA_t[:, dst], in0=FA_t[:, dst],
                    scalar1=self.smc(SM.sh_b2 + c), scalar2=None, op0=ALU.add), em.VS, 1)
        init_tok = lastv
        self.gb_bc_col = self.SCT + 768 + 48
        em.wait("T", em.VS, init_tok)
        t1 = em.do("T", lambda e: e.matmul(
            self.bank(6, 8, 0, 1), self.smc(SM.gate_b, 1, 0, 8),
            self.identf[0:8, 0:8], is_transpose=True, start=True, stop=True), em.TS, 1)
        em.wait("V", em.TS, t1)
        v1 = em.do("V", lambda e: e.tensor_copy(
            SC_t[0:1, self.gb_bc_col + 8: self.gb_bc_col + 16], self.bank(6, 8, 0, 1)), em.VS, 1)
        em.wait("T", em.VS, v1)
        t2 = em.do("T", lambda e: e.matmul(
            self.bank(6, 8), self.onesf[0:1, 0:128],
            SC_t[0:1, self.gb_bc_col + 8: self.gb_bc_col + 16],
            start=True, stop=True), em.TS, 1)
        em.wait("V", em.TS, t2)
        v2 = em.do("V", lambda e: e.tensor_copy(
            SC_t[:, self.gb_bc_col: self.gb_bc_col + 8], self.bank(6, 8)), em.VS, 1)
        em.wait("T", em.VS, v2)
        self.emit_router(
            8,
            lambda tp, k: FA_t[:, ACC + k * TOK + tp * 128: ACC + k * TOK + (tp + 1) * 128],
            lambda k: self.smc(SM.gate_w + k * 8, 8),
            WE, l1=None)
        if self.stage and self.stage[0] == "r0":
            if _ROUTER_RAW is None or _ROUTER_RAW is False:
                self.emit_wbc(8, WE, WTC, WBC, 0)
            self.dump_stage([(SC_t, WE, 128, False), (SC_t, WBC, 2048, False),
                             (SC_t, WTC, 2048, False)])
            self._early = True
            return
        parts = _PARTS if _PARTS is not None else {"shared", "experts"}
        for half in range(2):
            tts = [half * 2, half * 2 + 1]
            for e8 in range(9):
                is_sh = e8 == 8
                if is_sh and "shared" not in parts:
                    continue
                if (not is_sh) and "experts" not in parts:
                    continue
                if not is_sh:
                    self.emit_wbc(8, WE, WTC, WBC, e8)
                if is_sh:
                    wsrc1 = lambda m: self.sh_w1[:, m * 128:(m + 1) * 128].rearrange(
                        "(kc p) m -> p kc m", p=128)
                    b1 = SM.sh_b1
                else:
                    wsrc1 = lambda m, e8=e8: self.ew1[e8, :, m * 128:(m + 1) * 128].rearrange(
                        "(kc p) m -> p kc m", p=128)
                    b1 = SM.eb1 + e8 * 32

                def evup(m, tt, b, tok, b1=b1, half=half):
                    em.wait("S", em.TS, tok)
                    ttl = tt - half * 2
                    v = em.do("S", lambda e, m=m, ttl=ttl, b=b: e.activation(
                        out=A_t[:, H0 + m * 1024 + ttl * 512: H0 + m * 1024 + (ttl + 1) * 512],
                        in_=self.bank(b), func=AF.Gelu_apprx_tanh,
                        bias=self.smc(b1 + m)), em.SS, 1)
                    return (em.SS, v)
                em.wait("T", em.VS, init_tok)
                self.emit_proj(32, NCH, wsrc1,
                               lambda k, tt: A_t[:, self.XN + k * TOK + tt * 512: self.XN + k * TOK + (tt + 1) * 512],
                               evup, tts=tts, name=f"ffn0up{e8}")
                if is_sh:
                    wsrc2 = lambda m: self.sh_w2[:, m * 128:(m + 1) * 128].rearrange(
                        "(kc p) m -> p kc m", p=128)
                else:
                    wsrc2 = lambda m, e8=e8: self.ew2[e8, :, m * 128:(m + 1) * 128].rearrange(
                        "(kc p) m -> p kc m", p=128)

                def evdn(m, tt, b, tok, e8=e8, is_sh=is_sh, half=half):
                    ttl = tt - half * 2
                    dst = slice(ACC + m * TOK + tt * 512, ACC + m * TOK + (tt + 1) * 512)
                    em.wait("V", em.TS, tok)
                    em.wait("V", em.VS, em.VS.n)
                    if is_sh:
                        v = em.do("V", lambda e, b=b, dst=dst, m=m: e.scalar_tensor_tensor(
                            out=FA_t[:, dst], in0=self.bank(b),
                            scalar=self.smc(SM.sh_b2 + m), in1=FA_t[:, dst],
                            op0=ALU.add, op1=ALU.add), em.VS, 1)
                    else:
                        v0 = em.do("V", lambda e, b=b, m=m, e8=e8: e.scalar_tensor_tensor(
                            out=SC_t[:, self.SCT: self.SCT + 512], in0=self.bank(b),
                            scalar=self.smc(SM.eb2 + e8 * 8 + m),
                            in1=SC_t[:, WBC + tt * 512: WBC + (tt + 1) * 512],
                            op0=ALU.add, op1=ALU.mult), em.VS, 1)
                        em.wait("V", em.VS, v0)
                        v = em.do("V", lambda e, dst=dst: e.tensor_tensor(
                            out=FA_t[:, dst], in0=SC_t[:, self.SCT: self.SCT + 512],
                            in1=FA_t[:, dst], op=ALU.add), em.VS, 1)
                    return (em.VS, v)
                self.emit_proj(NCH, 32, wsrc2,
                               lambda k, tt, half=half: A_t[:, H0 + k * 1024 + (tt - half * 2) * 512: H0 + k * 1024 + (tt - half * 2 + 1) * 512],
                               evdn, tts=tts, name=f"ffn0dn{e8}",
                               tdeps=[(em.SS, em.SS.n)])
        for c in range(NCH):
            em.wait("Y", em.VS, em.VS.n)
            em.dma("Y", lambda e, c=c: e.dma_start(
                out=self.f_sp[:, c * TOK:(c + 1) * TOK],
                in_=FA_t[:, ACC + c * TOK: ACC + (c + 1) * TOK]))
        self.acc_col = ACC
        self.A.release(amark)
        self.fa_acc_mark = famark

    def emit_ffn1(self):
        em = self
        A_t, FA_t, SC_t, W_t = self.A_t, self.FA_t, self.SC_t, self.W_t
        self.pb_rot = 0
        self.bank_rel = {}
        amark = self.A.mark()
        famark = self.FA.mark()
        wmark1 = self.W.mark()
        UP = self.A.alloc(8192)
        HH = self.A.alloc(4096 * 2)
        XF = self.FA.alloc(8192)
        ACC = self.FA.alloc(8192)
        WE = self.SCT + 640
        WTC, WBC = 0, 2048
        RING = self.SCT
        wtmp = self.W.alloc(1024)
        em.dma("Y", lambda e: e.dma_start(
            out=self.mr_sp[0:1, 0:TOK], in_=SC_t[0:1, self.MBC: self.MBC + TOK]))
        em.dma("Y", lambda e: e.dma_start(
            out=self.mr_sp[0:1, TOK: 2 * TOK], in_=SC_t[0:1, self.RBC: self.RBC + TOK]))
        for half in range(2):
            tts = [half * 2, half * 2 + 1]
            if half == 1:
                t1 = em.dma("Y", lambda e: e.dma_start(
                    out=SC_t[0:1, self.MBC: self.MBC + TOK], in_=self.mr_sp[0:1, 0:TOK]))
                t2 = em.dma("Y", lambda e: e.dma_start(
                    out=SC_t[0:1, self.RBC: self.RBC + TOK], in_=self.mr_sp[0:1, TOK: 2 * TOK]))
                em.dwait("V", t1)
                em.dwait("V", t2)
                em.wait("V", em.VS, em.VS.n)
                rv = em.do("V", lambda e: e.memset(SC_t[:, 5116: 5120], 0.0), em.VS, 1)
                self.bcast_inplace(self.MBC, rv)
                rv2 = em.do("V", lambda e: e.memset(SC_t[:, 5116: 5120], 0.0), em.VS, 1)
                self.bcast_inplace(self.RBC, rv2)
            for it in range(_L1_ITERS):
                lastv = None
                prev_rd = None
                if it == 0:
                    for c in range(NCH):
                        for ttl in range(2):
                            tt = half * 2 + ttl
                            if prev_rd is not None:
                                em.wait("Y", em.VS, prev_rd)
                            t = em.dma("Y", lambda e, c=c, tt=tt: e.dma_start(
                                out=SC_t[:, RING: RING + 512],
                                in_=self.xres[:, c * TOK + tt * 512: c * TOK + (tt + 1) * 512]))
                            em.dwait("V", t)
                            if lastv is not None:
                                em.wait("V", em.VS, lastv)
                            dst = slice(XF + c * 1024 + ttl * 512, XF + c * 1024 + (ttl + 1) * 512)
                            dsta = slice(ACC + c * 1024 + ttl * 512, ACC + c * 1024 + (ttl + 1) * 512)
                            v1 = em.do("V", lambda e, dst=dst, tt=tt: e.tensor_tensor(
                                out=FA_t[:, dst], in0=SC_t[:, RING: RING + 512],
                                in1=SC_t[:, self.MBC + tt * 512: self.MBC + (tt + 1) * 512],
                                op=ALU.subtract), em.VS, 1)
                            prev_rd = v1
                            em.wait("V", em.VS, v1)
                            v2 = em.do("V", lambda e, dst=dst, tt=tt: e.tensor_tensor(
                                out=FA_t[:, dst], in0=FA_t[:, dst],
                                in1=SC_t[:, self.RBC + tt * 512: self.RBC + (tt + 1) * 512],
                                op=ALU.mult), em.VS, 1)
                            em.wait("V", em.VS, v2)
                            lastv = em.do("V", lambda e, dst=dst, dsta=dsta: e.tensor_copy(
                                FA_t[:, dsta], FA_t[:, dst]), em.VS, 1)
                else:
                    em.wait("V", em.VS, em.VS.n)
                    for c in range(NCH):
                        for ttl in range(2):
                            dst = slice(XF + c * 1024 + ttl * 512, XF + c * 1024 + (ttl + 1) * 512)
                            dsta = slice(ACC + c * 1024 + ttl * 512, ACC + c * 1024 + (ttl + 1) * 512)
                            lastv = em.do("V", lambda e, dst=dst, dsta=dsta: e.tensor_copy(
                                FA_t[:, dsta], FA_t[:, dst]), em.VS, 1)
                init_tok = lastv
                self.rb_bc_col = self.SCT + 768 + 32
                self.gb_bc_col = self.SCT + 768 + 40
                for bcol, rowoff in ((self.rb_bc_col, SM.rb_row), (self.gb_bc_col, SM.gb_row)):
                    em.wait("T", em.VS, em.VS.n)
                    tb = em.do("T", lambda e, bcol=bcol, rowoff=rowoff, it=it: e.matmul(
                        self.bank(6, 6), self.onesf[0:1, 0:128],
                        self.small[0:1, rowoff + it * 6: rowoff + it * 6 + 6],
                        start=True, stop=True), em.TS, 1)
                    em.wait("V", em.TS, tb)
                    vb = em.do("V", lambda e, bcol=bcol: e.tensor_copy(
                        SC_t[:, bcol: bcol + 6], self.bank(6, 6)), em.VS, 1)
                    em.wait("T", em.VS, vb)
                em.wait("T", em.VS, init_tok)
                self.emit_router(
                    6,
                    lambda tp, k: FA_t[:, XF + k * 1024 + tp * 128: XF + k * 1024 + (tp + 1) * 128],
                    lambda k, it=it: self.smc(SM.router_W + it * 48 + k * 6, 6),
                    WE, l1=(self.rb_bc_col, self.gb_bc_col), ntok=1024)
                if self.stage and self.stage[0] == "r1":
                    self.emit_wbc(6, WE, WTC, WBC, 0, ntok=1024)
                    self.dump_stage([(SC_t, WE, 48, False), (SC_t, WBC, 2048, False)])
                    self._early = True
                    return
                for e8 in range(8):
                    is_sh = e8 >= 6
                    ns = e8 - 6
                    parts = _L1_PARTS if _L1_PARTS is not None else {"exp", "sh"}
                    if is_sh and "sh" not in parts:
                        continue
                    if (not is_sh) and "exp" not in parts:
                        continue
                    if not is_sh:
                        self.emit_wbc(6, WE, WTC, WBC, e8, ntok=1024)
                    if is_sh:
                        wsrc1 = lambda m, ns=ns: self.share_Wup[:, ns * 1024 + m * 128: ns * 1024 + (m + 1) * 128].rearrange(
                            "(kc p) m -> p kc m", p=128)
                        b1 = SM.share_bup + ns * 8
                    else:
                        wsrc1 = lambda m, e8=e8: self.exp_Wup[e8, :, m * 128:(m + 1) * 128].rearrange(
                            "(kc p) m -> p kc m", p=128)
                        b1 = SM.exp_bup + e8 * 8

                    def evup(m, tt, b, tok, b1=b1, half=half):
                        ttl = tt - half * 2
                        em.wait("S", em.TS, tok)
                        em.wait("S", em.TS, em.TS.n if False else tok)
                        v = em.do("S", lambda e, m=m, ttl=ttl, b=b, b1=b1: e.activation(
                            out=A_t[:, UP + m * 1024 + ttl * 512: UP + m * 1024 + (ttl + 1) * 512],
                            in_=self.bank(b), func=AF.Identity,
                            bias=self.smc(b1 + m)), em.SS, 1)
                        return (em.SS, v)
                    em.wait("T", em.VS, init_tok)
                    self.emit_proj(8, NCH, wsrc1,
                                   lambda k, tt: A_t[:, self.XN + k * TOK + tt * 512: self.XN + k * TOK + (tt + 1) * 512],
                                   evup, tts=tts, name=f"f1up{e8}")
                    if is_sh:
                        wsrc2 = lambda m: self.share_Wg[:, m * 128:(m + 1) * 128].rearrange(
                            "(kc p) m -> p kc m", p=128)
                        bg = SM.share_bg
                    else:
                        wsrc2 = lambda m, e8=e8: self.exp_Wg[e8, :, m * 128:(m + 1) * 128].rearrange(
                            "(kc p) m -> p kc m", p=128)
                        bg = SM.exp_bg + e8 * 8
                    gtmp_rel = {}

                    def evg(m, tt, b, tok, bg=bg, half=half, gtmp_rel=gtmp_rel):
                        ttl = tt - half * 2
                        r = (m * 2 + ttl) % 2
                        em.wait("S", em.TS, tok)
                        if r in gtmp_rel:
                            em.wait("S", gtmp_rel[r][0], gtmp_rel[r][1])
                        sv = em.do("S", lambda e, b=b, r=r, bg=bg, m=m: e.activation(
                            out=W_t[:, wtmp + r * 512: wtmp + (r + 1) * 512],
                            in_=self.bank(b), func=AF.Silu,
                            bias=self.smc(bg + m)), em.SS, 1)
                        em.wait("V", em.SS, sv)
                        vv = em.do("V", lambda e, m=m, ttl=ttl, r=r: e.tensor_tensor(
                            out=A_t[:, HH + m * 1024 + ttl * 512: HH + m * 1024 + (ttl + 1) * 512],
                            in0=W_t[:, wtmp + r * 512: wtmp + (r + 1) * 512],
                            in1=A_t[:, UP + m * 1024 + ttl * 512: UP + m * 1024 + (ttl + 1) * 512],
                            op=ALU.mult), em.VS, 1)
                        gtmp_rel[r] = (em.VS, vv)
                        return (em.VS, vv)
                    self.emit_proj(8, NCH, wsrc2,
                                   lambda k, tt, half=half: A_t[:, UP + k * 1024 + (tt - half * 2) * 512: UP + k * 1024 + (tt - half * 2 + 1) * 512],
                                   evg, tts=tts, name=f"f1g{e8}",
                                   tdeps=[(em.SS, em.SS.n)])
                    if is_sh:
                        wsrc3 = lambda m: self.share_Wdown[:, m * 128:(m + 1) * 128].rearrange(
                            "(kc p) m -> p kc m", p=128)
                        bd = SM.share_bdown
                    else:
                        wsrc3 = lambda m, e8=e8: self.exp_Wdown[e8, :, m * 128:(m + 1) * 128].rearrange(
                            "(kc p) m -> p kc m", p=128)
                        bd = SM.exp_bdown + e8 * 8

                    def evdn(m, tt, b, tok, e8=e8, is_sh=is_sh, bd=bd, half=half):
                        ttl = tt - half * 2
                        dst = slice(ACC + m * 1024 + ttl * 512, ACC + m * 1024 + (ttl + 1) * 512)
                        em.wait("V", em.TS, tok)
                        em.wait("V", em.VS, em.VS.n)
                        if is_sh:
                            v = em.do("V", lambda e, b=b, dst=dst, bd=bd, m=m: e.scalar_tensor_tensor(
                                out=FA_t[:, dst], in0=self.bank(b),
                                scalar=self.smc(bd + m), in1=FA_t[:, dst],
                                op0=ALU.add, op1=ALU.add), em.VS, 1)
                        else:
                            v0 = em.do("V", lambda e, b=b, m=m, bd=bd, ttl=ttl: e.scalar_tensor_tensor(
                                out=SC_t[:, RING: RING + 512], in0=self.bank(b),
                                scalar=self.smc(bd + m),
                                in1=SC_t[:, WBC + ttl * 512: WBC + (ttl + 1) * 512],
                                op0=ALU.add, op1=ALU.mult), em.VS, 1)
                            em.wait("V", em.VS, v0)
                            v = em.do("V", lambda e, dst=dst: e.tensor_tensor(
                                out=FA_t[:, dst], in0=SC_t[:, RING: RING + 512],
                                in1=FA_t[:, dst], op=ALU.add), em.VS, 1)
                        return (em.VS, v)
                    self.emit_proj(8, NCH, wsrc3,
                                   lambda k, tt, half=half: A_t[:, HH + k * 1024 + (tt - half * 2) * 512: HH + k * 1024 + (tt - half * 2 + 1) * 512],
                                   evdn, tts=tts, name=f"f1dn{e8}",
                                   tdeps=[(em.VS, em.VS.n)])
                for b in (0, 1):
                    if self.bank_rel.get(b) is not None:
                        em.wait("T", self.bank_rel[b][0], self.bank_rel[b][1])
                sq_rel = None
                for c in range(NCH):
                    for ttl in range(2):
                        sl = slice(ACC + c * 1024 + ttl * 512, ACC + c * 1024 + (ttl + 1) * 512)
                        em.wait("S", em.VS, em.VS.n)
                        if sq_rel is not None:
                            em.wait("S", em.TS, sq_rel)
                        sq = em.do("S", lambda e, sl=sl: e.activation(
                            out=SC_t[:, RING: RING + 512], in_=FA_t[:, sl],
                            func=AF.Square), em.SS, 1)
                        em.wait("T", em.SS, sq)
                        sq_rel = em.do("T", lambda e, c=c, ttl=ttl: e.matmul(
                            self.PS_t[0:1, ttl * 512: (ttl + 1) * 512],
                            self.onesf[:, 0:1], SC_t[:, RING: RING + 512],
                            start=(c == 0), stop=(c == NCH - 1)), em.TS, 1)
                em.wait("V", em.TS, sq_rel)
                rrow = SC_t[0:1, WBC: WBC + 1024]
                v1 = em.do("V", lambda e, rrow=rrow: e.tensor_scalar(
                    out=rrow, in0=self.PS_t[0:1, 0:1024], scalar1=1.0 / D,
                    scalar2=EPS_RMS, op0=ALU.mult, op1=ALU.add), em.VS, 1)
                em.wait("S", em.VS, v1)
                s1 = em.do("S", lambda e, rrow=rrow: e.activation(
                    out=rrow, in_=rrow, func=AF.Sqrt), em.SS, 1)
                em.wait("V", em.SS, s1)
                v2 = em.do("V", lambda e, rrow=rrow: e.reciprocal(rrow, rrow), em.VS, 1)
                em.wait("T", em.VS, v2)
                for b in (4, 5):
                    if self.bank_rel.get(b) is not None:
                        em.wait("T", self.bank_rel[b][0], self.bank_rel[b][1])
                for ttl in range(2):
                    tb = em.do("T", lambda e, ttl=ttl: e.matmul(
                        self.bank(4 + ttl), self.onesf[0:1, 0:128],
                        SC_t[0:1, WBC + ttl * 512: WBC + (ttl + 1) * 512],
                        start=True, stop=True), em.TS, 1)
                    em.wait("V", em.TS, tb)
                    vb = em.do("V", lambda e, ttl=ttl: e.tensor_copy(
                        SC_t[:, WBC + ttl * 512: WBC + (ttl + 1) * 512],
                        self.bank(4 + ttl)), em.VS, 1)
                    em.wait("T", em.VS, vb)
                lastv = None
                for c in range(NCH):
                    for ttl in range(2):
                        tt = half * 2 + ttl
                        sl = slice(ACC + c * 1024 + ttl * 512, ACC + c * 1024 + (ttl + 1) * 512)
                        dst = slice(XF + c * 1024 + ttl * 512, XF + c * 1024 + (ttl + 1) * 512)
                        em.wait("V", em.VS, em.VS.n)
                        v1 = em.do("V", lambda e, sl=sl, dst=dst, c=c, it=it, ttl=ttl: e.scalar_tensor_tensor(
                            out=FA_t[:, dst], in0=FA_t[:, sl],
                            scalar=self.smc(SM.norm_w + it * 8 + c),
                            in1=SC_t[:, WBC + ttl * 512: WBC + (ttl + 1) * 512],
                            op0=ALU.mult, op1=ALU.mult), em.VS, 1)
                        em.wait("V", em.VS, v1)
                        lastv = em.do("V", lambda e, dst=dst, c=c, tt=tt: e.tensor_copy(
                            A_t[:, self.XN + c * TOK + tt * 512: self.XN + c * TOK + (tt + 1) * 512],
                            FA_t[:, dst]), em.VS, 1)
                if it == _L1_ITERS - 1:
                    for c in range(NCH):
                        em.wait("Y", em.VS, em.VS.n)
                        em.dma("Y", lambda e, c=c, half=half: e.dma_start(
                            out=self.f_sp[:, c * TOK + half * 1024: c * TOK + (half + 1) * 1024],
                            in_=FA_t[:, XF + c * 1024: XF + (c + 1) * 1024]))
        self.W.release(wmark1)
        self.A.release(amark)
        self.FA.release(famark)

    def emit_combine0(self):
        em = self
        FA_t, SC_t = self.FA_t, self.SC_t
        ACC = self.acc_col
        xr = [0, 512]
        sr = [1024, 1536]
        gr = [2048, 2560]
        rel = {}
        for c in range(NCH):
            for tt in range(NTT):
                i = (c * NTT + tt) % 2
                for key, col, srcap in (("x", xr[i], self.x_fm[c * 128:(c + 1) * 128, tt * 512:(tt + 1) * 512]),
                                        ("s", sr[i], self.s_sp[:, c * TOK + tt * 512: c * TOK + (tt + 1) * 512])):
                    if (key + "w", i) in rel:
                        em.wait("Y", rel[(key + "w", i)][0], rel[(key + "w", i)][1])
                    ld = em.dma("Y", lambda e, col=col, srcap=srcap: e.dma_start(
                        out=SC_t[:, col: col + 512], in_=srcap))
                    rel[(key, i, "ld")] = ld
                sl = slice(ACC + c * TOK + tt * 512, ACC + c * TOK + (tt + 1) * 512)
                if ("g", i) in rel:
                    em.wait("S", rel[("g", i)][0], rel[("g", i)][1])
                gv = em.do("S", lambda e, sl=sl, i=i: e.activation(
                    out=SC_t[:, gr[i]: gr[i] + 512], in_=FA_t[:, sl], func=AF.Silu), em.SS, 1)
                em.dwait("V", rel[("s", i, "ld")])
                em.wait("V", em.SS, gv)
                if ("v2", i) in rel:
                    em.wait("V", rel[("v2", i)][0], rel[("v2", i)][1])
                v1 = em.do("V", lambda e, i=i: e.tensor_tensor(
                    out=SC_t[:, sr[i]: sr[i] + 512], in0=SC_t[:, sr[i]: sr[i] + 512],
                    in1=SC_t[:, gr[i]: gr[i] + 512], op=ALU.mult), em.VS, 1)
                rel[("g", i)] = (em.VS, v1)
                rel[("sw", i)] = (em.VS, v1)
                em.dwait("V", rel[("x", i, "ld")])
                em.wait("V", em.VS, v1)
                v2 = em.do("V", lambda e, i=i: e.tensor_tensor(
                    out=SC_t[:, xr[i]: xr[i] + 512], in0=SC_t[:, sr[i]: sr[i] + 512],
                    in1=SC_t[:, xr[i]: xr[i] + 512], op=ALU.add), em.VS, 1)
                em.wait("Y", em.VS, v2)
                ot = em.dma("Y", lambda e, c=c, tt=tt, i=i: e.dma_start(
                    out=self.xres[:, c * TOK + tt * 512: c * TOK + (tt + 1) * 512],
                    in_=SC_t[:, xr[i]: xr[i] + 512]))
                rel[("v2", i)] = ot
                rel[("xw", i)] = ot
        self.FA.release(self.fa_acc_mark)

    def dump_stage(self, pieces):
        em = self
        self.barrier()
        total = sum(p[2] for p in pieces)
        assert total <= self.stage[1]
        off = 0
        for (t, c0, n, isbf) in pieces:
            eng = "G" if isbf else "Y"
            em.dma(eng, lambda e, t=t, c0=c0, n=n, off=off: e.dma_start(
                out=self.dbg[:, off: off + n], in_=t[:, c0: c0 + n],
                max_dma_last_dim=4096))
            off += n
        self.barrier()


def build_program(stage=None):
    b = Builder(stage)
    nc = b.build()
    return nc


def kernel(**inputs):
    stage = _STAGE
    nc = build_program(stage)
    xf = np.asarray(inputs["L0_attn_x"], np.float32).reshape(BATCH * T, D)
    shared = {
        "constb": _pack_consts(),
        "w_qkvT": np.ascontiguousarray(np.asarray(inputs["L0_attn_Wqkv"], np.float32).T),
        "w_outT": np.ascontiguousarray(np.asarray(inputs["L0_attn_Wout"], np.float32).T),
        "sh_w1": np.ascontiguousarray(np.asarray(inputs["L0_ffn_sh_w1"], np.float32)),
        "sh_w2": np.ascontiguousarray(np.asarray(inputs["L0_ffn_sh_w2"], np.float32)),
        "ew1": np.ascontiguousarray(np.asarray(inputs["L0_ffn_ew1"], np.float32)),
        "ew2": np.ascontiguousarray(np.asarray(inputs["L0_ffn_ew2"], np.float32)),
        "w_qT": np.ascontiguousarray(np.asarray(inputs["L1_attn_Wq"], np.float32).T),
        "w_kT": np.ascontiguousarray(np.asarray(inputs["L1_attn_Wk"], np.float32).T),
        "w_vT": np.ascontiguousarray(np.asarray(inputs["L1_attn_Wv"], np.float32).T),
        "w_oT": np.ascontiguousarray(np.asarray(inputs["L1_attn_Wo"], np.float32).T),
        "exp_Wup": np.ascontiguousarray(np.asarray(inputs["L1_ffn_exp_Wup"], np.float32)),
        "exp_Wg": np.ascontiguousarray(np.asarray(inputs["L1_ffn_exp_Wg"], np.float32)),
        "exp_Wdown": np.ascontiguousarray(np.asarray(inputs["L1_ffn_exp_Wdown"], np.float32)),
        "share_Wup": np.ascontiguousarray(np.asarray(inputs["L1_ffn_share_Wup"], np.float32)),
        "share_Wg": np.ascontiguousarray(np.asarray(inputs["L1_ffn_share_Wg"], np.float32)),
        "share_Wdown": np.ascontiguousarray(np.asarray(inputs["L1_ffn_share_Wdown"], np.float32)),
    }
    in_maps = []
    for c in range(NCORE):
        m = dict(shared)
        m["x_fm"] = np.ascontiguousarray(xf[c * TOK:(c + 1) * TOK, :].T)
        m["smallp"] = _pack_small(inputs, c)
        in_maps.append(m)
    res = run_bass_kernel_spmd(nc, in_maps, core_ids=list(range(NCORE)))
    if stage is not None:
        return [np.asarray(r["dbg"]) for r in res.results]
    out = np.empty((BATCH * T, D), np.float32)
    for c in range(NCORE):
        out[c * TOK:(c + 1) * TOK, :] = np.asarray(res.results[c]["out_fm"]).T
    return out.reshape(BATCH, T, D)


# version 04
# speedup vs baseline: 1.2281x; 1.2281x over previous
import numpy as np
from contextlib import ExitStack

import concourse.bass as bass
import concourse.mybir as mybir
from concourse.bass_utils import run_bass_kernel_spmd

F32 = mybir.dt.float32
BF16 = mybir.dt.bfloat16
I32 = mybir.dt.int32
AF = mybir.ActivationFunctionType
ALU = mybir.AluOpType
AX = mybir.AxisListType

BATCH, T, D = 2, 8192, 1024
NCORE = 8
TOK = (BATCH * T) // NCORE
NCH = D // 128
NTT = TOK // 512
HEADS = 16
NHP = 8
DH = 64
L0_CS = 1024
FF_I = 4096
L0_E = 8
L1_E = 6
L1_I = 1024
L1_C = 2
EPS_LN = 1e-5
EPS_RMS = 1e-6

_STAGE = None
_PARTS = None
_ROUTER_RAW = False
_L1_ITERS = 2
_L1_PARTS = None
_ATTN_PARTS = None
_WO_MM_ONLY = False

ENGINES = ("T", "S", "V", "G", "Y")


class Sem:
    __slots__ = ("h", "n", "name")

    def __init__(self, h, name):
        self.h, self.n, self.name = h, 0, name


class Arena:
    def __init__(self, t, cols, name):
        self.t, self.cols, self.name = t, cols, name
        self.off = 0

    def alloc(self, n):
        assert self.off + n <= self.cols, f"{self.name} overflow {self.off}+{n}>{self.cols}"
        r = self.off
        self.off += n
        return r

    def mark(self):
        return self.off

    def release(self, m):
        self.off = m


class SmallMap:
    def __init__(self):
        self._o = 0
        t = self.take
        self.bout = t(8); self.sh_b1 = t(32); self.sh_b2 = t(8)
        self.eb1 = t(256); self.eb2 = t(64)
        self.bq = t(8); self.bk = t(8); self.bv = t(8); self.bo = t(8)
        self.exp_bup = t(48); self.exp_bg = t(48); self.exp_bdown = t(48)
        self.share_bup = t(16); self.share_bg = t(8); self.share_bdown = t(8)
        self.norm_w = t(16); self.gate_w = t(64); self.router_W = t(96)
        self.logpos = t(16)
        self.wl1 = t(1); self.bl1 = t(1); self.wl2t = t(16); self.bl2 = t(1)
        self.gate_b = t(1); self.router_b = t(2); self.gate_bias = t(2)
        self.q0col = t(1)
        self.hsel = t(2)
        self.rb_row = t(12)
        self.gb_row = t(12)
        self.persist = self._o
        self.logpos_row = t(TOK)
        self.wl1row = t(64)
        self.total = self._o

    def take(self, n):
        r = self._o
        self._o += n
        return r


SM = SmallMap()


def _pack_small(inp, core):
    cols = []

    def vec(v):
        v = np.asarray(v, np.float32).reshape(-1)
        assert v.size % 128 == 0
        cols.append(v.reshape(-1, 128).T)

    def pad_col(v):
        c = np.zeros((128, 1), np.float32)
        v = np.asarray(v, np.float32).reshape(-1)
        c[: v.size, 0] = v
        cols.append(c)

    q0 = core * TOK
    vec(inp["L0_attn_bout"]); vec(inp["L0_ffn_sh_b1"]); vec(inp["L0_ffn_sh_b2"])
    vec(np.asarray(inp["L0_ffn_eb1"]).reshape(-1))
    vec(np.asarray(inp["L0_ffn_eb2"]).reshape(-1))
    vec(inp["L1_attn_bq"]); vec(inp["L1_attn_bk"]); vec(inp["L1_attn_bv"]); vec(inp["L1_attn_bo"])
    vec(np.asarray(inp["L1_ffn_exp_bup"]).reshape(-1))
    vec(np.asarray(inp["L1_ffn_exp_bg"]).reshape(-1))
    vec(np.asarray(inp["L1_ffn_exp_bdown"]).reshape(-1))
    vec(inp["L1_ffn_share_bup"]); vec(inp["L1_ffn_share_bg"]); vec(inp["L1_ffn_share_bdown"])
    vec(np.asarray(inp["L1_ffn_norm_w"]).reshape(-1))
    gw = np.asarray(inp["L0_ffn_gate_w"], np.float32)
    cols.append(gw.reshape(8, 128, 8).transpose(1, 0, 2).reshape(128, 64))
    rw = np.asarray(inp["L1_ffn_router_W"], np.float32)
    cols.append(rw.reshape(2, 8, 128, 6).transpose(2, 0, 1, 3).reshape(128, 96))
    lp = np.log(np.arange(q0, q0 + TOK, dtype=np.float64) + 1.0).astype(np.float32)
    cols.append(lp.reshape(16, 128).T)
    pad_col(np.asarray(inp["L0_attn_Wl1"], np.float32)[:, 0])
    pad_col(inp["L0_attn_bl1"])
    w2t = np.zeros((128, 16), np.float32)
    w2t[:64, :] = np.asarray(inp["L0_attn_Wl2"], np.float32).T
    cols.append(w2t)
    pad_col(inp["L0_attn_bl2"])
    pad_col(inp["L0_ffn_gate_b"])
    rb = np.zeros((128, 2), np.float32)
    rb[:6, :] = np.asarray(inp["L1_ffn_router_b"], np.float32).T
    cols.append(rb)
    gb = np.zeros((128, 2), np.float32)
    gb[:6, :] = np.asarray(inp["L1_ffn_gate_bias"], np.float32).T
    cols.append(gb)
    cols.append(np.full((128, 1), float(q0), np.float32))
    hs = np.zeros((128, 2), np.float32)
    hs[:64, 0] = 1.0
    hs[64:, 1] = 1.0
    cols.append(hs)
    rr = np.zeros((128, 12), np.float32)
    rr[0, :] = np.asarray(inp["L1_ffn_router_b"], np.float32).reshape(-1)
    cols.append(rr)
    gg = np.zeros((128, 12), np.float32)
    gg[0, :] = np.asarray(inp["L1_ffn_gate_bias"], np.float32).reshape(-1)
    cols.append(gg)
    lr = np.zeros((128, TOK), np.float32)
    lr[0, :] = lp
    cols.append(lr)
    w1r = np.zeros((128, 64), np.float32)
    w1r[0, :] = np.asarray(inp["L0_attn_Wl1"], np.float32)[:, 0]
    cols.append(w1r)
    out = np.concatenate(cols, axis=1).astype(np.float32)
    assert out.shape[1] == SM.total
    return np.ascontiguousarray(out)


def _pack_consts():
    import ml_dtypes
    ident = np.eye(128, dtype=np.float32)
    ones = np.ones((128, 128), dtype=np.float32)
    return np.ascontiguousarray(
        np.concatenate([ident, ones], axis=1).astype(ml_dtypes.bfloat16))


class Builder:
    def __init__(self, stage=None):
        self.stage = stage
        self.nc = bass.Bass("TRN2", target_bir_lowering=False, debug=False)
        self.q = {e: [] for e in ENGINES}
        self.floor = {e: {} for e in ENGINES}
        self.st = ExitStack()

    def wait(self, e, sem, val):
        if val is None or val <= 0:
            return
        if self.floor[e].get(sem.name, 0) >= val:
            return
        self.floor[e][sem.name] = val
        self.q[e].append(lambda eng, s=sem.h, v=val: eng.wait_ge(s, v))

    def do(self, e, build, sem=None, k=1):
        if sem is None:
            self.q[e].append(build)
            return None
        sem.n += k
        val = sem.n
        self.q[e].append(lambda eng, b=build, s=sem.h, kk=k: b(eng).then_inc(s, kk))
        return val

    def dma(self, qeng, build):
        pool = self.dmapool[qeng]
        idx = self.dmaidx[qeng]
        self.dmaidx[qeng] = (idx + 1) % len(pool)
        s = pool[idx]
        self.wait(qeng, s, s.n)
        val = self.do(qeng, build, s, 16)
        return (s, val)

    def dwait(self, e, tok):
        self.wait(e, tok[0], tok[1])

    def barrier(self):
        for qe, pool in self.dmapool.items():
            for s in pool:
                self.wait(qe, s, s.n)
        for e, s in (("T", self.TS), ("S", self.SS), ("V", self.VS), ("G", self.GS)):
            self.wait(e, s, s.n)
        toks = {}
        for e in ENGINES:
            s = self.BAR[e]
            s.n += 1
            toks[e] = s.n
            self.q[e].append(lambda eng, sh=s.h: eng.nop(nofuse=True).then_inc(sh, 1))
        for e in ENGINES:
            for o in ENGINES:
                if o != e:
                    self.wait(e, self.BAR[o], toks[o])

    def sem(self, name):
        return Sem(self.st.enter_context(self.nc.semaphore(name)), name)

    def build(self):
        nc = self.nc
        stage = self.stage

        def din(name, shape, dt=F32):
            return nc.dram_tensor(name, shape, dt, kind="ExternalInput").ap()

        self.x_fm = din("x_fm", [D, TOK])
        self.smallp = din("smallp", [128, SM.total])
        self.constb = din("constb", [128, 256], BF16)
        self.w_qkvT = din("w_qkvT", [D, 3 * D])
        self.w_outT = din("w_outT", [D, D])
        self.sh_w1 = din("sh_w1", [D, FF_I])
        self.sh_w2 = din("sh_w2", [FF_I, D])
        self.ew1 = din("ew1", [L0_E, D, FF_I])
        self.ew2 = din("ew2", [L0_E, FF_I, D])
        self.w_qT = din("w_qT", [D, D])
        self.w_kT = din("w_kT", [D, D])
        self.w_vT = din("w_vT", [D, D])
        self.w_oT = din("w_oT", [D, D])
        self.exp_Wup = din("exp_Wup", [L1_E, D, L1_I])
        self.exp_Wg = din("exp_Wg", [L1_E, L1_I, L1_I])
        self.exp_Wdown = din("exp_Wdown", [L1_E, L1_I, D])
        self.share_Wup = din("share_Wup", [D, 2 * L1_I])
        self.share_Wg = din("share_Wg", [L1_I, L1_I])
        self.share_Wdown = din("share_Wdown", [L1_I, D])

        self.out_fm = nc.dram_tensor("out_fm", [D, TOK], F32, kind="ExternalOutput").ap()
        self.xres = nc.dram_tensor("xres", [128, NCH * TOK], F32).ap()
        self.s_sp = nc.dram_tensor("s_sp", [128, NCH * TOK], F32).ap()
        self.f_sp = nc.dram_tensor("f_sp", [128, NCH * TOK], F32).ap()
        self.KV_K0 = 0
        self.KV_V0 = NCH * TOK
        self.KV_ST = self.KV_V0 + (TOK // 128) * HEADS * 65
        self.KV_C = self.KV_ST + 1
        self.mr_sp = nc.dram_tensor("mr_sp", [1, 2 * TOK], F32).ap()
        self.kv_loc = nc.dram_tensor("kv_loc", [128, self.KV_C], BF16).ap()
        self.kv_in_h = nc.dram_tensor("kv_in", [128, self.KV_C], BF16)
        self.kv_out_h = nc.dram_tensor("kv_out", [4 * 128, self.KV_C], BF16)
        self.kv_in = self.kv_in_h.ap()
        self.kv_out = self.kv_out_h.ap()
        self.dbg = None
        if stage is not None:
            self.dbg = nc.dram_tensor("dbg", [128, stage[1]], F32, kind="ExternalOutput").ap()

        st = self.st
        with st:
            self.BAR = {e: self.sem("bar" + e) for e in ENGINES}
            self.CC = self.sem("cc")
            self.TS = self.sem("t")
            self.SS = self.sem("s")
            self.VS = self.sem("v")
            self.GS = self.sem("g")
            self.dmapool = {
                "Y": [self.sem(f"dy{i}") for i in range(10)],
                "G": [self.sem(f"dg{i}") for i in range(10)],
            }
            self.dmaidx = {"Y": 0, "G": 0}

            sb = lambda n, s, d=F32: st.enter_context(nc.sbuf_tensor(n, s, d))
            self.small = sb("small", [128, SM.persist])
            self.cidb = sb("cidb", [128, 128], BF16)
            self.cones = sb("cones", [128, 128], BF16)
            self.onesf = sb("onesf", [128, 128], F32)
            self.identf = sb("identf", [128, 128], F32)
            self.iox = st.enter_context(nc.sbuf_tensor("iox", [128, 512], I32))
            self.A_t = sb("A", [128, 49152], BF16)
            self.FA_t = sb("FA", [128, 16384], F32)
            self.W_t = sb("W", [128, 8192], BF16)
            self.SC_t = sb("SC", [128, 5120], F32)
            self.PS_t = st.enter_context(nc.psum_tensor("PS", [128, 3584], F32))
            self.PSB_t = st.enter_context(nc.psum_tensor("PSB", [128, 1024], BF16))
            self.A = Arena(self.A_t, 49152, "A")
            self.FA = Arena(self.FA_t, 16384, "FA")
            self.W = Arena(self.W_t, 8192, "W")
            self.SC = Arena(self.SC_t, 5120, "SC")

            self.emit_all()
            with nc.Block() as blk:
                for key, dec in (("T", blk.tensor), ("S", blk.scalar),
                                 ("V", blk.vector), ("G", blk.gpsimd),
                                 ("Y", blk.sync)):
                    def make(key):
                        def body(eng):
                            for fn in self.q[key]:
                                fn(eng)
                        return body
                    dec(make(key))
        return nc

    def bank(self, i, n=512, p0=0, p1=128):
        assert 0 <= i < 7
        return self.PS_t[p0:p1, i * 512: i * 512 + n]

    def smc(self, off, n=1, p0=0, p1=128):
        return self.small[p0:p1, off: off + n]

    def emit_all(self):
        em = self
        t1 = em.dma("Y", lambda e: e.dma_start(out=em.small[:], in_=em.smallp[:, 0:SM.persist]))
        t2 = em.dma("Y", lambda e: e.dma_start(out=em.cidb[:], in_=em.constb[:, 0:128]))
        t3 = em.dma("Y", lambda e: e.dma_start(out=em.cones[:], in_=em.constb[:, 128:256]))
        vt = em.do("V", lambda e: e.memset(em.onesf[:], 1.0), em.VS, 1)
        em.dwait("V", t2)
        vt2 = em.do("V", lambda e: e.tensor_copy(em.identf[:], em.cidb[:]), em.VS, 1)
        for e in ENGINES:
            for t in (t1, t2, t3):
                em.dwait(e, t)
            em.wait(e, em.VS, vt2)

        self.MBC, self.RBC, self.SCT = 0, 2048, 4096

        self.XN = self.A.alloc(NCH * TOK)
        self.QA = self.A.alloc(NCH * TOK)
        self.LAM = self.SCT
        self.B0C = self.SCT + 256
        self.B1C = self.SCT + 257
        self.QM2 = self.SCT + 258
        self.KM2 = self.SCT + 259
        self.TMP1 = self.SCT + 260
        self.KALL = self.SCT + 264

        self.xn_tok = self.emit_ln(lambda c: self.x_fm[c * 128:(c + 1) * 128, :], EPS_LN)
        if self.stage and self.stage[0] == "ln0":
            self.dump_stage([(self.A_t, self.XN, NCH * TOK, True),
                             (self.SC_t, 0, 2 * TOK, False)])
            return
        self.barrier()
        self.emit_qkv(0)
        self.barrier()
        if self.stage and self.stage[0] == "qkv0":
            self.dump_stage([(self.A_t, self.QA, NCH * TOK, True)])
            return
        self.emit_attn(0)
        self.barrier()
        if self.stage and self.stage[0] == "s0":
            self.dump_from_dram(self.s_sp)
            return
        self.emit_ffn(0)
        if getattr(self, "_early", False):
            return
        self.barrier()
        if self.stage and self.stage[0] == "f0":
            self.dump_from_dram(self.f_sp)
            return
        self.emit_combine0()
        self.barrier()
        if self.stage and self.stage[0] == "x1":
            self.dump_from_dram(self.xres)
            return
        self.xn_tok = self.emit_ln(lambda c: self.xres[:, c * TOK:(c + 1) * TOK], EPS_LN)
        self.barrier()
        self.emit_qkv(1)
        self.barrier()
        self.emit_ffn(1)
        if getattr(self, "_early", False):
            return
        self.barrier()
        if self.stage and self.stage[0] == "f1":
            self.dump_from_dram(self.f_sp)
            return
        self.emit_attn(1)
        self.barrier()
        if self.stage and self.stage[0] == "s1":
            self.dump_from_dram(self.s_sp)
            return

    def dump_from_dram(self, src):
        em = self
        self.barrier()
        for c in range(NCH):
            for tt in range(NTT):
                t = em.dma("Y", lambda e, c=c, tt=tt: e.dma_start(
                    out=self.SC_t[:, 0:512],
                    in_=src[:, c * TOK + tt * 512: c * TOK + (tt + 1) * 512]))
                em.dwait("Y", t)
                t2 = em.dma("Y", lambda e, c=c, tt=tt: e.dma_start(
                    out=self.dbg[:, c * TOK + tt * 512: c * TOK + (tt + 1) * 512],
                    in_=self.SC_t[:, 0:512]))
                em.dwait("Y", t2)
        self.barrier()

    def emit_proj(self, MC, KC, wsrc_fn, rhs_fn, evac_fn, tts=None, name="", tdeps=()):
        em = self
        W_t = self.W_t
        if tts is None:
            tts = list(range(NTT))
        wm = self.W.mark()
        R = min(3, (self.W.cols - self.W.off) // (KC * 128))
        assert R >= 2, f"W ring too small for KC={KC}"
        slots = [self.W.alloc(KC * 128) for _ in range(R)]
        slot_rel = [self.TS.n] * R
        wtok = {}

        def prefetch(m):
            s = m % R
            if slot_rel[s] is not None:
                em.wait("G", em.TS, slot_rel[s])
            wtok[m] = em.dma("G", lambda e, m=m, s=s: e.dma_start(
                out=W_t[:, slots[s]: slots[s] + KC * 128].rearrange(
                    "p (kc m) -> p kc m", m=128),
                in_=wsrc_fn(m), max_dma_last_dim=4096))

        for m in range(min(MC, R - 1)):
            prefetch(m)
        for sem_, val_ in tdeps:
            self.wait("T", sem_, val_)
        NTg = len(tts)
        for m in range(MC):
            s = m % R
            if m + R - 1 < MC:
                prefetch(m + R - 1)
            em.dwait("T", wtok[m])
            bks = []
            for ti in range(NTg):
                b = self.pb_rot
                self.pb_rot = (self.pb_rot + 1) % 6
                bks.append(b)
                if self.bank_rel.get(b) is not None:
                    em.wait("T", self.bank_rel[b][0], self.bank_rel[b][1])
            toks = [None] * NTg
            for k in range(KC):
                for ti, tt in enumerate(tts):
                    toks[ti] = em.do("T", lambda e, tt=tt, k=k, s=s, b=bks[ti]: e.matmul(
                        self.bank(b),
                        W_t[:, slots[s] + k * 128: slots[s] + (k + 1) * 128],
                        rhs_fn(k, tt), start=(k == 0), stop=(k == KC - 1)),
                        em.TS if k == KC - 1 else None, 1)
            for ti, tt in enumerate(tts):
                self.bank_rel[bks[ti]] = evac_fn(m, tt, bks[ti], toks[ti])
            slot_rel[s] = toks[-1]
        self.W.release(wm)

    def emit_qkv(self, layer):
        em = self
        A_t, FA_t, SC_t, W_t = self.A_t, self.FA_t, self.SC_t, self.W_t
        self.pb_rot = 0
        self.bank_rel = {}
        amark = self.A.mark()
        famark = self.FA.mark()
        ktile = [self.A.alloc(TOK) for _ in range(2)]
        vtile = [self.A.alloc(TOK) for _ in range(2)]
        vexp = [self.A.alloc(16 * 130) for _ in range(2)]
        stb = [self.A.alloc(1) for _ in range(2)]
        sqr = [self.FA.alloc(TOK) for _ in range(2)]
        kvdst = self.kv_loc if layer == 0 else self.kv_in
        mz0 = em.do("V", lambda e: e.memset(A_t[:, vexp[0]: vexp[0] + 16 * 130], 1.0), em.VS, 1)
        mz1 = em.do("V", lambda e: e.memset(A_t[:, vexp[1]: vexp[1] + 16 * 130], 1.0), em.VS, 1)
        zq = em.do("V", lambda e: e.memset(SC_t[:, self.QM2: self.QM2 + 2], 0.0), em.VS, 1)
        ring_rel = {}

        def rhs_xn(k, tt):
            return A_t[:, self.XN + k * TOK + tt * 512: self.XN + k * TOK + (tt + 1) * 512]

        def stats_emit(tile_a0, dstcol, m, evtoks):
            r = m % 2
            if ("sq", r) in ring_rel:
                em.wait("S", em.TS, ring_rel[("sq", r)])
            for ev in evtoks:
                em.wait("S", ev[0], ev[1])
            sq = em.do("S", lambda e, tile_a0=tile_a0, r=r: e.activation(
                out=FA_t[:, sqr[r]: sqr[r] + TOK],
                in_=A_t[:, tile_a0: tile_a0 + TOK], func=AF.Square), em.SS, 1)
            em.wait("T", em.SS, sq)
            for tt in range(NTT):
                tk = em.do("T", lambda e, tt=tt: e.matmul(
                    self.bank(6, 512, 0, 2),
                    self.smc(SM.hsel, 2),
                    FA_t[:, sqr[r] + tt * 512: sqr[r] + (tt + 1) * 512],
                    start=True, stop=True), em.TS, 1)
                em.wait("V", em.TS, tk)
                v1 = em.do("V", lambda e: e.tensor_reduce(
                    SC_t[0:2, self.TMP1: self.TMP1 + 1],
                    self.bank(6, 512, 0, 2), axis=AX.X, op=ALU.max), em.VS, 1)
                em.wait("V", em.VS, v1)
                v2 = em.do("V", lambda e, dstcol=dstcol: e.tensor_tensor(
                    out=SC_t[0:2, dstcol: dstcol + 1],
                    in0=SC_t[0:2, self.TMP1: self.TMP1 + 1],
                    in1=SC_t[0:2, dstcol: dstcol + 1], op=ALU.max), em.VS, 1)
                em.wait("T", em.VS, v2)
            ring_rel[("sq", r)] = em.TS.n
            return (em.SS, sq)

        if layer == 0:
            wsrc = lambda m: self.w_qkvT[:, m * 128:(m + 1) * 128].rearrange(
                "(kc p) m -> p kc m", p=128)
            MC = 24

            def evac(m, tt, b, tok):
                if m < 8:
                    em.wait("S", em.TS, tok)
                    v = em.do("S", lambda e, m=m, tt=tt, b=b: e.activation(
                        out=A_t[:, self.QA + m * TOK + tt * 512: self.QA + m * TOK + (tt + 1) * 512],
                        in_=self.bank(b), func=AF.Identity), em.SS, 1)
                    return (em.SS, v)
                kind = "k" if m < 16 else "v"
                mc = m - 8 if m < 16 else m - 16
                r = mc % 2
                ring = ktile if kind == "k" else vtile
                if tt == 0:
                    for rt in ring_rel.get((kind, r), []):
                        em.wait("S", rt[0], rt[1])
                em.wait("S", em.TS, tok)
                v = em.do("S", lambda e, ring=ring, r=r, tt=tt, b=b: e.activation(
                    out=A_t[:, ring[r] + tt * 512: ring[r] + (tt + 1) * 512],
                    in_=self.bank(b), func=AF.Identity), em.SS, 1)
                if tt == NTT - 1:
                    self.finish_tile(kind, mc, r, ring, vexp, kvdst, ring_rel, stats_emit)
                return (em.SS, v)
            em.wait("T", em.VS, self.xn_tok)
            self.emit_proj(MC, NCH, wsrc, rhs_xn, evac, name="qkv0")
            self.emit_lam()
        else:
            for which, wt, bia in (("q", self.w_qT, SM.bq), ("k", self.w_kT, SM.bk),
                                   ("v", self.w_vT, SM.bv)):
                wsrc = lambda m, wt=wt: wt[:, m * 128:(m + 1) * 128].rearrange(
                    "(kc p) m -> p kc m", p=128)

                def evac(m, tt, b, tok, which=which, bia=bia):
                    if which == "q":
                        em.wait("S", em.TS, tok)
                        v = em.do("S", lambda e, m=m, tt=tt, b=b: e.activation(
                            out=A_t[:, self.QA + m * TOK + tt * 512: self.QA + m * TOK + (tt + 1) * 512],
                            in_=self.bank(b), func=AF.Identity,
                            bias=self.smc(SM.bq + m)), em.SS, 1)
                        return (em.SS, v)
                    kind = which
                    r = m % 2
                    ring = ktile if kind == "k" else vtile
                    if tt == 0:
                        for rt in ring_rel.get((kind, r), []):
                            em.wait("S", rt[0], rt[1])
                    em.wait("S", em.TS, tok)
                    v = em.do("S", lambda e, ring=ring, r=r, tt=tt, b=b, m=m, bia=bia: e.activation(
                        out=A_t[:, ring[r] + tt * 512: ring[r] + (tt + 1) * 512],
                        in_=self.bank(b), func=AF.Identity,
                        bias=self.smc(bia + m)), em.SS, 1)
                    if tt == NTT - 1:
                        self.finish_tile(kind, m, r, ring, vexp, kvdst, ring_rel, stats_emit)
                    return (em.SS, v)
                em.wait("T", em.VS, self.xn_tok)
                self.emit_proj(8, NCH, wsrc, rhs_xn, evac, name="qkv1" + which)
        self.finalize_stats(layer, stb)
        if layer == 1:
            for qe, pool in self.dmapool.items():
                for s in pool:
                    em.wait(qe, s, s.n)
                    em.wait("G", s, s.n)
            em.wait("G", em.VS, em.VS.n)
            em.wait("G", em.SS, em.SS.n)
            em.wait("G", em.TS, em.TS.n)
            cc = em.do("G", lambda e: e.collective_compute(
                "AllGather", ALU.bypass, ins=[self.kv_in],
                outs=[self.kv_out],
                replica_groups=[[0, 1, 2, 3], [4, 5, 6, 7]]), em.CC, 1)
            self.cc_tok = cc
        self.A.release(amark)
        self.FA.release(famark)

    def finish_tile(self, kind, mc, r, ring, vexp, kvdst, ring_rel, stats_emit):
        em = self
        A_t = self.A_t
        evtok = (em.SS, em.SS.n)
        if kind == "k":
            sqtok = stats_emit(ring[r], self.KM2, mc, [evtok])
            em.wait("Y", evtok[0], evtok[1])
            t = em.dma("Y", lambda e, r=r, mc=mc, ring=ring: e.dma_start(
                out=kvdst[:, self.KV_K0 + mc * TOK: self.KV_K0 + (mc + 1) * TOK],
                in_=A_t[:, ring[r]: ring[r] + TOK]))
            ring_rel[(kind, r)] = [t, sqtok]
        else:
            if ("vx", r) in ring_rel:
                em.wait("V", *ring_rel[("vx", r)])
            vt = getattr(self, "psb_rel", None)
            tr = None
            for tc in range(16):
                em.wait("T", evtok[0], evtok[1])
                if vt is not None:
                    em.wait("T", em.VS, vt)
                tr = em.do("T", lambda e, r=r, tc=tc: e.transpose(
                    self.PSB_t[:, 0:128],
                    A_t[:, ring[r] + tc * 128: ring[r] + (tc + 1) * 128],
                    self.cidb[:]), em.TS, 1)
                em.wait("V", em.TS, tr)
                em.do("V", lambda e, r=r, tc=tc: e.tensor_copy(
                    A_t[:, vexp[r] + tc * 130: vexp[r] + tc * 130 + 64],
                    self.PSB_t[:, 0:64]), em.VS, 1)
                vt = em.do("V", lambda e, r=r, tc=tc: e.tensor_copy(
                    A_t[:, vexp[r] + tc * 130 + 65: vexp[r] + tc * 130 + 129],
                    self.PSB_t[:, 64:128]), em.VS, 1)
            self.psb_rel = vt
            em.wait("Y", em.VS, vt)
            t = em.dma("Y", lambda e, r=r, mc=mc, kvdst=kvdst: e.dma_start(
                out=kvdst[:, self.KV_V0: self.KV_V0 + 16 * 1040].rearrange(
                    "p (tc cc) -> p tc cc", cc=1040)[:, :, 2 * mc * 65: 2 * mc * 65 + 130],
                in_=A_t[:, vexp[r]: vexp[r] + 16 * 130].rearrange(
                    "p (tc cc) -> p tc cc", cc=130)))
            ring_rel[(kind, r)] = [(em.TS, tr)]
            ring_rel[("vx", r)] = t

    def finalize_stats(self, layer, stb):
        em = self
        SC_t = self.SC_t
        for col in (self.QM2, self.KM2):
            t = em.dma("Y", lambda e, col=col: e.dma_start(
                out=SC_t[0:1, self.TMP1 + 1: self.TMP1 + 2],
                in_=SC_t[1:2, col: col + 1],
                allow_slow_non_contiguous=True))
            em.dwait("V", t)
            em.wait("V", em.VS, em.VS.n)
            v = em.do("V", lambda e, col=col: e.tensor_tensor(
                out=SC_t[0:1, col: col + 1], in0=SC_t[0:1, self.TMP1 + 1: self.TMP1 + 2],
                in1=SC_t[0:1, col: col + 1], op=ALU.max), em.VS, 1)
            em.wait("Y", em.VS, v)
        vals = {}
        for nm, col in (("q", self.QM2), ("k", self.KM2)):
            em.wait("T", em.VS, em.VS.n)
            tk = em.do("T", lambda e, col=col: e.matmul(
                self.bank(6, 1), self.onesf[0:1, 0:128],
                SC_t[0:1, col: col + 1], start=True, stop=True), em.TS, 1)
            em.wait("V", em.TS, tk)
            dst = self.TMP1 + 2 if nm == "q" else self.TMP1 + 3
            v = em.do("V", lambda e, dst=dst: e.tensor_copy(
                SC_t[:, dst: dst + 1], self.bank(6, 1)), em.VS, 1)
            em.wait("T", em.VS, v)
            vals[nm] = dst
            vals[nm + "tok"] = v
        if layer == 1:
            em.wait("V", em.VS, vals["ktok"])
            vb = em.do("V", lambda e: e.tensor_copy(
                self.A_t[:, stb[0]: stb[0] + 1], SC_t[:, vals["k"]: vals["k"] + 1]), em.VS, 1)
            em.wait("Y", em.VS, vb)
            em.dma("Y", lambda e: e.dma_start(
                out=self.kv_in[:, self.KV_ST: self.KV_ST + 1],
                in_=self.A_t[:, stb[0]: stb[0] + 1],
                allow_slow_non_contiguous=True))
            self.qm_bc = vals["q"]
            return
        em.wait("V", em.VS, vals["ktok"])
        em.wait("V", em.VS, vals["qtok"])
        v1 = em.do("V", lambda e: e.tensor_tensor(
            out=SC_t[:, self.B0C: self.B0C + 1], in0=SC_t[:, vals["q"]: vals["q"] + 1],
            in1=SC_t[:, vals["k"]: vals["k"] + 1], op=ALU.mult), em.VS, 1)
        em.wait("S", em.VS, v1)
        s1 = em.do("S", lambda e: e.activation(
            out=SC_t[:, self.B0C: self.B0C + 1], in_=SC_t[:, self.B0C: self.B0C + 1],
            func=AF.Sqrt), em.SS, 1)
        em.wait("V", em.SS, s1)
        em.do("V", lambda e: e.tensor_scalar(
            out=SC_t[:, self.B0C: self.B0C + 1], in0=SC_t[:, self.B0C: self.B0C + 1],
            scalar1=-0.125, scalar2=None, op0=ALU.mult), em.VS, 1)

    def emit_b1(self):
        em = self
        SC_t = self.SC_t
        famark = self.FA.mark()
        stf = self.FA.alloc(4)
        for e in ("Y", "G"):
            em.wait(e, em.CC, self.cc_tok)
        t = em.dma("G", lambda e: e.dma_start(
            out=self.FA_t[:, stf: stf + 4].rearrange("p (r c) -> p r c", c=1),
            in_=self.kv_out.rearrange("(r p) c -> p r c", p=128)[:, :, self.KV_ST: self.KV_ST + 1],
            allow_slow_non_contiguous=True))
        em.dwait("V", t)
        v1 = em.do("V", lambda e: e.tensor_reduce(
            SC_t[:, self.KALL: self.KALL + 1], self.FA_t[:, stf: stf + 4],
            axis=AX.X, op=ALU.max), em.VS, 1)
        em.wait("V", em.VS, v1)
        v2 = em.do("V", lambda e: e.tensor_tensor(
            out=SC_t[:, self.B1C: self.B1C + 1], in0=SC_t[:, self.KALL: self.KALL + 1],
            in1=SC_t[:, self.qm_bc: self.qm_bc + 1], op=ALU.mult), em.VS, 1)
        em.wait("S", em.VS, v2)
        s1 = em.do("S", lambda e: e.activation(
            out=SC_t[:, self.B1C: self.B1C + 1], in_=SC_t[:, self.B1C: self.B1C + 1],
            func=AF.Sqrt), em.SS, 1)
        em.wait("V", em.SS, s1)
        em.do("V", lambda e: e.tensor_scalar(
            out=SC_t[:, self.B1C: self.B1C + 1], in0=SC_t[:, self.B1C: self.B1C + 1],
            scalar1=-0.125, scalar2=-1.0, op0=ALU.mult, op1=ALU.add), em.VS, 1)
        self.FA.release(famark)

    def emit_lam(self):
        em = self
        FA_t, SC_t = self.FA_t, self.SC_t
        famark = self.FA.mark()
        lrow = self.FA.alloc(TOK)
        wrow = self.FA.alloc(64)
        h1 = self.FA.alloc(TOK)
        lfm = self.FA.alloc(TOK)
        t1 = em.dma("Y", lambda e: e.dma_start(
            out=FA_t[0:1, lrow: lrow + TOK], in_=self.smallp[0:1, SM.logpos_row: SM.logpos_row + TOK]))
        t2 = em.dma("Y", lambda e: e.dma_start(
            out=FA_t[0:1, wrow: wrow + 64], in_=self.smallp[0:1, SM.wl1row: SM.wl1row + 64]))
        for tt in range(NTT):
            em.dwait("T", t1)
            em.dwait("T", t2)
            em.wait("T", em.SS, em.SS.n)
            mk = em.do("T", lambda e, tt=tt: e.matmul(
                self.bank(5, 512, 0, 64), FA_t[0:1, wrow: wrow + 64],
                FA_t[0:1, lrow + tt * 512: lrow + (tt + 1) * 512],
                start=True, stop=True), em.TS, 1)
            em.wait("S", em.TS, mk)
            s1 = em.do("S", lambda e, tt=tt: e.activation(
                out=FA_t[0:64, h1 + tt * 512: h1 + (tt + 1) * 512],
                in_=self.bank(5, 512, 0, 64), func=AF.Relu,
                bias=self.smc(SM.bl1, 1, 0, 64)), em.SS, 1)
            em.wait("T", em.SS, s1)
            mk2 = em.do("T", lambda e, tt=tt: e.matmul(
                self.bank(5, 512, 0, 16), self.smc(SM.wl2t, 16, 0, 64),
                FA_t[0:64, h1 + tt * 512: h1 + (tt + 1) * 512],
                start=True, stop=True), em.TS, 1)
            em.wait("S", em.TS, mk2)
            s2 = em.do("S", lambda e, tt=tt: e.activation(
                out=FA_t[0:16, lfm + tt * 512: lfm + (tt + 1) * 512],
                in_=self.bank(5, 512, 0, 16), func=AF.Sigmoid,
                bias=self.smc(SM.bl2, 1, 0, 16)), em.SS, 1)
            em.wait("T", em.SS, s2)
        vtok = None
        for tc in range(16):
            if vtok is not None:
                em.wait("T", em.VS, vtok)
            tr = em.do("T", lambda e, tc=tc: e.matmul(
                self.bank(5, 16), FA_t[0:16, lfm + tc * 128: lfm + (tc + 1) * 128],
                self.identf[0:16, 0:16], is_transpose=True,
                start=True, stop=True), em.TS, 1)
            em.wait("V", em.TS, tr)
            vtok = em.do("V", lambda e, tc=tc: e.tensor_scalar(
                out=SC_t[:, self.LAM + tc * 16: self.LAM + (tc + 1) * 16],
                in0=self.bank(5, 16), scalar1=0.125, scalar2=None,
                op0=ALU.mult), em.VS, 1)
        em.wait("T", em.VS, vtok)
        self.FA.release(famark)

    def emit_ln(self, xsrc_fn, eps):
        em = self
        FA_t, A_t, SC_t, PS_t = self.FA_t, self.A_t, self.SC_t, self.PS_t
        famark = self.FA.mark()
        X0 = self.FA.alloc(NCH * TOK)
        MBC, RBC, SCT = self.MBC, self.RBC, self.SCT
        XN = self.XN
        ld = []
        for c in range(NCH):
            ld.append(em.dma("Y", lambda e, c=c: e.dma_start(
                out=FA_t[:, X0 + c * TOK: X0 + (c + 1) * TOK], in_=xsrc_fn(c))))
        t_sum = None
        for c in range(NCH):
            em.dwait("T", ld[c])
            for tt in range(NTT):
                t_sum = em.do("T", lambda e, c=c, tt=tt: e.matmul(
                    PS_t[0:1, tt * 512:(tt + 1) * 512], em.onesf[:, 0:1],
                    FA_t[:, X0 + c * TOK + tt * 512: X0 + c * TOK + (tt + 1) * 512],
                    start=(c == 0), stop=(c == NCH - 1)),
                    em.TS if c == NCH - 1 else None, 1)
        em.wait("V", em.TS, t_sum)
        mrow = SC_t[0:1, MBC: MBC + TOK]
        v_m = em.do("V", lambda e: e.tensor_scalar(
            out=mrow, in0=PS_t[0:1, 0:TOK], scalar1=1.0 / D, scalar2=None,
            op0=ALU.mult), em.VS, 1)
        mb_tok = self.bcast_inplace(MBC, v_m)
        sq_tok = None
        d_cols = SC_t[:, SCT: SCT + 1024]
        for c in range(NCH):
            for half in range(2):
                sl = slice(X0 + c * TOK + half * 1024, X0 + c * TOK + (half + 1) * 1024)
                asl = slice(XN + c * TOK + half * 1024, XN + c * TOK + (half + 1) * 1024)
                em.wait("V", em.VS, mb_tok)
                if sq_tok is not None:
                    em.wait("V", em.SS, sq_tok)
                v_d = em.do("V", lambda e, sl=sl: e.tensor_tensor(
                    out=d_cols, in0=FA_t[:, sl],
                    in1=SC_t[:, MBC + (sl.start - X0) % TOK: MBC + (sl.start - X0) % TOK + 1024],
                    op=ALU.subtract), em.VS, 1)
                v_b = em.do("V", lambda e, sl=sl, asl=asl: e.tensor_tensor(
                    out=A_t[:, asl], in0=FA_t[:, sl],
                    in1=SC_t[:, MBC + (sl.start - X0) % TOK: MBC + (sl.start - X0) % TOK + 1024],
                    op=ALU.subtract), em.VS, 1)
                em.wait("S", em.VS, v_b)
                sq_tok = em.do("S", lambda e, sl=sl: e.activation(
                    out=FA_t[:, sl], in_=d_cols, func=AF.Square), em.SS, 1)
                em.wait("T", em.SS, sq_tok)
                for tt in range(2):
                    t_sq = em.do("T", lambda e, sl=sl, tt=tt, c=c, half=half: e.matmul(
                        PS_t[0:1, half * 1024 + tt * 512: half * 1024 + (tt + 1) * 512],
                        em.onesf[:, 0:1],
                        FA_t[:, sl.start + tt * 512: sl.start + (tt + 1) * 512],
                        start=(c == 0), stop=(c == NCH - 1)),
                        em.TS if c == NCH - 1 else None, 1)
        em.wait("V", em.TS, t_sq)
        rrow = SC_t[0:1, RBC: RBC + TOK]
        v_v = em.do("V", lambda e: e.tensor_scalar(
            out=rrow, in0=PS_t[0:1, 0:TOK], scalar1=1.0 / D,
            scalar2=eps, op0=ALU.mult, op1=ALU.add), em.VS, 1)
        em.wait("S", em.VS, v_v)
        s_r = em.do("S", lambda e: e.activation(out=rrow, in_=rrow, func=AF.Sqrt), em.SS, 1)
        em.wait("V", em.SS, s_r)
        v_r = em.do("V", lambda e: e.reciprocal(rrow, rrow), em.VS, 1)
        rb_tok = self.bcast_inplace(RBC, v_r)
        xn_tok = None
        for c in range(NCH):
            em.wait("V", em.VS, rb_tok)
            xn_tok = em.do("V", lambda e, c=c: e.tensor_tensor(
                out=A_t[:, self.XN + c * TOK: self.XN + (c + 1) * TOK],
                in0=A_t[:, self.XN + c * TOK: self.XN + (c + 1) * TOK],
                in1=SC_t[:, RBC: RBC + TOK], op=ALU.mult), em.VS, 1)
        self.FA.release(famark)
        return xn_tok

    def bcast_inplace(self, col0, row_tok):
        em = self
        SC_t, PS_t = self.SC_t, self.PS_t
        for b in (4, 5):
            br = getattr(self, "bank_rel", {})
            if br.get(b) is not None:
                em.wait("T", br[b][0], br[b][1])
        em.wait("T", em.VS, row_tok)
        toks = []
        for tt in range(NTT):
            t1 = em.do("T", lambda e, tt=tt: e.matmul(
                self.bank(4 + (tt % 2)), em.onesf[0:1, 0:128],
                SC_t[0:1, col0 + tt * 512: col0 + (tt + 1) * 512],
                start=True, stop=True), em.TS, 1)
            em.wait("V", em.TS, t1)
            v1 = em.do("V", lambda e, tt=tt: e.tensor_copy(
                SC_t[:, col0 + tt * 512: col0 + (tt + 1) * 512],
                self.bank(4 + (tt % 2))), em.VS, 1)
            em.wait("T", em.VS, v1)
            toks.append(v1)
        return toks[-1]

    def emit_attn(self, layer):
        em = self
        A_t, FA_t, SC_t, W_t = self.A_t, self.FA_t, self.SC_t, self.W_t
        self.pb_rot = 0
        self.bank_rel = {}
        kvsrc = self.kv_loc if layer == 0 else None
        if layer == 0:
            KCOLS, VCOLS, NKC = 1024, 8 * 130, 8
            amark = self.A.mark()
            KS = [self.A.alloc(KCOLS) for _ in range(3)]
            VSL = [self.A.alloc(VCOLS) for _ in range(3)]
            AT = [self.A.alloc(1024) for _ in range(3)]
            HB = self.A.alloc(8 * 512)
            famark = self.FA.mark()
            dnm_c = self.FA.alloc(512)
            sring = [self.FA.alloc(512) for _ in range(2)]
            bias_col = SC_t[:, self.B0C: self.B0C + 1]
        else:
            KCOLS, VCOLS, NKC = 4096, 32 * 130, 64
            KS = [0, 4096, 8192]
            HB = 12288
            VSL = [32768, 36928, 41088]
            AT = [45248, 46272, 47296]
            dnm_c = 0
            sring = [512, 1024]
            gcol = 1536
            x1r = [2048, 2560]
            f1r = [3072, 3584]
            bias_col = SC_t[:, self.B1C: self.B1C + 1]
            self.emit_b1()
            em.wait("S", em.VS, em.VS.n)
            em.wait("V", em.VS, em.VS.n)
        nhalf = NKC * 128 // (KCOLS)
        cph = NKC // nhalf
        wmark = self.W.mark()
        WR = [self.W.alloc(4096) for _ in range(2)]
        wo_src = self.w_outT if layer == 0 else self.w_oT
        wotok = {}
        wo_rel = {}

        def wo_prefetch(tq, mh):
            i = tq * 2 + mh
            s = i % 2
            if s in wo_rel:
                em.wait("G", em.TS, wo_rel[s])
            wotok[i] = em.dma("G", lambda e, mh=mh, s=s: e.dma_start(
                out=W_t[:, WR[s]: WR[s] + 4096].rearrange("p (kc m) -> p kc m", m=512),
                in_=wo_src[:, mh * 512:(mh + 1) * 512].rearrange(
                    "(kc p) m -> p kc m", p=128), max_dma_last_dim=4096))

        slab_tok = {}
        slab_rel = {}

        def load_slab(tq, idx):
            hp, half = divmod(idx, nhalf)
            sl = idx % 3
            if sl in slab_rel:
                em.wait("Y", em.TS, slab_rel[sl])
            cq = tq // 2
            if layer == 0:
                ktok = em.dma("Y", lambda e, hp=hp, sl=sl, cq=cq: e.dma_start(
                    out=A_t[:, KS[sl]: KS[sl] + KCOLS],
                    in_=kvsrc[:, hp * TOK + cq * 1024: hp * TOK + cq * 1024 + 1024]))
                vtok = em.dma("Y", lambda e, hp=hp, sl=sl, cq=cq: e.dma_start(
                    out=A_t[:, VSL[sl]: VSL[sl] + VCOLS].rearrange(
                        "p (tc cc) -> p tc cc", cc=130),
                    in_=kvsrc[:, self.KV_V0 + cq * 8 * 1040:
                              self.KV_V0 + (cq * 8 + 8) * 1040].rearrange(
                        "p (tc cc) -> p tc cc", cc=1040)[:, :, 2 * hp * 65: 2 * hp * 65 + 130]))
            else:
                kvo = self.kv_out.rearrange("(r p) c -> p r c", p=128)
                r0 = half * 2
                ktok = em.dma("Y", lambda e, hp=hp, sl=sl, r0=r0, kvo=kvo: e.dma_start(
                    out=A_t[:, KS[sl]: KS[sl] + KCOLS].rearrange(
                        "p (r t) -> p r t", t=TOK),
                    in_=kvo[:, r0: r0 + 2, hp * TOK: (hp + 1) * TOK]))
                vts = []
                for ri in range(2):
                    vts.append(em.dma("Y", lambda e, hp=hp, sl=sl, r0=r0, ri=ri, kvo=kvo: e.dma_start(
                        out=A_t[:, VSL[sl] + ri * 16 * 130: VSL[sl] + (ri + 1) * 16 * 130].rearrange(
                            "p (tc cc) -> p tc cc", cc=130),
                        in_=kvo[:, r0 + ri,
                                self.KV_V0: self.KV_V0 + 16 * 1040].rearrange(
                            "p (tc cc) -> p tc cc", cc=1040)[:, :, 2 * hp * 65: 2 * hp * 65 + 130])))
                vtok = vts
            slab_tok[idx] = (ktok, vtok)

        def delta_build(tq):
            rlist = [c for c in range(64) if c * 128 + 127 - tq * 512 > 4097]
            pos = {}
            gtok = None
            vtok = None
            for i, c in enumerate(rlist):
                pos[c] = i * 512
                base = tq * 512 - c * 128 + 8191 - 2047
                if gtok is not None:
                    em.wait("G", em.VS, vtok)
                gtok = em.do("G", lambda e, base=base: e.iota(
                    self.iox[:], base=base, pattern=[[1, 512]],
                    channel_multiplier=-1), em.GS, 1)
                em.wait("V", em.GS, gtok)
                i0 = i * 512
                v0 = em.do("V", lambda e, i0=i0: e.tensor_copy(
                    FA_t[:, i0: i0 + 512], self.iox[:]), em.VS, 1)
                em.wait("V", em.VS, v0)
                v1 = em.do("V", lambda e, i0=i0: e.tensor_scalar(
                    out=FA_t[:, i0: i0 + 512], in0=FA_t[:, i0: i0 + 512],
                    scalar1=self.smc(SM.q0col), scalar2=None, op0=ALU.add), em.VS, 1)
                em.wait("V", em.VS, v1)
                v1b = em.do("V", lambda e, i0=i0: e.tensor_scalar(
                    out=SC_t[:, 4608: 4608 + 512], in0=FA_t[:, i0: i0 + 512],
                    scalar1=-1.0, scalar2=None, op0=ALU.mult), em.VS, 1)
                em.wait("V", em.VS, v1b)
                v2 = em.do("V", lambda e, i0=i0: e.tensor_tensor(
                    out=FA_t[:, i0: i0 + 512], in0=FA_t[:, i0: i0 + 512],
                    in1=SC_t[:, 4608: 4608 + 512], op=ALU.max), em.VS, 1)
                em.wait("V", em.VS, v2)
                vtok = em.do("V", lambda e, i0=i0: e.tensor_scalar(
                    out=FA_t[:, i0: i0 + 512], in0=FA_t[:, i0: i0 + 512],
                    scalar1=-1.0 / 256.0, scalar2=2047.0 / 256.0,
                    op0=ALU.mult, op1=ALU.add), em.VS, 1)
                em.wait("V", em.VS, vtok)
                vtok = em.do("V", lambda e, i0=i0: e.tensor_scalar(
                    out=FA_t[:, i0: i0 + 512], in0=FA_t[:, i0: i0 + 512],
                    scalar1=0.0, scalar2=None, op0=ALU.max), em.VS, 1)
            return pos, vtok

        at_rel = {}
        exp_tok = {}
        av_bank_rel = None
        self.av_prev_rel = None
        prev_sl = 0
        for tq in range(4):
            wo_prefetch(tq, 0)
            wo_prefetch(tq, 1)
            if layer == 1:
                dpos, dtok = delta_build(tq)
            else:
                dpos, dtok = {}, None
            load_slab(tq, 0)
            nidx = NHP * nhalf
            for idx in range(nidx):
                hp, half = divmod(idx, nhalf)
                if idx + 1 < nidx:
                    load_slab(tq, idx + 1)
                elif tq < 3:
                    pass
                sl = idx % 3
                ktok, vtok_d = slab_tok[idx]
                em.dwait("T", ktok)
                for vv in (vtok_d if isinstance(vtok_d, list) else [vtok_d]):
                    em.dwait("T", vv)
                def emit_av(kc, sl_, ats):
                    ex2 = exp_tok[kc]
                    em.wait("T", ex2[0], ex2[1])
                    if kc == 0 and self.av_prev_rel is not None:
                        em.wait("T", self.av_prev_rel[0], self.av_prev_rel[1])
                    kcl_ = kc % cph
                    ave = em.do("T", lambda e, sl=sl_, kcl=kcl_, ats=ats, kc=kc: e.matmul(
                        self.bank(4, 512, 0, 65),
                        A_t[:, VSL[sl] + kcl * 130: VSL[sl] + kcl * 130 + 65],
                        A_t[:, AT[ats]: AT[ats] + 512],
                        start=(kc == 0), stop=(kc == NKC - 1)),
                        em.TS, 1)
                    avo = em.do("T", lambda e, sl=sl_, kcl=kcl_, ats=ats, kc=kc: e.matmul(
                        self.bank(5, 512, 0, 65),
                        A_t[:, VSL[sl] + kcl * 130 + 65: VSL[sl] + kcl * 130 + 130],
                        A_t[:, AT[ats] + 512: AT[ats] + 1024],
                        start=(kc == 0), stop=(kc == NKC - 1)),
                        em.TS, 1)
                    at_rel[ats] = (em.TS, avo)
                    slab_rel[sl_] = em.TS.n
                    self.last_av = (ave, avo)
                for kcl in range(cph):
                    kc = half * cph + kcl
                    be = (kc % 2) * 2
                    bo = be + 1
                    for b in (be, bo):
                        if self.bank_rel.get(b) is not None:
                            em.wait("T", self.bank_rel[b][0], self.bank_rel[b][1])
                    mm_e = em.do("T", lambda e, sl=sl, kcl=kcl, hp=hp, tq=tq, be=be: e.matmul(
                        self.bank(be),
                        A_t[0:64, KS[sl] + kcl * 128: KS[sl] + (kcl + 1) * 128],
                        A_t[0:64, self.QA + hp * TOK + tq * 512: self.QA + hp * TOK + (tq + 1) * 512],
                        start=True, stop=True), em.TS, 1)
                    mm_o = em.do("T", lambda e, sl=sl, kcl=kcl, hp=hp, tq=tq, bo=bo: e.matmul(
                        self.bank(bo),
                        A_t[64:128, KS[sl] + kcl * 128: KS[sl] + (kcl + 1) * 128],
                        A_t[64:128, self.QA + hp * TOK + tq * 512: self.QA + hp * TOK + (tq + 1) * 512],
                        start=True, stop=True), em.TS, 1)
                    dtk = None
                    if layer == 1 and kc in dpos:
                        em.wait("V", em.TS, mm_o)
                        em.wait("V", em.VS, dtok)
                        em.do("V", lambda e, be=be, kc=kc, dpos=dpos: e.tensor_tensor(
                            out=self.bank(be), in0=self.bank(be),
                            in1=FA_t[:, dpos[kc]: dpos[kc] + 512], op=ALU.add), em.VS, 1)
                        dtk = em.do("V", lambda e, bo=bo, kc=kc, dpos=dpos: e.tensor_tensor(
                            out=self.bank(bo), in0=self.bank(bo),
                            in1=FA_t[:, dpos[kc]: dpos[kc] + 512], op=ALU.add), em.VS, 1)
                    ats = kc % 3
                    if at_rel.get(ats) is not None:
                        em.wait("S", at_rel[ats][0], at_rel[ats][1])
                    em.wait("S", em.TS, mm_o)
                    if dtk is not None:
                        em.wait("S", em.VS, dtk)
                    if layer == 1:
                        ex = em.do("S", lambda e, be=be, ats=ats: e.activation(
                            out=A_t[:, AT[ats]: AT[ats] + 1024],
                            in_=self.PS_t[:, be * 512: (be + 2) * 512],
                            func=AF.Exp, scale=0.125, bias=bias_col), em.SS, 1)
                    else:
                        cq = tq // 2
                        em.do("S", lambda e, be=be, ats=ats, cq=cq, kcl=kcl, hp=hp: e.activation(
                            out=A_t[:, AT[ats]: AT[ats] + 512],
                            in_=self.bank(be),
                            func=AF.Exp,
                            scale=SC_t[:, self.LAM + (cq * 8 + kcl) * 16 + 2 * hp:
                                       self.LAM + (cq * 8 + kcl) * 16 + 2 * hp + 1],
                            bias=bias_col), em.SS, 1)
                        ex = em.do("S", lambda e, bo=bo, ats=ats, cq=cq, kcl=kcl, hp=hp: e.activation(
                            out=A_t[:, AT[ats] + 512: AT[ats] + 1024],
                            in_=self.bank(bo),
                            func=AF.Exp,
                            scale=SC_t[:, self.LAM + (cq * 8 + kcl) * 16 + 2 * hp + 1:
                                       self.LAM + (cq * 8 + kcl) * 16 + 2 * hp + 2],
                            bias=bias_col), em.SS, 1)
                    self.bank_rel[be] = (em.SS, ex)
                    self.bank_rel[bo] = (em.SS, ex)
                    exp_tok[kc] = (em.SS, ex)
                    if _ATTN_PARTS is not None and "av" not in _ATTN_PARTS:
                        at_rel[ats] = (em.SS, ex)
                        self.last_av = None
                        continue
                    if kc > 0:
                        emit_av(kc - 1, sl if kcl > 0 else prev_sl, (kc - 1) % 3)
                    prev_sl = sl
                ap_ = _ATTN_PARTS if _ATTN_PARTS is not None else {"av", "evac", "wo"}
                slab_rel[sl] = em.TS.n
                if half == nhalf - 1:
                    if "av" in ap_:
                        emit_av(NKC - 1, sl, (NKC - 1) % 3)
                    slab_rel[sl] = em.TS.n
                    if "evac" in ap_ and "av" in ap_:
                        av_bank_rel = self.emit_av_evac(layer, hp, HB, dnm_c)
                        self.av_prev_rel = av_bank_rel
            ap_ = _ATTN_PARTS if _ATTN_PARTS is not None else {"av", "evac", "wo"}
            if "wo" in ap_:
                self.emit_wo(layer, tq, HB, WR, wotok, wo_rel, sring,
                             None if layer == 0 else (gcol, x1r, f1r), dnm_c)
        self.W.release(wmark)
        if layer == 0:
            self.A.release(amark)
            self.FA.release(famark)

    def emit_av_evac(self, layer, hp, HB, dnm_c):
        em = self
        A_t, SC_t, FA_t = self.A_t, self.SC_t, self.FA_t
        dn = (FA_t if layer == 0 else SC_t)
        ave, avo = self.last_av
        rel = None
        for j, avtok in ((0, ave), (1, avo)):
            bk = 4 + j
            em.wait("V", em.TS, avtok)
            v1 = em.do("V", lambda e, bk=bk: e.tensor_copy(
                dn[64:65, dnm_c: dnm_c + 512], self.bank(bk, 512, 64, 65)), em.VS, 1)
            em.wait("V", em.VS, v1)
            v2 = em.do("V", lambda e: e.reciprocal(
                dn[64:65, dnm_c: dnm_c + 512], dn[64:65, dnm_c: dnm_c + 512]), em.VS, 1)
            em.wait("T", em.VS, v2)
            t1 = em.do("T", lambda e: e.matmul(
                self.bank(6, 512, 0, 64), self.onesf[64:65, 0:64],
                dn[64:65, dnm_c: dnm_c + 512], start=True, stop=True), em.TS, 1)
            em.wait("V", em.TS, t1)
            v3 = em.do("V", lambda e: e.tensor_copy(
                dn[0:64, dnm_c: dnm_c + 512], self.bank(6, 512, 0, 64)), em.VS, 1)
            em.wait("V", em.VS, v3)
            v4 = em.do("V", lambda e, bk=bk, j=j, hp=hp: e.tensor_tensor(
                out=A_t[64 * j: 64 * j + 64, HB + hp * 512: HB + (hp + 1) * 512],
                in0=self.bank(bk, 512, 0, 64), in1=dn[0:64, dnm_c: dnm_c + 512],
                op=ALU.mult), em.VS, 1)
            rel = (em.VS, v4)
        return rel

    def emit_wo(self, layer, tq, HB, WR, wotok, wo_rel, sring, comb, dnm_c):
        em = self
        A_t, SC_t, FA_t, W_t = self.A_t, self.SC_t, self.FA_t, self.W_t
        hb_tok = em.VS.n
        if layer == 1:
            gcol, x1r, f1r = comb
            ldx = {}
            ldf = {}
        sr_rel = {}
        for mh in range(2):
            i = tq * 2 + mh
            s = i % 2
            em.dwait("T", wotok[i])
            for ml in range(4):
                m = mh * 4 + ml
                b = (self.pb_rot % 3) * 2
                self.pb_rot += 1
                for bb in (b, b + 1):
                    if self.bank_rel.get(bb) is not None:
                        em.wait("T", self.bank_rel[bb][0], self.bank_rel[bb][1])
                em.wait("T", em.VS, hb_tok)
                tok = None
                for par in range(2):
                    for hh in range(8):
                        j = 2 * hh + par
                        tok = em.do("T", lambda e, s=s, j=j, ml=ml, b=b, par=par, hh=hh: e.matmul(
                            self.bank(b + par),
                            W_t[64 * par: 64 * par + 64,
                                WR[s] + (j // 2) * 512 + ml * 128: WR[s] + (j // 2) * 512 + (ml + 1) * 128],
                            A_t[64 * par: 64 * par + 64, HB + (j // 2) * 512: HB + (j // 2 + 1) * 512],
                            start=(hh == 0), stop=(hh == 7)),
                            em.TS if hh == 7 else None, 1)
                wo_rel[s] = tok
                r = m % 2
                if _WO_MM_ONLY:
                    self.bank_rel[b] = (em.TS, tok)
                    self.bank_rel[b + 1] = (em.TS, tok)
                    continue
                if layer == 0:
                    if ("s", r) in sr_rel:
                        for rt in sr_rel[("s", r)]:
                            em.wait("S", rt[0], rt[1]) if rt[0].name.startswith("d") else em.wait("S", rt[0], rt[1])
                    em.wait("S", em.TS, tok)
                    sv = em.do("S", lambda e, m=m, b=b, r=r: e.activation(
                        out=FA_t[:, sring[r]: sring[r] + 512], in_=self.bank(b),
                        func=AF.Identity, bias=self.smc(SM.bout + m)), em.SS, 1)
                    self.bank_rel[b] = (em.SS, sv)
                    em.wait("V", em.SS, sv)
                    sv2 = em.do("V", lambda e, b=b, r=r: e.tensor_tensor(
                        out=FA_t[:, sring[r]: sring[r] + 512],
                        in0=self.bank(b + 1), in1=FA_t[:, sring[r]: sring[r] + 512],
                        op=ALU.add), em.VS, 1)
                    self.bank_rel[b + 1] = (em.VS, sv2)
                    em.wait("Y", em.VS, sv2)
                    dt = em.dma("Y", lambda e, m=m, tq=tq, r=r: e.dma_start(
                        out=self.s_sp[:, m * TOK + tq * 512: m * TOK + (tq + 1) * 512],
                        in_=FA_t[:, sring[r]: sring[r] + 512]))
                    sr_rel[("s", r)] = [dt, (em.VS, sv2)]
                else:
                    if ("x", r) in sr_rel:
                        em.dwait("Y", sr_rel[("x", r)])
                    ldx[m] = em.dma("Y", lambda e, m=m, tq=tq, r=r: e.dma_start(
                        out=SC_t[:, x1r[r]: x1r[r] + 512],
                        in_=self.xres[:, m * TOK + tq * 512: m * TOK + (tq + 1) * 512]))
                    if ("f", r) in sr_rel:
                        em.wait("Y", sr_rel[("f", r)][0], sr_rel[("f", r)][1])
                    ldf[m] = em.dma("Y", lambda e, m=m, tq=tq, r=r: e.dma_start(
                        out=SC_t[:, f1r[r]: f1r[r] + 512],
                        in_=self.f_sp[:, m * TOK + tq * 512: m * TOK + (tq + 1) * 512]))
                    em.wait("S", em.TS, tok)
                    if ("s", r) in sr_rel:
                        em.wait("S", sr_rel[("s", r)][0], sr_rel[("s", r)][1])
                        em.wait("V", sr_rel[("s", r)][0], sr_rel[("s", r)][1])
                    sv0 = em.do("S", lambda e, m=m, b=b, r=r: e.activation(
                        out=SC_t[:, sring[r]: sring[r] + 512], in_=self.bank(b),
                        func=AF.Identity, bias=self.smc(SM.bo + m)), em.SS, 1)
                    self.bank_rel[b] = (em.SS, sv0)
                    em.wait("V", em.SS, sv0)
                    sv = em.do("V", lambda e, b=b, r=r: e.tensor_tensor(
                        out=SC_t[:, sring[r]: sring[r] + 512],
                        in0=self.bank(b + 1), in1=SC_t[:, sring[r]: sring[r] + 512],
                        op=ALU.add), em.VS, 1)
                    self.bank_rel[b + 1] = (em.VS, sv)
                    em.dwait("S", ldf[m])
                    if ("g", 0) in sr_rel:
                        em.wait("S", sr_rel[("g", 0)][0], sr_rel[("g", 0)][1])
                    gv = em.do("S", lambda e, r=r: e.activation(
                        out=SC_t[:, gcol: gcol + 512], in_=SC_t[:, f1r[r]: f1r[r] + 512],
                        func=AF.Silu), em.SS, 1)
                    sr_rel[("f", r)] = (em.SS, gv)
                    em.wait("V", em.SS, gv)
                    em.wait("V", em.VS, sv)
                    v1 = em.do("V", lambda e, r=r: e.tensor_tensor(
                        out=SC_t[:, sring[r]: sring[r] + 512],
                        in0=SC_t[:, sring[r]: sring[r] + 512],
                        in1=SC_t[:, gcol: gcol + 512], op=ALU.mult), em.VS, 1)
                    sr_rel[("s", r)] = (em.VS, v1)
                    sr_rel[("g", 0)] = (em.VS, v1)
                    em.dwait("V", ldx[m])
                    em.wait("V", em.VS, v1)
                    v2 = em.do("V", lambda e, r=r: e.tensor_tensor(
                        out=SC_t[:, x1r[r]: x1r[r] + 512],
                        in0=SC_t[:, sring[r]: sring[r] + 512],
                        in1=SC_t[:, x1r[r]: x1r[r] + 512], op=ALU.add), em.VS, 1)
                    em.wait("Y", em.VS, v2)
                    ot = em.dma("Y", lambda e, m=m, tq=tq, r=r: e.dma_start(
                        out=self.out_fm[m * 128:(m + 1) * 128, tq * 512:(tq + 1) * 512],
                        in_=SC_t[:, x1r[r]: x1r[r] + 512]))
                    sr_rel[("x", r)] = ot

    def emit_router(self, E, lhsT_fn, rhs_fn, wE_col, l1=None, ntok=TOK):
        em = self
        SC_t = self.SC_t
        NTP = ntok // 128
        for tp in range(NTP):
            em.wait("T", self.bank_rel.get(66, (em.VS, 0))[0],
                    self.bank_rel.get(66, (em.VS, 0))[1])
            tok = None
            for k in range(NCH):
                tok = em.do("T", lambda e, tp=tp, k=k, E=E: e.matmul(
                    self.bank(6, E), lhsT_fn(tp, k), rhs_fn(k),
                    start=(k == 0), stop=(k == NCH - 1)),
                    em.TS if k == NCH - 1 else None, 1)
            C0 = self.SCT + 768
            p_ = SC_t[:, wE_col + tp * E: wE_col + (tp + 1) * E]
            t1c = SC_t[:, C0: C0 + E]
            gate_src = t1c
            m1 = SC_t[:, C0 + 8: C0 + 9]
            m2 = SC_t[:, C0 + 9: C0 + 10]
            m3 = SC_t[:, C0 + 10: C0 + 11]
            em.wait("V", em.TS, tok)
            em.wait("V", em.SS, self.SS.n if False else em.SS.n)
            em.wait("V", em.VS, em.VS.n)
            if l1 is None:
                v = em.do("V", lambda e, p_=p_: e.tensor_tensor(
                    out=t1c, in0=self.bank(6, E), in1=SC_t[:, self.gb_bc_col: self.gb_bc_col + E],
                    op=ALU.add), em.VS, 1)
            else:
                rbc, gbc = l1
                v = em.do("V", lambda e, p_=p_, gate_src=gate_src: e.tensor_tensor(
                    out=t1c, in0=self.bank(6, E), in1=SC_t[:, rbc: rbc + E],
                    op=ALU.add), em.VS, 1)
            self.bank_rel[66] = (em.VS, v)
            em.wait("V", em.VS, v)
            if _ROUTER_RAW == "logits":
                em.do("V", lambda e, p_=p_: e.tensor_copy(p_, t1c), em.VS, 1)
                continue
            if l1 is None:
                v1 = em.do("V", lambda e, p_=p_, gate_src=gate_src: e.tensor_reduce(m1, gate_src, axis=AX.X, op=ALU.max), em.VS, 1)
                em.wait("V", em.VS, v1)
                if _ROUTER_RAW == "logits2":
                    em.do("V", lambda e, p_=p_: e.tensor_copy(p_, gate_src), em.VS, 1)
                    continue
                if _ROUTER_RAW == "max":
                    for jj in range(E):
                        em.do("V", lambda e, p_=p_, jj=jj: e.tensor_copy(
                            p_[:, jj: jj + 1], m1), em.VS, 1)
                    continue
                v2 = em.do("V", lambda e, p_=p_, gate_src=gate_src: e.tensor_scalar(
                    out=m2, in0=m1, scalar1=-1.0, scalar2=None, op0=ALU.mult), em.VS, 1)
                em.wait("S", em.VS, v2)
                s1 = em.do("S", lambda e, p_=p_, gate_src=gate_src: e.activation(
                    out=p_, in_=gate_src, func=AF.Exp, bias=m2, accum_out=m3), em.SS, 1)
                if _ROUTER_RAW == "exp":
                    em.wait("V", em.SS, s1)
                    continue
                if _ROUTER_RAW == "sum":
                    em.wait("V", em.SS, s1)
                    em.do("V", lambda e, p_=p_: e.tensor_scalar(
                        out=p_, in0=m3, scalar1=0.0, scalar2=None, op0=ALU.add), em.VS, 1)
                    continue
                em.wait("V", em.SS, s1)
                v3 = em.do("V", lambda e, p_=p_, gate_src=gate_src: e.reciprocal(m1, m3), em.VS, 1)
                em.wait("V", em.VS, v3)
                v4 = em.do("V", lambda e, p_=p_, gate_src=gate_src: e.tensor_scalar(
                    out=p_, in0=p_, scalar1=m1, scalar2=None, op0=ALU.mult), em.VS, 1)
                if _ROUTER_RAW == "softmax":
                    continue
                em.wait("V", em.VS, v4)
                v5 = em.do("V", lambda e, p_=p_, gate_src=gate_src: e.tensor_reduce(m1, p_, axis=AX.X, op=ALU.max), em.VS, 1)
                em.wait("V", em.VS, v5)
                v6 = em.do("V", lambda e, p_=p_, gate_src=gate_src: e.tensor_scalar(
                    out=t1c, in0=p_, scalar1=m1, scalar2=None, op0=ALU.is_ge), em.VS, 1)
                em.wait("V", em.VS, v6)
                v7 = em.do("V", lambda e, p_=p_, gate_src=gate_src: e.scalar_tensor_tensor(
                    out=t1c, in0=t1c, scalar=-2.0, in1=p_, op0=ALU.mult, op1=ALU.add), em.VS, 1)
                em.wait("V", em.VS, v7)
                v8 = em.do("V", lambda e, p_=p_, gate_src=gate_src: e.tensor_reduce(m2, t1c, axis=AX.X, op=ALU.max), em.VS, 1)
                em.wait("V", em.VS, v8)
                v9 = em.do("V", lambda e, p_=p_, gate_src=gate_src: e.tensor_tensor(out=m3, in0=m1, in1=m2, op=ALU.add), em.VS, 1)
                em.wait("V", em.VS, v9)
                v10 = em.do("V", lambda e, p_=p_, gate_src=gate_src: e.tensor_scalar(
                    out=m3, in0=m3, scalar1=1e-9, scalar2=None, op0=ALU.add), em.VS, 1)
                em.wait("V", em.VS, v10)
                v11 = em.do("V", lambda e, p_=p_, gate_src=gate_src: e.reciprocal(m3, m3), em.VS, 1)
                em.wait("V", em.VS, v11)
                v12 = em.do("V", lambda e, p_=p_, gate_src=gate_src: e.tensor_scalar(
                    out=t1c, in0=p_, scalar1=m2, scalar2=None, op0=ALU.is_ge), em.VS, 1)
                em.wait("V", em.VS, v12)
                v13 = em.do("V", lambda e, p_=p_, gate_src=gate_src: e.tensor_tensor(out=p_, in0=p_, in1=t1c, op=ALU.mult), em.VS, 1)
                em.wait("V", em.VS, v13)
                vfin = em.do("V", lambda e, p_=p_, gate_src=gate_src: e.tensor_scalar(
                    out=p_, in0=p_, scalar1=m3, scalar2=None, op0=ALU.mult), em.VS, 1)
                if _ROUTER_RAW == "snap":
                    em.wait("V", em.VS, vfin)
                    em.do("V", lambda e, p_=p_, tp=tp, E=E: e.tensor_copy(
                        SC_t[:, tp * E: (tp + 1) * E], p_), em.VS, 1)
            else:
                bsd = SC_t[:, C0 + 16: C0 + 16 + E]
                v1 = em.do("V", lambda e, gbc=gbc: e.tensor_tensor(
                    out=bsd, in0=gate_src, in1=SC_t[:, gbc: gbc + E], op=ALU.add), em.VS, 1)
                em.wait("V", em.VS, v1)
                v2 = em.do("V", lambda e, p_=p_, gate_src=gate_src: e.tensor_reduce(m1, bsd, axis=AX.X, op=ALU.max), em.VS, 1)
                em.wait("V", em.VS, v2)
                v3 = em.do("V", lambda e, p_=p_, gate_src=gate_src: e.tensor_scalar(
                    out=p_, in0=bsd, scalar1=m1, scalar2=None, op0=ALU.is_ge), em.VS, 1)
                em.wait("V", em.VS, v3)
                v4 = em.do("V", lambda e, p_=p_, gate_src=gate_src: e.scalar_tensor_tensor(
                    out=bsd, in0=p_, scalar=-1e30, in1=bsd, op0=ALU.mult, op1=ALU.add), em.VS, 1)
                em.wait("V", em.VS, v4)
                v5 = em.do("V", lambda e, p_=p_, gate_src=gate_src: e.tensor_reduce(m2, bsd, axis=AX.X, op=ALU.max), em.VS, 1)
                em.wait("V", em.VS, v5)
                v6 = em.do("V", lambda e, gbc=gbc: e.tensor_tensor(
                    out=bsd, in0=gate_src, in1=SC_t[:, gbc: gbc + E], op=ALU.add), em.VS, 1)
                em.wait("V", em.VS, v6)
                v7 = em.do("V", lambda e, p_=p_, gate_src=gate_src: e.tensor_scalar(
                    out=bsd, in0=bsd, scalar1=m2, scalar2=None, op0=ALU.is_ge), em.VS, 1)
                em.wait("V", em.VS, v7)
                v8 = em.do("V", lambda e, p_=p_, gate_src=gate_src: e.tensor_tensor(out=p_, in0=gate_src, in1=bsd, op=ALU.mult), em.VS, 1)
                em.wait("V", em.VS, v8)
                v9 = em.do("V", lambda e, p_=p_, gate_src=gate_src: e.tensor_scalar(
                    out=bsd, in0=bsd, scalar1=1e30, scalar2=-1e30, op0=ALU.mult, op1=ALU.add), em.VS, 1)
                em.wait("V", em.VS, v9)
                v10 = em.do("V", lambda e, p_=p_, gate_src=gate_src: e.tensor_tensor(out=p_, in0=p_, in1=bsd, op=ALU.add), em.VS, 1)
                em.wait("V", em.VS, v10)
                v11 = em.do("V", lambda e, p_=p_, gate_src=gate_src: e.tensor_reduce(m1, p_, axis=AX.X, op=ALU.max), em.VS, 1)
                em.wait("V", em.VS, v11)
                v12 = em.do("V", lambda e, p_=p_, gate_src=gate_src: e.tensor_scalar(
                    out=m2, in0=m1, scalar1=-1.0, scalar2=None, op0=ALU.mult), em.VS, 1)
                em.wait("S", em.VS, v12)
                s1 = em.do("S", lambda e, p_=p_, gate_src=gate_src: e.activation(
                    out=p_, in_=p_, func=AF.Exp, bias=m2, accum_out=m3), em.SS, 1)
                em.wait("V", em.SS, s1)
                v13 = em.do("V", lambda e, p_=p_, gate_src=gate_src: e.reciprocal(m1, m3), em.VS, 1)
                em.wait("V", em.VS, v13)
                em.do("V", lambda e, p_=p_, gate_src=gate_src: e.tensor_scalar(
                    out=p_, in0=p_, scalar1=m1, scalar2=None, op0=ALU.mult), em.VS, 1)

    def emit_wbc(self, E, wE_col, wT_col, wbc_col, e, ntok=TOK):
        em = self
        SC_t = self.SC_t
        NTP = ntok // 128
        if e == 0:
            for tp in range(NTP):
                em.wait("T", em.VS, em.VS.n)
                tr = em.do("T", lambda e_, tp=tp, E=E: e_.matmul(
                    self.bank(6, 128, 0, E),
                    SC_t[:, wE_col + tp * E: wE_col + (tp + 1) * E],
                    self.identf[:, 0:128], is_transpose=True,
                    start=True, stop=True), em.TS, 1)
                em.wait("V", em.TS, tr)
                vv = em.do("V", lambda e_, tp=tp, E=E: e_.tensor_copy(
                    SC_t[0:E, wT_col + tp * 128: wT_col + (tp + 1) * 128],
                    self.bank(6, 128, 0, E)), em.VS, 1)
                em.wait("T", em.VS, vv)
        for tt in range(ntok // 512):
            em.wait("T", em.VS, em.VS.n)
            t1 = em.do("T", lambda e_, tt=tt, E=E, e=e: e_.matmul(
                self.bank(6, 512, 0, 1), self.identf[0:E, e: e + 1],
                SC_t[0:E, wT_col + tt * 512: wT_col + (tt + 1) * 512],
                start=True, stop=True), em.TS, 1)
            em.wait("V", em.TS, t1)
            v1 = em.do("V", lambda e_, tt=tt: e_.tensor_copy(
                SC_t[0:1, wbc_col + tt * 512: wbc_col + (tt + 1) * 512],
                self.bank(6, 512, 0, 1)), em.VS, 1)
            em.wait("T", em.VS, v1)
            t2 = em.do("T", lambda e_, tt=tt: e_.matmul(
                self.bank(6, 512), self.onesf[0:1, 0:128],
                SC_t[0:1, wbc_col + tt * 512: wbc_col + (tt + 1) * 512],
                start=True, stop=True), em.TS, 1)
            em.wait("V", em.TS, t2)
            v2 = em.do("V", lambda e_, tt=tt: e_.tensor_copy(
                SC_t[:, wbc_col + tt * 512: wbc_col + (tt + 1) * 512],
                self.bank(6, 512)), em.VS, 1)
            em.wait("T", em.VS, v2)
        return em.VS.n

    def emit_ffn(self, layer):
        if layer == 0:
            self.emit_ffn0()
        else:
            self.emit_ffn1()

    def emit_ffn0(self):
        em = self
        A_t, FA_t, SC_t = self.A_t, self.FA_t, self.SC_t
        self.pb_rot = 0
        self.bank_rel = {}
        amark = self.A.mark()
        famark = self.FA.mark()
        H0 = self.QA
        assert self.A.mark() == 32768
        ACC = self.FA.alloc(NCH * TOK)
        WE, WTC, WBC = self.SCT, 0, 2048
        WE = self.SCT + 640
        ring = [self.SCT, None]
        lastv = None
        prev_rd = None
        for c in range(NCH):
            for tt in range(NTT):
                if prev_rd is not None:
                    em.wait("Y", em.VS, prev_rd)
                t = em.dma("Y", lambda e, c=c, tt=tt: e.dma_start(
                    out=SC_t[:, self.SCT: self.SCT + 512],
                    in_=self.x_fm[c * 128:(c + 1) * 128, tt * 512:(tt + 1) * 512]))
                em.dwait("V", t)
                if lastv is not None:
                    em.wait("V", em.VS, lastv)
                dst = slice(ACC + c * TOK + tt * 512, ACC + c * TOK + (tt + 1) * 512)
                v1 = em.do("V", lambda e, dst=dst, tt=tt: e.tensor_tensor(
                    out=FA_t[:, dst], in0=SC_t[:, self.SCT: self.SCT + 512],
                    in1=SC_t[:, self.MBC + tt * 512: self.MBC + (tt + 1) * 512],
                    op=ALU.subtract), em.VS, 1)
                prev_rd = v1
                em.wait("V", em.VS, v1)
                v2 = em.do("V", lambda e, dst=dst, tt=tt: e.tensor_tensor(
                    out=FA_t[:, dst], in0=FA_t[:, dst],
                    in1=SC_t[:, self.RBC + tt * 512: self.RBC + (tt + 1) * 512],
                    op=ALU.mult), em.VS, 1)
                em.wait("V", em.VS, v2)
                lastv = em.do("V", lambda e, dst=dst, c=c: e.tensor_scalar(
                    out=FA_t[:, dst], in0=FA_t[:, dst],
                    scalar1=self.smc(SM.sh_b2 + c), scalar2=None, op0=ALU.add), em.VS, 1)
        init_tok = lastv
        self.gb_bc_col = self.SCT + 768 + 48
        em.wait("T", em.VS, init_tok)
        t1 = em.do("T", lambda e: e.matmul(
            self.bank(6, 8, 0, 1), self.smc(SM.gate_b, 1, 0, 8),
            self.identf[0:8, 0:8], is_transpose=True, start=True, stop=True), em.TS, 1)
        em.wait("V", em.TS, t1)
        v1 = em.do("V", lambda e: e.tensor_copy(
            SC_t[0:1, self.gb_bc_col + 8: self.gb_bc_col + 16], self.bank(6, 8, 0, 1)), em.VS, 1)
        em.wait("T", em.VS, v1)
        t2 = em.do("T", lambda e: e.matmul(
            self.bank(6, 8), self.onesf[0:1, 0:128],
            SC_t[0:1, self.gb_bc_col + 8: self.gb_bc_col + 16],
            start=True, stop=True), em.TS, 1)
        em.wait("V", em.TS, t2)
        v2 = em.do("V", lambda e: e.tensor_copy(
            SC_t[:, self.gb_bc_col: self.gb_bc_col + 8], self.bank(6, 8)), em.VS, 1)
        em.wait("T", em.VS, v2)
        self.emit_router(
            8,
            lambda tp, k: FA_t[:, ACC + k * TOK + tp * 128: ACC + k * TOK + (tp + 1) * 128],
            lambda k: self.smc(SM.gate_w + k * 8, 8),
            WE, l1=None)
        if self.stage and self.stage[0] == "r0":
            if _ROUTER_RAW is None or _ROUTER_RAW is False:
                self.emit_wbc(8, WE, WTC, WBC, 0)
            self.dump_stage([(SC_t, WE, 128, False), (SC_t, WBC, 2048, False),
                             (SC_t, WTC, 2048, False)])
            self._early = True
            return
        parts = _PARTS if _PARTS is not None else {"shared", "experts"}
        for half in range(2):
            tts = [half * 2, half * 2 + 1]
            for e8 in range(9):
                is_sh = e8 == 8
                if is_sh and "shared" not in parts:
                    continue
                if (not is_sh) and "experts" not in parts:
                    continue
                if not is_sh:
                    self.emit_wbc(8, WE, WTC, WBC, e8)
                if is_sh:
                    wsrc1 = lambda m: self.sh_w1[:, m * 128:(m + 1) * 128].rearrange(
                        "(kc p) m -> p kc m", p=128)
                    b1 = SM.sh_b1
                else:
                    wsrc1 = lambda m, e8=e8: self.ew1[e8, :, m * 128:(m + 1) * 128].rearrange(
                        "(kc p) m -> p kc m", p=128)
                    b1 = SM.eb1 + e8 * 32

                def evup(m, tt, b, tok, b1=b1, half=half):
                    em.wait("S", em.TS, tok)
                    ttl = tt - half * 2
                    v = em.do("S", lambda e, m=m, ttl=ttl, b=b: e.activation(
                        out=A_t[:, H0 + m * 1024 + ttl * 512: H0 + m * 1024 + (ttl + 1) * 512],
                        in_=self.bank(b), func=AF.Gelu_apprx_tanh,
                        bias=self.smc(b1 + m)), em.SS, 1)
                    return (em.SS, v)
                em.wait("T", em.VS, init_tok)
                self.emit_proj(32, NCH, wsrc1,
                               lambda k, tt: A_t[:, self.XN + k * TOK + tt * 512: self.XN + k * TOK + (tt + 1) * 512],
                               evup, tts=tts, name=f"ffn0up{e8}")
                if is_sh:
                    wsrc2 = lambda m: self.sh_w2[:, m * 128:(m + 1) * 128].rearrange(
                        "(kc p) m -> p kc m", p=128)
                else:
                    wsrc2 = lambda m, e8=e8: self.ew2[e8, :, m * 128:(m + 1) * 128].rearrange(
                        "(kc p) m -> p kc m", p=128)

                def evdn(m, tt, b, tok, e8=e8, is_sh=is_sh, half=half):
                    ttl = tt - half * 2
                    dst = slice(ACC + m * TOK + tt * 512, ACC + m * TOK + (tt + 1) * 512)
                    em.wait("V", em.TS, tok)
                    em.wait("V", em.VS, em.VS.n)
                    if is_sh:
                        v = em.do("V", lambda e, b=b, dst=dst, m=m: e.scalar_tensor_tensor(
                            out=FA_t[:, dst], in0=self.bank(b),
                            scalar=self.smc(SM.sh_b2 + m), in1=FA_t[:, dst],
                            op0=ALU.add, op1=ALU.add), em.VS, 1)
                    else:
                        v0 = em.do("V", lambda e, b=b, m=m, e8=e8: e.scalar_tensor_tensor(
                            out=SC_t[:, self.SCT: self.SCT + 512], in0=self.bank(b),
                            scalar=self.smc(SM.eb2 + e8 * 8 + m),
                            in1=SC_t[:, WBC + tt * 512: WBC + (tt + 1) * 512],
                            op0=ALU.add, op1=ALU.mult), em.VS, 1)
                        em.wait("V", em.VS, v0)
                        v = em.do("V", lambda e, dst=dst: e.tensor_tensor(
                            out=FA_t[:, dst], in0=SC_t[:, self.SCT: self.SCT + 512],
                            in1=FA_t[:, dst], op=ALU.add), em.VS, 1)
                    return (em.VS, v)
                self.emit_proj(NCH, 32, wsrc2,
                               lambda k, tt, half=half: A_t[:, H0 + k * 1024 + (tt - half * 2) * 512: H0 + k * 1024 + (tt - half * 2 + 1) * 512],
                               evdn, tts=tts, name=f"ffn0dn{e8}",
                               tdeps=[(em.SS, em.SS.n)])
        for c in range(NCH):
            em.wait("Y", em.VS, em.VS.n)
            em.dma("Y", lambda e, c=c: e.dma_start(
                out=self.f_sp[:, c * TOK:(c + 1) * TOK],
                in_=FA_t[:, ACC + c * TOK: ACC + (c + 1) * TOK]))
        self.acc_col = ACC
        self.A.release(amark)
        self.fa_acc_mark = famark

    def emit_ffn1(self):
        em = self
        A_t, FA_t, SC_t, W_t = self.A_t, self.FA_t, self.SC_t, self.W_t
        self.pb_rot = 0
        self.bank_rel = {}
        amark = self.A.mark()
        famark = self.FA.mark()
        wmark1 = self.W.mark()
        UP = self.A.alloc(8192)
        HH = self.A.alloc(4096 * 2)
        XF = self.FA.alloc(8192)
        ACC = self.FA.alloc(8192)
        WE = self.SCT + 640
        WTC, WBC = 0, 2048
        RING = self.SCT
        wtmp = self.W.alloc(1024)
        em.dma("Y", lambda e: e.dma_start(
            out=self.mr_sp[0:1, 0:TOK], in_=SC_t[0:1, self.MBC: self.MBC + TOK]))
        em.dma("Y", lambda e: e.dma_start(
            out=self.mr_sp[0:1, TOK: 2 * TOK], in_=SC_t[0:1, self.RBC: self.RBC + TOK]))
        for half in range(2):
            tts = [half * 2, half * 2 + 1]
            if half == 1:
                t1 = em.dma("Y", lambda e: e.dma_start(
                    out=SC_t[0:1, self.MBC: self.MBC + TOK], in_=self.mr_sp[0:1, 0:TOK]))
                t2 = em.dma("Y", lambda e: e.dma_start(
                    out=SC_t[0:1, self.RBC: self.RBC + TOK], in_=self.mr_sp[0:1, TOK: 2 * TOK]))
                em.dwait("V", t1)
                em.dwait("V", t2)
                em.wait("V", em.VS, em.VS.n)
                rv = em.do("V", lambda e: e.memset(SC_t[:, 5116: 5120], 0.0), em.VS, 1)
                self.bcast_inplace(self.MBC, rv)
                rv2 = em.do("V", lambda e: e.memset(SC_t[:, 5116: 5120], 0.0), em.VS, 1)
                self.bcast_inplace(self.RBC, rv2)
            for it in range(_L1_ITERS):
                lastv = None
                prev_rd = None
                if it == 0:
                    for c in range(NCH):
                        for ttl in range(2):
                            tt = half * 2 + ttl
                            if prev_rd is not None:
                                em.wait("Y", em.VS, prev_rd)
                            t = em.dma("Y", lambda e, c=c, tt=tt: e.dma_start(
                                out=SC_t[:, RING: RING + 512],
                                in_=self.xres[:, c * TOK + tt * 512: c * TOK + (tt + 1) * 512]))
                            em.dwait("V", t)
                            if lastv is not None:
                                em.wait("V", em.VS, lastv)
                            dst = slice(XF + c * 1024 + ttl * 512, XF + c * 1024 + (ttl + 1) * 512)
                            dsta = slice(ACC + c * 1024 + ttl * 512, ACC + c * 1024 + (ttl + 1) * 512)
                            v1 = em.do("V", lambda e, dst=dst, tt=tt: e.tensor_tensor(
                                out=FA_t[:, dst], in0=SC_t[:, RING: RING + 512],
                                in1=SC_t[:, self.MBC + tt * 512: self.MBC + (tt + 1) * 512],
                                op=ALU.subtract), em.VS, 1)
                            prev_rd = v1
                            em.wait("V", em.VS, v1)
                            v2 = em.do("V", lambda e, dst=dst, tt=tt: e.tensor_tensor(
                                out=FA_t[:, dst], in0=FA_t[:, dst],
                                in1=SC_t[:, self.RBC + tt * 512: self.RBC + (tt + 1) * 512],
                                op=ALU.mult), em.VS, 1)
                            em.wait("V", em.VS, v2)
                            lastv = em.do("V", lambda e, dst=dst, dsta=dsta: e.tensor_copy(
                                FA_t[:, dsta], FA_t[:, dst]), em.VS, 1)
                else:
                    em.wait("V", em.VS, em.VS.n)
                    for c in range(NCH):
                        for ttl in range(2):
                            dst = slice(XF + c * 1024 + ttl * 512, XF + c * 1024 + (ttl + 1) * 512)
                            dsta = slice(ACC + c * 1024 + ttl * 512, ACC + c * 1024 + (ttl + 1) * 512)
                            lastv = em.do("V", lambda e, dst=dst, dsta=dsta: e.tensor_copy(
                                FA_t[:, dsta], FA_t[:, dst]), em.VS, 1)
                init_tok = lastv
                self.rb_bc_col = self.SCT + 768 + 32
                self.gb_bc_col = self.SCT + 768 + 40
                for bcol, rowoff in ((self.rb_bc_col, SM.rb_row), (self.gb_bc_col, SM.gb_row)):
                    em.wait("T", em.VS, em.VS.n)
                    tb = em.do("T", lambda e, bcol=bcol, rowoff=rowoff, it=it: e.matmul(
                        self.bank(6, 6), self.onesf[0:1, 0:128],
                        self.small[0:1, rowoff + it * 6: rowoff + it * 6 + 6],
                        start=True, stop=True), em.TS, 1)
                    em.wait("V", em.TS, tb)
                    vb = em.do("V", lambda e, bcol=bcol: e.tensor_copy(
                        SC_t[:, bcol: bcol + 6], self.bank(6, 6)), em.VS, 1)
                    em.wait("T", em.VS, vb)
                em.wait("T", em.VS, init_tok)
                self.emit_router(
                    6,
                    lambda tp, k: FA_t[:, XF + k * 1024 + tp * 128: XF + k * 1024 + (tp + 1) * 128],
                    lambda k, it=it: self.smc(SM.router_W + it * 48 + k * 6, 6),
                    WE, l1=(self.rb_bc_col, self.gb_bc_col), ntok=1024)
                if self.stage and self.stage[0] == "r1":
                    self.emit_wbc(6, WE, WTC, WBC, 0, ntok=1024)
                    self.dump_stage([(SC_t, WE, 48, False), (SC_t, WBC, 2048, False)])
                    self._early = True
                    return
                for e8 in range(8):
                    is_sh = e8 >= 6
                    ns = e8 - 6
                    parts = _L1_PARTS if _L1_PARTS is not None else {"exp", "sh"}
                    if is_sh and "sh" not in parts:
                        continue
                    if (not is_sh) and "exp" not in parts:
                        continue
                    if not is_sh:
                        self.emit_wbc(6, WE, WTC, WBC, e8, ntok=1024)
                    if is_sh:
                        wsrc1 = lambda m, ns=ns: self.share_Wup[:, ns * 1024 + m * 128: ns * 1024 + (m + 1) * 128].rearrange(
                            "(kc p) m -> p kc m", p=128)
                        b1 = SM.share_bup + ns * 8
                    else:
                        wsrc1 = lambda m, e8=e8: self.exp_Wup[e8, :, m * 128:(m + 1) * 128].rearrange(
                            "(kc p) m -> p kc m", p=128)
                        b1 = SM.exp_bup + e8 * 8

                    def evup(m, tt, b, tok, b1=b1, half=half):
                        ttl = tt - half * 2
                        em.wait("S", em.TS, tok)
                        em.wait("S", em.TS, em.TS.n if False else tok)
                        v = em.do("S", lambda e, m=m, ttl=ttl, b=b, b1=b1: e.activation(
                            out=A_t[:, UP + m * 1024 + ttl * 512: UP + m * 1024 + (ttl + 1) * 512],
                            in_=self.bank(b), func=AF.Identity,
                            bias=self.smc(b1 + m)), em.SS, 1)
                        return (em.SS, v)
                    em.wait("T", em.VS, init_tok)
                    self.emit_proj(8, NCH, wsrc1,
                                   lambda k, tt: A_t[:, self.XN + k * TOK + tt * 512: self.XN + k * TOK + (tt + 1) * 512],
                                   evup, tts=tts, name=f"f1up{e8}")
                    if is_sh:
                        wsrc2 = lambda m: self.share_Wg[:, m * 128:(m + 1) * 128].rearrange(
                            "(kc p) m -> p kc m", p=128)
                        bg = SM.share_bg
                    else:
                        wsrc2 = lambda m, e8=e8: self.exp_Wg[e8, :, m * 128:(m + 1) * 128].rearrange(
                            "(kc p) m -> p kc m", p=128)
                        bg = SM.exp_bg + e8 * 8
                    gtmp_rel = {}

                    def evg(m, tt, b, tok, bg=bg, half=half, gtmp_rel=gtmp_rel):
                        ttl = tt - half * 2
                        r = (m * 2 + ttl) % 2
                        em.wait("S", em.TS, tok)
                        if r in gtmp_rel:
                            em.wait("S", gtmp_rel[r][0], gtmp_rel[r][1])
                        sv = em.do("S", lambda e, b=b, r=r, bg=bg, m=m: e.activation(
                            out=W_t[:, wtmp + r * 512: wtmp + (r + 1) * 512],
                            in_=self.bank(b), func=AF.Silu,
                            bias=self.smc(bg + m)), em.SS, 1)
                        em.wait("V", em.SS, sv)
                        vv = em.do("V", lambda e, m=m, ttl=ttl, r=r: e.tensor_tensor(
                            out=A_t[:, HH + m * 1024 + ttl * 512: HH + m * 1024 + (ttl + 1) * 512],
                            in0=W_t[:, wtmp + r * 512: wtmp + (r + 1) * 512],
                            in1=A_t[:, UP + m * 1024 + ttl * 512: UP + m * 1024 + (ttl + 1) * 512],
                            op=ALU.mult), em.VS, 1)
                        gtmp_rel[r] = (em.VS, vv)
                        return (em.VS, vv)
                    self.emit_proj(8, NCH, wsrc2,
                                   lambda k, tt, half=half: A_t[:, UP + k * 1024 + (tt - half * 2) * 512: UP + k * 1024 + (tt - half * 2 + 1) * 512],
                                   evg, tts=tts, name=f"f1g{e8}",
                                   tdeps=[(em.SS, em.SS.n)])
                    if is_sh:
                        wsrc3 = lambda m: self.share_Wdown[:, m * 128:(m + 1) * 128].rearrange(
                            "(kc p) m -> p kc m", p=128)
                        bd = SM.share_bdown
                    else:
                        wsrc3 = lambda m, e8=e8: self.exp_Wdown[e8, :, m * 128:(m + 1) * 128].rearrange(
                            "(kc p) m -> p kc m", p=128)
                        bd = SM.exp_bdown + e8 * 8

                    def evdn(m, tt, b, tok, e8=e8, is_sh=is_sh, bd=bd, half=half):
                        ttl = tt - half * 2
                        dst = slice(ACC + m * 1024 + ttl * 512, ACC + m * 1024 + (ttl + 1) * 512)
                        em.wait("V", em.TS, tok)
                        em.wait("V", em.VS, em.VS.n)
                        if is_sh:
                            v = em.do("V", lambda e, b=b, dst=dst, bd=bd, m=m: e.scalar_tensor_tensor(
                                out=FA_t[:, dst], in0=self.bank(b),
                                scalar=self.smc(bd + m), in1=FA_t[:, dst],
                                op0=ALU.add, op1=ALU.add), em.VS, 1)
                        else:
                            v0 = em.do("V", lambda e, b=b, m=m, bd=bd, ttl=ttl: e.scalar_tensor_tensor(
                                out=SC_t[:, RING: RING + 512], in0=self.bank(b),
                                scalar=self.smc(bd + m),
                                in1=SC_t[:, WBC + ttl * 512: WBC + (ttl + 1) * 512],
                                op0=ALU.add, op1=ALU.mult), em.VS, 1)
                            em.wait("V", em.VS, v0)
                            v = em.do("V", lambda e, dst=dst: e.tensor_tensor(
                                out=FA_t[:, dst], in0=SC_t[:, RING: RING + 512],
                                in1=FA_t[:, dst], op=ALU.add), em.VS, 1)
                        return (em.VS, v)
                    self.emit_proj(8, NCH, wsrc3,
                                   lambda k, tt, half=half: A_t[:, HH + k * 1024 + (tt - half * 2) * 512: HH + k * 1024 + (tt - half * 2 + 1) * 512],
                                   evdn, tts=tts, name=f"f1dn{e8}",
                                   tdeps=[(em.VS, em.VS.n)])
                for b in (0, 1):
                    if self.bank_rel.get(b) is not None:
                        em.wait("T", self.bank_rel[b][0], self.bank_rel[b][1])
                sq_rel = None
                for c in range(NCH):
                    for ttl in range(2):
                        sl = slice(ACC + c * 1024 + ttl * 512, ACC + c * 1024 + (ttl + 1) * 512)
                        em.wait("S", em.VS, em.VS.n)
                        if sq_rel is not None:
                            em.wait("S", em.TS, sq_rel)
                        sq = em.do("S", lambda e, sl=sl: e.activation(
                            out=SC_t[:, RING: RING + 512], in_=FA_t[:, sl],
                            func=AF.Square), em.SS, 1)
                        em.wait("T", em.SS, sq)
                        sq_rel = em.do("T", lambda e, c=c, ttl=ttl: e.matmul(
                            self.PS_t[0:1, ttl * 512: (ttl + 1) * 512],
                            self.onesf[:, 0:1], SC_t[:, RING: RING + 512],
                            start=(c == 0), stop=(c == NCH - 1)), em.TS, 1)
                em.wait("V", em.TS, sq_rel)
                rrow = SC_t[0:1, WBC: WBC + 1024]
                v1 = em.do("V", lambda e, rrow=rrow: e.tensor_scalar(
                    out=rrow, in0=self.PS_t[0:1, 0:1024], scalar1=1.0 / D,
                    scalar2=EPS_RMS, op0=ALU.mult, op1=ALU.add), em.VS, 1)
                em.wait("S", em.VS, v1)
                s1 = em.do("S", lambda e, rrow=rrow: e.activation(
                    out=rrow, in_=rrow, func=AF.Sqrt), em.SS, 1)
                em.wait("V", em.SS, s1)
                v2 = em.do("V", lambda e, rrow=rrow: e.reciprocal(rrow, rrow), em.VS, 1)
                em.wait("T", em.VS, v2)
                for b in (4, 5):
                    if self.bank_rel.get(b) is not None:
                        em.wait("T", self.bank_rel[b][0], self.bank_rel[b][1])
                for ttl in range(2):
                    tb = em.do("T", lambda e, ttl=ttl: e.matmul(
                        self.bank(4 + ttl), self.onesf[0:1, 0:128],
                        SC_t[0:1, WBC + ttl * 512: WBC + (ttl + 1) * 512],
                        start=True, stop=True), em.TS, 1)
                    em.wait("V", em.TS, tb)
                    vb = em.do("V", lambda e, ttl=ttl: e.tensor_copy(
                        SC_t[:, WBC + ttl * 512: WBC + (ttl + 1) * 512],
                        self.bank(4 + ttl)), em.VS, 1)
                    em.wait("T", em.VS, vb)
                lastv = None
                for c in range(NCH):
                    for ttl in range(2):
                        tt = half * 2 + ttl
                        sl = slice(ACC + c * 1024 + ttl * 512, ACC + c * 1024 + (ttl + 1) * 512)
                        dst = slice(XF + c * 1024 + ttl * 512, XF + c * 1024 + (ttl + 1) * 512)
                        em.wait("V", em.VS, em.VS.n)
                        v1 = em.do("V", lambda e, sl=sl, dst=dst, c=c, it=it, ttl=ttl: e.scalar_tensor_tensor(
                            out=FA_t[:, dst], in0=FA_t[:, sl],
                            scalar=self.smc(SM.norm_w + it * 8 + c),
                            in1=SC_t[:, WBC + ttl * 512: WBC + (ttl + 1) * 512],
                            op0=ALU.mult, op1=ALU.mult), em.VS, 1)
                        em.wait("V", em.VS, v1)
                        lastv = em.do("V", lambda e, dst=dst, c=c, tt=tt: e.tensor_copy(
                            A_t[:, self.XN + c * TOK + tt * 512: self.XN + c * TOK + (tt + 1) * 512],
                            FA_t[:, dst]), em.VS, 1)
                if it == _L1_ITERS - 1:
                    for c in range(NCH):
                        em.wait("Y", em.VS, em.VS.n)
                        em.dma("Y", lambda e, c=c, half=half: e.dma_start(
                            out=self.f_sp[:, c * TOK + half * 1024: c * TOK + (half + 1) * 1024],
                            in_=FA_t[:, XF + c * 1024: XF + (c + 1) * 1024]))
        self.W.release(wmark1)
        self.A.release(amark)
        self.FA.release(famark)

    def emit_combine0(self):
        em = self
        FA_t, SC_t = self.FA_t, self.SC_t
        ACC = self.acc_col
        xr = [0, 512]
        sr = [1024, 1536]
        gr = [2048, 2560]
        rel = {}
        for c in range(NCH):
            for tt in range(NTT):
                i = (c * NTT + tt) % 2
                for key, col, srcap in (("x", xr[i], self.x_fm[c * 128:(c + 1) * 128, tt * 512:(tt + 1) * 512]),
                                        ("s", sr[i], self.s_sp[:, c * TOK + tt * 512: c * TOK + (tt + 1) * 512])):
                    if (key + "w", i) in rel:
                        em.wait("Y", rel[(key + "w", i)][0], rel[(key + "w", i)][1])
                    ld = em.dma("Y", lambda e, col=col, srcap=srcap: e.dma_start(
                        out=SC_t[:, col: col + 512], in_=srcap))
                    rel[(key, i, "ld")] = ld
                sl = slice(ACC + c * TOK + tt * 512, ACC + c * TOK + (tt + 1) * 512)
                if ("g", i) in rel:
                    em.wait("S", rel[("g", i)][0], rel[("g", i)][1])
                gv = em.do("S", lambda e, sl=sl, i=i: e.activation(
                    out=SC_t[:, gr[i]: gr[i] + 512], in_=FA_t[:, sl], func=AF.Silu), em.SS, 1)
                em.dwait("V", rel[("s", i, "ld")])
                em.wait("V", em.SS, gv)
                if ("v2", i) in rel:
                    em.wait("V", rel[("v2", i)][0], rel[("v2", i)][1])
                v1 = em.do("V", lambda e, i=i: e.tensor_tensor(
                    out=SC_t[:, sr[i]: sr[i] + 512], in0=SC_t[:, sr[i]: sr[i] + 512],
                    in1=SC_t[:, gr[i]: gr[i] + 512], op=ALU.mult), em.VS, 1)
                rel[("g", i)] = (em.VS, v1)
                rel[("sw", i)] = (em.VS, v1)
                em.dwait("V", rel[("x", i, "ld")])
                em.wait("V", em.VS, v1)
                v2 = em.do("V", lambda e, i=i: e.tensor_tensor(
                    out=SC_t[:, xr[i]: xr[i] + 512], in0=SC_t[:, sr[i]: sr[i] + 512],
                    in1=SC_t[:, xr[i]: xr[i] + 512], op=ALU.add), em.VS, 1)
                em.wait("Y", em.VS, v2)
                ot = em.dma("Y", lambda e, c=c, tt=tt, i=i: e.dma_start(
                    out=self.xres[:, c * TOK + tt * 512: c * TOK + (tt + 1) * 512],
                    in_=SC_t[:, xr[i]: xr[i] + 512]))
                rel[("v2", i)] = ot
                rel[("xw", i)] = ot
        self.FA.release(self.fa_acc_mark)

    def dump_stage(self, pieces):
        em = self
        self.barrier()
        total = sum(p[2] for p in pieces)
        assert total <= self.stage[1]
        off = 0
        for (t, c0, n, isbf) in pieces:
            eng = "G" if isbf else "Y"
            em.dma(eng, lambda e, t=t, c0=c0, n=n, off=off: e.dma_start(
                out=self.dbg[:, off: off + n], in_=t[:, c0: c0 + n],
                max_dma_last_dim=4096))
            off += n
        self.barrier()


def build_program(stage=None):
    b = Builder(stage)
    nc = b.build()
    return nc


def kernel(**inputs):
    stage = _STAGE
    nc = build_program(stage)
    xf = np.asarray(inputs["L0_attn_x"], np.float32).reshape(BATCH * T, D)
    shared = {
        "constb": _pack_consts(),
        "w_qkvT": np.ascontiguousarray(np.asarray(inputs["L0_attn_Wqkv"], np.float32).T),
        "w_outT": np.ascontiguousarray(np.asarray(inputs["L0_attn_Wout"], np.float32).T),
        "sh_w1": np.ascontiguousarray(np.asarray(inputs["L0_ffn_sh_w1"], np.float32)),
        "sh_w2": np.ascontiguousarray(np.asarray(inputs["L0_ffn_sh_w2"], np.float32)),
        "ew1": np.ascontiguousarray(np.asarray(inputs["L0_ffn_ew1"], np.float32)),
        "ew2": np.ascontiguousarray(np.asarray(inputs["L0_ffn_ew2"], np.float32)),
        "w_qT": np.ascontiguousarray(np.asarray(inputs["L1_attn_Wq"], np.float32).T),
        "w_kT": np.ascontiguousarray(np.asarray(inputs["L1_attn_Wk"], np.float32).T),
        "w_vT": np.ascontiguousarray(np.asarray(inputs["L1_attn_Wv"], np.float32).T),
        "w_oT": np.ascontiguousarray(np.asarray(inputs["L1_attn_Wo"], np.float32).T),
        "exp_Wup": np.ascontiguousarray(np.asarray(inputs["L1_ffn_exp_Wup"], np.float32)),
        "exp_Wg": np.ascontiguousarray(np.asarray(inputs["L1_ffn_exp_Wg"], np.float32)),
        "exp_Wdown": np.ascontiguousarray(np.asarray(inputs["L1_ffn_exp_Wdown"], np.float32)),
        "share_Wup": np.ascontiguousarray(np.asarray(inputs["L1_ffn_share_Wup"], np.float32)),
        "share_Wg": np.ascontiguousarray(np.asarray(inputs["L1_ffn_share_Wg"], np.float32)),
        "share_Wdown": np.ascontiguousarray(np.asarray(inputs["L1_ffn_share_Wdown"], np.float32)),
    }
    in_maps = []
    for c in range(NCORE):
        m = dict(shared)
        m["x_fm"] = np.ascontiguousarray(xf[c * TOK:(c + 1) * TOK, :].T)
        m["smallp"] = _pack_small(inputs, c)
        in_maps.append(m)
    res = run_bass_kernel_spmd(nc, in_maps, core_ids=list(range(NCORE)))
    if stage is not None:
        return [np.asarray(r["dbg"]) for r in res.results]
    out = np.empty((BATCH * T, D), np.float32)
    for c in range(NCORE):
        out[c * TOK:(c + 1) * TOK, :] = np.asarray(res.results[c]["out_fm"]).T
    return out.reshape(BATCH, T, D)


# version 05 (final)
# speedup vs baseline: 1.2783x; 1.0409x over previous
import numpy as np
from contextlib import ExitStack

import concourse.bass as bass
import concourse.mybir as mybir
from concourse.bass_utils import run_bass_kernel_spmd

F32 = mybir.dt.float32
BF16 = mybir.dt.bfloat16
I32 = mybir.dt.int32
AF = mybir.ActivationFunctionType
ALU = mybir.AluOpType
AX = mybir.AxisListType

BATCH, T, D = 2, 8192, 1024
NCORE = 8
TOK = (BATCH * T) // NCORE
NCH = D // 128
NTT = TOK // 512
HEADS = 16
NHP = 8
DH = 64
L0_CS = 1024
FF_I = 4096
L0_E = 8
L1_E = 6
L1_I = 1024
L1_C = 2
EPS_LN = 1e-5
EPS_RMS = 1e-6

_STAGE = None
_PARTS = None
_ROUTER_RAW = False
_L1_ITERS = 2
_L1_PARTS = None
_ATTN_PARTS = None
_WO_MM_ONLY = False

ENGINES = ("T", "S", "V", "G", "Y")


class Sem:
    __slots__ = ("h", "n", "name")

    def __init__(self, h, name):
        self.h, self.n, self.name = h, 0, name


class Arena:
    def __init__(self, t, cols, name):
        self.t, self.cols, self.name = t, cols, name
        self.off = 0

    def alloc(self, n):
        assert self.off + n <= self.cols, f"{self.name} overflow {self.off}+{n}>{self.cols}"
        r = self.off
        self.off += n
        return r

    def mark(self):
        return self.off

    def release(self, m):
        self.off = m


class SmallMap:
    def __init__(self):
        self._o = 0
        t = self.take
        self.bout = t(8); self.sh_b1 = t(32); self.sh_b2 = t(8)
        self.eb1 = t(256); self.eb2 = t(64)
        self.bq = t(8); self.bk = t(8); self.bv = t(8); self.bo = t(8)
        self.exp_bup = t(48); self.exp_bg = t(48); self.exp_bdown = t(48)
        self.share_bup = t(16); self.share_bg = t(8); self.share_bdown = t(8)
        self.norm_w = t(16); self.gate_w = t(64); self.router_W = t(96)
        self.logpos = t(16)
        self.wl1 = t(1); self.bl1 = t(1); self.wl2t = t(16); self.bl2 = t(1)
        self.gate_b = t(1); self.router_b = t(2); self.gate_bias = t(2)
        self.q0col = t(1)
        self.hsel = t(2)
        self.rb_row = t(12)
        self.gb_row = t(12)
        self.persist = self._o
        self.logpos_row = t(TOK)
        self.wl1row = t(64)
        self.total = self._o

    def take(self, n):
        r = self._o
        self._o += n
        return r


SM = SmallMap()


def _pack_small(inp, core):
    cols = []

    def vec(v):
        v = np.asarray(v, np.float32).reshape(-1)
        assert v.size % 128 == 0
        cols.append(v.reshape(-1, 128).T)

    def pad_col(v):
        c = np.zeros((128, 1), np.float32)
        v = np.asarray(v, np.float32).reshape(-1)
        c[: v.size, 0] = v
        cols.append(c)

    q0 = core * TOK
    vec(inp["L0_attn_bout"]); vec(inp["L0_ffn_sh_b1"]); vec(inp["L0_ffn_sh_b2"])
    vec(np.asarray(inp["L0_ffn_eb1"]).reshape(-1))
    vec(np.asarray(inp["L0_ffn_eb2"]).reshape(-1))
    vec(inp["L1_attn_bq"]); vec(inp["L1_attn_bk"]); vec(inp["L1_attn_bv"]); vec(inp["L1_attn_bo"])
    vec(np.asarray(inp["L1_ffn_exp_bup"]).reshape(-1))
    vec(np.asarray(inp["L1_ffn_exp_bg"]).reshape(-1))
    vec(np.asarray(inp["L1_ffn_exp_bdown"]).reshape(-1))
    vec(inp["L1_ffn_share_bup"]); vec(inp["L1_ffn_share_bg"]); vec(inp["L1_ffn_share_bdown"])
    vec(np.asarray(inp["L1_ffn_norm_w"]).reshape(-1))
    gw = np.asarray(inp["L0_ffn_gate_w"], np.float32)
    cols.append(gw.reshape(8, 128, 8).transpose(1, 0, 2).reshape(128, 64))
    rw = np.asarray(inp["L1_ffn_router_W"], np.float32)
    cols.append(rw.reshape(2, 8, 128, 6).transpose(2, 0, 1, 3).reshape(128, 96))
    lp = np.log(np.arange(q0, q0 + TOK, dtype=np.float64) + 1.0).astype(np.float32)
    cols.append(lp.reshape(16, 128).T)
    pad_col(np.asarray(inp["L0_attn_Wl1"], np.float32)[:, 0])
    pad_col(inp["L0_attn_bl1"])
    w2t = np.zeros((128, 16), np.float32)
    w2t[:64, :] = np.asarray(inp["L0_attn_Wl2"], np.float32).T
    cols.append(w2t)
    pad_col(inp["L0_attn_bl2"])
    pad_col(inp["L0_ffn_gate_b"])
    rb = np.zeros((128, 2), np.float32)
    rb[:6, :] = np.asarray(inp["L1_ffn_router_b"], np.float32).T
    cols.append(rb)
    gb = np.zeros((128, 2), np.float32)
    gb[:6, :] = np.asarray(inp["L1_ffn_gate_bias"], np.float32).T
    cols.append(gb)
    cols.append(np.full((128, 1), float(q0), np.float32))
    hs = np.zeros((128, 2), np.float32)
    hs[:64, 0] = 1.0
    hs[64:, 1] = 1.0
    cols.append(hs)
    rr = np.zeros((128, 12), np.float32)
    rr[0, :] = np.asarray(inp["L1_ffn_router_b"], np.float32).reshape(-1)
    cols.append(rr)
    gg = np.zeros((128, 12), np.float32)
    gg[0, :] = np.asarray(inp["L1_ffn_gate_bias"], np.float32).reshape(-1)
    cols.append(gg)
    lr = np.zeros((128, TOK), np.float32)
    lr[0, :] = lp
    cols.append(lr)
    w1r = np.zeros((128, 64), np.float32)
    w1r[0, :] = np.asarray(inp["L0_attn_Wl1"], np.float32)[:, 0]
    cols.append(w1r)
    out = np.concatenate(cols, axis=1).astype(np.float32)
    assert out.shape[1] == SM.total
    return np.ascontiguousarray(out)


def _pack_consts():
    import ml_dtypes
    ident = np.eye(128, dtype=np.float32)
    ones = np.ones((128, 128), dtype=np.float32)
    return np.ascontiguousarray(
        np.concatenate([ident, ones], axis=1).astype(ml_dtypes.bfloat16))


class Builder:
    def __init__(self, stage=None):
        self.stage = stage
        self.nc = bass.Bass("TRN2", target_bir_lowering=False, debug=False)
        self.q = {e: [] for e in ENGINES}
        self.floor = {e: {} for e in ENGINES}
        self.st = ExitStack()

    def wait(self, e, sem, val):
        if val is None or val <= 0:
            return
        if self.floor[e].get(sem.name, 0) >= val:
            return
        self.floor[e][sem.name] = val
        self.q[e].append(lambda eng, s=sem.h, v=val: eng.wait_ge(s, v))

    def do(self, e, build, sem=None, k=1):
        if sem is None:
            self.q[e].append(build)
            return None
        sem.n += k
        val = sem.n
        self.q[e].append(lambda eng, b=build, s=sem.h, kk=k: b(eng).then_inc(s, kk))
        return val

    def dma(self, qeng, build):
        pool = self.dmapool[qeng]
        idx = self.dmaidx[qeng]
        self.dmaidx[qeng] = (idx + 1) % len(pool)
        s = pool[idx]
        self.wait(qeng, s, s.n)
        val = self.do(qeng, build, s, 16)
        return (s, val)

    def dwait(self, e, tok):
        self.wait(e, tok[0], tok[1])

    def barrier(self):
        for qe, pool in self.dmapool.items():
            for s in pool:
                self.wait(qe, s, s.n)
        for e, s in (("T", self.TS), ("S", self.SS), ("V", self.VS), ("G", self.GS)):
            self.wait(e, s, s.n)
        toks = {}
        for e in ENGINES:
            s = self.BAR[e]
            s.n += 1
            toks[e] = s.n
            self.q[e].append(lambda eng, sh=s.h: eng.nop(nofuse=True).then_inc(sh, 1))
        for e in ENGINES:
            for o in ENGINES:
                if o != e:
                    self.wait(e, self.BAR[o], toks[o])

    def sem(self, name):
        return Sem(self.st.enter_context(self.nc.semaphore(name)), name)

    def build(self):
        nc = self.nc
        stage = self.stage

        def din(name, shape, dt=F32):
            return nc.dram_tensor(name, shape, dt, kind="ExternalInput").ap()

        self.x_fm = din("x_fm", [D, TOK])
        self.smallp = din("smallp", [128, SM.total])
        self.constb = din("constb", [128, 256], BF16)
        self.w_qkvT = din("w_qkvT", [D, 3 * D])
        self.w_outT = din("w_outT", [D, D])
        self.sh_w1 = din("sh_w1", [D, FF_I])
        self.sh_w2 = din("sh_w2", [FF_I, D])
        self.ew1 = din("ew1", [L0_E, D, FF_I])
        self.ew2 = din("ew2", [L0_E, FF_I, D])
        self.w_qT = din("w_qT", [D, D])
        self.w_kT = din("w_kT", [D, D])
        self.w_vT = din("w_vT", [D, D])
        self.w_oT = din("w_oT", [D, D])
        self.exp_Wup = din("exp_Wup", [L1_E, D, L1_I])
        self.exp_Wg = din("exp_Wg", [L1_E, L1_I, L1_I])
        self.exp_Wdown = din("exp_Wdown", [L1_E, L1_I, D])
        self.share_Wup = din("share_Wup", [D, 2 * L1_I])
        self.share_Wg = din("share_Wg", [L1_I, L1_I])
        self.share_Wdown = din("share_Wdown", [L1_I, D])

        self.out_fm = nc.dram_tensor("out_fm", [D, TOK], F32, kind="ExternalOutput").ap()
        self.xres = nc.dram_tensor("xres", [128, NCH * TOK], F32).ap()
        self.s_sp = nc.dram_tensor("s_sp", [128, NCH * TOK], F32).ap()
        self.f_sp = nc.dram_tensor("f_sp", [128, NCH * TOK], F32).ap()
        self.KV_K0 = 0
        self.KV_V0 = NCH * TOK
        self.KV_ST = self.KV_V0 + (TOK // 128) * HEADS * 65
        self.KV_C = self.KV_ST + 1
        self.mr_sp = nc.dram_tensor("mr_sp", [1, 2 * TOK], F32).ap()
        self.kv_loc = nc.dram_tensor("kv_loc", [128, self.KV_C], BF16).ap()
        self.kv_in_h = nc.dram_tensor("kv_in", [128, self.KV_C], BF16)
        self.kv_out_h = nc.dram_tensor("kv_out", [4 * 128, self.KV_C], BF16)
        self.kv_in = self.kv_in_h.ap()
        self.kv_out = self.kv_out_h.ap()
        self.dbg = None
        if stage is not None:
            self.dbg = nc.dram_tensor("dbg", [128, stage[1]], F32, kind="ExternalOutput").ap()

        st = self.st
        with st:
            self.BAR = {e: self.sem("bar" + e) for e in ENGINES}
            self.CC = self.sem("cc")
            self.TS = self.sem("t")
            self.SS = self.sem("s")
            self.VS = self.sem("v")
            self.GS = self.sem("g")
            self.dmapool = {
                "Y": [self.sem(f"dy{i}") for i in range(10)],
                "G": [self.sem(f"dg{i}") for i in range(10)],
            }
            self.dmaidx = {"Y": 0, "G": 0}

            sb = lambda n, s, d=F32: st.enter_context(nc.sbuf_tensor(n, s, d))
            self.small = sb("small", [128, SM.persist])
            self.cidb = sb("cidb", [128, 128], BF16)
            self.cones = sb("cones", [128, 128], BF16)
            self.onesf = sb("onesf", [128, 128], F32)
            self.identf = sb("identf", [128, 128], F32)
            self.iox = st.enter_context(nc.sbuf_tensor("iox", [128, 512], I32))
            self.A_t = sb("A", [128, 49152], BF16)
            self.FA_t = sb("FA", [128, 16384], F32)
            self.W_t = sb("W", [128, 8192], BF16)
            self.SC_t = sb("SC", [128, 5120], F32)
            self.PS_t = st.enter_context(nc.psum_tensor("PS", [128, 3584], F32))
            self.PSB_t = st.enter_context(nc.psum_tensor("PSB", [128, 1024], BF16))
            self.A = Arena(self.A_t, 49152, "A")
            self.FA = Arena(self.FA_t, 16384, "FA")
            self.W = Arena(self.W_t, 8192, "W")
            self.SC = Arena(self.SC_t, 5120, "SC")

            self.emit_all()
            with nc.Block() as blk:
                for key, dec in (("T", blk.tensor), ("S", blk.scalar),
                                 ("V", blk.vector), ("G", blk.gpsimd),
                                 ("Y", blk.sync)):
                    def make(key):
                        def body(eng):
                            for fn in self.q[key]:
                                fn(eng)
                        return body
                    dec(make(key))
        return nc

    def bank(self, i, n=512, p0=0, p1=128):
        assert 0 <= i < 7
        return self.PS_t[p0:p1, i * 512: i * 512 + n]

    def smc(self, off, n=1, p0=0, p1=128):
        return self.small[p0:p1, off: off + n]

    def emit_all(self):
        em = self
        t1 = em.dma("Y", lambda e: e.dma_start(out=em.small[:], in_=em.smallp[:, 0:SM.persist]))
        t2 = em.dma("Y", lambda e: e.dma_start(out=em.cidb[:], in_=em.constb[:, 0:128]))
        t3 = em.dma("Y", lambda e: e.dma_start(out=em.cones[:], in_=em.constb[:, 128:256]))
        vt = em.do("V", lambda e: e.memset(em.onesf[:], 1.0), em.VS, 1)
        em.dwait("V", t2)
        vt2 = em.do("V", lambda e: e.tensor_copy(em.identf[:], em.cidb[:]), em.VS, 1)
        for e in ENGINES:
            for t in (t1, t2, t3):
                em.dwait(e, t)
            em.wait(e, em.VS, vt2)

        self.MBC, self.RBC, self.SCT = 0, 2048, 4096

        self.XN = self.A.alloc(NCH * TOK)
        self.QA = self.A.alloc(NCH * TOK)
        self.LAM = self.SCT
        self.B0C = self.SCT + 256
        self.B1C = self.SCT + 257
        self.QM2 = self.SCT + 258
        self.KM2 = self.SCT + 259
        self.TMP1 = self.SCT + 260
        self.KALL = self.SCT + 264

        self.xn_tok = self.emit_ln(lambda c: self.x_fm[c * 128:(c + 1) * 128, :], EPS_LN)
        if self.stage and self.stage[0] == "ln0":
            self.dump_stage([(self.A_t, self.XN, NCH * TOK, True),
                             (self.SC_t, 0, 2 * TOK, False)])
            return
        self.barrier()
        self.emit_qkv(0)
        self.barrier()
        if self.stage and self.stage[0] == "qkv0":
            self.dump_stage([(self.A_t, self.QA, NCH * TOK, True)])
            return
        self.emit_attn(0)
        self.barrier()
        if self.stage and self.stage[0] == "s0":
            self.dump_from_dram(self.s_sp)
            return
        self.emit_ffn(0)
        if getattr(self, "_early", False):
            return
        self.barrier()
        if self.stage and self.stage[0] == "f0":
            self.dump_from_dram(self.f_sp)
            return
        self.emit_combine0()
        self.barrier()
        if self.stage and self.stage[0] == "x1":
            self.dump_from_dram(self.xres)
            return
        self.xn_tok = self.emit_ln(lambda c: self.xres[:, c * TOK:(c + 1) * TOK], EPS_LN)
        self.barrier()
        self.emit_qkv(1)
        self.barrier()
        self.emit_ffn(1)
        if getattr(self, "_early", False):
            return
        self.barrier()
        if self.stage and self.stage[0] == "f1":
            self.dump_from_dram(self.f_sp)
            return
        self.emit_attn(1)
        self.barrier()
        if self.stage and self.stage[0] == "s1":
            self.dump_from_dram(self.s_sp)
            return

    def dump_from_dram(self, src):
        em = self
        self.barrier()
        for c in range(NCH):
            for tt in range(NTT):
                t = em.dma("Y", lambda e, c=c, tt=tt: e.dma_start(
                    out=self.SC_t[:, 0:512],
                    in_=src[:, c * TOK + tt * 512: c * TOK + (tt + 1) * 512]))
                em.dwait("Y", t)
                t2 = em.dma("Y", lambda e, c=c, tt=tt: e.dma_start(
                    out=self.dbg[:, c * TOK + tt * 512: c * TOK + (tt + 1) * 512],
                    in_=self.SC_t[:, 0:512]))
                em.dwait("Y", t2)
        self.barrier()

    def emit_proj(self, MC, KC, wsrc_fn, rhs_fn, evac_fn, tts=None, name="", tdeps=()):
        em = self
        W_t = self.W_t
        if tts is None:
            tts = list(range(NTT))
        wm = self.W.mark()
        R = min(5, (self.W.cols - self.W.off) // (KC * 128))
        assert R >= 2, f"W ring too small for KC={KC}"
        slots = [self.W.alloc(KC * 128) for _ in range(R)]
        slot_rel = [self.TS.n] * R
        wtok = {}

        def prefetch(m):
            s = m % R
            if slot_rel[s] is not None:
                em.wait("G", em.TS, slot_rel[s])
            wtok[m] = em.dma("G", lambda e, m=m, s=s: e.dma_start(
                out=W_t[:, slots[s]: slots[s] + KC * 128].rearrange(
                    "p (kc m) -> p kc m", m=128),
                in_=wsrc_fn(m), max_dma_last_dim=4096))

        for m in range(min(MC, R - 1)):
            prefetch(m)
        for sem_, val_ in tdeps:
            self.wait("T", sem_, val_)
        NTg = len(tts)
        for m in range(MC):
            s = m % R
            if m + R - 1 < MC:
                prefetch(m + R - 1)
            em.dwait("T", wtok[m])
            bks = []
            for ti in range(NTg):
                b = self.pb_rot
                self.pb_rot = (self.pb_rot + 1) % 6
                bks.append(b)
                if self.bank_rel.get(b) is not None:
                    em.wait("T", self.bank_rel[b][0], self.bank_rel[b][1])
            toks = [None] * NTg
            for k in range(KC):
                for ti, tt in enumerate(tts):
                    toks[ti] = em.do("T", lambda e, tt=tt, k=k, s=s, b=bks[ti]: e.matmul(
                        self.bank(b),
                        W_t[:, slots[s] + k * 128: slots[s] + (k + 1) * 128],
                        rhs_fn(k, tt), start=(k == 0), stop=(k == KC - 1)),
                        em.TS if k == KC - 1 else None, 1)
            for ti, tt in enumerate(tts):
                self.bank_rel[bks[ti]] = evac_fn(m, tt, bks[ti], toks[ti])
            slot_rel[s] = toks[-1]
        self.W.release(wm)

    def emit_qkv(self, layer):
        em = self
        A_t, FA_t, SC_t, W_t = self.A_t, self.FA_t, self.SC_t, self.W_t
        self.pb_rot = 0
        self.bank_rel = {}
        amark = self.A.mark()
        famark = self.FA.mark()
        ktile = [self.A.alloc(TOK) for _ in range(2)]
        vtile = [self.A.alloc(TOK) for _ in range(2)]
        vexp = [self.A.alloc(16 * 130) for _ in range(2)]
        stb = [self.A.alloc(1) for _ in range(2)]
        sqr = [self.FA.alloc(TOK) for _ in range(2)]
        kvdst = self.kv_loc if layer == 0 else self.kv_in
        mz0 = em.do("V", lambda e: e.memset(A_t[:, vexp[0]: vexp[0] + 16 * 130], 1.0), em.VS, 1)
        mz1 = em.do("V", lambda e: e.memset(A_t[:, vexp[1]: vexp[1] + 16 * 130], 1.0), em.VS, 1)
        zq = em.do("V", lambda e: e.memset(SC_t[:, self.QM2: self.QM2 + 2], 0.0), em.VS, 1)
        ring_rel = {}

        def rhs_xn(k, tt):
            return A_t[:, self.XN + k * TOK + tt * 512: self.XN + k * TOK + (tt + 1) * 512]

        def stats_emit(tile_a0, dstcol, m, evtoks):
            r = m % 2
            if ("sq", r) in ring_rel:
                em.wait("S", em.TS, ring_rel[("sq", r)])
            for ev in evtoks:
                em.wait("S", ev[0], ev[1])
            sq = em.do("S", lambda e, tile_a0=tile_a0, r=r: e.activation(
                out=FA_t[:, sqr[r]: sqr[r] + TOK],
                in_=A_t[:, tile_a0: tile_a0 + TOK], func=AF.Square), em.SS, 1)
            em.wait("T", em.SS, sq)
            for tt in range(NTT):
                tk = em.do("T", lambda e, tt=tt: e.matmul(
                    self.bank(6, 512, 0, 2),
                    self.smc(SM.hsel, 2),
                    FA_t[:, sqr[r] + tt * 512: sqr[r] + (tt + 1) * 512],
                    start=True, stop=True), em.TS, 1)
                em.wait("V", em.TS, tk)
                v1 = em.do("V", lambda e: e.tensor_reduce(
                    SC_t[0:2, self.TMP1: self.TMP1 + 1],
                    self.bank(6, 512, 0, 2), axis=AX.X, op=ALU.max), em.VS, 1)
                em.wait("V", em.VS, v1)
                v2 = em.do("V", lambda e, dstcol=dstcol: e.tensor_tensor(
                    out=SC_t[0:2, dstcol: dstcol + 1],
                    in0=SC_t[0:2, self.TMP1: self.TMP1 + 1],
                    in1=SC_t[0:2, dstcol: dstcol + 1], op=ALU.max), em.VS, 1)
                em.wait("T", em.VS, v2)
            ring_rel[("sq", r)] = em.TS.n
            return (em.SS, sq)

        if layer == 0:
            wsrc = lambda m: self.w_qkvT[:, m * 128:(m + 1) * 128].rearrange(
                "(kc p) m -> p kc m", p=128)
            MC = 24

            def evac(m, tt, b, tok):
                if m < 8:
                    em.wait("S", em.TS, tok)
                    v = em.do("S", lambda e, m=m, tt=tt, b=b: e.activation(
                        out=A_t[:, self.QA + m * TOK + tt * 512: self.QA + m * TOK + (tt + 1) * 512],
                        in_=self.bank(b), func=AF.Identity), em.SS, 1)
                    return (em.SS, v)
                kind = "k" if m < 16 else "v"
                mc = m - 8 if m < 16 else m - 16
                r = mc % 2
                ring = ktile if kind == "k" else vtile
                if tt == 0:
                    for rt in ring_rel.get((kind, r), []):
                        em.wait("S", rt[0], rt[1])
                em.wait("S", em.TS, tok)
                v = em.do("S", lambda e, ring=ring, r=r, tt=tt, b=b: e.activation(
                    out=A_t[:, ring[r] + tt * 512: ring[r] + (tt + 1) * 512],
                    in_=self.bank(b), func=AF.Identity), em.SS, 1)
                if tt == NTT - 1:
                    self.finish_tile(kind, mc, r, ring, vexp, kvdst, ring_rel, stats_emit)
                return (em.SS, v)
            em.wait("T", em.VS, self.xn_tok)
            self.emit_proj(MC, NCH, wsrc, rhs_xn, evac, name="qkv0")
            self.emit_lam()
        else:
            for which, wt, bia in (("q", self.w_qT, SM.bq), ("k", self.w_kT, SM.bk),
                                   ("v", self.w_vT, SM.bv)):
                wsrc = lambda m, wt=wt: wt[:, m * 128:(m + 1) * 128].rearrange(
                    "(kc p) m -> p kc m", p=128)

                def evac(m, tt, b, tok, which=which, bia=bia):
                    if which == "q":
                        em.wait("S", em.TS, tok)
                        v = em.do("S", lambda e, m=m, tt=tt, b=b: e.activation(
                            out=A_t[:, self.QA + m * TOK + tt * 512: self.QA + m * TOK + (tt + 1) * 512],
                            in_=self.bank(b), func=AF.Identity,
                            bias=self.smc(SM.bq + m)), em.SS, 1)
                        return (em.SS, v)
                    kind = which
                    r = m % 2
                    ring = ktile if kind == "k" else vtile
                    if tt == 0:
                        for rt in ring_rel.get((kind, r), []):
                            em.wait("S", rt[0], rt[1])
                    em.wait("S", em.TS, tok)
                    v = em.do("S", lambda e, ring=ring, r=r, tt=tt, b=b, m=m, bia=bia: e.activation(
                        out=A_t[:, ring[r] + tt * 512: ring[r] + (tt + 1) * 512],
                        in_=self.bank(b), func=AF.Identity,
                        bias=self.smc(bia + m)), em.SS, 1)
                    if tt == NTT - 1:
                        self.finish_tile(kind, m, r, ring, vexp, kvdst, ring_rel, stats_emit)
                    return (em.SS, v)
                em.wait("T", em.VS, self.xn_tok)
                self.emit_proj(8, NCH, wsrc, rhs_xn, evac, name="qkv1" + which)
        self.finalize_stats(layer, stb)
        if layer == 1:
            for qe, pool in self.dmapool.items():
                for s in pool:
                    em.wait(qe, s, s.n)
                    em.wait("G", s, s.n)
            em.wait("G", em.VS, em.VS.n)
            em.wait("G", em.SS, em.SS.n)
            em.wait("G", em.TS, em.TS.n)
            cc = em.do("G", lambda e: e.collective_compute(
                "AllGather", ALU.bypass, ins=[self.kv_in],
                outs=[self.kv_out],
                replica_groups=[[0, 1, 2, 3], [4, 5, 6, 7]]), em.CC, 1)
            self.cc_tok = cc
        self.A.release(amark)
        self.FA.release(famark)

    def finish_tile(self, kind, mc, r, ring, vexp, kvdst, ring_rel, stats_emit):
        em = self
        A_t = self.A_t
        evtok = (em.SS, em.SS.n)
        if kind == "k":
            sqtok = stats_emit(ring[r], self.KM2, mc, [evtok])
            em.wait("Y", evtok[0], evtok[1])
            t = em.dma("Y", lambda e, r=r, mc=mc, ring=ring: e.dma_start(
                out=kvdst[:, self.KV_K0 + mc * TOK: self.KV_K0 + (mc + 1) * TOK],
                in_=A_t[:, ring[r]: ring[r] + TOK]))
            ring_rel[(kind, r)] = [t, sqtok]
        else:
            if ("vx", r) in ring_rel:
                em.wait("V", *ring_rel[("vx", r)])
            vt = getattr(self, "psb_rel", None)
            tr = None
            for tc in range(16):
                em.wait("T", evtok[0], evtok[1])
                if vt is not None:
                    em.wait("T", em.VS, vt)
                tr = em.do("T", lambda e, r=r, tc=tc: e.transpose(
                    self.PSB_t[:, 0:128],
                    A_t[:, ring[r] + tc * 128: ring[r] + (tc + 1) * 128],
                    self.cidb[:]), em.TS, 1)
                em.wait("V", em.TS, tr)
                em.do("V", lambda e, r=r, tc=tc: e.tensor_copy(
                    A_t[:, vexp[r] + tc * 130: vexp[r] + tc * 130 + 64],
                    self.PSB_t[:, 0:64]), em.VS, 1)
                vt = em.do("V", lambda e, r=r, tc=tc: e.tensor_copy(
                    A_t[:, vexp[r] + tc * 130 + 65: vexp[r] + tc * 130 + 129],
                    self.PSB_t[:, 64:128]), em.VS, 1)
            self.psb_rel = vt
            em.wait("Y", em.VS, vt)
            t = em.dma("Y", lambda e, r=r, mc=mc, kvdst=kvdst: e.dma_start(
                out=kvdst[:, self.KV_V0: self.KV_V0 + 16 * 1040].rearrange(
                    "p (tc cc) -> p tc cc", cc=1040)[:, :, 2 * mc * 65: 2 * mc * 65 + 130],
                in_=A_t[:, vexp[r]: vexp[r] + 16 * 130].rearrange(
                    "p (tc cc) -> p tc cc", cc=130)))
            ring_rel[(kind, r)] = [(em.TS, tr)]
            ring_rel[("vx", r)] = t

    def finalize_stats(self, layer, stb):
        em = self
        SC_t = self.SC_t
        for col in (self.QM2, self.KM2):
            t = em.dma("Y", lambda e, col=col: e.dma_start(
                out=SC_t[0:1, self.TMP1 + 1: self.TMP1 + 2],
                in_=SC_t[1:2, col: col + 1],
                allow_slow_non_contiguous=True))
            em.dwait("V", t)
            em.wait("V", em.VS, em.VS.n)
            v = em.do("V", lambda e, col=col: e.tensor_tensor(
                out=SC_t[0:1, col: col + 1], in0=SC_t[0:1, self.TMP1 + 1: self.TMP1 + 2],
                in1=SC_t[0:1, col: col + 1], op=ALU.max), em.VS, 1)
            em.wait("Y", em.VS, v)
        vals = {}
        for nm, col in (("q", self.QM2), ("k", self.KM2)):
            em.wait("T", em.VS, em.VS.n)
            tk = em.do("T", lambda e, col=col: e.matmul(
                self.bank(6, 1), self.onesf[0:1, 0:128],
                SC_t[0:1, col: col + 1], start=True, stop=True), em.TS, 1)
            em.wait("V", em.TS, tk)
            dst = self.TMP1 + 2 if nm == "q" else self.TMP1 + 3
            v = em.do("V", lambda e, dst=dst: e.tensor_copy(
                SC_t[:, dst: dst + 1], self.bank(6, 1)), em.VS, 1)
            em.wait("T", em.VS, v)
            vals[nm] = dst
            vals[nm + "tok"] = v
        if layer == 1:
            em.wait("V", em.VS, vals["ktok"])
            vb = em.do("V", lambda e: e.tensor_copy(
                self.A_t[:, stb[0]: stb[0] + 1], SC_t[:, vals["k"]: vals["k"] + 1]), em.VS, 1)
            em.wait("Y", em.VS, vb)
            em.dma("Y", lambda e: e.dma_start(
                out=self.kv_in[:, self.KV_ST: self.KV_ST + 1],
                in_=self.A_t[:, stb[0]: stb[0] + 1],
                allow_slow_non_contiguous=True))
            self.qm_bc = vals["q"]
            return
        em.wait("V", em.VS, vals["ktok"])
        em.wait("V", em.VS, vals["qtok"])
        v1 = em.do("V", lambda e: e.tensor_tensor(
            out=SC_t[:, self.B0C: self.B0C + 1], in0=SC_t[:, vals["q"]: vals["q"] + 1],
            in1=SC_t[:, vals["k"]: vals["k"] + 1], op=ALU.mult), em.VS, 1)
        em.wait("S", em.VS, v1)
        s1 = em.do("S", lambda e: e.activation(
            out=SC_t[:, self.B0C: self.B0C + 1], in_=SC_t[:, self.B0C: self.B0C + 1],
            func=AF.Sqrt), em.SS, 1)
        em.wait("V", em.SS, s1)
        em.do("V", lambda e: e.tensor_scalar(
            out=SC_t[:, self.B0C: self.B0C + 1], in0=SC_t[:, self.B0C: self.B0C + 1],
            scalar1=-0.125, scalar2=None, op0=ALU.mult), em.VS, 1)

    def emit_b1(self):
        em = self
        SC_t = self.SC_t
        famark = self.FA.mark()
        stf = self.FA.alloc(4)
        for e in ("Y", "G"):
            em.wait(e, em.CC, self.cc_tok)
        t = em.dma("G", lambda e: e.dma_start(
            out=self.FA_t[:, stf: stf + 4].rearrange("p (r c) -> p r c", c=1),
            in_=self.kv_out.rearrange("(r p) c -> p r c", p=128)[:, :, self.KV_ST: self.KV_ST + 1],
            allow_slow_non_contiguous=True))
        em.dwait("V", t)
        v1 = em.do("V", lambda e: e.tensor_reduce(
            SC_t[:, self.KALL: self.KALL + 1], self.FA_t[:, stf: stf + 4],
            axis=AX.X, op=ALU.max), em.VS, 1)
        em.wait("V", em.VS, v1)
        v2 = em.do("V", lambda e: e.tensor_tensor(
            out=SC_t[:, self.B1C: self.B1C + 1], in0=SC_t[:, self.KALL: self.KALL + 1],
            in1=SC_t[:, self.qm_bc: self.qm_bc + 1], op=ALU.mult), em.VS, 1)
        em.wait("S", em.VS, v2)
        s1 = em.do("S", lambda e: e.activation(
            out=SC_t[:, self.B1C: self.B1C + 1], in_=SC_t[:, self.B1C: self.B1C + 1],
            func=AF.Sqrt), em.SS, 1)
        em.wait("V", em.SS, s1)
        em.do("V", lambda e: e.tensor_scalar(
            out=SC_t[:, self.B1C: self.B1C + 1], in0=SC_t[:, self.B1C: self.B1C + 1],
            scalar1=-0.125, scalar2=-1.0, op0=ALU.mult, op1=ALU.add), em.VS, 1)
        self.FA.release(famark)

    def emit_lam(self):
        em = self
        FA_t, SC_t = self.FA_t, self.SC_t
        famark = self.FA.mark()
        lrow = self.FA.alloc(TOK)
        wrow = self.FA.alloc(64)
        h1 = self.FA.alloc(TOK)
        lfm = self.FA.alloc(TOK)
        t1 = em.dma("Y", lambda e: e.dma_start(
            out=FA_t[0:1, lrow: lrow + TOK], in_=self.smallp[0:1, SM.logpos_row: SM.logpos_row + TOK]))
        t2 = em.dma("Y", lambda e: e.dma_start(
            out=FA_t[0:1, wrow: wrow + 64], in_=self.smallp[0:1, SM.wl1row: SM.wl1row + 64]))
        for tt in range(NTT):
            em.dwait("T", t1)
            em.dwait("T", t2)
            em.wait("T", em.SS, em.SS.n)
            mk = em.do("T", lambda e, tt=tt: e.matmul(
                self.bank(5, 512, 0, 64), FA_t[0:1, wrow: wrow + 64],
                FA_t[0:1, lrow + tt * 512: lrow + (tt + 1) * 512],
                start=True, stop=True), em.TS, 1)
            em.wait("S", em.TS, mk)
            s1 = em.do("S", lambda e, tt=tt: e.activation(
                out=FA_t[0:64, h1 + tt * 512: h1 + (tt + 1) * 512],
                in_=self.bank(5, 512, 0, 64), func=AF.Relu,
                bias=self.smc(SM.bl1, 1, 0, 64)), em.SS, 1)
            em.wait("T", em.SS, s1)
            mk2 = em.do("T", lambda e, tt=tt: e.matmul(
                self.bank(5, 512, 0, 16), self.smc(SM.wl2t, 16, 0, 64),
                FA_t[0:64, h1 + tt * 512: h1 + (tt + 1) * 512],
                start=True, stop=True), em.TS, 1)
            em.wait("S", em.TS, mk2)
            s2 = em.do("S", lambda e, tt=tt: e.activation(
                out=FA_t[0:16, lfm + tt * 512: lfm + (tt + 1) * 512],
                in_=self.bank(5, 512, 0, 16), func=AF.Sigmoid,
                bias=self.smc(SM.bl2, 1, 0, 16)), em.SS, 1)
            em.wait("T", em.SS, s2)
        vtok = None
        for tc in range(16):
            if vtok is not None:
                em.wait("T", em.VS, vtok)
            tr = em.do("T", lambda e, tc=tc: e.matmul(
                self.bank(5, 16), FA_t[0:16, lfm + tc * 128: lfm + (tc + 1) * 128],
                self.identf[0:16, 0:16], is_transpose=True,
                start=True, stop=True), em.TS, 1)
            em.wait("V", em.TS, tr)
            vtok = em.do("V", lambda e, tc=tc: e.tensor_scalar(
                out=SC_t[:, self.LAM + tc * 16: self.LAM + (tc + 1) * 16],
                in0=self.bank(5, 16), scalar1=0.125, scalar2=None,
                op0=ALU.mult), em.VS, 1)
        em.wait("T", em.VS, vtok)
        self.FA.release(famark)

    def emit_ln(self, xsrc_fn, eps):
        em = self
        FA_t, A_t, SC_t, PS_t = self.FA_t, self.A_t, self.SC_t, self.PS_t
        famark = self.FA.mark()
        X0 = self.FA.alloc(NCH * TOK)
        MBC, RBC, SCT = self.MBC, self.RBC, self.SCT
        XN = self.XN
        ld = []
        for c in range(NCH):
            ld.append(em.dma("Y", lambda e, c=c: e.dma_start(
                out=FA_t[:, X0 + c * TOK: X0 + (c + 1) * TOK], in_=xsrc_fn(c))))
        t_sum = None
        for c in range(NCH):
            em.dwait("T", ld[c])
            for tt in range(NTT):
                t_sum = em.do("T", lambda e, c=c, tt=tt: e.matmul(
                    PS_t[0:1, tt * 512:(tt + 1) * 512], em.onesf[:, 0:1],
                    FA_t[:, X0 + c * TOK + tt * 512: X0 + c * TOK + (tt + 1) * 512],
                    start=(c == 0), stop=(c == NCH - 1)),
                    em.TS if c == NCH - 1 else None, 1)
        em.wait("V", em.TS, t_sum)
        mrow = SC_t[0:1, MBC: MBC + TOK]
        v_m = em.do("V", lambda e: e.tensor_scalar(
            out=mrow, in0=PS_t[0:1, 0:TOK], scalar1=1.0 / D, scalar2=None,
            op0=ALU.mult), em.VS, 1)
        mb_tok = self.bcast_inplace(MBC, v_m)
        sq_tok = None
        d_cols = SC_t[:, SCT: SCT + 1024]
        for c in range(NCH):
            for half in range(2):
                sl = slice(X0 + c * TOK + half * 1024, X0 + c * TOK + (half + 1) * 1024)
                asl = slice(XN + c * TOK + half * 1024, XN + c * TOK + (half + 1) * 1024)
                em.wait("V", em.VS, mb_tok)
                if sq_tok is not None:
                    em.wait("V", em.SS, sq_tok)
                v_d = em.do("V", lambda e, sl=sl: e.tensor_tensor(
                    out=d_cols, in0=FA_t[:, sl],
                    in1=SC_t[:, MBC + (sl.start - X0) % TOK: MBC + (sl.start - X0) % TOK + 1024],
                    op=ALU.subtract), em.VS, 1)
                v_b = em.do("V", lambda e, sl=sl, asl=asl: e.tensor_tensor(
                    out=A_t[:, asl], in0=FA_t[:, sl],
                    in1=SC_t[:, MBC + (sl.start - X0) % TOK: MBC + (sl.start - X0) % TOK + 1024],
                    op=ALU.subtract), em.VS, 1)
                em.wait("S", em.VS, v_b)
                sq_tok = em.do("S", lambda e, sl=sl: e.activation(
                    out=FA_t[:, sl], in_=d_cols, func=AF.Square), em.SS, 1)
                em.wait("T", em.SS, sq_tok)
                for tt in range(2):
                    t_sq = em.do("T", lambda e, sl=sl, tt=tt, c=c, half=half: e.matmul(
                        PS_t[0:1, half * 1024 + tt * 512: half * 1024 + (tt + 1) * 512],
                        em.onesf[:, 0:1],
                        FA_t[:, sl.start + tt * 512: sl.start + (tt + 1) * 512],
                        start=(c == 0), stop=(c == NCH - 1)),
                        em.TS if c == NCH - 1 else None, 1)
        em.wait("V", em.TS, t_sq)
        rrow = SC_t[0:1, RBC: RBC + TOK]
        v_v = em.do("V", lambda e: e.tensor_scalar(
            out=rrow, in0=PS_t[0:1, 0:TOK], scalar1=1.0 / D,
            scalar2=eps, op0=ALU.mult, op1=ALU.add), em.VS, 1)
        em.wait("S", em.VS, v_v)
        s_r = em.do("S", lambda e: e.activation(out=rrow, in_=rrow, func=AF.Sqrt), em.SS, 1)
        em.wait("V", em.SS, s_r)
        v_r = em.do("V", lambda e: e.reciprocal(rrow, rrow), em.VS, 1)
        rb_tok = self.bcast_inplace(RBC, v_r)
        xn_tok = None
        for c in range(NCH):
            em.wait("V", em.VS, rb_tok)
            xn_tok = em.do("V", lambda e, c=c: e.tensor_tensor(
                out=A_t[:, self.XN + c * TOK: self.XN + (c + 1) * TOK],
                in0=A_t[:, self.XN + c * TOK: self.XN + (c + 1) * TOK],
                in1=SC_t[:, RBC: RBC + TOK], op=ALU.mult), em.VS, 1)
        self.FA.release(famark)
        return xn_tok

    def bcast_inplace(self, col0, row_tok):
        em = self
        SC_t, PS_t = self.SC_t, self.PS_t
        for b in (4, 5):
            br = getattr(self, "bank_rel", {})
            if br.get(b) is not None:
                em.wait("T", br[b][0], br[b][1])
        em.wait("T", em.VS, row_tok)
        toks = []
        for tt in range(NTT):
            t1 = em.do("T", lambda e, tt=tt: e.matmul(
                self.bank(4 + (tt % 2)), em.onesf[0:1, 0:128],
                SC_t[0:1, col0 + tt * 512: col0 + (tt + 1) * 512],
                start=True, stop=True), em.TS, 1)
            em.wait("V", em.TS, t1)
            v1 = em.do("V", lambda e, tt=tt: e.tensor_copy(
                SC_t[:, col0 + tt * 512: col0 + (tt + 1) * 512],
                self.bank(4 + (tt % 2))), em.VS, 1)
            em.wait("T", em.VS, v1)
            toks.append(v1)
        return toks[-1]

    def emit_attn(self, layer):
        em = self
        A_t, FA_t, SC_t, W_t = self.A_t, self.FA_t, self.SC_t, self.W_t
        self.pb_rot = 0
        self.bank_rel = {}
        kvsrc = self.kv_loc if layer == 0 else None
        if layer == 0:
            KCOLS, VCOLS, NKC = 1024, 8 * 130, 8
            amark = self.A.mark()
            KS = [self.A.alloc(KCOLS) for _ in range(3)]
            VSL = [self.A.alloc(VCOLS) for _ in range(3)]
            AT = [self.A.alloc(1024) for _ in range(3)]
            HB = self.A.alloc(8 * 512)
            famark = self.FA.mark()
            dnm_c = self.FA.alloc(512)
            sring = [self.FA.alloc(512) for _ in range(2)]
            bias_col = SC_t[:, self.B0C: self.B0C + 1]
        else:
            KCOLS, VCOLS, NKC = 4096, 32 * 130, 64
            KS = [0, 4096, 8192]
            HB = 12288
            VSL = [32768, 36928, 41088]
            AT = [45248, 46272, 47296]
            dnm_c = 0
            sring = [512, 1024]
            gcol = 1536
            x1r = [2048, 2560]
            f1r = [3072, 3584]
            bias_col = SC_t[:, self.B1C: self.B1C + 1]
            self.emit_b1()
            em.wait("S", em.VS, em.VS.n)
            em.wait("V", em.VS, em.VS.n)
        nhalf = NKC * 128 // (KCOLS)
        cph = NKC // nhalf
        wmark = self.W.mark()
        WR = [self.W.alloc(4096) for _ in range(2)]
        wo_src = self.w_outT if layer == 0 else self.w_oT
        wotok = {}
        wo_rel = {}

        def wo_prefetch(tq, mh):
            i = tq * 2 + mh
            s = i % 2
            if s in wo_rel:
                em.wait("G", em.TS, wo_rel[s])
            wotok[i] = em.dma("G", lambda e, mh=mh, s=s: e.dma_start(
                out=W_t[:, WR[s]: WR[s] + 4096].rearrange("p (kc m) -> p kc m", m=512),
                in_=wo_src[:, mh * 512:(mh + 1) * 512].rearrange(
                    "(kc p) m -> p kc m", p=128), max_dma_last_dim=4096))

        slab_tok = {}
        slab_rel = {}

        def load_slab(tq, idx):
            hp, half = divmod(idx, nhalf)
            sl = idx % 3
            if sl in slab_rel:
                em.wait("Y", em.TS, slab_rel[sl])
            cq = tq // 2
            if layer == 0:
                ktok = em.dma("Y", lambda e, hp=hp, sl=sl, cq=cq: e.dma_start(
                    out=A_t[:, KS[sl]: KS[sl] + KCOLS],
                    in_=kvsrc[:, hp * TOK + cq * 1024: hp * TOK + cq * 1024 + 1024]))
                vtok = em.dma("Y", lambda e, hp=hp, sl=sl, cq=cq: e.dma_start(
                    out=A_t[:, VSL[sl]: VSL[sl] + VCOLS].rearrange(
                        "p (tc cc) -> p tc cc", cc=130),
                    in_=kvsrc[:, self.KV_V0 + cq * 8 * 1040:
                              self.KV_V0 + (cq * 8 + 8) * 1040].rearrange(
                        "p (tc cc) -> p tc cc", cc=1040)[:, :, 2 * hp * 65: 2 * hp * 65 + 130]))
            else:
                kvo = self.kv_out.rearrange("(r p) c -> p r c", p=128)
                r0 = half * 2
                ktok = em.dma("Y", lambda e, hp=hp, sl=sl, r0=r0, kvo=kvo: e.dma_start(
                    out=A_t[:, KS[sl]: KS[sl] + KCOLS].rearrange(
                        "p (r t) -> p r t", t=TOK),
                    in_=kvo[:, r0: r0 + 2, hp * TOK: (hp + 1) * TOK]))
                vts = []
                for ri in range(2):
                    vts.append(em.dma("Y", lambda e, hp=hp, sl=sl, r0=r0, ri=ri, kvo=kvo: e.dma_start(
                        out=A_t[:, VSL[sl] + ri * 16 * 130: VSL[sl] + (ri + 1) * 16 * 130].rearrange(
                            "p (tc cc) -> p tc cc", cc=130),
                        in_=kvo[:, r0 + ri,
                                self.KV_V0: self.KV_V0 + 16 * 1040].rearrange(
                            "p (tc cc) -> p tc cc", cc=1040)[:, :, 2 * hp * 65: 2 * hp * 65 + 130])))
                vtok = vts
            slab_tok[idx] = (ktok, vtok)

        def delta_build(tq):
            rlist = [c for c in range(64) if c * 128 + 127 - tq * 512 > 4097]
            pos = {}
            gtok = None
            vtok = None
            for i, c in enumerate(rlist):
                pos[c] = i * 512
                base = tq * 512 - c * 128 + 8191 - 2047
                if gtok is not None:
                    em.wait("G", em.VS, vtok)
                gtok = em.do("G", lambda e, base=base: e.iota(
                    self.iox[:], base=base, pattern=[[1, 512]],
                    channel_multiplier=-1), em.GS, 1)
                em.wait("V", em.GS, gtok)
                i0 = i * 512
                v0 = em.do("V", lambda e, i0=i0: e.tensor_copy(
                    FA_t[:, i0: i0 + 512], self.iox[:]), em.VS, 1)
                em.wait("V", em.VS, v0)
                v1 = em.do("V", lambda e, i0=i0: e.tensor_scalar(
                    out=FA_t[:, i0: i0 + 512], in0=FA_t[:, i0: i0 + 512],
                    scalar1=self.smc(SM.q0col), scalar2=None, op0=ALU.add), em.VS, 1)
                em.wait("V", em.VS, v1)
                v1b = em.do("V", lambda e, i0=i0: e.tensor_scalar(
                    out=SC_t[:, 4608: 4608 + 512], in0=FA_t[:, i0: i0 + 512],
                    scalar1=-1.0, scalar2=None, op0=ALU.mult), em.VS, 1)
                em.wait("V", em.VS, v1b)
                v2 = em.do("V", lambda e, i0=i0: e.tensor_tensor(
                    out=FA_t[:, i0: i0 + 512], in0=FA_t[:, i0: i0 + 512],
                    in1=SC_t[:, 4608: 4608 + 512], op=ALU.max), em.VS, 1)
                em.wait("V", em.VS, v2)
                vtok = em.do("V", lambda e, i0=i0: e.tensor_scalar(
                    out=FA_t[:, i0: i0 + 512], in0=FA_t[:, i0: i0 + 512],
                    scalar1=-1.0 / 256.0, scalar2=2047.0 / 256.0,
                    op0=ALU.mult, op1=ALU.add), em.VS, 1)
                em.wait("V", em.VS, vtok)
                vtok = em.do("V", lambda e, i0=i0: e.tensor_scalar(
                    out=FA_t[:, i0: i0 + 512], in0=FA_t[:, i0: i0 + 512],
                    scalar1=0.0, scalar2=None, op0=ALU.max), em.VS, 1)
            return pos, vtok

        at_rel = {}
        exp_tok = {}
        av_bank_rel = None
        self.av_prev_rel = None
        self.pending_evac = None
        chunk_sl = {}
        for tq in range(4):
            wo_prefetch(tq, 0)
            wo_prefetch(tq, 1)
            if layer == 1:
                dpos, dtok = delta_build(tq)
            else:
                dpos, dtok = {}, None
            load_slab(tq, 0)
            nidx = NHP * nhalf
            for idx in range(nidx):
                hp, half = divmod(idx, nhalf)
                if idx + 1 < nidx:
                    load_slab(tq, idx + 1)
                elif tq < 3:
                    pass
                sl = idx % 3
                ktok, vtok_d = slab_tok[idx]
                em.dwait("T", ktok)
                for vv in (vtok_d if isinstance(vtok_d, list) else [vtok_d]):
                    em.dwait("T", vv)
                def emit_av(kc, sl_, ats):
                    ex2 = exp_tok[kc]
                    em.wait("T", ex2[0], ex2[1])
                    if kc == 0 and self.av_prev_rel is not None:
                        em.wait("T", self.av_prev_rel[0], self.av_prev_rel[1])
                    kcl_ = kc % cph
                    ave = em.do("T", lambda e, sl=sl_, kcl=kcl_, ats=ats, kc=kc: e.matmul(
                        self.bank(4, 512, 0, 65),
                        A_t[:, VSL[sl] + kcl * 130: VSL[sl] + kcl * 130 + 65],
                        A_t[:, AT[ats]: AT[ats] + 512],
                        start=(kc == 0), stop=(kc == NKC - 1)),
                        em.TS, 1)
                    avo = em.do("T", lambda e, sl=sl_, kcl=kcl_, ats=ats, kc=kc: e.matmul(
                        self.bank(5, 512, 0, 65),
                        A_t[:, VSL[sl] + kcl * 130 + 65: VSL[sl] + kcl * 130 + 130],
                        A_t[:, AT[ats] + 512: AT[ats] + 1024],
                        start=(kc == 0), stop=(kc == NKC - 1)),
                        em.TS, 1)
                    at_rel[ats] = (em.TS, avo)
                    slab_rel[sl_] = em.TS.n
                    self.last_av = (ave, avo)
                for kcl in range(cph):
                    kc = half * cph + kcl
                    chunk_sl[kc] = sl
                    be = (kc % 2) * 2
                    bo = be + 1
                    for b in (be, bo):
                        if self.bank_rel.get(b) is not None:
                            em.wait("T", self.bank_rel[b][0], self.bank_rel[b][1])
                    mm_e = em.do("T", lambda e, sl=sl, kcl=kcl, hp=hp, tq=tq, be=be: e.matmul(
                        self.bank(be),
                        A_t[0:64, KS[sl] + kcl * 128: KS[sl] + (kcl + 1) * 128],
                        A_t[0:64, self.QA + hp * TOK + tq * 512: self.QA + hp * TOK + (tq + 1) * 512],
                        start=True, stop=True), em.TS, 1)
                    mm_o = em.do("T", lambda e, sl=sl, kcl=kcl, hp=hp, tq=tq, bo=bo: e.matmul(
                        self.bank(bo),
                        A_t[64:128, KS[sl] + kcl * 128: KS[sl] + (kcl + 1) * 128],
                        A_t[64:128, self.QA + hp * TOK + tq * 512: self.QA + hp * TOK + (tq + 1) * 512],
                        start=True, stop=True), em.TS, 1)
                    dtk = None
                    if layer == 1 and kc in dpos:
                        em.wait("V", em.TS, mm_o)
                        em.wait("V", em.VS, dtok)
                        em.do("V", lambda e, be=be, kc=kc, dpos=dpos: e.tensor_tensor(
                            out=self.bank(be), in0=self.bank(be),
                            in1=FA_t[:, dpos[kc]: dpos[kc] + 512], op=ALU.add), em.VS, 1)
                        dtk = em.do("V", lambda e, bo=bo, kc=kc, dpos=dpos: e.tensor_tensor(
                            out=self.bank(bo), in0=self.bank(bo),
                            in1=FA_t[:, dpos[kc]: dpos[kc] + 512], op=ALU.add), em.VS, 1)
                    ats = kc % 3
                    if at_rel.get(ats) is not None:
                        em.wait("S", at_rel[ats][0], at_rel[ats][1])
                    em.wait("S", em.TS, mm_o)
                    if dtk is not None:
                        em.wait("S", em.VS, dtk)
                    if layer == 1:
                        ex = em.do("S", lambda e, be=be, ats=ats: e.activation(
                            out=A_t[:, AT[ats]: AT[ats] + 1024],
                            in_=self.PS_t[:, be * 512: (be + 2) * 512],
                            func=AF.Exp, scale=0.125, bias=bias_col), em.SS, 1)
                    else:
                        cq = tq // 2
                        em.do("S", lambda e, be=be, ats=ats, cq=cq, kcl=kcl, hp=hp: e.activation(
                            out=A_t[:, AT[ats]: AT[ats] + 512],
                            in_=self.bank(be),
                            func=AF.Exp,
                            scale=SC_t[:, self.LAM + (cq * 8 + kcl) * 16 + 2 * hp:
                                       self.LAM + (cq * 8 + kcl) * 16 + 2 * hp + 1],
                            bias=bias_col), em.SS, 1)
                        ex = em.do("S", lambda e, bo=bo, ats=ats, cq=cq, kcl=kcl, hp=hp: e.activation(
                            out=A_t[:, AT[ats] + 512: AT[ats] + 1024],
                            in_=self.bank(bo),
                            func=AF.Exp,
                            scale=SC_t[:, self.LAM + (cq * 8 + kcl) * 16 + 2 * hp + 1:
                                       self.LAM + (cq * 8 + kcl) * 16 + 2 * hp + 2],
                            bias=bias_col), em.SS, 1)
                    self.bank_rel[be] = (em.SS, ex)
                    self.bank_rel[bo] = (em.SS, ex)
                    exp_tok[kc] = (em.SS, ex)
                    if _ATTN_PARTS is not None and "av" not in _ATTN_PARTS:
                        at_rel[ats] = (em.SS, ex)
                        self.last_av = None
                        continue
                    if kc == 2 and self.pending_evac is not None:
                        av_bank_rel = self.emit_av_evac(
                            layer, self.pending_evac, HB, dnm_c)
                        self.av_prev_rel = av_bank_rel
                        self.pending_evac = None
                    if kc >= 2:
                        emit_av(kc - 2, chunk_sl[kc - 2], (kc - 2) % 3)
                ap_ = _ATTN_PARTS if _ATTN_PARTS is not None else {"av", "evac", "wo"}
                slab_rel[sl] = em.TS.n
                if half == nhalf - 1:
                    if "av" in ap_:
                        for j in range(max(0, NKC - 2), NKC):
                            emit_av(j, chunk_sl[j], j % 3)
                    slab_rel[sl] = em.TS.n
                    if "evac" in ap_ and "av" in ap_:
                        if _ATTN_PARTS is not None:
                            av_bank_rel = self.emit_av_evac(layer, hp, HB, dnm_c)
                            self.av_prev_rel = av_bank_rel
                        else:
                            self.pending_evac = hp
            if self.pending_evac is not None:
                av_bank_rel = self.emit_av_evac(layer, self.pending_evac, HB, dnm_c)
                self.av_prev_rel = av_bank_rel
                self.pending_evac = None
            ap_ = _ATTN_PARTS if _ATTN_PARTS is not None else {"av", "evac", "wo"}
            if "wo" in ap_:
                self.emit_wo(layer, tq, HB, WR, wotok, wo_rel, sring,
                             None if layer == 0 else (gcol, x1r, f1r), dnm_c)
        self.W.release(wmark)
        if layer == 0:
            self.A.release(amark)
            self.FA.release(famark)

    def emit_av_evac(self, layer, hp, HB, dnm_c):
        em = self
        A_t, SC_t, FA_t = self.A_t, self.SC_t, self.FA_t
        dn = (FA_t if layer == 0 else SC_t)
        ave, avo = self.last_av
        rel = None
        for j, avtok in ((0, ave), (1, avo)):
            bk = 4 + j
            em.wait("V", em.TS, avtok)
            v1 = em.do("V", lambda e, bk=bk: e.tensor_copy(
                dn[64:65, dnm_c: dnm_c + 512], self.bank(bk, 512, 64, 65)), em.VS, 1)
            em.wait("V", em.VS, v1)
            v2 = em.do("V", lambda e: e.reciprocal(
                dn[64:65, dnm_c: dnm_c + 512], dn[64:65, dnm_c: dnm_c + 512]), em.VS, 1)
            em.wait("T", em.VS, v2)
            t1 = em.do("T", lambda e: e.matmul(
                self.bank(6, 512, 0, 64), self.onesf[64:65, 0:64],
                dn[64:65, dnm_c: dnm_c + 512], start=True, stop=True), em.TS, 1)
            em.wait("V", em.TS, t1)
            v3 = em.do("V", lambda e: e.tensor_copy(
                dn[0:64, dnm_c: dnm_c + 512], self.bank(6, 512, 0, 64)), em.VS, 1)
            em.wait("V", em.VS, v3)
            v4 = em.do("V", lambda e, bk=bk, j=j, hp=hp: e.tensor_tensor(
                out=A_t[64 * j: 64 * j + 64, HB + hp * 512: HB + (hp + 1) * 512],
                in0=self.bank(bk, 512, 0, 64), in1=dn[0:64, dnm_c: dnm_c + 512],
                op=ALU.mult), em.VS, 1)
            rel = (em.VS, v4)
        return rel

    def emit_wo(self, layer, tq, HB, WR, wotok, wo_rel, sring, comb, dnm_c):
        em = self
        A_t, SC_t, FA_t, W_t = self.A_t, self.SC_t, self.FA_t, self.W_t
        hb_tok = em.VS.n
        if layer == 1:
            gcol, x1r, f1r = comb
            ldx = {}
            ldf = {}
        sr_rel = {}
        for mh in range(2):
            i = tq * 2 + mh
            s = i % 2
            em.dwait("T", wotok[i])
            for ml in range(4):
                m = mh * 4 + ml
                b = (self.pb_rot % 3) * 2
                self.pb_rot += 1
                for bb in (b, b + 1):
                    if self.bank_rel.get(bb) is not None:
                        em.wait("T", self.bank_rel[bb][0], self.bank_rel[bb][1])
                em.wait("T", em.VS, hb_tok)
                tok = None
                for par in range(2):
                    for hh in range(8):
                        j = 2 * hh + par
                        tok = em.do("T", lambda e, s=s, j=j, ml=ml, b=b, par=par, hh=hh: e.matmul(
                            self.bank(b + par),
                            W_t[64 * par: 64 * par + 64,
                                WR[s] + (j // 2) * 512 + ml * 128: WR[s] + (j // 2) * 512 + (ml + 1) * 128],
                            A_t[64 * par: 64 * par + 64, HB + (j // 2) * 512: HB + (j // 2 + 1) * 512],
                            start=(hh == 0), stop=(hh == 7)),
                            em.TS if hh == 7 else None, 1)
                wo_rel[s] = tok
                r = m % 2
                if _WO_MM_ONLY:
                    self.bank_rel[b] = (em.TS, tok)
                    self.bank_rel[b + 1] = (em.TS, tok)
                    continue
                if layer == 0:
                    if ("s", r) in sr_rel:
                        for rt in sr_rel[("s", r)]:
                            em.wait("S", rt[0], rt[1]) if rt[0].name.startswith("d") else em.wait("S", rt[0], rt[1])
                    em.wait("S", em.TS, tok)
                    sv = em.do("S", lambda e, m=m, b=b, r=r: e.activation(
                        out=FA_t[:, sring[r]: sring[r] + 512], in_=self.bank(b),
                        func=AF.Identity, bias=self.smc(SM.bout + m)), em.SS, 1)
                    self.bank_rel[b] = (em.SS, sv)
                    em.wait("V", em.SS, sv)
                    sv2 = em.do("V", lambda e, b=b, r=r: e.tensor_tensor(
                        out=FA_t[:, sring[r]: sring[r] + 512],
                        in0=self.bank(b + 1), in1=FA_t[:, sring[r]: sring[r] + 512],
                        op=ALU.add), em.VS, 1)
                    self.bank_rel[b + 1] = (em.VS, sv2)
                    em.wait("Y", em.VS, sv2)
                    dt = em.dma("Y", lambda e, m=m, tq=tq, r=r: e.dma_start(
                        out=self.s_sp[:, m * TOK + tq * 512: m * TOK + (tq + 1) * 512],
                        in_=FA_t[:, sring[r]: sring[r] + 512]))
                    sr_rel[("s", r)] = [dt, (em.VS, sv2)]
                else:
                    if ("x", r) in sr_rel:
                        em.dwait("Y", sr_rel[("x", r)])
                    ldx[m] = em.dma("Y", lambda e, m=m, tq=tq, r=r: e.dma_start(
                        out=SC_t[:, x1r[r]: x1r[r] + 512],
                        in_=self.xres[:, m * TOK + tq * 512: m * TOK + (tq + 1) * 512]))
                    if ("f", r) in sr_rel:
                        em.wait("Y", sr_rel[("f", r)][0], sr_rel[("f", r)][1])
                    ldf[m] = em.dma("Y", lambda e, m=m, tq=tq, r=r: e.dma_start(
                        out=SC_t[:, f1r[r]: f1r[r] + 512],
                        in_=self.f_sp[:, m * TOK + tq * 512: m * TOK + (tq + 1) * 512]))
                    em.wait("S", em.TS, tok)
                    if ("s", r) in sr_rel:
                        em.wait("S", sr_rel[("s", r)][0], sr_rel[("s", r)][1])
                        em.wait("V", sr_rel[("s", r)][0], sr_rel[("s", r)][1])
                    sv0 = em.do("S", lambda e, m=m, b=b, r=r: e.activation(
                        out=SC_t[:, sring[r]: sring[r] + 512], in_=self.bank(b),
                        func=AF.Identity, bias=self.smc(SM.bo + m)), em.SS, 1)
                    self.bank_rel[b] = (em.SS, sv0)
                    em.wait("V", em.SS, sv0)
                    sv = em.do("V", lambda e, b=b, r=r: e.tensor_tensor(
                        out=SC_t[:, sring[r]: sring[r] + 512],
                        in0=self.bank(b + 1), in1=SC_t[:, sring[r]: sring[r] + 512],
                        op=ALU.add), em.VS, 1)
                    self.bank_rel[b + 1] = (em.VS, sv)
                    em.dwait("S", ldf[m])
                    if ("g", 0) in sr_rel:
                        em.wait("S", sr_rel[("g", 0)][0], sr_rel[("g", 0)][1])
                    gv = em.do("S", lambda e, r=r: e.activation(
                        out=SC_t[:, gcol: gcol + 512], in_=SC_t[:, f1r[r]: f1r[r] + 512],
                        func=AF.Silu), em.SS, 1)
                    sr_rel[("f", r)] = (em.SS, gv)
                    em.wait("V", em.SS, gv)
                    em.wait("V", em.VS, sv)
                    v1 = em.do("V", lambda e, r=r: e.tensor_tensor(
                        out=SC_t[:, sring[r]: sring[r] + 512],
                        in0=SC_t[:, sring[r]: sring[r] + 512],
                        in1=SC_t[:, gcol: gcol + 512], op=ALU.mult), em.VS, 1)
                    sr_rel[("s", r)] = (em.VS, v1)
                    sr_rel[("g", 0)] = (em.VS, v1)
                    em.dwait("V", ldx[m])
                    em.wait("V", em.VS, v1)
                    v2 = em.do("V", lambda e, r=r: e.tensor_tensor(
                        out=SC_t[:, x1r[r]: x1r[r] + 512],
                        in0=SC_t[:, sring[r]: sring[r] + 512],
                        in1=SC_t[:, x1r[r]: x1r[r] + 512], op=ALU.add), em.VS, 1)
                    em.wait("Y", em.VS, v2)
                    ot = em.dma("Y", lambda e, m=m, tq=tq, r=r: e.dma_start(
                        out=self.out_fm[m * 128:(m + 1) * 128, tq * 512:(tq + 1) * 512],
                        in_=SC_t[:, x1r[r]: x1r[r] + 512]))
                    sr_rel[("x", r)] = ot

    def emit_router(self, E, lhsT_fn, rhs_fn, wE_col, l1=None, ntok=TOK):
        em = self
        SC_t = self.SC_t
        NTP = ntok // 128
        for tp in range(NTP):
            em.wait("T", self.bank_rel.get(66, (em.VS, 0))[0],
                    self.bank_rel.get(66, (em.VS, 0))[1])
            tok = None
            for k in range(NCH):
                tok = em.do("T", lambda e, tp=tp, k=k, E=E: e.matmul(
                    self.bank(6, E), lhsT_fn(tp, k), rhs_fn(k),
                    start=(k == 0), stop=(k == NCH - 1)),
                    em.TS if k == NCH - 1 else None, 1)
            C0 = self.SCT + 768
            p_ = SC_t[:, wE_col + tp * E: wE_col + (tp + 1) * E]
            t1c = SC_t[:, C0: C0 + E]
            gate_src = t1c
            m1 = SC_t[:, C0 + 8: C0 + 9]
            m2 = SC_t[:, C0 + 9: C0 + 10]
            m3 = SC_t[:, C0 + 10: C0 + 11]
            em.wait("V", em.TS, tok)
            em.wait("V", em.SS, self.SS.n if False else em.SS.n)
            em.wait("V", em.VS, em.VS.n)
            if l1 is None:
                v = em.do("V", lambda e, p_=p_: e.tensor_tensor(
                    out=t1c, in0=self.bank(6, E), in1=SC_t[:, self.gb_bc_col: self.gb_bc_col + E],
                    op=ALU.add), em.VS, 1)
            else:
                rbc, gbc = l1
                v = em.do("V", lambda e, p_=p_, gate_src=gate_src: e.tensor_tensor(
                    out=t1c, in0=self.bank(6, E), in1=SC_t[:, rbc: rbc + E],
                    op=ALU.add), em.VS, 1)
            self.bank_rel[66] = (em.VS, v)
            em.wait("V", em.VS, v)
            if _ROUTER_RAW == "logits":
                em.do("V", lambda e, p_=p_: e.tensor_copy(p_, t1c), em.VS, 1)
                continue
            if l1 is None:
                v1 = em.do("V", lambda e, p_=p_, gate_src=gate_src: e.tensor_reduce(m1, gate_src, axis=AX.X, op=ALU.max), em.VS, 1)
                em.wait("V", em.VS, v1)
                if _ROUTER_RAW == "logits2":
                    em.do("V", lambda e, p_=p_: e.tensor_copy(p_, gate_src), em.VS, 1)
                    continue
                if _ROUTER_RAW == "max":
                    for jj in range(E):
                        em.do("V", lambda e, p_=p_, jj=jj: e.tensor_copy(
                            p_[:, jj: jj + 1], m1), em.VS, 1)
                    continue
                v2 = em.do("V", lambda e, p_=p_, gate_src=gate_src: e.tensor_scalar(
                    out=m2, in0=m1, scalar1=-1.0, scalar2=None, op0=ALU.mult), em.VS, 1)
                em.wait("S", em.VS, v2)
                s1 = em.do("S", lambda e, p_=p_, gate_src=gate_src: e.activation(
                    out=p_, in_=gate_src, func=AF.Exp, bias=m2, accum_out=m3), em.SS, 1)
                if _ROUTER_RAW == "exp":
                    em.wait("V", em.SS, s1)
                    continue
                if _ROUTER_RAW == "sum":
                    em.wait("V", em.SS, s1)
                    em.do("V", lambda e, p_=p_: e.tensor_scalar(
                        out=p_, in0=m3, scalar1=0.0, scalar2=None, op0=ALU.add), em.VS, 1)
                    continue
                em.wait("V", em.SS, s1)
                v3 = em.do("V", lambda e, p_=p_, gate_src=gate_src: e.reciprocal(m1, m3), em.VS, 1)
                em.wait("V", em.VS, v3)
                v4 = em.do("V", lambda e, p_=p_, gate_src=gate_src: e.tensor_scalar(
                    out=p_, in0=p_, scalar1=m1, scalar2=None, op0=ALU.mult), em.VS, 1)
                if _ROUTER_RAW == "softmax":
                    continue
                em.wait("V", em.VS, v4)
                v5 = em.do("V", lambda e, p_=p_, gate_src=gate_src: e.tensor_reduce(m1, p_, axis=AX.X, op=ALU.max), em.VS, 1)
                em.wait("V", em.VS, v5)
                v6 = em.do("V", lambda e, p_=p_, gate_src=gate_src: e.tensor_scalar(
                    out=t1c, in0=p_, scalar1=m1, scalar2=None, op0=ALU.is_ge), em.VS, 1)
                em.wait("V", em.VS, v6)
                v7 = em.do("V", lambda e, p_=p_, gate_src=gate_src: e.scalar_tensor_tensor(
                    out=t1c, in0=t1c, scalar=-2.0, in1=p_, op0=ALU.mult, op1=ALU.add), em.VS, 1)
                em.wait("V", em.VS, v7)
                v8 = em.do("V", lambda e, p_=p_, gate_src=gate_src: e.tensor_reduce(m2, t1c, axis=AX.X, op=ALU.max), em.VS, 1)
                em.wait("V", em.VS, v8)
                v9 = em.do("V", lambda e, p_=p_, gate_src=gate_src: e.tensor_tensor(out=m3, in0=m1, in1=m2, op=ALU.add), em.VS, 1)
                em.wait("V", em.VS, v9)
                v10 = em.do("V", lambda e, p_=p_, gate_src=gate_src: e.tensor_scalar(
                    out=m3, in0=m3, scalar1=1e-9, scalar2=None, op0=ALU.add), em.VS, 1)
                em.wait("V", em.VS, v10)
                v11 = em.do("V", lambda e, p_=p_, gate_src=gate_src: e.reciprocal(m3, m3), em.VS, 1)
                em.wait("V", em.VS, v11)
                v12 = em.do("V", lambda e, p_=p_, gate_src=gate_src: e.tensor_scalar(
                    out=t1c, in0=p_, scalar1=m2, scalar2=None, op0=ALU.is_ge), em.VS, 1)
                em.wait("V", em.VS, v12)
                v13 = em.do("V", lambda e, p_=p_, gate_src=gate_src: e.tensor_tensor(out=p_, in0=p_, in1=t1c, op=ALU.mult), em.VS, 1)
                em.wait("V", em.VS, v13)
                vfin = em.do("V", lambda e, p_=p_, gate_src=gate_src: e.tensor_scalar(
                    out=p_, in0=p_, scalar1=m3, scalar2=None, op0=ALU.mult), em.VS, 1)
                if _ROUTER_RAW == "snap":
                    em.wait("V", em.VS, vfin)
                    em.do("V", lambda e, p_=p_, tp=tp, E=E: e.tensor_copy(
                        SC_t[:, tp * E: (tp + 1) * E], p_), em.VS, 1)
            else:
                bsd = SC_t[:, C0 + 16: C0 + 16 + E]
                v1 = em.do("V", lambda e, gbc=gbc: e.tensor_tensor(
                    out=bsd, in0=gate_src, in1=SC_t[:, gbc: gbc + E], op=ALU.add), em.VS, 1)
                em.wait("V", em.VS, v1)
                v2 = em.do("V", lambda e, p_=p_, gate_src=gate_src: e.tensor_reduce(m1, bsd, axis=AX.X, op=ALU.max), em.VS, 1)
                em.wait("V", em.VS, v2)
                v3 = em.do("V", lambda e, p_=p_, gate_src=gate_src: e.tensor_scalar(
                    out=p_, in0=bsd, scalar1=m1, scalar2=None, op0=ALU.is_ge), em.VS, 1)
                em.wait("V", em.VS, v3)
                v4 = em.do("V", lambda e, p_=p_, gate_src=gate_src: e.scalar_tensor_tensor(
                    out=bsd, in0=p_, scalar=-1e30, in1=bsd, op0=ALU.mult, op1=ALU.add), em.VS, 1)
                em.wait("V", em.VS, v4)
                v5 = em.do("V", lambda e, p_=p_, gate_src=gate_src: e.tensor_reduce(m2, bsd, axis=AX.X, op=ALU.max), em.VS, 1)
                em.wait("V", em.VS, v5)
                v6 = em.do("V", lambda e, gbc=gbc: e.tensor_tensor(
                    out=bsd, in0=gate_src, in1=SC_t[:, gbc: gbc + E], op=ALU.add), em.VS, 1)
                em.wait("V", em.VS, v6)
                v7 = em.do("V", lambda e, p_=p_, gate_src=gate_src: e.tensor_scalar(
                    out=bsd, in0=bsd, scalar1=m2, scalar2=None, op0=ALU.is_ge), em.VS, 1)
                em.wait("V", em.VS, v7)
                v8 = em.do("V", lambda e, p_=p_, gate_src=gate_src: e.tensor_tensor(out=p_, in0=gate_src, in1=bsd, op=ALU.mult), em.VS, 1)
                em.wait("V", em.VS, v8)
                v9 = em.do("V", lambda e, p_=p_, gate_src=gate_src: e.tensor_scalar(
                    out=bsd, in0=bsd, scalar1=1e30, scalar2=-1e30, op0=ALU.mult, op1=ALU.add), em.VS, 1)
                em.wait("V", em.VS, v9)
                v10 = em.do("V", lambda e, p_=p_, gate_src=gate_src: e.tensor_tensor(out=p_, in0=p_, in1=bsd, op=ALU.add), em.VS, 1)
                em.wait("V", em.VS, v10)
                v11 = em.do("V", lambda e, p_=p_, gate_src=gate_src: e.tensor_reduce(m1, p_, axis=AX.X, op=ALU.max), em.VS, 1)
                em.wait("V", em.VS, v11)
                v12 = em.do("V", lambda e, p_=p_, gate_src=gate_src: e.tensor_scalar(
                    out=m2, in0=m1, scalar1=-1.0, scalar2=None, op0=ALU.mult), em.VS, 1)
                em.wait("S", em.VS, v12)
                s1 = em.do("S", lambda e, p_=p_, gate_src=gate_src: e.activation(
                    out=p_, in_=p_, func=AF.Exp, bias=m2, accum_out=m3), em.SS, 1)
                em.wait("V", em.SS, s1)
                v13 = em.do("V", lambda e, p_=p_, gate_src=gate_src: e.reciprocal(m1, m3), em.VS, 1)
                em.wait("V", em.VS, v13)
                em.do("V", lambda e, p_=p_, gate_src=gate_src: e.tensor_scalar(
                    out=p_, in0=p_, scalar1=m1, scalar2=None, op0=ALU.mult), em.VS, 1)

    def emit_wbc(self, E, wE_col, wT_col, wbc_col, e, ntok=TOK):
        em = self
        SC_t = self.SC_t
        NTP = ntok // 128
        if e == 0:
            for tp in range(NTP):
                em.wait("T", em.VS, em.VS.n)
                tr = em.do("T", lambda e_, tp=tp, E=E: e_.matmul(
                    self.bank(6, 128, 0, E),
                    SC_t[:, wE_col + tp * E: wE_col + (tp + 1) * E],
                    self.identf[:, 0:128], is_transpose=True,
                    start=True, stop=True), em.TS, 1)
                em.wait("V", em.TS, tr)
                vv = em.do("V", lambda e_, tp=tp, E=E: e_.tensor_copy(
                    SC_t[0:E, wT_col + tp * 128: wT_col + (tp + 1) * 128],
                    self.bank(6, 128, 0, E)), em.VS, 1)
                em.wait("T", em.VS, vv)
        for tt in range(ntok // 512):
            em.wait("T", em.VS, em.VS.n)
            t1 = em.do("T", lambda e_, tt=tt, E=E, e=e: e_.matmul(
                self.bank(6, 512, 0, 1), self.identf[0:E, e: e + 1],
                SC_t[0:E, wT_col + tt * 512: wT_col + (tt + 1) * 512],
                start=True, stop=True), em.TS, 1)
            em.wait("V", em.TS, t1)
            v1 = em.do("V", lambda e_, tt=tt: e_.tensor_copy(
                SC_t[0:1, wbc_col + tt * 512: wbc_col + (tt + 1) * 512],
                self.bank(6, 512, 0, 1)), em.VS, 1)
            em.wait("T", em.VS, v1)
            t2 = em.do("T", lambda e_, tt=tt: e_.matmul(
                self.bank(6, 512), self.onesf[0:1, 0:128],
                SC_t[0:1, wbc_col + tt * 512: wbc_col + (tt + 1) * 512],
                start=True, stop=True), em.TS, 1)
            em.wait("V", em.TS, t2)
            v2 = em.do("V", lambda e_, tt=tt: e_.tensor_copy(
                SC_t[:, wbc_col + tt * 512: wbc_col + (tt + 1) * 512],
                self.bank(6, 512)), em.VS, 1)
            em.wait("T", em.VS, v2)
        return em.VS.n

    def emit_ffn(self, layer):
        if layer == 0:
            self.emit_ffn0()
        else:
            self.emit_ffn1()

    def emit_ffn0(self):
        em = self
        A_t, FA_t, SC_t = self.A_t, self.FA_t, self.SC_t
        self.pb_rot = 0
        self.bank_rel = {}
        amark = self.A.mark()
        famark = self.FA.mark()
        H0 = self.QA
        assert self.A.mark() == 32768
        ACC = self.FA.alloc(NCH * TOK)
        WE, WTC, WBC = self.SCT, 0, 2048
        WE = self.SCT + 640
        ring = [self.SCT, None]
        lastv = None
        prev_rd = None
        for c in range(NCH):
            for tt in range(NTT):
                if prev_rd is not None:
                    em.wait("Y", em.VS, prev_rd)
                t = em.dma("Y", lambda e, c=c, tt=tt: e.dma_start(
                    out=SC_t[:, self.SCT: self.SCT + 512],
                    in_=self.x_fm[c * 128:(c + 1) * 128, tt * 512:(tt + 1) * 512]))
                em.dwait("V", t)
                if lastv is not None:
                    em.wait("V", em.VS, lastv)
                dst = slice(ACC + c * TOK + tt * 512, ACC + c * TOK + (tt + 1) * 512)
                v1 = em.do("V", lambda e, dst=dst, tt=tt: e.tensor_tensor(
                    out=FA_t[:, dst], in0=SC_t[:, self.SCT: self.SCT + 512],
                    in1=SC_t[:, self.MBC + tt * 512: self.MBC + (tt + 1) * 512],
                    op=ALU.subtract), em.VS, 1)
                prev_rd = v1
                em.wait("V", em.VS, v1)
                v2 = em.do("V", lambda e, dst=dst, tt=tt: e.tensor_tensor(
                    out=FA_t[:, dst], in0=FA_t[:, dst],
                    in1=SC_t[:, self.RBC + tt * 512: self.RBC + (tt + 1) * 512],
                    op=ALU.mult), em.VS, 1)
                em.wait("V", em.VS, v2)
                lastv = em.do("V", lambda e, dst=dst, c=c: e.tensor_scalar(
                    out=FA_t[:, dst], in0=FA_t[:, dst],
                    scalar1=self.smc(SM.sh_b2 + c), scalar2=None, op0=ALU.add), em.VS, 1)
        init_tok = lastv
        self.gb_bc_col = self.SCT + 768 + 48
        em.wait("T", em.VS, init_tok)
        t1 = em.do("T", lambda e: e.matmul(
            self.bank(6, 8, 0, 1), self.smc(SM.gate_b, 1, 0, 8),
            self.identf[0:8, 0:8], is_transpose=True, start=True, stop=True), em.TS, 1)
        em.wait("V", em.TS, t1)
        v1 = em.do("V", lambda e: e.tensor_copy(
            SC_t[0:1, self.gb_bc_col + 8: self.gb_bc_col + 16], self.bank(6, 8, 0, 1)), em.VS, 1)
        em.wait("T", em.VS, v1)
        t2 = em.do("T", lambda e: e.matmul(
            self.bank(6, 8), self.onesf[0:1, 0:128],
            SC_t[0:1, self.gb_bc_col + 8: self.gb_bc_col + 16],
            start=True, stop=True), em.TS, 1)
        em.wait("V", em.TS, t2)
        v2 = em.do("V", lambda e: e.tensor_copy(
            SC_t[:, self.gb_bc_col: self.gb_bc_col + 8], self.bank(6, 8)), em.VS, 1)
        em.wait("T", em.VS, v2)
        self.emit_router(
            8,
            lambda tp, k: FA_t[:, ACC + k * TOK + tp * 128: ACC + k * TOK + (tp + 1) * 128],
            lambda k: self.smc(SM.gate_w + k * 8, 8),
            WE, l1=None)
        if self.stage and self.stage[0] == "r0":
            if _ROUTER_RAW is None or _ROUTER_RAW is False:
                self.emit_wbc(8, WE, WTC, WBC, 0)
            self.dump_stage([(SC_t, WE, 128, False), (SC_t, WBC, 2048, False),
                             (SC_t, WTC, 2048, False)])
            self._early = True
            return
        parts = _PARTS if _PARTS is not None else {"shared", "experts"}
        for half in range(2):
            tts = [half * 2, half * 2 + 1]
            for e8 in range(9):
                is_sh = e8 == 8
                if is_sh and "shared" not in parts:
                    continue
                if (not is_sh) and "experts" not in parts:
                    continue
                if not is_sh:
                    self.emit_wbc(8, WE, WTC, WBC, e8)
                if is_sh:
                    wsrc1 = lambda m: self.sh_w1[:, m * 128:(m + 1) * 128].rearrange(
                        "(kc p) m -> p kc m", p=128)
                    b1 = SM.sh_b1
                else:
                    wsrc1 = lambda m, e8=e8: self.ew1[e8, :, m * 128:(m + 1) * 128].rearrange(
                        "(kc p) m -> p kc m", p=128)
                    b1 = SM.eb1 + e8 * 32

                def evup(m, tt, b, tok, b1=b1, half=half):
                    em.wait("S", em.TS, tok)
                    ttl = tt - half * 2
                    v = em.do("S", lambda e, m=m, ttl=ttl, b=b: e.activation(
                        out=A_t[:, H0 + m * 1024 + ttl * 512: H0 + m * 1024 + (ttl + 1) * 512],
                        in_=self.bank(b), func=AF.Gelu_apprx_tanh,
                        bias=self.smc(b1 + m)), em.SS, 1)
                    return (em.SS, v)
                em.wait("T", em.VS, init_tok)
                self.emit_proj(32, NCH, wsrc1,
                               lambda k, tt: A_t[:, self.XN + k * TOK + tt * 512: self.XN + k * TOK + (tt + 1) * 512],
                               evup, tts=tts, name=f"ffn0up{e8}")
                if is_sh:
                    wsrc2 = lambda m: self.sh_w2[:, m * 128:(m + 1) * 128].rearrange(
                        "(kc p) m -> p kc m", p=128)
                else:
                    wsrc2 = lambda m, e8=e8: self.ew2[e8, :, m * 128:(m + 1) * 128].rearrange(
                        "(kc p) m -> p kc m", p=128)

                def evdn(m, tt, b, tok, e8=e8, is_sh=is_sh, half=half):
                    ttl = tt - half * 2
                    dst = slice(ACC + m * TOK + tt * 512, ACC + m * TOK + (tt + 1) * 512)
                    em.wait("V", em.TS, tok)
                    em.wait("V", em.VS, em.VS.n)
                    if is_sh:
                        v = em.do("V", lambda e, b=b, dst=dst, m=m: e.scalar_tensor_tensor(
                            out=FA_t[:, dst], in0=self.bank(b),
                            scalar=self.smc(SM.sh_b2 + m), in1=FA_t[:, dst],
                            op0=ALU.add, op1=ALU.add), em.VS, 1)
                    else:
                        v0 = em.do("V", lambda e, b=b, m=m, e8=e8: e.scalar_tensor_tensor(
                            out=SC_t[:, self.SCT: self.SCT + 512], in0=self.bank(b),
                            scalar=self.smc(SM.eb2 + e8 * 8 + m),
                            in1=SC_t[:, WBC + tt * 512: WBC + (tt + 1) * 512],
                            op0=ALU.add, op1=ALU.mult), em.VS, 1)
                        em.wait("V", em.VS, v0)
                        v = em.do("V", lambda e, dst=dst: e.tensor_tensor(
                            out=FA_t[:, dst], in0=SC_t[:, self.SCT: self.SCT + 512],
                            in1=FA_t[:, dst], op=ALU.add), em.VS, 1)
                    return (em.VS, v)
                self.emit_proj(NCH, 32, wsrc2,
                               lambda k, tt, half=half: A_t[:, H0 + k * 1024 + (tt - half * 2) * 512: H0 + k * 1024 + (tt - half * 2 + 1) * 512],
                               evdn, tts=tts, name=f"ffn0dn{e8}",
                               tdeps=[(em.SS, em.SS.n)])
        for c in range(NCH):
            em.wait("Y", em.VS, em.VS.n)
            em.dma("Y", lambda e, c=c: e.dma_start(
                out=self.f_sp[:, c * TOK:(c + 1) * TOK],
                in_=FA_t[:, ACC + c * TOK: ACC + (c + 1) * TOK]))
        self.acc_col = ACC
        self.A.release(amark)
        self.fa_acc_mark = famark

    def emit_ffn1(self):
        em = self
        A_t, FA_t, SC_t, W_t = self.A_t, self.FA_t, self.SC_t, self.W_t
        self.pb_rot = 0
        self.bank_rel = {}
        amark = self.A.mark()
        famark = self.FA.mark()
        wmark1 = self.W.mark()
        UP = self.A.alloc(8192)
        HH = self.A.alloc(4096 * 2)
        XF = self.FA.alloc(8192)
        ACC = self.FA.alloc(8192)
        WE = self.SCT + 640
        WTC, WBC = 0, 2048
        RING = self.SCT
        wtmp = self.W.alloc(1024)
        em.dma("Y", lambda e: e.dma_start(
            out=self.mr_sp[0:1, 0:TOK], in_=SC_t[0:1, self.MBC: self.MBC + TOK]))
        em.dma("Y", lambda e: e.dma_start(
            out=self.mr_sp[0:1, TOK: 2 * TOK], in_=SC_t[0:1, self.RBC: self.RBC + TOK]))
        for half in range(2):
            tts = [half * 2, half * 2 + 1]
            if half == 1:
                t1 = em.dma("Y", lambda e: e.dma_start(
                    out=SC_t[0:1, self.MBC: self.MBC + TOK], in_=self.mr_sp[0:1, 0:TOK]))
                t2 = em.dma("Y", lambda e: e.dma_start(
                    out=SC_t[0:1, self.RBC: self.RBC + TOK], in_=self.mr_sp[0:1, TOK: 2 * TOK]))
                em.dwait("V", t1)
                em.dwait("V", t2)
                em.wait("V", em.VS, em.VS.n)
                rv = em.do("V", lambda e: e.memset(SC_t[:, 5116: 5120], 0.0), em.VS, 1)
                self.bcast_inplace(self.MBC, rv)
                rv2 = em.do("V", lambda e: e.memset(SC_t[:, 5116: 5120], 0.0), em.VS, 1)
                self.bcast_inplace(self.RBC, rv2)
            for it in range(_L1_ITERS):
                lastv = None
                prev_rd = None
                if it == 0:
                    for c in range(NCH):
                        for ttl in range(2):
                            tt = half * 2 + ttl
                            if prev_rd is not None:
                                em.wait("Y", em.VS, prev_rd)
                            t = em.dma("Y", lambda e, c=c, tt=tt: e.dma_start(
                                out=SC_t[:, RING: RING + 512],
                                in_=self.xres[:, c * TOK + tt * 512: c * TOK + (tt + 1) * 512]))
                            em.dwait("V", t)
                            if lastv is not None:
                                em.wait("V", em.VS, lastv)
                            dst = slice(XF + c * 1024 + ttl * 512, XF + c * 1024 + (ttl + 1) * 512)
                            dsta = slice(ACC + c * 1024 + ttl * 512, ACC + c * 1024 + (ttl + 1) * 512)
                            v1 = em.do("V", lambda e, dst=dst, tt=tt: e.tensor_tensor(
                                out=FA_t[:, dst], in0=SC_t[:, RING: RING + 512],
                                in1=SC_t[:, self.MBC + tt * 512: self.MBC + (tt + 1) * 512],
                                op=ALU.subtract), em.VS, 1)
                            prev_rd = v1
                            em.wait("V", em.VS, v1)
                            v2 = em.do("V", lambda e, dst=dst, tt=tt: e.tensor_tensor(
                                out=FA_t[:, dst], in0=FA_t[:, dst],
                                in1=SC_t[:, self.RBC + tt * 512: self.RBC + (tt + 1) * 512],
                                op=ALU.mult), em.VS, 1)
                            em.wait("V", em.VS, v2)
                            lastv = em.do("V", lambda e, dst=dst, dsta=dsta: e.tensor_copy(
                                FA_t[:, dsta], FA_t[:, dst]), em.VS, 1)
                else:
                    em.wait("V", em.VS, em.VS.n)
                    for c in range(NCH):
                        for ttl in range(2):
                            dst = slice(XF + c * 1024 + ttl * 512, XF + c * 1024 + (ttl + 1) * 512)
                            dsta = slice(ACC + c * 1024 + ttl * 512, ACC + c * 1024 + (ttl + 1) * 512)
                            lastv = em.do("V", lambda e, dst=dst, dsta=dsta: e.tensor_copy(
                                FA_t[:, dsta], FA_t[:, dst]), em.VS, 1)
                init_tok = lastv
                self.rb_bc_col = self.SCT + 768 + 32
                self.gb_bc_col = self.SCT + 768 + 40
                for bcol, rowoff in ((self.rb_bc_col, SM.rb_row), (self.gb_bc_col, SM.gb_row)):
                    em.wait("T", em.VS, em.VS.n)
                    tb = em.do("T", lambda e, bcol=bcol, rowoff=rowoff, it=it: e.matmul(
                        self.bank(6, 6), self.onesf[0:1, 0:128],
                        self.small[0:1, rowoff + it * 6: rowoff + it * 6 + 6],
                        start=True, stop=True), em.TS, 1)
                    em.wait("V", em.TS, tb)
                    vb = em.do("V", lambda e, bcol=bcol: e.tensor_copy(
                        SC_t[:, bcol: bcol + 6], self.bank(6, 6)), em.VS, 1)
                    em.wait("T", em.VS, vb)
                em.wait("T", em.VS, init_tok)
                self.emit_router(
                    6,
                    lambda tp, k: FA_t[:, XF + k * 1024 + tp * 128: XF + k * 1024 + (tp + 1) * 128],
                    lambda k, it=it: self.smc(SM.router_W + it * 48 + k * 6, 6),
                    WE, l1=(self.rb_bc_col, self.gb_bc_col), ntok=1024)
                if self.stage and self.stage[0] == "r1":
                    self.emit_wbc(6, WE, WTC, WBC, 0, ntok=1024)
                    self.dump_stage([(SC_t, WE, 48, False), (SC_t, WBC, 2048, False)])
                    self._early = True
                    return
                for e8 in range(8):
                    is_sh = e8 >= 6
                    ns = e8 - 6
                    parts = _L1_PARTS if _L1_PARTS is not None else {"exp", "sh"}
                    if is_sh and "sh" not in parts:
                        continue
                    if (not is_sh) and "exp" not in parts:
                        continue
                    if not is_sh:
                        self.emit_wbc(6, WE, WTC, WBC, e8, ntok=1024)
                    if is_sh:
                        wsrc1 = lambda m, ns=ns: self.share_Wup[:, ns * 1024 + m * 128: ns * 1024 + (m + 1) * 128].rearrange(
                            "(kc p) m -> p kc m", p=128)
                        b1 = SM.share_bup + ns * 8
                    else:
                        wsrc1 = lambda m, e8=e8: self.exp_Wup[e8, :, m * 128:(m + 1) * 128].rearrange(
                            "(kc p) m -> p kc m", p=128)
                        b1 = SM.exp_bup + e8 * 8

                    def evup(m, tt, b, tok, b1=b1, half=half):
                        ttl = tt - half * 2
                        em.wait("S", em.TS, tok)
                        em.wait("S", em.TS, em.TS.n if False else tok)
                        v = em.do("S", lambda e, m=m, ttl=ttl, b=b, b1=b1: e.activation(
                            out=A_t[:, UP + m * 1024 + ttl * 512: UP + m * 1024 + (ttl + 1) * 512],
                            in_=self.bank(b), func=AF.Identity,
                            bias=self.smc(b1 + m)), em.SS, 1)
                        return (em.SS, v)
                    em.wait("T", em.VS, init_tok)
                    self.emit_proj(8, NCH, wsrc1,
                                   lambda k, tt: A_t[:, self.XN + k * TOK + tt * 512: self.XN + k * TOK + (tt + 1) * 512],
                                   evup, tts=tts, name=f"f1up{e8}")
                    if is_sh:
                        wsrc2 = lambda m: self.share_Wg[:, m * 128:(m + 1) * 128].rearrange(
                            "(kc p) m -> p kc m", p=128)
                        bg = SM.share_bg
                    else:
                        wsrc2 = lambda m, e8=e8: self.exp_Wg[e8, :, m * 128:(m + 1) * 128].rearrange(
                            "(kc p) m -> p kc m", p=128)
                        bg = SM.exp_bg + e8 * 8
                    gtmp_rel = {}

                    def evg(m, tt, b, tok, bg=bg, half=half, gtmp_rel=gtmp_rel):
                        ttl = tt - half * 2
                        r = (m * 2 + ttl) % 2
                        em.wait("S", em.TS, tok)
                        if r in gtmp_rel:
                            em.wait("S", gtmp_rel[r][0], gtmp_rel[r][1])
                        sv = em.do("S", lambda e, b=b, r=r, bg=bg, m=m: e.activation(
                            out=W_t[:, wtmp + r * 512: wtmp + (r + 1) * 512],
                            in_=self.bank(b), func=AF.Silu,
                            bias=self.smc(bg + m)), em.SS, 1)
                        em.wait("V", em.SS, sv)
                        vv = em.do("V", lambda e, m=m, ttl=ttl, r=r: e.tensor_tensor(
                            out=A_t[:, HH + m * 1024 + ttl * 512: HH + m * 1024 + (ttl + 1) * 512],
                            in0=W_t[:, wtmp + r * 512: wtmp + (r + 1) * 512],
                            in1=A_t[:, UP + m * 1024 + ttl * 512: UP + m * 1024 + (ttl + 1) * 512],
                            op=ALU.mult), em.VS, 1)
                        gtmp_rel[r] = (em.VS, vv)
                        return (em.VS, vv)
                    self.emit_proj(8, NCH, wsrc2,
                                   lambda k, tt, half=half: A_t[:, UP + k * 1024 + (tt - half * 2) * 512: UP + k * 1024 + (tt - half * 2 + 1) * 512],
                                   evg, tts=tts, name=f"f1g{e8}",
                                   tdeps=[(em.SS, em.SS.n)])
                    if is_sh:
                        wsrc3 = lambda m: self.share_Wdown[:, m * 128:(m + 1) * 128].rearrange(
                            "(kc p) m -> p kc m", p=128)
                        bd = SM.share_bdown
                    else:
                        wsrc3 = lambda m, e8=e8: self.exp_Wdown[e8, :, m * 128:(m + 1) * 128].rearrange(
                            "(kc p) m -> p kc m", p=128)
                        bd = SM.exp_bdown + e8 * 8

                    def evdn(m, tt, b, tok, e8=e8, is_sh=is_sh, bd=bd, half=half):
                        ttl = tt - half * 2
                        dst = slice(ACC + m * 1024 + ttl * 512, ACC + m * 1024 + (ttl + 1) * 512)
                        em.wait("V", em.TS, tok)
                        em.wait("V", em.VS, em.VS.n)
                        if is_sh:
                            v = em.do("V", lambda e, b=b, dst=dst, bd=bd, m=m: e.scalar_tensor_tensor(
                                out=FA_t[:, dst], in0=self.bank(b),
                                scalar=self.smc(bd + m), in1=FA_t[:, dst],
                                op0=ALU.add, op1=ALU.add), em.VS, 1)
                        else:
                            v0 = em.do("V", lambda e, b=b, m=m, bd=bd, ttl=ttl: e.scalar_tensor_tensor(
                                out=SC_t[:, RING: RING + 512], in0=self.bank(b),
                                scalar=self.smc(bd + m),
                                in1=SC_t[:, WBC + ttl * 512: WBC + (ttl + 1) * 512],
                                op0=ALU.add, op1=ALU.mult), em.VS, 1)
                            em.wait("V", em.VS, v0)
                            v = em.do("V", lambda e, dst=dst: e.tensor_tensor(
                                out=FA_t[:, dst], in0=SC_t[:, RING: RING + 512],
                                in1=FA_t[:, dst], op=ALU.add), em.VS, 1)
                        return (em.VS, v)
                    self.emit_proj(8, NCH, wsrc3,
                                   lambda k, tt, half=half: A_t[:, HH + k * 1024 + (tt - half * 2) * 512: HH + k * 1024 + (tt - half * 2 + 1) * 512],
                                   evdn, tts=tts, name=f"f1dn{e8}",
                                   tdeps=[(em.VS, em.VS.n)])
                for b in (0, 1):
                    if self.bank_rel.get(b) is not None:
                        em.wait("T", self.bank_rel[b][0], self.bank_rel[b][1])
                sq_rel = None
                for c in range(NCH):
                    for ttl in range(2):
                        sl = slice(ACC + c * 1024 + ttl * 512, ACC + c * 1024 + (ttl + 1) * 512)
                        em.wait("S", em.VS, em.VS.n)
                        if sq_rel is not None:
                            em.wait("S", em.TS, sq_rel)
                        sq = em.do("S", lambda e, sl=sl: e.activation(
                            out=SC_t[:, RING: RING + 512], in_=FA_t[:, sl],
                            func=AF.Square), em.SS, 1)
                        em.wait("T", em.SS, sq)
                        sq_rel = em.do("T", lambda e, c=c, ttl=ttl: e.matmul(
                            self.PS_t[0:1, ttl * 512: (ttl + 1) * 512],
                            self.onesf[:, 0:1], SC_t[:, RING: RING + 512],
                            start=(c == 0), stop=(c == NCH - 1)), em.TS, 1)
                em.wait("V", em.TS, sq_rel)
                rrow = SC_t[0:1, WBC: WBC + 1024]
                v1 = em.do("V", lambda e, rrow=rrow: e.tensor_scalar(
                    out=rrow, in0=self.PS_t[0:1, 0:1024], scalar1=1.0 / D,
                    scalar2=EPS_RMS, op0=ALU.mult, op1=ALU.add), em.VS, 1)
                em.wait("S", em.VS, v1)
                s1 = em.do("S", lambda e, rrow=rrow: e.activation(
                    out=rrow, in_=rrow, func=AF.Sqrt), em.SS, 1)
                em.wait("V", em.SS, s1)
                v2 = em.do("V", lambda e, rrow=rrow: e.reciprocal(rrow, rrow), em.VS, 1)
                em.wait("T", em.VS, v2)
                for b in (4, 5):
                    if self.bank_rel.get(b) is not None:
                        em.wait("T", self.bank_rel[b][0], self.bank_rel[b][1])
                for ttl in range(2):
                    tb = em.do("T", lambda e, ttl=ttl: e.matmul(
                        self.bank(4 + ttl), self.onesf[0:1, 0:128],
                        SC_t[0:1, WBC + ttl * 512: WBC + (ttl + 1) * 512],
                        start=True, stop=True), em.TS, 1)
                    em.wait("V", em.TS, tb)
                    vb = em.do("V", lambda e, ttl=ttl: e.tensor_copy(
                        SC_t[:, WBC + ttl * 512: WBC + (ttl + 1) * 512],
                        self.bank(4 + ttl)), em.VS, 1)
                    em.wait("T", em.VS, vb)
                lastv = None
                for c in range(NCH):
                    for ttl in range(2):
                        tt = half * 2 + ttl
                        sl = slice(ACC + c * 1024 + ttl * 512, ACC + c * 1024 + (ttl + 1) * 512)
                        dst = slice(XF + c * 1024 + ttl * 512, XF + c * 1024 + (ttl + 1) * 512)
                        em.wait("V", em.VS, em.VS.n)
                        v1 = em.do("V", lambda e, sl=sl, dst=dst, c=c, it=it, ttl=ttl: e.scalar_tensor_tensor(
                            out=FA_t[:, dst], in0=FA_t[:, sl],
                            scalar=self.smc(SM.norm_w + it * 8 + c),
                            in1=SC_t[:, WBC + ttl * 512: WBC + (ttl + 1) * 512],
                            op0=ALU.mult, op1=ALU.mult), em.VS, 1)
                        em.wait("V", em.VS, v1)
                        lastv = em.do("V", lambda e, dst=dst, c=c, tt=tt: e.tensor_copy(
                            A_t[:, self.XN + c * TOK + tt * 512: self.XN + c * TOK + (tt + 1) * 512],
                            FA_t[:, dst]), em.VS, 1)
                if it == _L1_ITERS - 1:
                    for c in range(NCH):
                        em.wait("Y", em.VS, em.VS.n)
                        em.dma("Y", lambda e, c=c, half=half: e.dma_start(
                            out=self.f_sp[:, c * TOK + half * 1024: c * TOK + (half + 1) * 1024],
                            in_=FA_t[:, XF + c * 1024: XF + (c + 1) * 1024]))
        self.W.release(wmark1)
        self.A.release(amark)
        self.FA.release(famark)

    def emit_combine0(self):
        em = self
        FA_t, SC_t = self.FA_t, self.SC_t
        ACC = self.acc_col
        xr = [0, 512]
        sr = [1024, 1536]
        gr = [2048, 2560]
        rel = {}
        for c in range(NCH):
            for tt in range(NTT):
                i = (c * NTT + tt) % 2
                for key, col, srcap in (("x", xr[i], self.x_fm[c * 128:(c + 1) * 128, tt * 512:(tt + 1) * 512]),
                                        ("s", sr[i], self.s_sp[:, c * TOK + tt * 512: c * TOK + (tt + 1) * 512])):
                    if (key + "w", i) in rel:
                        em.wait("Y", rel[(key + "w", i)][0], rel[(key + "w", i)][1])
                    ld = em.dma("Y", lambda e, col=col, srcap=srcap: e.dma_start(
                        out=SC_t[:, col: col + 512], in_=srcap))
                    rel[(key, i, "ld")] = ld
                sl = slice(ACC + c * TOK + tt * 512, ACC + c * TOK + (tt + 1) * 512)
                if ("g", i) in rel:
                    em.wait("S", rel[("g", i)][0], rel[("g", i)][1])
                gv = em.do("S", lambda e, sl=sl, i=i: e.activation(
                    out=SC_t[:, gr[i]: gr[i] + 512], in_=FA_t[:, sl], func=AF.Silu), em.SS, 1)
                em.dwait("V", rel[("s", i, "ld")])
                em.wait("V", em.SS, gv)
                if ("v2", i) in rel:
                    em.wait("V", rel[("v2", i)][0], rel[("v2", i)][1])
                v1 = em.do("V", lambda e, i=i: e.tensor_tensor(
                    out=SC_t[:, sr[i]: sr[i] + 512], in0=SC_t[:, sr[i]: sr[i] + 512],
                    in1=SC_t[:, gr[i]: gr[i] + 512], op=ALU.mult), em.VS, 1)
                rel[("g", i)] = (em.VS, v1)
                rel[("sw", i)] = (em.VS, v1)
                em.dwait("V", rel[("x", i, "ld")])
                em.wait("V", em.VS, v1)
                v2 = em.do("V", lambda e, i=i: e.tensor_tensor(
                    out=SC_t[:, xr[i]: xr[i] + 512], in0=SC_t[:, sr[i]: sr[i] + 512],
                    in1=SC_t[:, xr[i]: xr[i] + 512], op=ALU.add), em.VS, 1)
                em.wait("Y", em.VS, v2)
                ot = em.dma("Y", lambda e, c=c, tt=tt, i=i: e.dma_start(
                    out=self.xres[:, c * TOK + tt * 512: c * TOK + (tt + 1) * 512],
                    in_=SC_t[:, xr[i]: xr[i] + 512]))
                rel[("v2", i)] = ot
                rel[("xw", i)] = ot
        self.FA.release(self.fa_acc_mark)

    def dump_stage(self, pieces):
        em = self
        self.barrier()
        total = sum(p[2] for p in pieces)
        assert total <= self.stage[1]
        off = 0
        for (t, c0, n, isbf) in pieces:
            eng = "G" if isbf else "Y"
            em.dma(eng, lambda e, t=t, c0=c0, n=n, off=off: e.dma_start(
                out=self.dbg[:, off: off + n], in_=t[:, c0: c0 + n],
                max_dma_last_dim=4096))
            off += n
        self.barrier()


def build_program(stage=None):
    b = Builder(stage)
    nc = b.build()
    return nc


def kernel(**inputs):
    stage = _STAGE
    nc = build_program(stage)
    xf = np.asarray(inputs["L0_attn_x"], np.float32).reshape(BATCH * T, D)
    shared = {
        "constb": _pack_consts(),
        "w_qkvT": np.ascontiguousarray(np.asarray(inputs["L0_attn_Wqkv"], np.float32).T),
        "w_outT": np.ascontiguousarray(np.asarray(inputs["L0_attn_Wout"], np.float32).T),
        "sh_w1": np.ascontiguousarray(np.asarray(inputs["L0_ffn_sh_w1"], np.float32)),
        "sh_w2": np.ascontiguousarray(np.asarray(inputs["L0_ffn_sh_w2"], np.float32)),
        "ew1": np.ascontiguousarray(np.asarray(inputs["L0_ffn_ew1"], np.float32)),
        "ew2": np.ascontiguousarray(np.asarray(inputs["L0_ffn_ew2"], np.float32)),
        "w_qT": np.ascontiguousarray(np.asarray(inputs["L1_attn_Wq"], np.float32).T),
        "w_kT": np.ascontiguousarray(np.asarray(inputs["L1_attn_Wk"], np.float32).T),
        "w_vT": np.ascontiguousarray(np.asarray(inputs["L1_attn_Wv"], np.float32).T),
        "w_oT": np.ascontiguousarray(np.asarray(inputs["L1_attn_Wo"], np.float32).T),
        "exp_Wup": np.ascontiguousarray(np.asarray(inputs["L1_ffn_exp_Wup"], np.float32)),
        "exp_Wg": np.ascontiguousarray(np.asarray(inputs["L1_ffn_exp_Wg"], np.float32)),
        "exp_Wdown": np.ascontiguousarray(np.asarray(inputs["L1_ffn_exp_Wdown"], np.float32)),
        "share_Wup": np.ascontiguousarray(np.asarray(inputs["L1_ffn_share_Wup"], np.float32)),
        "share_Wg": np.ascontiguousarray(np.asarray(inputs["L1_ffn_share_Wg"], np.float32)),
        "share_Wdown": np.ascontiguousarray(np.asarray(inputs["L1_ffn_share_Wdown"], np.float32)),
    }
    in_maps = []
    for c in range(NCORE):
        m = dict(shared)
        m["x_fm"] = np.ascontiguousarray(xf[c * TOK:(c + 1) * TOK, :].T)
        m["smallp"] = _pack_small(inputs, c)
        in_maps.append(m)
    res = run_bass_kernel_spmd(nc, in_maps, core_ids=list(range(NCORE)))
    if stage is not None:
        return [np.asarray(r["dbg"]) for r in res.results]
    out = np.empty((BATCH * T, D), np.float32)
    for c in range(NCORE):
        out[c * TOK:(c + 1) * TOK, :] = np.asarray(res.results[c]["out_fm"]).T
    return out.reshape(BATCH, T, D)


# version 06
# speedup vs baseline: 1.2786x; 1.0002x over previous
import numpy as np
from contextlib import ExitStack

import concourse.bass as bass
import concourse.mybir as mybir
from concourse.bass_utils import run_bass_kernel_spmd

F32 = mybir.dt.float32
BF16 = mybir.dt.bfloat16
I32 = mybir.dt.int32
AF = mybir.ActivationFunctionType
ALU = mybir.AluOpType
AX = mybir.AxisListType

BATCH, T, D = 2, 8192, 1024
NCORE = 8
TOK = (BATCH * T) // NCORE
NCH = D // 128
NTT = TOK // 512
HEADS = 16
NHP = 8
DH = 64
L0_CS = 1024
FF_I = 4096
L0_E = 8
L1_E = 6
L1_I = 1024
L1_C = 2
EPS_LN = 1e-5
EPS_RMS = 1e-6

_STAGE = None
_PARTS = None
_ROUTER_RAW = False
_L1_ITERS = 2
_L1_PARTS = None
_ATTN_PARTS = None
_WO_MM_ONLY = False

ENGINES = ("T", "S", "V", "G", "Y")


class Sem:
    __slots__ = ("h", "n", "name")

    def __init__(self, h, name):
        self.h, self.n, self.name = h, 0, name


class Arena:
    def __init__(self, t, cols, name):
        self.t, self.cols, self.name = t, cols, name
        self.off = 0

    def alloc(self, n):
        assert self.off + n <= self.cols, f"{self.name} overflow {self.off}+{n}>{self.cols}"
        r = self.off
        self.off += n
        return r

    def mark(self):
        return self.off

    def release(self, m):
        self.off = m


class SmallMap:
    def __init__(self):
        self._o = 0
        t = self.take
        self.bout = t(8); self.sh_b1 = t(32); self.sh_b2 = t(8)
        self.eb1 = t(256); self.eb2 = t(64)
        self.bq = t(8); self.bk = t(8); self.bv = t(8); self.bo = t(8)
        self.exp_bup = t(48); self.exp_bg = t(48); self.exp_bdown = t(48)
        self.share_bup = t(16); self.share_bg = t(8); self.share_bdown = t(8)
        self.norm_w = t(16); self.gate_w = t(64); self.router_W = t(96)
        self.logpos = t(16)
        self.wl1 = t(1); self.bl1 = t(1); self.wl2t = t(16); self.bl2 = t(1)
        self.gate_b = t(1); self.router_b = t(2); self.gate_bias = t(2)
        self.q0col = t(1)
        self.hsel = t(2)
        self.rb_row = t(12)
        self.gb_row = t(12)
        self.persist = self._o
        self.logpos_row = t(TOK)
        self.wl1row = t(64)
        self.total = self._o

    def take(self, n):
        r = self._o
        self._o += n
        return r


SM = SmallMap()


def _pack_small(inp, core):
    cols = []

    def vec(v):
        v = np.asarray(v, np.float32).reshape(-1)
        assert v.size % 128 == 0
        cols.append(v.reshape(-1, 128).T)

    def pad_col(v):
        c = np.zeros((128, 1), np.float32)
        v = np.asarray(v, np.float32).reshape(-1)
        c[: v.size, 0] = v
        cols.append(c)

    q0 = core * TOK
    vec(inp["L0_attn_bout"]); vec(inp["L0_ffn_sh_b1"]); vec(inp["L0_ffn_sh_b2"])
    vec(np.asarray(inp["L0_ffn_eb1"]).reshape(-1))
    vec(np.asarray(inp["L0_ffn_eb2"]).reshape(-1))
    vec(inp["L1_attn_bq"]); vec(inp["L1_attn_bk"]); vec(inp["L1_attn_bv"]); vec(inp["L1_attn_bo"])
    vec(np.asarray(inp["L1_ffn_exp_bup"]).reshape(-1))
    vec(np.asarray(inp["L1_ffn_exp_bg"]).reshape(-1))
    vec(np.asarray(inp["L1_ffn_exp_bdown"]).reshape(-1))
    vec(inp["L1_ffn_share_bup"]); vec(inp["L1_ffn_share_bg"]); vec(inp["L1_ffn_share_bdown"])
    vec(np.asarray(inp["L1_ffn_norm_w"]).reshape(-1))
    gw = np.asarray(inp["L0_ffn_gate_w"], np.float32)
    cols.append(gw.reshape(8, 128, 8).transpose(1, 0, 2).reshape(128, 64))
    rw = np.asarray(inp["L1_ffn_router_W"], np.float32)
    cols.append(rw.reshape(2, 8, 128, 6).transpose(2, 0, 1, 3).reshape(128, 96))
    lp = np.log(np.arange(q0, q0 + TOK, dtype=np.float64) + 1.0).astype(np.float32)
    cols.append(lp.reshape(16, 128).T)
    pad_col(np.asarray(inp["L0_attn_Wl1"], np.float32)[:, 0])
    pad_col(inp["L0_attn_bl1"])
    w2t = np.zeros((128, 16), np.float32)
    w2t[:64, :] = np.asarray(inp["L0_attn_Wl2"], np.float32).T
    cols.append(w2t)
    pad_col(inp["L0_attn_bl2"])
    pad_col(inp["L0_ffn_gate_b"])
    rb = np.zeros((128, 2), np.float32)
    rb[:6, :] = np.asarray(inp["L1_ffn_router_b"], np.float32).T
    cols.append(rb)
    gb = np.zeros((128, 2), np.float32)
    gb[:6, :] = np.asarray(inp["L1_ffn_gate_bias"], np.float32).T
    cols.append(gb)
    cols.append(np.full((128, 1), float(q0), np.float32))
    hs = np.zeros((128, 2), np.float32)
    hs[:64, 0] = 1.0
    hs[64:, 1] = 1.0
    cols.append(hs)
    rr = np.zeros((128, 12), np.float32)
    rr[0, :] = np.asarray(inp["L1_ffn_router_b"], np.float32).reshape(-1)
    cols.append(rr)
    gg = np.zeros((128, 12), np.float32)
    gg[0, :] = np.asarray(inp["L1_ffn_gate_bias"], np.float32).reshape(-1)
    cols.append(gg)
    lr = np.zeros((128, TOK), np.float32)
    lr[0, :] = lp
    cols.append(lr)
    w1r = np.zeros((128, 64), np.float32)
    w1r[0, :] = np.asarray(inp["L0_attn_Wl1"], np.float32)[:, 0]
    cols.append(w1r)
    out = np.concatenate(cols, axis=1).astype(np.float32)
    assert out.shape[1] == SM.total
    return np.ascontiguousarray(out)


def _pack_consts():
    import ml_dtypes
    ident = np.eye(128, dtype=np.float32)
    ones = np.ones((128, 128), dtype=np.float32)
    return np.ascontiguousarray(
        np.concatenate([ident, ones], axis=1).astype(ml_dtypes.bfloat16))


class Builder:
    def __init__(self, stage=None):
        self.stage = stage
        self.nc = bass.Bass("TRN2", target_bir_lowering=False, debug=False)
        self.q = {e: [] for e in ENGINES}
        self.floor = {e: {} for e in ENGINES}
        self.st = ExitStack()

    def wait(self, e, sem, val):
        if val is None or val <= 0:
            return
        if self.floor[e].get(sem.name, 0) >= val:
            return
        self.floor[e][sem.name] = val
        self.q[e].append(lambda eng, s=sem.h, v=val: eng.wait_ge(s, v))

    def do(self, e, build, sem=None, k=1):
        if sem is None:
            self.q[e].append(build)
            return None
        sem.n += k
        val = sem.n
        self.q[e].append(lambda eng, b=build, s=sem.h, kk=k: b(eng).then_inc(s, kk))
        return val

    def dma(self, qeng, build):
        pool = self.dmapool[qeng]
        idx = self.dmaidx[qeng]
        self.dmaidx[qeng] = (idx + 1) % len(pool)
        s = pool[idx]
        self.wait(qeng, s, s.n)
        val = self.do(qeng, build, s, 16)
        return (s, val)

    def dwait(self, e, tok):
        self.wait(e, tok[0], tok[1])

    def barrier(self):
        for qe, pool in self.dmapool.items():
            for s in pool:
                self.wait(qe, s, s.n)
        for e, s in (("T", self.TS), ("S", self.SS), ("V", self.VS), ("G", self.GS)):
            self.wait(e, s, s.n)
        toks = {}
        for e in ENGINES:
            s = self.BAR[e]
            s.n += 1
            toks[e] = s.n
            self.q[e].append(lambda eng, sh=s.h: eng.nop(nofuse=True).then_inc(sh, 1))
        for e in ENGINES:
            for o in ENGINES:
                if o != e:
                    self.wait(e, self.BAR[o], toks[o])

    def sem(self, name):
        return Sem(self.st.enter_context(self.nc.semaphore(name)), name)

    def build(self):
        nc = self.nc
        stage = self.stage

        def din(name, shape, dt=F32):
            return nc.dram_tensor(name, shape, dt, kind="ExternalInput").ap()

        self.x_fm = din("x_fm", [D, TOK])
        self.smallp = din("smallp", [128, SM.total])
        self.constb = din("constb", [128, 256], BF16)
        self.w_qkvT = din("w_qkvT", [D, 3 * D])
        self.w_outT = din("w_outT", [D, D])
        self.sh_w1 = din("sh_w1", [D, FF_I])
        self.sh_w2 = din("sh_w2", [FF_I, D])
        self.ew1 = din("ew1", [L0_E, D, FF_I])
        self.ew2 = din("ew2", [L0_E, FF_I, D])
        self.w_qT = din("w_qT", [D, D])
        self.w_kT = din("w_kT", [D, D])
        self.w_vT = din("w_vT", [D, D])
        self.w_oT = din("w_oT", [D, D])
        self.exp_Wup = din("exp_Wup", [L1_E, D, L1_I])
        self.exp_Wg = din("exp_Wg", [L1_E, L1_I, L1_I])
        self.exp_Wdown = din("exp_Wdown", [L1_E, L1_I, D])
        self.share_Wup = din("share_Wup", [D, 2 * L1_I])
        self.share_Wg = din("share_Wg", [L1_I, L1_I])
        self.share_Wdown = din("share_Wdown", [L1_I, D])

        self.out_fm = nc.dram_tensor("out_fm", [D, TOK], F32, kind="ExternalOutput").ap()
        self.xres = nc.dram_tensor("xres", [128, NCH * TOK], F32).ap()
        self.s_sp = nc.dram_tensor("s_sp", [128, NCH * TOK], F32).ap()
        self.f_sp = nc.dram_tensor("f_sp", [128, NCH * TOK], F32).ap()
        self.KV_K0 = 0
        self.KV_V0 = NCH * TOK
        self.KV_ST = self.KV_V0 + (TOK // 128) * HEADS * 65
        self.KV_C = self.KV_ST + 1
        self.mr_sp = nc.dram_tensor("mr_sp", [1, 2 * TOK], F32).ap()
        self.kv_loc = nc.dram_tensor("kv_loc", [128, self.KV_C], BF16).ap()
        self.kv_in_h = nc.dram_tensor("kv_in", [128, self.KV_C], BF16)
        self.kv_out_h = nc.dram_tensor("kv_out", [4 * 128, self.KV_C], BF16)
        self.kv_in = self.kv_in_h.ap()
        self.kv_out = self.kv_out_h.ap()
        self.dbg = None
        if stage is not None:
            self.dbg = nc.dram_tensor("dbg", [128, stage[1]], F32, kind="ExternalOutput").ap()

        st = self.st
        with st:
            self.BAR = {e: self.sem("bar" + e) for e in ENGINES}
            self.CC = self.sem("cc")
            self.TS = self.sem("t")
            self.SS = self.sem("s")
            self.VS = self.sem("v")
            self.GS = self.sem("g")
            self.dmapool = {
                "Y": [self.sem(f"dy{i}") for i in range(10)],
                "G": [self.sem(f"dg{i}") for i in range(10)],
            }
            self.dmaidx = {"Y": 0, "G": 0}

            sb = lambda n, s, d=F32: st.enter_context(nc.sbuf_tensor(n, s, d))
            self.small = sb("small", [128, SM.persist])
            self.cidb = sb("cidb", [128, 128], BF16)
            self.cones = sb("cones", [128, 128], BF16)
            self.onesf = sb("onesf", [128, 128], F32)
            self.identf = sb("identf", [128, 128], F32)
            self.iox = st.enter_context(nc.sbuf_tensor("iox", [128, 512], I32))
            self.A_t = sb("A", [128, 49152], BF16)
            self.FA_t = sb("FA", [128, 16384], F32)
            self.W_t = sb("W", [128, 8192], BF16)
            self.SC_t = sb("SC", [128, 5120], F32)
            self.PS_t = st.enter_context(nc.psum_tensor("PS", [128, 3584], F32))
            self.PSB_t = st.enter_context(nc.psum_tensor("PSB", [128, 1024], BF16))
            self.A = Arena(self.A_t, 49152, "A")
            self.FA = Arena(self.FA_t, 16384, "FA")
            self.W = Arena(self.W_t, 8192, "W")
            self.SC = Arena(self.SC_t, 5120, "SC")

            self.emit_all()
            with nc.Block() as blk:
                for key, dec in (("T", blk.tensor), ("S", blk.scalar),
                                 ("V", blk.vector), ("G", blk.gpsimd),
                                 ("Y", blk.sync)):
                    def make(key):
                        def body(eng):
                            for fn in self.q[key]:
                                fn(eng)
                        return body
                    dec(make(key))
        return nc

    def bank(self, i, n=512, p0=0, p1=128):
        assert 0 <= i < 7
        return self.PS_t[p0:p1, i * 512: i * 512 + n]

    def smc(self, off, n=1, p0=0, p1=128):
        return self.small[p0:p1, off: off + n]

    def emit_all(self):
        em = self
        t1 = em.dma("Y", lambda e: e.dma_start(out=em.small[:], in_=em.smallp[:, 0:SM.persist]))
        t2 = em.dma("Y", lambda e: e.dma_start(out=em.cidb[:], in_=em.constb[:, 0:128]))
        t3 = em.dma("Y", lambda e: e.dma_start(out=em.cones[:], in_=em.constb[:, 128:256]))
        vt = em.do("V", lambda e: e.memset(em.onesf[:], 1.0), em.VS, 1)
        em.dwait("V", t2)
        vt2 = em.do("V", lambda e: e.tensor_copy(em.identf[:], em.cidb[:]), em.VS, 1)
        for e in ENGINES:
            for t in (t1, t2, t3):
                em.dwait(e, t)
            em.wait(e, em.VS, vt2)

        self.MBC, self.RBC, self.SCT = 0, 2048, 4096

        self.XN = self.A.alloc(NCH * TOK)
        self.QA = self.A.alloc(NCH * TOK)
        self.LAM = self.SCT
        self.B0C = self.SCT + 256
        self.B1C = self.SCT + 257
        self.QM2 = self.SCT + 258
        self.KM2 = self.SCT + 259
        self.TMP1 = self.SCT + 260
        self.KALL = self.SCT + 264

        self.xn_tok = self.emit_ln(lambda c: self.x_fm[c * 128:(c + 1) * 128, :], EPS_LN)
        if self.stage and self.stage[0] == "ln0":
            self.dump_stage([(self.A_t, self.XN, NCH * TOK, True),
                             (self.SC_t, 0, 2 * TOK, False)])
            return
        self.barrier()
        self.emit_qkv(0)
        self.barrier()
        if self.stage and self.stage[0] == "qkv0":
            self.dump_stage([(self.A_t, self.QA, NCH * TOK, True)])
            return
        self.emit_attn(0)
        self.barrier()
        if self.stage and self.stage[0] == "s0":
            self.dump_from_dram(self.s_sp)
            return
        self.emit_ffn(0)
        if getattr(self, "_early", False):
            return
        self.barrier()
        if self.stage and self.stage[0] == "f0":
            self.dump_from_dram(self.f_sp)
            return
        self.emit_combine0()
        self.barrier()
        if self.stage and self.stage[0] == "x1":
            self.dump_from_dram(self.xres)
            return
        self.xn_tok = self.emit_ln(lambda c: self.xres[:, c * TOK:(c + 1) * TOK], EPS_LN)
        self.barrier()
        self.emit_qkv(1)
        self.barrier()
        self.emit_ffn(1)
        if getattr(self, "_early", False):
            return
        self.barrier()
        if self.stage and self.stage[0] == "f1":
            self.dump_from_dram(self.f_sp)
            return
        self.emit_attn(1)
        self.barrier()
        if self.stage and self.stage[0] == "s1":
            self.dump_from_dram(self.s_sp)
            return

    def dump_from_dram(self, src):
        em = self
        self.barrier()
        for c in range(NCH):
            for tt in range(NTT):
                t = em.dma("Y", lambda e, c=c, tt=tt: e.dma_start(
                    out=self.SC_t[:, 0:512],
                    in_=src[:, c * TOK + tt * 512: c * TOK + (tt + 1) * 512]))
                em.dwait("Y", t)
                t2 = em.dma("Y", lambda e, c=c, tt=tt: e.dma_start(
                    out=self.dbg[:, c * TOK + tt * 512: c * TOK + (tt + 1) * 512],
                    in_=self.SC_t[:, 0:512]))
                em.dwait("Y", t2)
        self.barrier()

    def emit_proj(self, MC, KC, wsrc_fn, rhs_fn, evac_fn, tts=None, name="", tdeps=()):
        em = self
        W_t = self.W_t
        if tts is None:
            tts = list(range(NTT))
        wm = self.W.mark()
        R = min(5, (self.W.cols - self.W.off) // (KC * 128))
        assert R >= 2, f"W ring too small for KC={KC}"
        slots = [self.W.alloc(KC * 128) for _ in range(R)]
        slot_rel = [self.TS.n] * R
        wtok = {}

        def prefetch(m):
            s = m % R
            if slot_rel[s] is not None:
                em.wait("G", em.TS, slot_rel[s])
            wtok[m] = em.dma("G", lambda e, m=m, s=s: e.dma_start(
                out=W_t[:, slots[s]: slots[s] + KC * 128].rearrange(
                    "p (kc m) -> p kc m", m=128),
                in_=wsrc_fn(m), max_dma_last_dim=4096))

        for m in range(min(MC, R - 1)):
            prefetch(m)
        for sem_, val_ in tdeps:
            self.wait("T", sem_, val_)
        NTg = len(tts)
        for m in range(MC):
            s = m % R
            if m + R - 1 < MC:
                prefetch(m + R - 1)
            em.dwait("T", wtok[m])
            bks = []
            for ti in range(NTg):
                b = self.pb_rot
                self.pb_rot = (self.pb_rot + 1) % 6
                bks.append(b)
                if self.bank_rel.get(b) is not None:
                    em.wait("T", self.bank_rel[b][0], self.bank_rel[b][1])
            toks = [None] * NTg
            for k in range(KC):
                for ti, tt in enumerate(tts):
                    toks[ti] = em.do("T", lambda e, tt=tt, k=k, s=s, b=bks[ti]: e.matmul(
                        self.bank(b),
                        W_t[:, slots[s] + k * 128: slots[s] + (k + 1) * 128],
                        rhs_fn(k, tt), start=(k == 0), stop=(k == KC - 1)),
                        em.TS if k == KC - 1 else None, 1)
            for ti, tt in enumerate(tts):
                self.bank_rel[bks[ti]] = evac_fn(m, tt, bks[ti], toks[ti])
            slot_rel[s] = toks[-1]
        self.W.release(wm)

    def emit_qkv(self, layer):
        em = self
        A_t, FA_t, SC_t, W_t = self.A_t, self.FA_t, self.SC_t, self.W_t
        self.pb_rot = 0
        self.bank_rel = {}
        amark = self.A.mark()
        famark = self.FA.mark()
        ktile = [self.A.alloc(TOK) for _ in range(2)]
        vtile = [self.A.alloc(TOK) for _ in range(2)]
        vexp = [self.A.alloc(16 * 130) for _ in range(2)]
        stb = [self.A.alloc(1) for _ in range(2)]
        sqr = [self.FA.alloc(TOK) for _ in range(2)]
        kvdst = self.kv_loc if layer == 0 else self.kv_in
        mz0 = em.do("V", lambda e: e.memset(A_t[:, vexp[0]: vexp[0] + 16 * 130], 1.0), em.VS, 1)
        mz1 = em.do("V", lambda e: e.memset(A_t[:, vexp[1]: vexp[1] + 16 * 130], 1.0), em.VS, 1)
        zq = em.do("V", lambda e: e.memset(SC_t[:, self.QM2: self.QM2 + 2], 0.0), em.VS, 1)
        ring_rel = {}

        def rhs_xn(k, tt):
            return A_t[:, self.XN + k * TOK + tt * 512: self.XN + k * TOK + (tt + 1) * 512]

        def stats_emit(tile_a0, dstcol, m, evtoks):
            r = m % 2
            if ("sq", r) in ring_rel:
                em.wait("S", em.TS, ring_rel[("sq", r)])
            for ev in evtoks:
                em.wait("S", ev[0], ev[1])
            sq = em.do("S", lambda e, tile_a0=tile_a0, r=r: e.activation(
                out=FA_t[:, sqr[r]: sqr[r] + TOK],
                in_=A_t[:, tile_a0: tile_a0 + TOK], func=AF.Square), em.SS, 1)
            em.wait("T", em.SS, sq)
            for tt in range(NTT):
                tk = em.do("T", lambda e, tt=tt: e.matmul(
                    self.bank(6, 512, 0, 2),
                    self.smc(SM.hsel, 2),
                    FA_t[:, sqr[r] + tt * 512: sqr[r] + (tt + 1) * 512],
                    start=True, stop=True), em.TS, 1)
                em.wait("V", em.TS, tk)
                v1 = em.do("V", lambda e: e.tensor_reduce(
                    SC_t[0:2, self.TMP1: self.TMP1 + 1],
                    self.bank(6, 512, 0, 2), axis=AX.X, op=ALU.max), em.VS, 1)
                em.wait("V", em.VS, v1)
                v2 = em.do("V", lambda e, dstcol=dstcol: e.tensor_tensor(
                    out=SC_t[0:2, dstcol: dstcol + 1],
                    in0=SC_t[0:2, self.TMP1: self.TMP1 + 1],
                    in1=SC_t[0:2, dstcol: dstcol + 1], op=ALU.max), em.VS, 1)
                em.wait("T", em.VS, v2)
            ring_rel[("sq", r)] = em.TS.n
            return (em.SS, sq)

        if layer == 0:
            wsrc = lambda m: self.w_qkvT[:, m * 128:(m + 1) * 128].rearrange(
                "(kc p) m -> p kc m", p=128)
            MC = 24

            def evac(m, tt, b, tok):
                if m < 8:
                    em.wait("S", em.TS, tok)
                    v = em.do("S", lambda e, m=m, tt=tt, b=b: e.activation(
                        out=A_t[:, self.QA + m * TOK + tt * 512: self.QA + m * TOK + (tt + 1) * 512],
                        in_=self.bank(b), func=AF.Identity), em.SS, 1)
                    return (em.SS, v)
                kind = "k" if m < 16 else "v"
                mc = m - 8 if m < 16 else m - 16
                r = mc % 2
                ring = ktile if kind == "k" else vtile
                if tt == 0:
                    for rt in ring_rel.get((kind, r), []):
                        em.wait("S", rt[0], rt[1])
                em.wait("S", em.TS, tok)
                v = em.do("S", lambda e, ring=ring, r=r, tt=tt, b=b: e.activation(
                    out=A_t[:, ring[r] + tt * 512: ring[r] + (tt + 1) * 512],
                    in_=self.bank(b), func=AF.Identity), em.SS, 1)
                if tt == NTT - 1:
                    self.finish_tile(kind, mc, r, ring, vexp, kvdst, ring_rel, stats_emit)
                return (em.SS, v)
            em.wait("T", em.VS, self.xn_tok)
            self.emit_proj(MC, NCH, wsrc, rhs_xn, evac, name="qkv0")
            self.emit_lam()
        else:
            for which, wt, bia in (("q", self.w_qT, SM.bq), ("k", self.w_kT, SM.bk),
                                   ("v", self.w_vT, SM.bv)):
                wsrc = lambda m, wt=wt: wt[:, m * 128:(m + 1) * 128].rearrange(
                    "(kc p) m -> p kc m", p=128)

                def evac(m, tt, b, tok, which=which, bia=bia):
                    if which == "q":
                        em.wait("S", em.TS, tok)
                        v = em.do("S", lambda e, m=m, tt=tt, b=b: e.activation(
                            out=A_t[:, self.QA + m * TOK + tt * 512: self.QA + m * TOK + (tt + 1) * 512],
                            in_=self.bank(b), func=AF.Identity,
                            bias=self.smc(SM.bq + m)), em.SS, 1)
                        return (em.SS, v)
                    kind = which
                    r = m % 2
                    ring = ktile if kind == "k" else vtile
                    if tt == 0:
                        for rt in ring_rel.get((kind, r), []):
                            em.wait("S", rt[0], rt[1])
                    em.wait("S", em.TS, tok)
                    v = em.do("S", lambda e, ring=ring, r=r, tt=tt, b=b, m=m, bia=bia: e.activation(
                        out=A_t[:, ring[r] + tt * 512: ring[r] + (tt + 1) * 512],
                        in_=self.bank(b), func=AF.Identity,
                        bias=self.smc(bia + m)), em.SS, 1)
                    if tt == NTT - 1:
                        self.finish_tile(kind, m, r, ring, vexp, kvdst, ring_rel, stats_emit)
                    return (em.SS, v)
                em.wait("T", em.VS, self.xn_tok)
                self.emit_proj(8, NCH, wsrc, rhs_xn, evac, name="qkv1" + which)
        self.finalize_stats(layer, stb)
        if layer == 1:
            for qe, pool in self.dmapool.items():
                for s in pool:
                    em.wait(qe, s, s.n)
                    em.wait("G", s, s.n)
            em.wait("G", em.VS, em.VS.n)
            em.wait("G", em.SS, em.SS.n)
            em.wait("G", em.TS, em.TS.n)
            cc = em.do("G", lambda e: e.collective_compute(
                "AllGather", ALU.bypass, ins=[self.kv_in],
                outs=[self.kv_out],
                replica_groups=[[0, 1, 2, 3], [4, 5, 6, 7]]), em.CC, 1)
            self.cc_tok = cc
        self.A.release(amark)
        self.FA.release(famark)

    def finish_tile(self, kind, mc, r, ring, vexp, kvdst, ring_rel, stats_emit):
        em = self
        A_t = self.A_t
        evtok = (em.SS, em.SS.n)
        if kind == "k":
            sqtok = stats_emit(ring[r], self.KM2, mc, [evtok])
            em.wait("Y", evtok[0], evtok[1])
            t = em.dma("Y", lambda e, r=r, mc=mc, ring=ring: e.dma_start(
                out=kvdst[:, self.KV_K0 + mc * TOK: self.KV_K0 + (mc + 1) * TOK],
                in_=A_t[:, ring[r]: ring[r] + TOK]))
            ring_rel[(kind, r)] = [t, sqtok]
        else:
            if ("vx", r) in ring_rel:
                em.wait("V", *ring_rel[("vx", r)])
            vt = getattr(self, "psb_rel", None)
            tr = None
            for tc in range(16):
                em.wait("T", evtok[0], evtok[1])
                if vt is not None:
                    em.wait("T", em.VS, vt)
                tr = em.do("T", lambda e, r=r, tc=tc: e.transpose(
                    self.PSB_t[:, 0:128],
                    A_t[:, ring[r] + tc * 128: ring[r] + (tc + 1) * 128],
                    self.cidb[:]), em.TS, 1)
                em.wait("V", em.TS, tr)
                em.do("V", lambda e, r=r, tc=tc: e.tensor_copy(
                    A_t[:, vexp[r] + tc * 130: vexp[r] + tc * 130 + 64],
                    self.PSB_t[:, 0:64]), em.VS, 1)
                vt = em.do("V", lambda e, r=r, tc=tc: e.tensor_copy(
                    A_t[:, vexp[r] + tc * 130 + 65: vexp[r] + tc * 130 + 129],
                    self.PSB_t[:, 64:128]), em.VS, 1)
            self.psb_rel = vt
            em.wait("Y", em.VS, vt)
            t = em.dma("Y", lambda e, r=r, mc=mc, kvdst=kvdst: e.dma_start(
                out=kvdst[:, self.KV_V0: self.KV_V0 + 16 * 1040].rearrange(
                    "p (tc cc) -> p tc cc", cc=1040)[:, :, 2 * mc * 65: 2 * mc * 65 + 130],
                in_=A_t[:, vexp[r]: vexp[r] + 16 * 130].rearrange(
                    "p (tc cc) -> p tc cc", cc=130)))
            ring_rel[(kind, r)] = [(em.TS, tr)]
            ring_rel[("vx", r)] = t

    def finalize_stats(self, layer, stb):
        em = self
        SC_t = self.SC_t
        for col in (self.QM2, self.KM2):
            t = em.dma("Y", lambda e, col=col: e.dma_start(
                out=SC_t[0:1, self.TMP1 + 1: self.TMP1 + 2],
                in_=SC_t[1:2, col: col + 1],
                allow_slow_non_contiguous=True))
            em.dwait("V", t)
            em.wait("V", em.VS, em.VS.n)
            v = em.do("V", lambda e, col=col: e.tensor_tensor(
                out=SC_t[0:1, col: col + 1], in0=SC_t[0:1, self.TMP1 + 1: self.TMP1 + 2],
                in1=SC_t[0:1, col: col + 1], op=ALU.max), em.VS, 1)
            em.wait("Y", em.VS, v)
        vals = {}
        for nm, col in (("q", self.QM2), ("k", self.KM2)):
            em.wait("T", em.VS, em.VS.n)
            tk = em.do("T", lambda e, col=col: e.matmul(
                self.bank(6, 1), self.onesf[0:1, 0:128],
                SC_t[0:1, col: col + 1], start=True, stop=True), em.TS, 1)
            em.wait("V", em.TS, tk)
            dst = self.TMP1 + 2 if nm == "q" else self.TMP1 + 3
            v = em.do("V", lambda e, dst=dst: e.tensor_copy(
                SC_t[:, dst: dst + 1], self.bank(6, 1)), em.VS, 1)
            em.wait("T", em.VS, v)
            vals[nm] = dst
            vals[nm + "tok"] = v
        if layer == 1:
            em.wait("V", em.VS, vals["ktok"])
            vb = em.do("V", lambda e: e.tensor_copy(
                self.A_t[:, stb[0]: stb[0] + 1], SC_t[:, vals["k"]: vals["k"] + 1]), em.VS, 1)
            em.wait("Y", em.VS, vb)
            em.dma("Y", lambda e: e.dma_start(
                out=self.kv_in[:, self.KV_ST: self.KV_ST + 1],
                in_=self.A_t[:, stb[0]: stb[0] + 1],
                allow_slow_non_contiguous=True))
            self.qm_bc = vals["q"]
            return
        em.wait("V", em.VS, vals["ktok"])
        em.wait("V", em.VS, vals["qtok"])
        v1 = em.do("V", lambda e: e.tensor_tensor(
            out=SC_t[:, self.B0C: self.B0C + 1], in0=SC_t[:, vals["q"]: vals["q"] + 1],
            in1=SC_t[:, vals["k"]: vals["k"] + 1], op=ALU.mult), em.VS, 1)
        em.wait("S", em.VS, v1)
        s1 = em.do("S", lambda e: e.activation(
            out=SC_t[:, self.B0C: self.B0C + 1], in_=SC_t[:, self.B0C: self.B0C + 1],
            func=AF.Sqrt), em.SS, 1)
        em.wait("V", em.SS, s1)
        em.do("V", lambda e: e.tensor_scalar(
            out=SC_t[:, self.B0C: self.B0C + 1], in0=SC_t[:, self.B0C: self.B0C + 1],
            scalar1=-0.125, scalar2=None, op0=ALU.mult), em.VS, 1)

    def emit_b1(self):
        em = self
        SC_t = self.SC_t
        famark = self.FA.mark()
        stf = self.FA.alloc(4)
        for e in ("Y", "G"):
            em.wait(e, em.CC, self.cc_tok)
        t = em.dma("G", lambda e: e.dma_start(
            out=self.FA_t[:, stf: stf + 4].rearrange("p (r c) -> p r c", c=1),
            in_=self.kv_out.rearrange("(r p) c -> p r c", p=128)[:, :, self.KV_ST: self.KV_ST + 1],
            allow_slow_non_contiguous=True))
        em.dwait("V", t)
        v1 = em.do("V", lambda e: e.tensor_reduce(
            SC_t[:, self.KALL: self.KALL + 1], self.FA_t[:, stf: stf + 4],
            axis=AX.X, op=ALU.max), em.VS, 1)
        em.wait("V", em.VS, v1)
        v2 = em.do("V", lambda e: e.tensor_tensor(
            out=SC_t[:, self.B1C: self.B1C + 1], in0=SC_t[:, self.KALL: self.KALL + 1],
            in1=SC_t[:, self.qm_bc: self.qm_bc + 1], op=ALU.mult), em.VS, 1)
        em.wait("S", em.VS, v2)
        s1 = em.do("S", lambda e: e.activation(
            out=SC_t[:, self.B1C: self.B1C + 1], in_=SC_t[:, self.B1C: self.B1C + 1],
            func=AF.Sqrt), em.SS, 1)
        em.wait("V", em.SS, s1)
        em.do("V", lambda e: e.tensor_scalar(
            out=SC_t[:, self.B1C: self.B1C + 1], in0=SC_t[:, self.B1C: self.B1C + 1],
            scalar1=-0.125, scalar2=-1.0, op0=ALU.mult, op1=ALU.add), em.VS, 1)
        self.FA.release(famark)

    def emit_lam(self):
        em = self
        FA_t, SC_t = self.FA_t, self.SC_t
        famark = self.FA.mark()
        lrow = self.FA.alloc(TOK)
        wrow = self.FA.alloc(64)
        h1 = self.FA.alloc(TOK)
        lfm = self.FA.alloc(TOK)
        t1 = em.dma("Y", lambda e: e.dma_start(
            out=FA_t[0:1, lrow: lrow + TOK], in_=self.smallp[0:1, SM.logpos_row: SM.logpos_row + TOK]))
        t2 = em.dma("Y", lambda e: e.dma_start(
            out=FA_t[0:1, wrow: wrow + 64], in_=self.smallp[0:1, SM.wl1row: SM.wl1row + 64]))
        for tt in range(NTT):
            em.dwait("T", t1)
            em.dwait("T", t2)
            em.wait("T", em.SS, em.SS.n)
            mk = em.do("T", lambda e, tt=tt: e.matmul(
                self.bank(5, 512, 0, 64), FA_t[0:1, wrow: wrow + 64],
                FA_t[0:1, lrow + tt * 512: lrow + (tt + 1) * 512],
                start=True, stop=True), em.TS, 1)
            em.wait("S", em.TS, mk)
            s1 = em.do("S", lambda e, tt=tt: e.activation(
                out=FA_t[0:64, h1 + tt * 512: h1 + (tt + 1) * 512],
                in_=self.bank(5, 512, 0, 64), func=AF.Relu,
                bias=self.smc(SM.bl1, 1, 0, 64)), em.SS, 1)
            em.wait("T", em.SS, s1)
            mk2 = em.do("T", lambda e, tt=tt: e.matmul(
                self.bank(5, 512, 0, 16), self.smc(SM.wl2t, 16, 0, 64),
                FA_t[0:64, h1 + tt * 512: h1 + (tt + 1) * 512],
                start=True, stop=True), em.TS, 1)
            em.wait("S", em.TS, mk2)
            s2 = em.do("S", lambda e, tt=tt: e.activation(
                out=FA_t[0:16, lfm + tt * 512: lfm + (tt + 1) * 512],
                in_=self.bank(5, 512, 0, 16), func=AF.Sigmoid,
                bias=self.smc(SM.bl2, 1, 0, 16)), em.SS, 1)
            em.wait("T", em.SS, s2)
        vtok = None
        for tc in range(16):
            if vtok is not None:
                em.wait("T", em.VS, vtok)
            tr = em.do("T", lambda e, tc=tc: e.matmul(
                self.bank(5, 16), FA_t[0:16, lfm + tc * 128: lfm + (tc + 1) * 128],
                self.identf[0:16, 0:16], is_transpose=True,
                start=True, stop=True), em.TS, 1)
            em.wait("V", em.TS, tr)
            vtok = em.do("V", lambda e, tc=tc: e.tensor_scalar(
                out=SC_t[:, self.LAM + tc * 16: self.LAM + (tc + 1) * 16],
                in0=self.bank(5, 16), scalar1=0.125, scalar2=None,
                op0=ALU.mult), em.VS, 1)
        em.wait("T", em.VS, vtok)
        self.FA.release(famark)

    def emit_ln(self, xsrc_fn, eps):
        em = self
        FA_t, A_t, SC_t, PS_t = self.FA_t, self.A_t, self.SC_t, self.PS_t
        famark = self.FA.mark()
        X0 = self.FA.alloc(NCH * TOK)
        MBC, RBC, SCT = self.MBC, self.RBC, self.SCT
        XN = self.XN
        ld = []
        for c in range(NCH):
            ld.append(em.dma("Y", lambda e, c=c: e.dma_start(
                out=FA_t[:, X0 + c * TOK: X0 + (c + 1) * TOK], in_=xsrc_fn(c))))
        t_sum = None
        for c in range(NCH):
            em.dwait("T", ld[c])
            for tt in range(NTT):
                t_sum = em.do("T", lambda e, c=c, tt=tt: e.matmul(
                    PS_t[0:1, tt * 512:(tt + 1) * 512], em.onesf[:, 0:1],
                    FA_t[:, X0 + c * TOK + tt * 512: X0 + c * TOK + (tt + 1) * 512],
                    start=(c == 0), stop=(c == NCH - 1)),
                    em.TS if c == NCH - 1 else None, 1)
        em.wait("V", em.TS, t_sum)
        mrow = SC_t[0:1, MBC: MBC + TOK]
        v_m = em.do("V", lambda e: e.tensor_scalar(
            out=mrow, in0=PS_t[0:1, 0:TOK], scalar1=1.0 / D, scalar2=None,
            op0=ALU.mult), em.VS, 1)
        mb_tok = self.bcast_inplace(MBC, v_m)
        sq_tok = None
        d_cols = SC_t[:, SCT: SCT + 1024]
        for c in range(NCH):
            for half in range(2):
                sl = slice(X0 + c * TOK + half * 1024, X0 + c * TOK + (half + 1) * 1024)
                asl = slice(XN + c * TOK + half * 1024, XN + c * TOK + (half + 1) * 1024)
                em.wait("V", em.VS, mb_tok)
                if sq_tok is not None:
                    em.wait("V", em.SS, sq_tok)
                v_d = em.do("V", lambda e, sl=sl: e.tensor_tensor(
                    out=d_cols, in0=FA_t[:, sl],
                    in1=SC_t[:, MBC + (sl.start - X0) % TOK: MBC + (sl.start - X0) % TOK + 1024],
                    op=ALU.subtract), em.VS, 1)
                v_b = em.do("V", lambda e, sl=sl, asl=asl: e.tensor_tensor(
                    out=A_t[:, asl], in0=FA_t[:, sl],
                    in1=SC_t[:, MBC + (sl.start - X0) % TOK: MBC + (sl.start - X0) % TOK + 1024],
                    op=ALU.subtract), em.VS, 1)
                em.wait("S", em.VS, v_b)
                sq_tok = em.do("S", lambda e, sl=sl: e.activation(
                    out=FA_t[:, sl], in_=d_cols, func=AF.Square), em.SS, 1)
                em.wait("T", em.SS, sq_tok)
                for tt in range(2):
                    t_sq = em.do("T", lambda e, sl=sl, tt=tt, c=c, half=half: e.matmul(
                        PS_t[0:1, half * 1024 + tt * 512: half * 1024 + (tt + 1) * 512],
                        em.onesf[:, 0:1],
                        FA_t[:, sl.start + tt * 512: sl.start + (tt + 1) * 512],
                        start=(c == 0), stop=(c == NCH - 1)),
                        em.TS if c == NCH - 1 else None, 1)
        em.wait("V", em.TS, t_sq)
        rrow = SC_t[0:1, RBC: RBC + TOK]
        v_v = em.do("V", lambda e: e.tensor_scalar(
            out=rrow, in0=PS_t[0:1, 0:TOK], scalar1=1.0 / D,
            scalar2=eps, op0=ALU.mult, op1=ALU.add), em.VS, 1)
        em.wait("S", em.VS, v_v)
        s_r = em.do("S", lambda e: e.activation(out=rrow, in_=rrow, func=AF.Sqrt), em.SS, 1)
        em.wait("V", em.SS, s_r)
        v_r = em.do("V", lambda e: e.reciprocal(rrow, rrow), em.VS, 1)
        rb_tok = self.bcast_inplace(RBC, v_r)
        xn_tok = None
        for c in range(NCH):
            em.wait("V", em.VS, rb_tok)
            xn_tok = em.do("V", lambda e, c=c: e.tensor_tensor(
                out=A_t[:, self.XN + c * TOK: self.XN + (c + 1) * TOK],
                in0=A_t[:, self.XN + c * TOK: self.XN + (c + 1) * TOK],
                in1=SC_t[:, RBC: RBC + TOK], op=ALU.mult), em.VS, 1)
        self.FA.release(famark)
        return xn_tok

    def bcast_inplace(self, col0, row_tok):
        em = self
        SC_t, PS_t = self.SC_t, self.PS_t
        for b in (4, 5):
            br = getattr(self, "bank_rel", {})
            if br.get(b) is not None:
                em.wait("T", br[b][0], br[b][1])
        em.wait("T", em.VS, row_tok)
        toks = []
        for tt in range(NTT):
            t1 = em.do("T", lambda e, tt=tt: e.matmul(
                self.bank(4 + (tt % 2)), em.onesf[0:1, 0:128],
                SC_t[0:1, col0 + tt * 512: col0 + (tt + 1) * 512],
                start=True, stop=True), em.TS, 1)
            em.wait("V", em.TS, t1)
            v1 = em.do("V", lambda e, tt=tt: e.tensor_copy(
                SC_t[:, col0 + tt * 512: col0 + (tt + 1) * 512],
                self.bank(4 + (tt % 2))), em.VS, 1)
            em.wait("T", em.VS, v1)
            toks.append(v1)
        return toks[-1]

    def emit_attn(self, layer):
        em = self
        A_t, FA_t, SC_t, W_t = self.A_t, self.FA_t, self.SC_t, self.W_t
        self.pb_rot = 0
        self.bank_rel = {}
        kvsrc = self.kv_loc if layer == 0 else None
        if layer == 0:
            KCOLS, VCOLS, NKC = 1024, 8 * 130, 8
            amark = self.A.mark()
            KS = [self.A.alloc(KCOLS) for _ in range(3)]
            VSL = [self.A.alloc(VCOLS) for _ in range(3)]
            AT = [self.A.alloc(1024) for _ in range(3)]
            HB = self.A.alloc(8 * 512)
            famark = self.FA.mark()
            dnm_c = self.FA.alloc(512)
            sring = [self.FA.alloc(512) for _ in range(2)]
            bias_col = SC_t[:, self.B0C: self.B0C + 1]
        else:
            KCOLS, VCOLS, NKC = 4096, 32 * 130, 64
            KS = [0, 4096, 8192]
            HB = 12288
            VSL = [32768, 36928, 41088]
            AT = [45248, 46272, 47296]
            dnm_c = 0
            sring = [512, 1024]
            gcol = 1536
            x1r = [2048, 2560]
            f1r = [3072, 3584]
            bias_col = SC_t[:, self.B1C: self.B1C + 1]
            self.emit_b1()
            em.wait("S", em.VS, em.VS.n)
            em.wait("V", em.VS, em.VS.n)
        nhalf = NKC * 128 // (KCOLS)
        cph = NKC // nhalf
        wmark = self.W.mark()
        WR = [self.W.alloc(4096) for _ in range(2)]
        wo_src = self.w_outT if layer == 0 else self.w_oT
        wotok = {}
        wo_rel = {}

        def wo_prefetch(tq, mh):
            i = tq * 2 + mh
            s = i % 2
            if s in wo_rel:
                em.wait("G", em.TS, wo_rel[s])
            wotok[i] = em.dma("G", lambda e, mh=mh, s=s: e.dma_start(
                out=W_t[:, WR[s]: WR[s] + 4096].rearrange("p (kc m) -> p kc m", m=512),
                in_=wo_src[:, mh * 512:(mh + 1) * 512].rearrange(
                    "(kc p) m -> p kc m", p=128), max_dma_last_dim=4096))

        slab_tok = {}
        slab_rel = {}

        def load_slab(tq, idx):
            hp, half = divmod(idx, nhalf)
            sl = idx % 3
            if sl in slab_rel:
                em.wait("Y", em.TS, slab_rel[sl])
            cq = tq // 2
            if layer == 0:
                ktok = em.dma("Y", lambda e, hp=hp, sl=sl, cq=cq: e.dma_start(
                    out=A_t[:, KS[sl]: KS[sl] + KCOLS],
                    in_=kvsrc[:, hp * TOK + cq * 1024: hp * TOK + cq * 1024 + 1024]))
                vtok = em.dma("Y", lambda e, hp=hp, sl=sl, cq=cq: e.dma_start(
                    out=A_t[:, VSL[sl]: VSL[sl] + VCOLS].rearrange(
                        "p (tc cc) -> p tc cc", cc=130),
                    in_=kvsrc[:, self.KV_V0 + cq * 8 * 1040:
                              self.KV_V0 + (cq * 8 + 8) * 1040].rearrange(
                        "p (tc cc) -> p tc cc", cc=1040)[:, :, 2 * hp * 65: 2 * hp * 65 + 130]))
            else:
                kvo = self.kv_out.rearrange("(r p) c -> p r c", p=128)
                r0 = half * 2
                ktok = em.dma("Y", lambda e, hp=hp, sl=sl, r0=r0, kvo=kvo: e.dma_start(
                    out=A_t[:, KS[sl]: KS[sl] + KCOLS].rearrange(
                        "p (r t) -> p r t", t=TOK),
                    in_=kvo[:, r0: r0 + 2, hp * TOK: (hp + 1) * TOK]))
                vts = []
                for ri in range(2):
                    vts.append(em.dma("Y", lambda e, hp=hp, sl=sl, r0=r0, ri=ri, kvo=kvo: e.dma_start(
                        out=A_t[:, VSL[sl] + ri * 16 * 130: VSL[sl] + (ri + 1) * 16 * 130].rearrange(
                            "p (tc cc) -> p tc cc", cc=130),
                        in_=kvo[:, r0 + ri,
                                self.KV_V0: self.KV_V0 + 16 * 1040].rearrange(
                            "p (tc cc) -> p tc cc", cc=1040)[:, :, 2 * hp * 65: 2 * hp * 65 + 130])))
                vtok = vts
            slab_tok[idx] = (ktok, vtok)

        def delta_build(tq):
            rlist = [c for c in range(64) if c * 128 + 127 - tq * 512 > 4097]
            pos = {}
            gtok = None
            vtok = None
            for i, c in enumerate(rlist):
                pos[c] = i * 512
                base = tq * 512 - c * 128 + 8191 - 2047
                if gtok is not None:
                    em.wait("G", em.VS, vtok)
                gtok = em.do("G", lambda e, base=base: e.iota(
                    self.iox[:], base=base, pattern=[[1, 512]],
                    channel_multiplier=-1), em.GS, 1)
                em.wait("V", em.GS, gtok)
                i0 = i * 512
                v0 = em.do("V", lambda e, i0=i0: e.tensor_copy(
                    FA_t[:, i0: i0 + 512], self.iox[:]), em.VS, 1)
                em.wait("V", em.VS, v0)
                v1 = em.do("V", lambda e, i0=i0: e.tensor_scalar(
                    out=FA_t[:, i0: i0 + 512], in0=FA_t[:, i0: i0 + 512],
                    scalar1=self.smc(SM.q0col), scalar2=None, op0=ALU.add), em.VS, 1)
                em.wait("V", em.VS, v1)
                v1b = em.do("V", lambda e, i0=i0: e.tensor_scalar(
                    out=SC_t[:, 4608: 4608 + 512], in0=FA_t[:, i0: i0 + 512],
                    scalar1=-1.0, scalar2=None, op0=ALU.mult), em.VS, 1)
                em.wait("V", em.VS, v1b)
                v2 = em.do("V", lambda e, i0=i0: e.tensor_tensor(
                    out=FA_t[:, i0: i0 + 512], in0=FA_t[:, i0: i0 + 512],
                    in1=SC_t[:, 4608: 4608 + 512], op=ALU.max), em.VS, 1)
                em.wait("V", em.VS, v2)
                vtok = em.do("V", lambda e, i0=i0: e.tensor_scalar(
                    out=FA_t[:, i0: i0 + 512], in0=FA_t[:, i0: i0 + 512],
                    scalar1=-1.0 / 256.0, scalar2=2047.0 / 256.0,
                    op0=ALU.mult, op1=ALU.add), em.VS, 1)
                em.wait("V", em.VS, vtok)
                vtok = em.do("V", lambda e, i0=i0: e.tensor_scalar(
                    out=FA_t[:, i0: i0 + 512], in0=FA_t[:, i0: i0 + 512],
                    scalar1=0.0, scalar2=None, op0=ALU.max), em.VS, 1)
            return pos, vtok

        at_rel = {}
        exp_tok = {}
        av_bank_rel = None
        self.av_prev_rel = None
        self.pending_evac = None
        chunk_sl = {}
        for tq in range(4):
            wo_prefetch(tq, 0)
            wo_prefetch(tq, 1)
            if layer == 1:
                dpos, dtok = delta_build(tq)
            else:
                dpos, dtok = {}, None
            if tq == 0:
                load_slab(tq, 0)
            nidx = NHP * nhalf
            for idx in range(nidx):
                hp, half = divmod(idx, nhalf)
                if idx + 1 < nidx:
                    load_slab(tq, idx + 1)
                sl = idx % 3
                ktok, vtok_d = slab_tok[idx]
                em.dwait("T", ktok)
                for vv in (vtok_d if isinstance(vtok_d, list) else [vtok_d]):
                    em.dwait("T", vv)
                def emit_av(kc, sl_, ats):
                    ex2 = exp_tok[kc]
                    em.wait("T", ex2[0], ex2[1])
                    if kc == 0:
                        if self.av_prev_rel is not None:
                            em.wait("T", self.av_prev_rel[0], self.av_prev_rel[1])
                        for b45 in (4, 5):
                            if self.bank_rel.get(b45) is not None:
                                em.wait("T", self.bank_rel[b45][0], self.bank_rel[b45][1])
                    kcl_ = kc % cph
                    ave = em.do("T", lambda e, sl=sl_, kcl=kcl_, ats=ats, kc=kc: e.matmul(
                        self.bank(4, 512, 0, 65),
                        A_t[:, VSL[sl] + kcl * 130: VSL[sl] + kcl * 130 + 65],
                        A_t[:, AT[ats]: AT[ats] + 512],
                        start=(kc == 0), stop=(kc == NKC - 1)),
                        em.TS, 1)
                    avo = em.do("T", lambda e, sl=sl_, kcl=kcl_, ats=ats, kc=kc: e.matmul(
                        self.bank(5, 512, 0, 65),
                        A_t[:, VSL[sl] + kcl * 130 + 65: VSL[sl] + kcl * 130 + 130],
                        A_t[:, AT[ats] + 512: AT[ats] + 1024],
                        start=(kc == 0), stop=(kc == NKC - 1)),
                        em.TS, 1)
                    at_rel[ats] = (em.TS, avo)
                    slab_rel[sl_] = em.TS.n
                    self.last_av = (ave, avo)
                for kcl in range(cph):
                    kc = half * cph + kcl
                    chunk_sl[kc] = sl
                    be = (kc % 2) * 2
                    bo = be + 1
                    for b in (be, bo):
                        if self.bank_rel.get(b) is not None:
                            em.wait("T", self.bank_rel[b][0], self.bank_rel[b][1])
                    mm_e = em.do("T", lambda e, sl=sl, kcl=kcl, hp=hp, tq=tq, be=be: e.matmul(
                        self.bank(be),
                        A_t[0:64, KS[sl] + kcl * 128: KS[sl] + (kcl + 1) * 128],
                        A_t[0:64, self.QA + hp * TOK + tq * 512: self.QA + hp * TOK + (tq + 1) * 512],
                        start=True, stop=True), em.TS, 1)
                    mm_o = em.do("T", lambda e, sl=sl, kcl=kcl, hp=hp, tq=tq, bo=bo: e.matmul(
                        self.bank(bo),
                        A_t[64:128, KS[sl] + kcl * 128: KS[sl] + (kcl + 1) * 128],
                        A_t[64:128, self.QA + hp * TOK + tq * 512: self.QA + hp * TOK + (tq + 1) * 512],
                        start=True, stop=True), em.TS, 1)
                    dtk = None
                    if layer == 1 and kc in dpos:
                        em.wait("V", em.TS, mm_o)
                        em.wait("V", em.VS, dtok)
                        em.do("V", lambda e, be=be, kc=kc, dpos=dpos: e.tensor_tensor(
                            out=self.bank(be), in0=self.bank(be),
                            in1=FA_t[:, dpos[kc]: dpos[kc] + 512], op=ALU.add), em.VS, 1)
                        dtk = em.do("V", lambda e, bo=bo, kc=kc, dpos=dpos: e.tensor_tensor(
                            out=self.bank(bo), in0=self.bank(bo),
                            in1=FA_t[:, dpos[kc]: dpos[kc] + 512], op=ALU.add), em.VS, 1)
                    ats = kc % 3
                    if at_rel.get(ats) is not None:
                        em.wait("S", at_rel[ats][0], at_rel[ats][1])
                    em.wait("S", em.TS, mm_o)
                    if dtk is not None:
                        em.wait("S", em.VS, dtk)
                    if layer == 1:
                        ex = em.do("S", lambda e, be=be, ats=ats: e.activation(
                            out=A_t[:, AT[ats]: AT[ats] + 1024],
                            in_=self.PS_t[:, be * 512: (be + 2) * 512],
                            func=AF.Exp, scale=0.125, bias=bias_col), em.SS, 1)
                    else:
                        cq = tq // 2
                        em.do("S", lambda e, be=be, ats=ats, cq=cq, kcl=kcl, hp=hp: e.activation(
                            out=A_t[:, AT[ats]: AT[ats] + 512],
                            in_=self.bank(be),
                            func=AF.Exp,
                            scale=SC_t[:, self.LAM + (cq * 8 + kcl) * 16 + 2 * hp:
                                       self.LAM + (cq * 8 + kcl) * 16 + 2 * hp + 1],
                            bias=bias_col), em.SS, 1)
                        ex = em.do("S", lambda e, bo=bo, ats=ats, cq=cq, kcl=kcl, hp=hp: e.activation(
                            out=A_t[:, AT[ats] + 512: AT[ats] + 1024],
                            in_=self.bank(bo),
                            func=AF.Exp,
                            scale=SC_t[:, self.LAM + (cq * 8 + kcl) * 16 + 2 * hp + 1:
                                       self.LAM + (cq * 8 + kcl) * 16 + 2 * hp + 2],
                            bias=bias_col), em.SS, 1)
                    self.bank_rel[be] = (em.SS, ex)
                    self.bank_rel[bo] = (em.SS, ex)
                    exp_tok[kc] = (em.SS, ex)
                    if _ATTN_PARTS is not None and "av" not in _ATTN_PARTS:
                        at_rel[ats] = (em.SS, ex)
                        self.last_av = None
                        continue
                    if kc == 2 and self.pending_evac is not None:
                        av_bank_rel = self.emit_av_evac(
                            layer, self.pending_evac, HB, dnm_c)
                        self.av_prev_rel = av_bank_rel
                        self.pending_evac = None
                    if kc >= 2:
                        emit_av(kc - 2, chunk_sl[kc - 2], (kc - 2) % 3)
                ap_ = _ATTN_PARTS if _ATTN_PARTS is not None else {"av", "evac", "wo"}
                slab_rel[sl] = em.TS.n
                if half == nhalf - 1:
                    if "av" in ap_:
                        for j in range(max(0, NKC - 2), NKC):
                            emit_av(j, chunk_sl[j], j % 3)
                    slab_rel[sl] = em.TS.n
                    if "evac" in ap_ and "av" in ap_:
                        if _ATTN_PARTS is not None:
                            av_bank_rel = self.emit_av_evac(layer, hp, HB, dnm_c)
                            self.av_prev_rel = av_bank_rel
                        else:
                            self.pending_evac = hp
            if tq < 3:
                load_slab(tq + 1, 0)
            if self.pending_evac is not None:
                av_bank_rel = self.emit_av_evac(layer, self.pending_evac, HB, dnm_c)
                self.av_prev_rel = av_bank_rel
                self.pending_evac = None
            ap_ = _ATTN_PARTS if _ATTN_PARTS is not None else {"av", "evac", "wo"}
            if "wo" in ap_:
                self.emit_wo(layer, tq, HB, WR, wotok, wo_rel, sring,
                             None if layer == 0 else (gcol, x1r, f1r), dnm_c)
        self.W.release(wmark)
        if layer == 0:
            self.A.release(amark)
            self.FA.release(famark)

    def emit_av_evac(self, layer, hp, HB, dnm_c):
        em = self
        A_t, SC_t, FA_t = self.A_t, self.SC_t, self.FA_t
        dn = (FA_t if layer == 0 else SC_t)
        ave, avo = self.last_av
        rel = None
        for j, avtok in ((0, ave), (1, avo)):
            bk = 4 + j
            em.wait("V", em.TS, avtok)
            v1 = em.do("V", lambda e, bk=bk: e.tensor_copy(
                dn[64:65, dnm_c: dnm_c + 512], self.bank(bk, 512, 64, 65)), em.VS, 1)
            em.wait("V", em.VS, v1)
            v2 = em.do("V", lambda e: e.reciprocal(
                dn[64:65, dnm_c: dnm_c + 512], dn[64:65, dnm_c: dnm_c + 512]), em.VS, 1)
            em.wait("T", em.VS, v2)
            t1 = em.do("T", lambda e: e.matmul(
                self.bank(6, 512, 0, 64), self.onesf[64:65, 0:64],
                dn[64:65, dnm_c: dnm_c + 512], start=True, stop=True), em.TS, 1)
            em.wait("V", em.TS, t1)
            v3 = em.do("V", lambda e: e.tensor_copy(
                dn[0:64, dnm_c: dnm_c + 512], self.bank(6, 512, 0, 64)), em.VS, 1)
            em.wait("V", em.VS, v3)
            v4 = em.do("V", lambda e, bk=bk, j=j, hp=hp: e.tensor_tensor(
                out=A_t[64 * j: 64 * j + 64, HB + hp * 512: HB + (hp + 1) * 512],
                in0=self.bank(bk, 512, 0, 64), in1=dn[0:64, dnm_c: dnm_c + 512],
                op=ALU.mult), em.VS, 1)
            rel = (em.VS, v4)
        return rel

    def emit_wo(self, layer, tq, HB, WR, wotok, wo_rel, sring, comb, dnm_c):
        em = self
        A_t, SC_t, FA_t, W_t = self.A_t, self.SC_t, self.FA_t, self.W_t
        hb_tok = em.VS.n
        if layer == 1:
            gcol, x1r, f1r = comb
            ldx = {}
            ldf = {}
        sr_rel = {}
        for mh in range(2):
            i = tq * 2 + mh
            s = i % 2
            em.dwait("T", wotok[i])
            for ml in range(4):
                m = mh * 4 + ml
                b = (self.pb_rot % 3) * 2
                self.pb_rot += 1
                for bb in (b, b + 1):
                    if self.bank_rel.get(bb) is not None:
                        em.wait("T", self.bank_rel[bb][0], self.bank_rel[bb][1])
                em.wait("T", em.VS, hb_tok)
                tok = None
                for par in range(2):
                    for hh in range(8):
                        j = 2 * hh + par
                        tok = em.do("T", lambda e, s=s, j=j, ml=ml, b=b, par=par, hh=hh: e.matmul(
                            self.bank(b + par),
                            W_t[64 * par: 64 * par + 64,
                                WR[s] + (j // 2) * 512 + ml * 128: WR[s] + (j // 2) * 512 + (ml + 1) * 128],
                            A_t[64 * par: 64 * par + 64, HB + (j // 2) * 512: HB + (j // 2 + 1) * 512],
                            start=(hh == 0), stop=(hh == 7)),
                            em.TS if hh == 7 else None, 1)
                wo_rel[s] = tok
                r = m % 2
                if _WO_MM_ONLY:
                    self.bank_rel[b] = (em.TS, tok)
                    self.bank_rel[b + 1] = (em.TS, tok)
                    continue
                if layer == 0:
                    if ("s", r) in sr_rel:
                        for rt in sr_rel[("s", r)]:
                            em.wait("S", rt[0], rt[1]) if rt[0].name.startswith("d") else em.wait("S", rt[0], rt[1])
                    em.wait("S", em.TS, tok)
                    sv = em.do("S", lambda e, m=m, b=b, r=r: e.activation(
                        out=FA_t[:, sring[r]: sring[r] + 512], in_=self.bank(b),
                        func=AF.Identity, bias=self.smc(SM.bout + m)), em.SS, 1)
                    self.bank_rel[b] = (em.SS, sv)
                    em.wait("V", em.SS, sv)
                    sv2 = em.do("V", lambda e, b=b, r=r: e.tensor_tensor(
                        out=FA_t[:, sring[r]: sring[r] + 512],
                        in0=self.bank(b + 1), in1=FA_t[:, sring[r]: sring[r] + 512],
                        op=ALU.add), em.VS, 1)
                    self.bank_rel[b + 1] = (em.VS, sv2)
                    em.wait("Y", em.VS, sv2)
                    dt = em.dma("Y", lambda e, m=m, tq=tq, r=r: e.dma_start(
                        out=self.s_sp[:, m * TOK + tq * 512: m * TOK + (tq + 1) * 512],
                        in_=FA_t[:, sring[r]: sring[r] + 512]))
                    sr_rel[("s", r)] = [dt, (em.VS, sv2)]
                else:
                    if ("x", r) in sr_rel:
                        em.dwait("Y", sr_rel[("x", r)])
                    ldx[m] = em.dma("Y", lambda e, m=m, tq=tq, r=r: e.dma_start(
                        out=SC_t[:, x1r[r]: x1r[r] + 512],
                        in_=self.xres[:, m * TOK + tq * 512: m * TOK + (tq + 1) * 512]))
                    if ("f", r) in sr_rel:
                        em.wait("Y", sr_rel[("f", r)][0], sr_rel[("f", r)][1])
                    ldf[m] = em.dma("Y", lambda e, m=m, tq=tq, r=r: e.dma_start(
                        out=SC_t[:, f1r[r]: f1r[r] + 512],
                        in_=self.f_sp[:, m * TOK + tq * 512: m * TOK + (tq + 1) * 512]))
                    em.wait("S", em.TS, tok)
                    if ("s", r) in sr_rel:
                        em.wait("S", sr_rel[("s", r)][0], sr_rel[("s", r)][1])
                        em.wait("V", sr_rel[("s", r)][0], sr_rel[("s", r)][1])
                    sv0 = em.do("S", lambda e, m=m, b=b, r=r: e.activation(
                        out=SC_t[:, sring[r]: sring[r] + 512], in_=self.bank(b),
                        func=AF.Identity, bias=self.smc(SM.bo + m)), em.SS, 1)
                    self.bank_rel[b] = (em.SS, sv0)
                    em.wait("V", em.SS, sv0)
                    sv = em.do("V", lambda e, b=b, r=r: e.tensor_tensor(
                        out=SC_t[:, sring[r]: sring[r] + 512],
                        in0=self.bank(b + 1), in1=SC_t[:, sring[r]: sring[r] + 512],
                        op=ALU.add), em.VS, 1)
                    self.bank_rel[b + 1] = (em.VS, sv)
                    em.dwait("S", ldf[m])
                    if ("g", 0) in sr_rel:
                        em.wait("S", sr_rel[("g", 0)][0], sr_rel[("g", 0)][1])
                    gv = em.do("S", lambda e, r=r: e.activation(
                        out=SC_t[:, gcol: gcol + 512], in_=SC_t[:, f1r[r]: f1r[r] + 512],
                        func=AF.Silu), em.SS, 1)
                    sr_rel[("f", r)] = (em.SS, gv)
                    em.wait("V", em.SS, gv)
                    em.wait("V", em.VS, sv)
                    v1 = em.do("V", lambda e, r=r: e.tensor_tensor(
                        out=SC_t[:, sring[r]: sring[r] + 512],
                        in0=SC_t[:, sring[r]: sring[r] + 512],
                        in1=SC_t[:, gcol: gcol + 512], op=ALU.mult), em.VS, 1)
                    sr_rel[("s", r)] = (em.VS, v1)
                    sr_rel[("g", 0)] = (em.VS, v1)
                    em.dwait("V", ldx[m])
                    em.wait("V", em.VS, v1)
                    v2 = em.do("V", lambda e, r=r: e.tensor_tensor(
                        out=SC_t[:, x1r[r]: x1r[r] + 512],
                        in0=SC_t[:, sring[r]: sring[r] + 512],
                        in1=SC_t[:, x1r[r]: x1r[r] + 512], op=ALU.add), em.VS, 1)
                    em.wait("Y", em.VS, v2)
                    ot = em.dma("Y", lambda e, m=m, tq=tq, r=r: e.dma_start(
                        out=self.out_fm[m * 128:(m + 1) * 128, tq * 512:(tq + 1) * 512],
                        in_=SC_t[:, x1r[r]: x1r[r] + 512]))
                    sr_rel[("x", r)] = ot

    def emit_router(self, E, lhsT_fn, rhs_fn, wE_col, l1=None, ntok=TOK):
        em = self
        SC_t = self.SC_t
        NTP = ntok // 128
        for tp in range(NTP):
            em.wait("T", self.bank_rel.get(66, (em.VS, 0))[0],
                    self.bank_rel.get(66, (em.VS, 0))[1])
            tok = None
            for k in range(NCH):
                tok = em.do("T", lambda e, tp=tp, k=k, E=E: e.matmul(
                    self.bank(6, E), lhsT_fn(tp, k), rhs_fn(k),
                    start=(k == 0), stop=(k == NCH - 1)),
                    em.TS if k == NCH - 1 else None, 1)
            C0 = self.SCT + 768
            p_ = SC_t[:, wE_col + tp * E: wE_col + (tp + 1) * E]
            t1c = SC_t[:, C0: C0 + E]
            gate_src = t1c
            m1 = SC_t[:, C0 + 8: C0 + 9]
            m2 = SC_t[:, C0 + 9: C0 + 10]
            m3 = SC_t[:, C0 + 10: C0 + 11]
            em.wait("V", em.TS, tok)
            em.wait("V", em.SS, self.SS.n if False else em.SS.n)
            em.wait("V", em.VS, em.VS.n)
            if l1 is None:
                v = em.do("V", lambda e, p_=p_: e.tensor_tensor(
                    out=t1c, in0=self.bank(6, E), in1=SC_t[:, self.gb_bc_col: self.gb_bc_col + E],
                    op=ALU.add), em.VS, 1)
            else:
                rbc, gbc = l1
                v = em.do("V", lambda e, p_=p_, gate_src=gate_src: e.tensor_tensor(
                    out=t1c, in0=self.bank(6, E), in1=SC_t[:, rbc: rbc + E],
                    op=ALU.add), em.VS, 1)
            self.bank_rel[66] = (em.VS, v)
            em.wait("V", em.VS, v)
            if _ROUTER_RAW == "logits":
                em.do("V", lambda e, p_=p_: e.tensor_copy(p_, t1c), em.VS, 1)
                continue
            if l1 is None:
                v1 = em.do("V", lambda e, p_=p_, gate_src=gate_src: e.tensor_reduce(m1, gate_src, axis=AX.X, op=ALU.max), em.VS, 1)
                em.wait("V", em.VS, v1)
                if _ROUTER_RAW == "logits2":
                    em.do("V", lambda e, p_=p_: e.tensor_copy(p_, gate_src), em.VS, 1)
                    continue
                if _ROUTER_RAW == "max":
                    for jj in range(E):
                        em.do("V", lambda e, p_=p_, jj=jj: e.tensor_copy(
                            p_[:, jj: jj + 1], m1), em.VS, 1)
                    continue
                v2 = em.do("V", lambda e, p_=p_, gate_src=gate_src: e.tensor_scalar(
                    out=m2, in0=m1, scalar1=-1.0, scalar2=None, op0=ALU.mult), em.VS, 1)
                em.wait("S", em.VS, v2)
                s1 = em.do("S", lambda e, p_=p_, gate_src=gate_src: e.activation(
                    out=p_, in_=gate_src, func=AF.Exp, bias=m2, accum_out=m3), em.SS, 1)
                if _ROUTER_RAW == "exp":
                    em.wait("V", em.SS, s1)
                    continue
                if _ROUTER_RAW == "sum":
                    em.wait("V", em.SS, s1)
                    em.do("V", lambda e, p_=p_: e.tensor_scalar(
                        out=p_, in0=m3, scalar1=0.0, scalar2=None, op0=ALU.add), em.VS, 1)
                    continue
                em.wait("V", em.SS, s1)
                v3 = em.do("V", lambda e, p_=p_, gate_src=gate_src: e.reciprocal(m1, m3), em.VS, 1)
                em.wait("V", em.VS, v3)
                v4 = em.do("V", lambda e, p_=p_, gate_src=gate_src: e.tensor_scalar(
                    out=p_, in0=p_, scalar1=m1, scalar2=None, op0=ALU.mult), em.VS, 1)
                if _ROUTER_RAW == "softmax":
                    continue
                em.wait("V", em.VS, v4)
                v5 = em.do("V", lambda e, p_=p_, gate_src=gate_src: e.tensor_reduce(m1, p_, axis=AX.X, op=ALU.max), em.VS, 1)
                em.wait("V", em.VS, v5)
                v6 = em.do("V", lambda e, p_=p_, gate_src=gate_src: e.tensor_scalar(
                    out=t1c, in0=p_, scalar1=m1, scalar2=None, op0=ALU.is_ge), em.VS, 1)
                em.wait("V", em.VS, v6)
                v7 = em.do("V", lambda e, p_=p_, gate_src=gate_src: e.scalar_tensor_tensor(
                    out=t1c, in0=t1c, scalar=-2.0, in1=p_, op0=ALU.mult, op1=ALU.add), em.VS, 1)
                em.wait("V", em.VS, v7)
                v8 = em.do("V", lambda e, p_=p_, gate_src=gate_src: e.tensor_reduce(m2, t1c, axis=AX.X, op=ALU.max), em.VS, 1)
                em.wait("V", em.VS, v8)
                v9 = em.do("V", lambda e, p_=p_, gate_src=gate_src: e.tensor_tensor(out=m3, in0=m1, in1=m2, op=ALU.add), em.VS, 1)
                em.wait("V", em.VS, v9)
                v10 = em.do("V", lambda e, p_=p_, gate_src=gate_src: e.tensor_scalar(
                    out=m3, in0=m3, scalar1=1e-9, scalar2=None, op0=ALU.add), em.VS, 1)
                em.wait("V", em.VS, v10)
                v11 = em.do("V", lambda e, p_=p_, gate_src=gate_src: e.reciprocal(m3, m3), em.VS, 1)
                em.wait("V", em.VS, v11)
                v12 = em.do("V", lambda e, p_=p_, gate_src=gate_src: e.tensor_scalar(
                    out=t1c, in0=p_, scalar1=m2, scalar2=None, op0=ALU.is_ge), em.VS, 1)
                em.wait("V", em.VS, v12)
                v13 = em.do("V", lambda e, p_=p_, gate_src=gate_src: e.tensor_tensor(out=p_, in0=p_, in1=t1c, op=ALU.mult), em.VS, 1)
                em.wait("V", em.VS, v13)
                vfin = em.do("V", lambda e, p_=p_, gate_src=gate_src: e.tensor_scalar(
                    out=p_, in0=p_, scalar1=m3, scalar2=None, op0=ALU.mult), em.VS, 1)
                if _ROUTER_RAW == "snap":
                    em.wait("V", em.VS, vfin)
                    em.do("V", lambda e, p_=p_, tp=tp, E=E: e.tensor_copy(
                        SC_t[:, tp * E: (tp + 1) * E], p_), em.VS, 1)
            else:
                bsd = SC_t[:, C0 + 16: C0 + 16 + E]
                v1 = em.do("V", lambda e, gbc=gbc: e.tensor_tensor(
                    out=bsd, in0=gate_src, in1=SC_t[:, gbc: gbc + E], op=ALU.add), em.VS, 1)
                em.wait("V", em.VS, v1)
                v2 = em.do("V", lambda e, p_=p_, gate_src=gate_src: e.tensor_reduce(m1, bsd, axis=AX.X, op=ALU.max), em.VS, 1)
                em.wait("V", em.VS, v2)
                v3 = em.do("V", lambda e, p_=p_, gate_src=gate_src: e.tensor_scalar(
                    out=p_, in0=bsd, scalar1=m1, scalar2=None, op0=ALU.is_ge), em.VS, 1)
                em.wait("V", em.VS, v3)
                v4 = em.do("V", lambda e, p_=p_, gate_src=gate_src: e.scalar_tensor_tensor(
                    out=bsd, in0=p_, scalar=-1e30, in1=bsd, op0=ALU.mult, op1=ALU.add), em.VS, 1)
                em.wait("V", em.VS, v4)
                v5 = em.do("V", lambda e, p_=p_, gate_src=gate_src: e.tensor_reduce(m2, bsd, axis=AX.X, op=ALU.max), em.VS, 1)
                em.wait("V", em.VS, v5)
                v6 = em.do("V", lambda e, gbc=gbc: e.tensor_tensor(
                    out=bsd, in0=gate_src, in1=SC_t[:, gbc: gbc + E], op=ALU.add), em.VS, 1)
                em.wait("V", em.VS, v6)
                v7 = em.do("V", lambda e, p_=p_, gate_src=gate_src: e.tensor_scalar(
                    out=bsd, in0=bsd, scalar1=m2, scalar2=None, op0=ALU.is_ge), em.VS, 1)
                em.wait("V", em.VS, v7)
                v8 = em.do("V", lambda e, p_=p_, gate_src=gate_src: e.tensor_tensor(out=p_, in0=gate_src, in1=bsd, op=ALU.mult), em.VS, 1)
                em.wait("V", em.VS, v8)
                v9 = em.do("V", lambda e, p_=p_, gate_src=gate_src: e.tensor_scalar(
                    out=bsd, in0=bsd, scalar1=1e30, scalar2=-1e30, op0=ALU.mult, op1=ALU.add), em.VS, 1)
                em.wait("V", em.VS, v9)
                v10 = em.do("V", lambda e, p_=p_, gate_src=gate_src: e.tensor_tensor(out=p_, in0=p_, in1=bsd, op=ALU.add), em.VS, 1)
                em.wait("V", em.VS, v10)
                v11 = em.do("V", lambda e, p_=p_, gate_src=gate_src: e.tensor_reduce(m1, p_, axis=AX.X, op=ALU.max), em.VS, 1)
                em.wait("V", em.VS, v11)
                v12 = em.do("V", lambda e, p_=p_, gate_src=gate_src: e.tensor_scalar(
                    out=m2, in0=m1, scalar1=-1.0, scalar2=None, op0=ALU.mult), em.VS, 1)
                em.wait("S", em.VS, v12)
                s1 = em.do("S", lambda e, p_=p_, gate_src=gate_src: e.activation(
                    out=p_, in_=p_, func=AF.Exp, bias=m2, accum_out=m3), em.SS, 1)
                em.wait("V", em.SS, s1)
                v13 = em.do("V", lambda e, p_=p_, gate_src=gate_src: e.reciprocal(m1, m3), em.VS, 1)
                em.wait("V", em.VS, v13)
                em.do("V", lambda e, p_=p_, gate_src=gate_src: e.tensor_scalar(
                    out=p_, in0=p_, scalar1=m1, scalar2=None, op0=ALU.mult), em.VS, 1)

    def emit_wbc(self, E, wE_col, wT_col, wbc_col, e, ntok=TOK):
        em = self
        SC_t = self.SC_t
        NTP = ntok // 128
        if e == 0:
            for tp in range(NTP):
                em.wait("T", em.VS, em.VS.n)
                tr = em.do("T", lambda e_, tp=tp, E=E: e_.matmul(
                    self.bank(6, 128, 0, E),
                    SC_t[:, wE_col + tp * E: wE_col + (tp + 1) * E],
                    self.identf[:, 0:128], is_transpose=True,
                    start=True, stop=True), em.TS, 1)
                em.wait("V", em.TS, tr)
                vv = em.do("V", lambda e_, tp=tp, E=E: e_.tensor_copy(
                    SC_t[0:E, wT_col + tp * 128: wT_col + (tp + 1) * 128],
                    self.bank(6, 128, 0, E)), em.VS, 1)
                em.wait("T", em.VS, vv)
        for tt in range(ntok // 512):
            em.wait("T", em.VS, em.VS.n)
            t1 = em.do("T", lambda e_, tt=tt, E=E, e=e: e_.matmul(
                self.bank(6, 512, 0, 1), self.identf[0:E, e: e + 1],
                SC_t[0:E, wT_col + tt * 512: wT_col + (tt + 1) * 512],
                start=True, stop=True), em.TS, 1)
            em.wait("V", em.TS, t1)
            v1 = em.do("V", lambda e_, tt=tt: e_.tensor_copy(
                SC_t[0:1, wbc_col + tt * 512: wbc_col + (tt + 1) * 512],
                self.bank(6, 512, 0, 1)), em.VS, 1)
            em.wait("T", em.VS, v1)
            t2 = em.do("T", lambda e_, tt=tt: e_.matmul(
                self.bank(6, 512), self.onesf[0:1, 0:128],
                SC_t[0:1, wbc_col + tt * 512: wbc_col + (tt + 1) * 512],
                start=True, stop=True), em.TS, 1)
            em.wait("V", em.TS, t2)
            v2 = em.do("V", lambda e_, tt=tt: e_.tensor_copy(
                SC_t[:, wbc_col + tt * 512: wbc_col + (tt + 1) * 512],
                self.bank(6, 512)), em.VS, 1)
            em.wait("T", em.VS, v2)
        return em.VS.n

    def emit_ffn(self, layer):
        if layer == 0:
            self.emit_ffn0()
        else:
            self.emit_ffn1()

    def emit_ffn0(self):
        em = self
        A_t, FA_t, SC_t = self.A_t, self.FA_t, self.SC_t
        self.pb_rot = 0
        self.bank_rel = {}
        amark = self.A.mark()
        famark = self.FA.mark()
        H0 = self.QA
        assert self.A.mark() == 32768
        ACC = self.FA.alloc(NCH * TOK)
        WE, WTC, WBC = self.SCT, 0, 2048
        WE = self.SCT + 640
        ring = [self.SCT, None]
        lastv = None
        prev_rd = None
        for c in range(NCH):
            for tt in range(NTT):
                if prev_rd is not None:
                    em.wait("Y", em.VS, prev_rd)
                t = em.dma("Y", lambda e, c=c, tt=tt: e.dma_start(
                    out=SC_t[:, self.SCT: self.SCT + 512],
                    in_=self.x_fm[c * 128:(c + 1) * 128, tt * 512:(tt + 1) * 512]))
                em.dwait("V", t)
                if lastv is not None:
                    em.wait("V", em.VS, lastv)
                dst = slice(ACC + c * TOK + tt * 512, ACC + c * TOK + (tt + 1) * 512)
                v1 = em.do("V", lambda e, dst=dst, tt=tt: e.tensor_tensor(
                    out=FA_t[:, dst], in0=SC_t[:, self.SCT: self.SCT + 512],
                    in1=SC_t[:, self.MBC + tt * 512: self.MBC + (tt + 1) * 512],
                    op=ALU.subtract), em.VS, 1)
                prev_rd = v1
                em.wait("V", em.VS, v1)
                v2 = em.do("V", lambda e, dst=dst, tt=tt: e.tensor_tensor(
                    out=FA_t[:, dst], in0=FA_t[:, dst],
                    in1=SC_t[:, self.RBC + tt * 512: self.RBC + (tt + 1) * 512],
                    op=ALU.mult), em.VS, 1)
                em.wait("V", em.VS, v2)
                lastv = em.do("V", lambda e, dst=dst, c=c: e.tensor_scalar(
                    out=FA_t[:, dst], in0=FA_t[:, dst],
                    scalar1=self.smc(SM.sh_b2 + c), scalar2=None, op0=ALU.add), em.VS, 1)
        init_tok = lastv
        self.gb_bc_col = self.SCT + 768 + 48
        em.wait("T", em.VS, init_tok)
        t1 = em.do("T", lambda e: e.matmul(
            self.bank(6, 8, 0, 1), self.smc(SM.gate_b, 1, 0, 8),
            self.identf[0:8, 0:8], is_transpose=True, start=True, stop=True), em.TS, 1)
        em.wait("V", em.TS, t1)
        v1 = em.do("V", lambda e: e.tensor_copy(
            SC_t[0:1, self.gb_bc_col + 8: self.gb_bc_col + 16], self.bank(6, 8, 0, 1)), em.VS, 1)
        em.wait("T", em.VS, v1)
        t2 = em.do("T", lambda e: e.matmul(
            self.bank(6, 8), self.onesf[0:1, 0:128],
            SC_t[0:1, self.gb_bc_col + 8: self.gb_bc_col + 16],
            start=True, stop=True), em.TS, 1)
        em.wait("V", em.TS, t2)
        v2 = em.do("V", lambda e: e.tensor_copy(
            SC_t[:, self.gb_bc_col: self.gb_bc_col + 8], self.bank(6, 8)), em.VS, 1)
        em.wait("T", em.VS, v2)
        self.emit_router(
            8,
            lambda tp, k: FA_t[:, ACC + k * TOK + tp * 128: ACC + k * TOK + (tp + 1) * 128],
            lambda k: self.smc(SM.gate_w + k * 8, 8),
            WE, l1=None)
        if self.stage and self.stage[0] == "r0":
            if _ROUTER_RAW is None or _ROUTER_RAW is False:
                self.emit_wbc(8, WE, WTC, WBC, 0)
            self.dump_stage([(SC_t, WE, 128, False), (SC_t, WBC, 2048, False),
                             (SC_t, WTC, 2048, False)])
            self._early = True
            return
        parts = _PARTS if _PARTS is not None else {"shared", "experts"}
        for half in range(2):
            tts = [half * 2, half * 2 + 1]
            for e8 in range(9):
                is_sh = e8 == 8
                if is_sh and "shared" not in parts:
                    continue
                if (not is_sh) and "experts" not in parts:
                    continue
                if not is_sh:
                    self.emit_wbc(8, WE, WTC, WBC, e8)
                if is_sh:
                    wsrc1 = lambda m: self.sh_w1[:, m * 128:(m + 1) * 128].rearrange(
                        "(kc p) m -> p kc m", p=128)
                    b1 = SM.sh_b1
                else:
                    wsrc1 = lambda m, e8=e8: self.ew1[e8, :, m * 128:(m + 1) * 128].rearrange(
                        "(kc p) m -> p kc m", p=128)
                    b1 = SM.eb1 + e8 * 32

                def evup(m, tt, b, tok, b1=b1, half=half):
                    em.wait("S", em.TS, tok)
                    ttl = tt - half * 2
                    v = em.do("S", lambda e, m=m, ttl=ttl, b=b: e.activation(
                        out=A_t[:, H0 + m * 1024 + ttl * 512: H0 + m * 1024 + (ttl + 1) * 512],
                        in_=self.bank(b), func=AF.Gelu_apprx_tanh,
                        bias=self.smc(b1 + m)), em.SS, 1)
                    return (em.SS, v)
                em.wait("T", em.VS, init_tok)
                self.emit_proj(32, NCH, wsrc1,
                               lambda k, tt: A_t[:, self.XN + k * TOK + tt * 512: self.XN + k * TOK + (tt + 1) * 512],
                               evup, tts=tts, name=f"ffn0up{e8}")
                if is_sh:
                    wsrc2 = lambda m: self.sh_w2[:, m * 128:(m + 1) * 128].rearrange(
                        "(kc p) m -> p kc m", p=128)
                else:
                    wsrc2 = lambda m, e8=e8: self.ew2[e8, :, m * 128:(m + 1) * 128].rearrange(
                        "(kc p) m -> p kc m", p=128)

                def evdn(m, tt, b, tok, e8=e8, is_sh=is_sh, half=half):
                    ttl = tt - half * 2
                    dst = slice(ACC + m * TOK + tt * 512, ACC + m * TOK + (tt + 1) * 512)
                    em.wait("V", em.TS, tok)
                    em.wait("V", em.VS, em.VS.n)
                    if is_sh:
                        v = em.do("V", lambda e, b=b, dst=dst, m=m: e.scalar_tensor_tensor(
                            out=FA_t[:, dst], in0=self.bank(b),
                            scalar=self.smc(SM.sh_b2 + m), in1=FA_t[:, dst],
                            op0=ALU.add, op1=ALU.add), em.VS, 1)
                    else:
                        v0 = em.do("V", lambda e, b=b, m=m, e8=e8: e.scalar_tensor_tensor(
                            out=SC_t[:, self.SCT: self.SCT + 512], in0=self.bank(b),
                            scalar=self.smc(SM.eb2 + e8 * 8 + m),
                            in1=SC_t[:, WBC + tt * 512: WBC + (tt + 1) * 512],
                            op0=ALU.add, op1=ALU.mult), em.VS, 1)
                        em.wait("V", em.VS, v0)
                        v = em.do("V", lambda e, dst=dst: e.tensor_tensor(
                            out=FA_t[:, dst], in0=SC_t[:, self.SCT: self.SCT + 512],
                            in1=FA_t[:, dst], op=ALU.add), em.VS, 1)
                    return (em.VS, v)
                self.emit_proj(NCH, 32, wsrc2,
                               lambda k, tt, half=half: A_t[:, H0 + k * 1024 + (tt - half * 2) * 512: H0 + k * 1024 + (tt - half * 2 + 1) * 512],
                               evdn, tts=tts, name=f"ffn0dn{e8}",
                               tdeps=[(em.SS, em.SS.n)])
        for c in range(NCH):
            em.wait("Y", em.VS, em.VS.n)
            em.dma("Y", lambda e, c=c: e.dma_start(
                out=self.f_sp[:, c * TOK:(c + 1) * TOK],
                in_=FA_t[:, ACC + c * TOK: ACC + (c + 1) * TOK]))
        self.acc_col = ACC
        self.A.release(amark)
        self.fa_acc_mark = famark

    def emit_ffn1(self):
        em = self
        A_t, FA_t, SC_t, W_t = self.A_t, self.FA_t, self.SC_t, self.W_t
        self.pb_rot = 0
        self.bank_rel = {}
        amark = self.A.mark()
        famark = self.FA.mark()
        wmark1 = self.W.mark()
        UP = self.A.alloc(8192)
        HH = self.A.alloc(4096 * 2)
        XF = self.FA.alloc(8192)
        ACC = self.FA.alloc(8192)
        WE = self.SCT + 640
        WTC, WBC = 0, 2048
        RING = self.SCT
        wtmp = self.W.alloc(1024)
        em.dma("Y", lambda e: e.dma_start(
            out=self.mr_sp[0:1, 0:TOK], in_=SC_t[0:1, self.MBC: self.MBC + TOK]))
        em.dma("Y", lambda e: e.dma_start(
            out=self.mr_sp[0:1, TOK: 2 * TOK], in_=SC_t[0:1, self.RBC: self.RBC + TOK]))
        for half in range(2):
            tts = [half * 2, half * 2 + 1]
            if half == 1:
                t1 = em.dma("Y", lambda e: e.dma_start(
                    out=SC_t[0:1, self.MBC: self.MBC + TOK], in_=self.mr_sp[0:1, 0:TOK]))
                t2 = em.dma("Y", lambda e: e.dma_start(
                    out=SC_t[0:1, self.RBC: self.RBC + TOK], in_=self.mr_sp[0:1, TOK: 2 * TOK]))
                em.dwait("V", t1)
                em.dwait("V", t2)
                em.wait("V", em.VS, em.VS.n)
                rv = em.do("V", lambda e: e.memset(SC_t[:, 5116: 5120], 0.0), em.VS, 1)
                self.bcast_inplace(self.MBC, rv)
                rv2 = em.do("V", lambda e: e.memset(SC_t[:, 5116: 5120], 0.0), em.VS, 1)
                self.bcast_inplace(self.RBC, rv2)
            for it in range(_L1_ITERS):
                lastv = None
                prev_rd = None
                if it == 0:
                    for c in range(NCH):
                        for ttl in range(2):
                            tt = half * 2 + ttl
                            if prev_rd is not None:
                                em.wait("Y", em.VS, prev_rd)
                            t = em.dma("Y", lambda e, c=c, tt=tt: e.dma_start(
                                out=SC_t[:, RING: RING + 512],
                                in_=self.xres[:, c * TOK + tt * 512: c * TOK + (tt + 1) * 512]))
                            em.dwait("V", t)
                            if lastv is not None:
                                em.wait("V", em.VS, lastv)
                            dst = slice(XF + c * 1024 + ttl * 512, XF + c * 1024 + (ttl + 1) * 512)
                            dsta = slice(ACC + c * 1024 + ttl * 512, ACC + c * 1024 + (ttl + 1) * 512)
                            v1 = em.do("V", lambda e, dst=dst, tt=tt: e.tensor_tensor(
                                out=FA_t[:, dst], in0=SC_t[:, RING: RING + 512],
                                in1=SC_t[:, self.MBC + tt * 512: self.MBC + (tt + 1) * 512],
                                op=ALU.subtract), em.VS, 1)
                            prev_rd = v1
                            em.wait("V", em.VS, v1)
                            v2 = em.do("V", lambda e, dst=dst, tt=tt: e.tensor_tensor(
                                out=FA_t[:, dst], in0=FA_t[:, dst],
                                in1=SC_t[:, self.RBC + tt * 512: self.RBC + (tt + 1) * 512],
                                op=ALU.mult), em.VS, 1)
                            em.wait("V", em.VS, v2)
                            lastv = em.do("V", lambda e, dst=dst, dsta=dsta: e.tensor_copy(
                                FA_t[:, dsta], FA_t[:, dst]), em.VS, 1)
                else:
                    em.wait("V", em.VS, em.VS.n)
                    for c in range(NCH):
                        for ttl in range(2):
                            dst = slice(XF + c * 1024 + ttl * 512, XF + c * 1024 + (ttl + 1) * 512)
                            dsta = slice(ACC + c * 1024 + ttl * 512, ACC + c * 1024 + (ttl + 1) * 512)
                            lastv = em.do("V", lambda e, dst=dst, dsta=dsta: e.tensor_copy(
                                FA_t[:, dsta], FA_t[:, dst]), em.VS, 1)
                init_tok = lastv
                self.rb_bc_col = self.SCT + 768 + 32
                self.gb_bc_col = self.SCT + 768 + 40
                for bcol, rowoff in ((self.rb_bc_col, SM.rb_row), (self.gb_bc_col, SM.gb_row)):
                    em.wait("T", em.VS, em.VS.n)
                    tb = em.do("T", lambda e, bcol=bcol, rowoff=rowoff, it=it: e.matmul(
                        self.bank(6, 6), self.onesf[0:1, 0:128],
                        self.small[0:1, rowoff + it * 6: rowoff + it * 6 + 6],
                        start=True, stop=True), em.TS, 1)
                    em.wait("V", em.TS, tb)
                    vb = em.do("V", lambda e, bcol=bcol: e.tensor_copy(
                        SC_t[:, bcol: bcol + 6], self.bank(6, 6)), em.VS, 1)
                    em.wait("T", em.VS, vb)
                em.wait("T", em.VS, init_tok)
                self.emit_router(
                    6,
                    lambda tp, k: FA_t[:, XF + k * 1024 + tp * 128: XF + k * 1024 + (tp + 1) * 128],
                    lambda k, it=it: self.smc(SM.router_W + it * 48 + k * 6, 6),
                    WE, l1=(self.rb_bc_col, self.gb_bc_col), ntok=1024)
                if self.stage and self.stage[0] == "r1":
                    self.emit_wbc(6, WE, WTC, WBC, 0, ntok=1024)
                    self.dump_stage([(SC_t, WE, 48, False), (SC_t, WBC, 2048, False)])
                    self._early = True
                    return
                for e8 in range(8):
                    is_sh = e8 >= 6
                    ns = e8 - 6
                    parts = _L1_PARTS if _L1_PARTS is not None else {"exp", "sh"}
                    if is_sh and "sh" not in parts:
                        continue
                    if (not is_sh) and "exp" not in parts:
                        continue
                    if not is_sh:
                        self.emit_wbc(6, WE, WTC, WBC, e8, ntok=1024)
                    if is_sh:
                        wsrc1 = lambda m, ns=ns: self.share_Wup[:, ns * 1024 + m * 128: ns * 1024 + (m + 1) * 128].rearrange(
                            "(kc p) m -> p kc m", p=128)
                        b1 = SM.share_bup + ns * 8
                    else:
                        wsrc1 = lambda m, e8=e8: self.exp_Wup[e8, :, m * 128:(m + 1) * 128].rearrange(
                            "(kc p) m -> p kc m", p=128)
                        b1 = SM.exp_bup + e8 * 8

                    def evup(m, tt, b, tok, b1=b1, half=half):
                        ttl = tt - half * 2
                        em.wait("S", em.TS, tok)
                        em.wait("S", em.TS, em.TS.n if False else tok)
                        v = em.do("S", lambda e, m=m, ttl=ttl, b=b, b1=b1: e.activation(
                            out=A_t[:, UP + m * 1024 + ttl * 512: UP + m * 1024 + (ttl + 1) * 512],
                            in_=self.bank(b), func=AF.Identity,
                            bias=self.smc(b1 + m)), em.SS, 1)
                        return (em.SS, v)
                    em.wait("T", em.VS, init_tok)
                    self.emit_proj(8, NCH, wsrc1,
                                   lambda k, tt: A_t[:, self.XN + k * TOK + tt * 512: self.XN + k * TOK + (tt + 1) * 512],
                                   evup, tts=tts, name=f"f1up{e8}")
                    if is_sh:
                        wsrc2 = lambda m: self.share_Wg[:, m * 128:(m + 1) * 128].rearrange(
                            "(kc p) m -> p kc m", p=128)
                        bg = SM.share_bg
                    else:
                        wsrc2 = lambda m, e8=e8: self.exp_Wg[e8, :, m * 128:(m + 1) * 128].rearrange(
                            "(kc p) m -> p kc m", p=128)
                        bg = SM.exp_bg + e8 * 8
                    gtmp_rel = {}

                    def evg(m, tt, b, tok, bg=bg, half=half, gtmp_rel=gtmp_rel):
                        ttl = tt - half * 2
                        r = (m * 2 + ttl) % 2
                        em.wait("S", em.TS, tok)
                        if r in gtmp_rel:
                            em.wait("S", gtmp_rel[r][0], gtmp_rel[r][1])
                        sv = em.do("S", lambda e, b=b, r=r, bg=bg, m=m: e.activation(
                            out=W_t[:, wtmp + r * 512: wtmp + (r + 1) * 512],
                            in_=self.bank(b), func=AF.Silu,
                            bias=self.smc(bg + m)), em.SS, 1)
                        em.wait("V", em.SS, sv)
                        vv = em.do("V", lambda e, m=m, ttl=ttl, r=r: e.tensor_tensor(
                            out=A_t[:, HH + m * 1024 + ttl * 512: HH + m * 1024 + (ttl + 1) * 512],
                            in0=W_t[:, wtmp + r * 512: wtmp + (r + 1) * 512],
                            in1=A_t[:, UP + m * 1024 + ttl * 512: UP + m * 1024 + (ttl + 1) * 512],
                            op=ALU.mult), em.VS, 1)
                        gtmp_rel[r] = (em.VS, vv)
                        return (em.VS, vv)
                    self.emit_proj(8, NCH, wsrc2,
                                   lambda k, tt, half=half: A_t[:, UP + k * 1024 + (tt - half * 2) * 512: UP + k * 1024 + (tt - half * 2 + 1) * 512],
                                   evg, tts=tts, name=f"f1g{e8}",
                                   tdeps=[(em.SS, em.SS.n)])
                    if is_sh:
                        wsrc3 = lambda m: self.share_Wdown[:, m * 128:(m + 1) * 128].rearrange(
                            "(kc p) m -> p kc m", p=128)
                        bd = SM.share_bdown
                    else:
                        wsrc3 = lambda m, e8=e8: self.exp_Wdown[e8, :, m * 128:(m + 1) * 128].rearrange(
                            "(kc p) m -> p kc m", p=128)
                        bd = SM.exp_bdown + e8 * 8

                    def evdn(m, tt, b, tok, e8=e8, is_sh=is_sh, bd=bd, half=half):
                        ttl = tt - half * 2
                        dst = slice(ACC + m * 1024 + ttl * 512, ACC + m * 1024 + (ttl + 1) * 512)
                        em.wait("V", em.TS, tok)
                        em.wait("V", em.VS, em.VS.n)
                        if is_sh:
                            v = em.do("V", lambda e, b=b, dst=dst, bd=bd, m=m: e.scalar_tensor_tensor(
                                out=FA_t[:, dst], in0=self.bank(b),
                                scalar=self.smc(bd + m), in1=FA_t[:, dst],
                                op0=ALU.add, op1=ALU.add), em.VS, 1)
                        else:
                            v0 = em.do("V", lambda e, b=b, m=m, bd=bd, ttl=ttl: e.scalar_tensor_tensor(
                                out=SC_t[:, RING: RING + 512], in0=self.bank(b),
                                scalar=self.smc(bd + m),
                                in1=SC_t[:, WBC + ttl * 512: WBC + (ttl + 1) * 512],
                                op0=ALU.add, op1=ALU.mult), em.VS, 1)
                            em.wait("V", em.VS, v0)
                            v = em.do("V", lambda e, dst=dst: e.tensor_tensor(
                                out=FA_t[:, dst], in0=SC_t[:, RING: RING + 512],
                                in1=FA_t[:, dst], op=ALU.add), em.VS, 1)
                        return (em.VS, v)
                    self.emit_proj(8, NCH, wsrc3,
                                   lambda k, tt, half=half: A_t[:, HH + k * 1024 + (tt - half * 2) * 512: HH + k * 1024 + (tt - half * 2 + 1) * 512],
                                   evdn, tts=tts, name=f"f1dn{e8}",
                                   tdeps=[(em.VS, em.VS.n)])
                for b in (0, 1):
                    if self.bank_rel.get(b) is not None:
                        em.wait("T", self.bank_rel[b][0], self.bank_rel[b][1])
                sq_rel = None
                for c in range(NCH):
                    for ttl in range(2):
                        sl = slice(ACC + c * 1024 + ttl * 512, ACC + c * 1024 + (ttl + 1) * 512)
                        em.wait("S", em.VS, em.VS.n)
                        if sq_rel is not None:
                            em.wait("S", em.TS, sq_rel)
                        sq = em.do("S", lambda e, sl=sl: e.activation(
                            out=SC_t[:, RING: RING + 512], in_=FA_t[:, sl],
                            func=AF.Square), em.SS, 1)
                        em.wait("T", em.SS, sq)
                        sq_rel = em.do("T", lambda e, c=c, ttl=ttl: e.matmul(
                            self.PS_t[0:1, ttl * 512: (ttl + 1) * 512],
                            self.onesf[:, 0:1], SC_t[:, RING: RING + 512],
                            start=(c == 0), stop=(c == NCH - 1)), em.TS, 1)
                em.wait("V", em.TS, sq_rel)
                rrow = SC_t[0:1, WBC: WBC + 1024]
                v1 = em.do("V", lambda e, rrow=rrow: e.tensor_scalar(
                    out=rrow, in0=self.PS_t[0:1, 0:1024], scalar1=1.0 / D,
                    scalar2=EPS_RMS, op0=ALU.mult, op1=ALU.add), em.VS, 1)
                em.wait("S", em.VS, v1)
                s1 = em.do("S", lambda e, rrow=rrow: e.activation(
                    out=rrow, in_=rrow, func=AF.Sqrt), em.SS, 1)
                em.wait("V", em.SS, s1)
                v2 = em.do("V", lambda e, rrow=rrow: e.reciprocal(rrow, rrow), em.VS, 1)
                em.wait("T", em.VS, v2)
                for b in (4, 5):
                    if self.bank_rel.get(b) is not None:
                        em.wait("T", self.bank_rel[b][0], self.bank_rel[b][1])
                for ttl in range(2):
                    tb = em.do("T", lambda e, ttl=ttl: e.matmul(
                        self.bank(4 + ttl), self.onesf[0:1, 0:128],
                        SC_t[0:1, WBC + ttl * 512: WBC + (ttl + 1) * 512],
                        start=True, stop=True), em.TS, 1)
                    em.wait("V", em.TS, tb)
                    vb = em.do("V", lambda e, ttl=ttl: e.tensor_copy(
                        SC_t[:, WBC + ttl * 512: WBC + (ttl + 1) * 512],
                        self.bank(4 + ttl)), em.VS, 1)
                    em.wait("T", em.VS, vb)
                lastv = None
                for c in range(NCH):
                    for ttl in range(2):
                        tt = half * 2 + ttl
                        sl = slice(ACC + c * 1024 + ttl * 512, ACC + c * 1024 + (ttl + 1) * 512)
                        dst = slice(XF + c * 1024 + ttl * 512, XF + c * 1024 + (ttl + 1) * 512)
                        em.wait("V", em.VS, em.VS.n)
                        v1 = em.do("V", lambda e, sl=sl, dst=dst, c=c, it=it, ttl=ttl: e.scalar_tensor_tensor(
                            out=FA_t[:, dst], in0=FA_t[:, sl],
                            scalar=self.smc(SM.norm_w + it * 8 + c),
                            in1=SC_t[:, WBC + ttl * 512: WBC + (ttl + 1) * 512],
                            op0=ALU.mult, op1=ALU.mult), em.VS, 1)
                        em.wait("V", em.VS, v1)
                        lastv = em.do("V", lambda e, dst=dst, c=c, tt=tt: e.tensor_copy(
                            A_t[:, self.XN + c * TOK + tt * 512: self.XN + c * TOK + (tt + 1) * 512],
                            FA_t[:, dst]), em.VS, 1)
                if it == _L1_ITERS - 1:
                    for c in range(NCH):
                        em.wait("Y", em.VS, em.VS.n)
                        em.dma("Y", lambda e, c=c, half=half: e.dma_start(
                            out=self.f_sp[:, c * TOK + half * 1024: c * TOK + (half + 1) * 1024],
                            in_=FA_t[:, XF + c * 1024: XF + (c + 1) * 1024]))
        self.W.release(wmark1)
        self.A.release(amark)
        self.FA.release(famark)

    def emit_combine0(self):
        em = self
        FA_t, SC_t = self.FA_t, self.SC_t
        ACC = self.acc_col
        xr = [0, 512]
        sr = [1024, 1536]
        gr = [2048, 2560]
        rel = {}
        for c in range(NCH):
            for tt in range(NTT):
                i = (c * NTT + tt) % 2
                for key, col, srcap in (("x", xr[i], self.x_fm[c * 128:(c + 1) * 128, tt * 512:(tt + 1) * 512]),
                                        ("s", sr[i], self.s_sp[:, c * TOK + tt * 512: c * TOK + (tt + 1) * 512])):
                    if (key + "w", i) in rel:
                        em.wait("Y", rel[(key + "w", i)][0], rel[(key + "w", i)][1])
                    ld = em.dma("Y", lambda e, col=col, srcap=srcap: e.dma_start(
                        out=SC_t[:, col: col + 512], in_=srcap))
                    rel[(key, i, "ld")] = ld
                sl = slice(ACC + c * TOK + tt * 512, ACC + c * TOK + (tt + 1) * 512)
                if ("g", i) in rel:
                    em.wait("S", rel[("g", i)][0], rel[("g", i)][1])
                gv = em.do("S", lambda e, sl=sl, i=i: e.activation(
                    out=SC_t[:, gr[i]: gr[i] + 512], in_=FA_t[:, sl], func=AF.Silu), em.SS, 1)
                em.dwait("V", rel[("s", i, "ld")])
                em.wait("V", em.SS, gv)
                if ("v2", i) in rel:
                    em.wait("V", rel[("v2", i)][0], rel[("v2", i)][1])
                v1 = em.do("V", lambda e, i=i: e.tensor_tensor(
                    out=SC_t[:, sr[i]: sr[i] + 512], in0=SC_t[:, sr[i]: sr[i] + 512],
                    in1=SC_t[:, gr[i]: gr[i] + 512], op=ALU.mult), em.VS, 1)
                rel[("g", i)] = (em.VS, v1)
                rel[("sw", i)] = (em.VS, v1)
                em.dwait("V", rel[("x", i, "ld")])
                em.wait("V", em.VS, v1)
                v2 = em.do("V", lambda e, i=i: e.tensor_tensor(
                    out=SC_t[:, xr[i]: xr[i] + 512], in0=SC_t[:, sr[i]: sr[i] + 512],
                    in1=SC_t[:, xr[i]: xr[i] + 512], op=ALU.add), em.VS, 1)
                em.wait("Y", em.VS, v2)
                ot = em.dma("Y", lambda e, c=c, tt=tt, i=i: e.dma_start(
                    out=self.xres[:, c * TOK + tt * 512: c * TOK + (tt + 1) * 512],
                    in_=SC_t[:, xr[i]: xr[i] + 512]))
                rel[("v2", i)] = ot
                rel[("xw", i)] = ot
        self.FA.release(self.fa_acc_mark)

    def dump_stage(self, pieces):
        em = self
        self.barrier()
        total = sum(p[2] for p in pieces)
        assert total <= self.stage[1]
        off = 0
        for (t, c0, n, isbf) in pieces:
            eng = "G" if isbf else "Y"
            em.dma(eng, lambda e, t=t, c0=c0, n=n, off=off: e.dma_start(
                out=self.dbg[:, off: off + n], in_=t[:, c0: c0 + n],
                max_dma_last_dim=4096))
            off += n
        self.barrier()


def build_program(stage=None):
    b = Builder(stage)
    nc = b.build()
    return nc


def kernel(**inputs):
    stage = _STAGE
    nc = build_program(stage)
    xf = np.asarray(inputs["L0_attn_x"], np.float32).reshape(BATCH * T, D)
    shared = {
        "constb": _pack_consts(),
        "w_qkvT": np.ascontiguousarray(np.asarray(inputs["L0_attn_Wqkv"], np.float32).T),
        "w_outT": np.ascontiguousarray(np.asarray(inputs["L0_attn_Wout"], np.float32).T),
        "sh_w1": np.ascontiguousarray(np.asarray(inputs["L0_ffn_sh_w1"], np.float32)),
        "sh_w2": np.ascontiguousarray(np.asarray(inputs["L0_ffn_sh_w2"], np.float32)),
        "ew1": np.ascontiguousarray(np.asarray(inputs["L0_ffn_ew1"], np.float32)),
        "ew2": np.ascontiguousarray(np.asarray(inputs["L0_ffn_ew2"], np.float32)),
        "w_qT": np.ascontiguousarray(np.asarray(inputs["L1_attn_Wq"], np.float32).T),
        "w_kT": np.ascontiguousarray(np.asarray(inputs["L1_attn_Wk"], np.float32).T),
        "w_vT": np.ascontiguousarray(np.asarray(inputs["L1_attn_Wv"], np.float32).T),
        "w_oT": np.ascontiguousarray(np.asarray(inputs["L1_attn_Wo"], np.float32).T),
        "exp_Wup": np.ascontiguousarray(np.asarray(inputs["L1_ffn_exp_Wup"], np.float32)),
        "exp_Wg": np.ascontiguousarray(np.asarray(inputs["L1_ffn_exp_Wg"], np.float32)),
        "exp_Wdown": np.ascontiguousarray(np.asarray(inputs["L1_ffn_exp_Wdown"], np.float32)),
        "share_Wup": np.ascontiguousarray(np.asarray(inputs["L1_ffn_share_Wup"], np.float32)),
        "share_Wg": np.ascontiguousarray(np.asarray(inputs["L1_ffn_share_Wg"], np.float32)),
        "share_Wdown": np.ascontiguousarray(np.asarray(inputs["L1_ffn_share_Wdown"], np.float32)),
    }
    in_maps = []
    for c in range(NCORE):
        m = dict(shared)
        m["x_fm"] = np.ascontiguousarray(xf[c * TOK:(c + 1) * TOK, :].T)
        m["smallp"] = _pack_small(inputs, c)
        in_maps.append(m)
    res = run_bass_kernel_spmd(nc, in_maps, core_ids=list(range(NCORE)))
    if stage is not None:
        return [np.asarray(r["dbg"]) for r in res.results]
    out = np.empty((BATCH * T, D), np.float32)
    for c in range(NCORE):
        out[c * TOK:(c + 1) * TOK, :] = np.asarray(res.results[c]["out_fm"]).T
    return out.reshape(BATCH, T, D)
